# version 08
# baseline (speedup 1.0000x reference)
import os
import numpy as np
import ml_dtypes
from contextlib import ExitStack
import concourse.bass as bass
import concourse.mybir as mybir
from concourse.bass_utils import run_bass_kernel_spmd
from concourse.alu_op_type import AluOpType as OP
F32 = mybir.dt.float32
BF16 = mybir.dt.bfloat16
AF = mybir.ActivationFunctionType
AX = mybir.AxisListType
D = 1024
T = 2048
BC = 2
NT = BC * T
PB = D // 128
NTT = NT // 512
N_CORES = 8
EPS = 1e-05
ENGINES = ('tensor', 'vector', 'scalar', 'gpsimd', 'sync')
_UID = [0]

def uname(base):
    _UID[0] += 1
    return f'{base}_u{_UID[0]}'

def hview(ap, free_off, dims):
    return bass.AP(ap.tensor, ap.offset + free_off, dims)

def xtile(ap, tt, width=512):
    return hview(ap, tt * PB * width, [[PB * NT, 128], [1, PB * width]])

def xchunk(ap, db, tok):
    return hview(ap, (tok // 512) * PB * 512 + db * 512, [[PB * NT, 128], [1, 512]])

class K:

    def __init__(self, nc, es):
        self.nc = nc
        self.es = es
        self.streams = {e: [] for e in ENGINES}
        self.sems = {}
        self.cnt = {}
        self.eng_sem = {'tensor': 't', 'vector': 'v', 'scalar': 's', 'gpsimd': 'p', 'sync': 'y'}
        self.selfser = ('vector', 'scalar', 'gpsimd')
        self._last = {e: 0 for e in self.selfser}

    def sem(self, name):
        if name not in self.sems:
            self.sems[name] = self.es.enter_context(self.nc.semaphore(name))
            self.cnt[name] = 0
        return self.sems[name]

    def mark(self, name):
        self.sem(name)
        return self.cnt[name]

    def raw(self, eng, build):
        self.streams[eng].append(lambda e, b=build: b(e))

    def op(self, eng, build, dn=1):
        name = self.eng_sem[eng]
        h = self.sem(name)
        if eng in self.selfser and self._last[eng] > 0:
            prev = self._last[eng]
            self.streams[eng].append(lambda e, h=h, prev=prev: e.wait_ge(h, prev))
        self.cnt[name] += dn
        tgt = self.cnt[name]
        if eng in self.selfser:
            self._last[eng] = tgt
        self.streams[eng].append(lambda e, b=build, h=h, dn=dn: b(e).then_inc(h, dn))
        return tgt

    def tmile(self):
        return self.op('tensor', lambda t: t.nop())

    def dma(self, eng, build, slot=None):
        q = {'sync': 'y', 'gpsimd': 'g', 'scalar': 'c'}[eng]
        name = q + ':' + (slot or 'q')
        h = self.sem(name)
        self.cnt[name] += 16
        tgt = self.cnt[name]
        self.streams[eng].append(lambda e, b=build, h=h: b(e).then_inc(h, 16))
        return (name, tgt)

    def wait(self, eng, name, val):
        if val and val > 0:
            h = self.sem(name)
            self.streams[eng].append(lambda e, h=h, val=val: e.wait_ge(h, val))

    def waits(self, eng, marks):
        for name, val in marks.items():
            self.wait(eng, name, val)

    def allmarks(self):
        return {n: self.cnt[n] for n in self.cnt if self.cnt[n] > 0}

    def barrier(self):
        marks = self.allmarks()
        for e in ENGINES:
            self.waits(e, marks)

    def finalize(self, block):
        for e in ENGINES:
            fns = self.streams[e]
            if fns:
                getattr(block, e)(lambda eng, fns=fns: [f(eng) for f in fns])

def emit_consts(k, es, cin):
    nc = k.nc
    c = {}
    c['ones1_bf'] = es.enter_context(nc.sbuf_tensor(uname('c_ones1_bf'), [1, 128], BF16))
    c['ones1_f'] = es.enter_context(nc.sbuf_tensor(uname('c_ones1_f'), [1, 128], F32))
    c['eps_col'] = es.enter_context(nc.sbuf_tensor(uname('c_eps'), [128, 1], F32))
    c['ident_bf'] = es.enter_context(nc.sbuf_tensor(uname('c_ident_bf'), [128, 128], BF16))
    k.op('gpsimd', lambda g, c=c: g.memset(c['ones1_bf'][:], 1.0))
    k.op('gpsimd', lambda g, c=c: g.memset(c['ones1_f'][:], 1.0))
    k.op('gpsimd', lambda g, c=c: g.memset(c['eps_col'][:], EPS))
    nc.const_aps.register(F32, EPS, c['eps_col'][:])
    c['eps2_col'] = es.enter_context(nc.sbuf_tensor(uname('c_eps2'), [128, 1], F32))
    k.op('gpsimd', lambda g, c=c: g.memset(c['eps2_col'][:], EPS * EPS))
    nc.const_aps.register(F32, EPS * EPS, c['eps2_col'][:])
    if 'ident' in cin:
        k.dma('gpsimd', lambda g, c=c, cin=cin: g.dma_start(out=c['ident_bf'][:], in_=cin['ident']), slot='ci')
    k.barrier()
    return c

def emit_ln(k, src_hbm, consts, dst_sb=None, dst_hbm=None, gamma_col=None, beta_col=None, double=False, x_sb_out=None, dual=None, dst_hbm_fn=None):
    nc = k.nc
    ones1_bf = consts['ones1_bf']
    es = ExitStack()
    with es:
        sb = lambda n, s, d, es=es, nc=nc: es.enter_context(nc.sbuf_tensor(uname(n), s, d))
        ps_ = lambda n, s, es=es, nc=nc: es.enter_context(nc.psum_tensor(uname(n), s, F32))
        need_f32 = dual is not None or x_sb_out is not None
        xcol = [sb(f'ln_xcol{i}', [128, PB * 512], F32) for i in range(2)] if need_f32 else [None, None]
        xsq = sb('ln_xsq', [128, PB * 512], BF16)
        xb = [sb(f'ln_xb{i}', [128, PB * 512], BF16) for i in range(2)]
        nrows = 4096 if dual is not None else (3584 if double else 2560)
        rows = sb('ln_rows', [1, nrows], F32)
        crow = [sb(f'ln_crow{i}', [1, 512], BF16) for i in range(2)]
        orow = [sb(f'ln_orow{i}', [1, 512], BF16) for i in range(2)]
        cb_sb = [sb(f'ln_cb{i}', [128, 512], F32) for i in range(2)]
        ob_sb = [sb(f'ln_ob{i}', [128, 512], F32) for i in range(2)]
        if dual is not None:
            xb2 = sb('ln_xb2', [128, PB * 512], F32)
            cb2_sb = [sb(f'ln_cb2{i}', [128, 512], F32) for i in range(2)]
            ob2_sb = [sb(f'ln_ob2{i}', [128, 512], F32) for i in range(2)]
            if 'gw_sb' in dual:
                plg = [ps_(f'ln_plg{i}', [128, 8]) for i in range(2)]
        onesb = sb('ln_onesb', [128, 1], BF16)
        ps = [ps_(f'ln_ps{i}', [1, 512]) for i in range(2)]
        ps2 = [ps_(f'ln_ps2{i}', [1, 512]) for i in range(2)]
        pbc = ps_('ln_pbc', [128, 512])
        pbo = ps_('ln_pbo', [128, 512])
        ms = k.op('gpsimd', lambda g, onesb=onesb: g.memset(onesb[:], 1.0))
        k.wait('tensor', 'p', ms)
        a_sq = [0] * NTT
        a_f32 = [None] * NTT
        a_sums = [0] * NTT
        m_mu = [0] * NTT
        m_vv = [0] * NTT
        b_rm = [0] * NTT
        b_mbc = [0] * NTT
        b_copy = [0] * NTT
        b_v = [0] * NTT
        b_g = [0] * NTT
        b_dv = [0] * NTT
        b_dg = [0] * NTT
        sd = [None] * NTT
        sd32 = [None] * NTT
        lg_t = [0] * NTT
        lgv = {0: 0, 1: 0}
        for tt in range(NTT + 1):
            if tt >= 1:
                bt = tt - 1
                par = bt % 2
                buf = xcol[par]
                mu = rows[0:1, 0:512]
                m2 = rows[0:1, 512:1024]
                var = rows[0:1, 1024:1536]
                s1 = rows[0:1, 1536:2048]
                s2 = rows[0:1, 2048:2560]
                k.wait('vector', 't', a_sums[bt])
                if bt >= 1:
                    k.wait('vector', 't', b_mbc[bt - 1])
                m_mu[bt] = k.op('vector', lambda v, mu=mu, ps=ps, par=par: v.tensor_scalar(out=mu, in0=ps[par][:, :], scalar1=1.0 / D, scalar2=None, op0=OP.mult))
                k.op('vector', lambda v, m2=m2, mu=mu: v.tensor_tensor(out=m2, in0=mu, in1=mu, op=OP.mult))
                vv = k.op('vector', lambda v, var=var, ps2=ps2, m2=m2, par=par: v.scalar_tensor_tensor(out=var, in0=ps2[par][:, :], scalar=1.0 / D, in1=m2, op0=OP.mult, op1=OP.subtract))
                m_vv[bt] = vv
                k.wait('scalar', 'v', vv)
                if bt >= 1:
                    k.wait('scalar', 't', b_mbc[bt - 1])
                if double:
                    rowv = k.op('scalar', lambda s, s1=s1, var=var: s.activation(out=s1, in_=var, func=AF.Sqrt, scale=1.0 + EPS, bias=EPS * EPS))
                else:
                    rowv = k.op('scalar', lambda s, s1=s1, var=var: s.activation(out=s1, in_=var, func=AF.Sqrt, bias=EPS))
                if dual is not None:
                    rowv = k.op('scalar', lambda s, s2=s2, var=var: s.activation(out=s2, in_=var, func=AF.Sqrt, scale=1.0 + EPS, bias=EPS * EPS))
                b_rm[bt] = rowv
            if tt < NTT:
                apar = tt % 2
                if tt >= 2:
                    k.wait('gpsimd', 'v', b_v[tt - 2])
                    k.wait('gpsimd', 'p', b_g[tt - 2])
                    k.wait('gpsimd', 't', a_sums[tt - 2])
                    k.wait('gpsimd', 's', a_sq[tt - 2])
                    if sd[tt - 2]:
                        k.wait('gpsimd', *sd[tt - 2])
                gmark = k.dma('gpsimd', lambda g, tt=tt, par=apar, xb=xb, src_hbm=src_hbm: g.dma_start(out=xb[par][:], in_=xtile(src_hbm, tt)), slot=f'lnb{apar}')
                if need_f32:
                    abuf = xcol[apar]
                    if tt >= 2:
                        k.wait('sync', 'v', b_v[tt - 2])
                        k.wait('sync', 'p', b_g[tt - 2])
                    a_f32[tt] = k.dma('sync', lambda y, tt=tt, buf=abuf, src_hbm=src_hbm: y.dma_start(out=buf[:], in_=xtile(src_hbm, tt)), slot=f'lnx{apar}')
                k.wait('scalar', *gmark)
                if tt >= 1:
                    k.wait('scalar', 't', a_sums[tt - 1])
                sq = k.op('scalar', lambda s, par=apar, xb=xb, xsq=xsq: s.activation(out=xsq[:], in_=xb[par][:], func=AF.Square))
                a_sq[tt] = sq
                k.wait('tensor', *gmark)
                k.wait('tensor', 's', sq)
                if tt >= 2:
                    k.wait('tensor', 'v', m_vv[tt - 2])
                for db in range(PB - 1):
                    k.raw('tensor', lambda t, db=db, ps=ps, onesb=onesb, xb=xb, par=apar: t.matmul(ps[par][:, :], onesb[:, :], xb[par][:, db * 512:(db + 1) * 512], start=db == 0, stop=False))
                k.raw('tensor', lambda t, ps=ps, onesb=onesb, xb=xb, par=apar: t.matmul(ps[par][:, :], onesb[:, :], xb[par][:, (PB - 1) * 512:PB * 512], start=False, stop=True))
                for db in range(PB - 1):
                    k.raw('tensor', lambda t, db=db, ps2=ps2, onesb=onesb, xsq=xsq, par=apar: t.matmul(ps2[par][:, :], onesb[:, :], xsq[:, db * 512:(db + 1) * 512], start=db == 0, stop=False))
                a_sums[tt] = k.op('tensor', lambda t, ps2=ps2, onesb=onesb, xsq=xsq, par=apar: t.matmul(ps2[par][:, :], onesb[:, :], xsq[:, (PB - 1) * 512:PB * 512], start=False, stop=True))
            if tt >= 1:
                bt = tt - 1
                par = bt % 2
                buf = xcol[par]
                rowv = b_rm[bt]
                mu = rows[0:1, 0:512]
                s1 = rows[0:1, 1536:2048]
                s2 = rows[0:1, 2048:2560]
                k.wait('tensor', 's', rowv)
                k.wait('tensor', 'v', m_vv[bt])
                if bt >= 1:
                    k.wait('tensor', 'v', b_copy[bt - 1])
                    if dual is not None:
                        k.wait('tensor', 'v', b_dv[bt - 1])
                        k.wait('tensor', 'p', b_dg[bt - 1])
                k.raw('tensor', lambda t, pbc=pbc, consts=consts, s1=s1: t.matmul(pbc[:, :], consts['ones1_f'][0:1, :], s1, start=True, stop=True))
                mbc = k.op('tensor', lambda t, pbo=pbo, consts=consts, mu=mu: t.matmul(pbo[:, :], consts['ones1_f'][0:1, :], mu, start=True, stop=True))
                b_mbc[bt] = mbc
                k.wait('vector', 't', mbc)
                if bt >= 2:
                    k.wait('vector', 'p', b_g[bt - 2])
                rcp = k.op('vector', lambda v, par=par, cb_sb=cb_sb, pbc=pbc: v.reciprocal(out=cb_sb[par][:], in_=pbc[:, :]))
                if dual is not None:
                    vob = k.op('vector', lambda v, par=par, ob_sb=ob_sb, pbo=pbo: v.tensor_copy(out=ob_sb[par][:], in_=pbo[:, :]))
                    k.wait('tensor', 'v', vob)
                    mbc2 = k.op('tensor', lambda t, pbc=pbc, consts=consts, s2=s2: t.matmul(pbc[:, :], consts['ones1_f'][0:1, :], s2, start=True, stop=True))
                    dv = 0
                    dg = 0
                    if bt >= 1 and sd32[bt - 1]:
                        k.wait('vector', *sd32[bt - 1])
                        k.wait('gpsimd', *sd32[bt - 1])
                    if bt >= 1 and 'gw_sb' in dual:
                        k.wait('gpsimd', 't', lg_t[bt - 1])
                    k.wait('vector', 't', mbc2)
                    k.op('vector', lambda v, par=par, cb2_sb=cb2_sb, pbc=pbc: v.reciprocal(out=cb2_sb[par][:], in_=pbc[:, :]))
                    vd2 = k.op('vector', lambda v, par=par, ob2_sb=ob2_sb, pbo=pbo: v.tensor_copy(out=ob2_sb[par][:], in_=pbo[:, :]))
                    k.wait('gpsimd', 'v', vd2)
                    k.wait('vector', *a_f32[bt])
                    k.wait('gpsimd', *a_f32[bt])
                    for db in range(PB):
                        eng = 'vector' if db < 6 else 'gpsimd'
                        x2s = xb2[:, db * 512:(db + 1) * 512]
                        xcs = buf[:, db * 512:(db + 1) * 512]
                        k.op(eng, lambda v, par=par, xcs=xcs, x2s=x2s, ob2_sb=ob2_sb: v.tensor_tensor(out=x2s, in0=xcs, in1=ob2_sb[par][:], op=OP.subtract))
                        k.op(eng, lambda v, par=par, x2s=x2s, cb2_sb=cb2_sb: v.tensor_tensor(out=x2s, in0=x2s, in1=cb2_sb[par][:], op=OP.mult))
                        dm = k.op(eng, lambda v, x2s=x2s, db=db, dual=dual: v.tensor_scalar(out=x2s, in0=x2s, scalar1=dual['gamma_col'][:, db:db + 1], scalar2=dual['beta_col'][:, db:db + 1], op0=OP.mult, op1=OP.add))
                        if eng == 'vector':
                            dv = dm
                        else:
                            dg = dm
                    b_dv[bt] = dv
                    b_dg[bt] = dg
                if bt >= 2 and sd[bt - 2]:
                    k.wait('vector', *sd[bt - 2])
                    k.wait('gpsimd', *sd[bt - 2])
                apl = 0
                for db in range(PB):
                    eng = 'vector' if dual is None or db < 6 else 'gpsimd'
                    xbs = xb[par][:, db * 512:(db + 1) * 512]
                    if eng == 'gpsimd':
                        k.wait(eng, 'v', rcp)
                        if dual is not None:
                            k.wait(eng, 'v', vob)
                        k.op(eng, lambda v, par=par, xbs=xbs, ob_sb=ob_sb: v.tensor_tensor(out=xbs, in0=xbs, in1=ob_sb[par][:], op=OP.subtract))
                    else:
                        k.op(eng, lambda v, xbs=xbs, pbo=pbo: v.tensor_tensor(out=xbs, in0=xbs, in1=pbo[:, :], op=OP.subtract))
                    m_ = k.op(eng, lambda v, par=par, xbs=xbs, cb_sb=cb_sb: v.tensor_tensor(out=xbs, in0=xbs, in1=cb_sb[par][:], op=OP.mult))
                    if gamma_col is not None:
                        m_ = k.op(eng, lambda v, xbs=xbs, db=db, gamma_col=gamma_col, beta_col=beta_col: v.tensor_scalar(out=xbs, in0=xbs, scalar1=gamma_col[:, db:db + 1], scalar2=beta_col[:, db:db + 1], op0=OP.mult, op1=OP.add))
                    if eng == 'vector':
                        apl = m_
                    if dst_sb is not None:
                        ceng = 'vector' if db < 4 else 'gpsimd'
                        if ceng == 'gpsimd':
                            k.wait(ceng, 'v', m_)
                        k.op(ceng, lambda v, xbs=xbs, db=db, bt=bt, dst_sb=dst_sb: v.tensor_copy(out=dst_sb[:, db * NT + bt * 512:db * NT + bt * 512 + 512], in_=xbs))
                b_copy[bt] = apl
                if x_sb_out is not None:
                    k.wait('vector', *a_f32[bt])
                    k.wait('gpsimd', *a_f32[bt])
                    for db in range(PB):
                        eng = 'vector' if db < 4 else 'gpsimd'
                        xcs = buf[:, db * 512:(db + 1) * 512]
                        k.op(eng, lambda v, xcs=xcs, db=db, bt=bt, x_sb_out=x_sb_out: v.tensor_copy(out=x_sb_out[:, db * NT + bt * 512:db * NT + bt * 512 + 512], in_=xcs))
                b_v[bt] = k.mark('v')
                b_g[bt] = k.mark('p')
                if dual is not None and 'dst32_fn' in dual:
                    k.wait('scalar', 'v', b_dv[bt])
                    k.wait('scalar', 'p', b_dg[bt])
                    if bt >= 1 and sd32[bt - 1]:
                        k.wait('scalar', *sd32[bt - 1])
                    sd32[bt] = k.dma('scalar', lambda s, bt=bt, dual=dual, xb2=xb2: s.dma_start(out=dual['dst32_fn'](bt), in_=xb2[:]), slot='ln32')
                if dual is not None and 'gw_sb' in dual:
                    k.wait('tensor', 'v', b_dv[bt])
                    k.wait('tensor', 'p', b_dg[bt])
                    for ts4 in range(4):
                        qpar = ts4 % 2
                        k.wait('tensor', 'v', lgv[qpar])
                        for db in range(PB - 1):
                            k.raw('tensor', lambda t, db=db, ts4=ts4, qpar=qpar, plg=plg, xb2=xb2, dual=dual: t.matmul(plg[qpar][:, :], xb2[:, db * 512 + ts4 * 128:db * 512 + ts4 * 128 + 128], dual['gw_sb'][:, db * 8:(db + 1) * 8], start=db == 0, stop=False))
                        db = PB - 1
                        mlg = k.op('tensor', lambda t, db=db, ts4=ts4, qpar=qpar, plg=plg, xb2=xb2, dual=dual: t.matmul(plg[qpar][:, :], xb2[:, db * 512 + ts4 * 128:db * 512 + ts4 * 128 + 128], dual['gw_sb'][:, db * 8:(db + 1) * 8], start=False, stop=True))
                        k.wait('vector', 't', mlg)
                        lgv[qpar] = k.op('vector', lambda v, bt=bt, ts4=ts4, qpar=qpar, dual=dual, plg=plg: v.tensor_copy(out=dual['logits_sb'][:, (bt * 4 + ts4) * 8:(bt * 4 + ts4) * 8 + 8], in_=plg[qpar][:, :]))
                        lg_t[bt] = mlg
                if dst_hbm is not None or dst_hbm_fn is not None:
                    k.wait('scalar', 'v', b_v[bt])
                    k.wait('scalar', 'p', b_g[bt])
                    if bt >= 1 and sd[bt - 1]:
                        k.wait('scalar', *sd[bt - 1])
                    dstap = dst_hbm_fn(bt) if dst_hbm_fn is not None else xtile(dst_hbm, bt)
                    sd[bt] = k.dma('scalar', lambda s, dstap=dstap, par=par, xb=xb: s.dma_start(out=dstap, in_=xb[par][:]), slot='lns')
        k.barrier()

def load_weight_bf16(k, w_hbm, dst_sb, rows, cols, slot='w'):
    nb = rows // 128
    src = bass.AP(w_hbm.tensor, w_hbm.offset, [[cols, 128], [128 * cols, nb], [1, cols]])
    return k.dma('gpsimd', lambda g: g.dma_start(out=dst_sb[:, 0:nb * cols], in_=src), slot=slot)

def emit_ssm(k, x_hbm, out_hbm, W, consts):
    nc = k.nc
    es = ExitStack()
    with es:
        sb = lambda n, s, d, es=es, nc=nc: es.enter_context(nc.sbuf_tensor(uname(n), s, d))
        psum = lambda n, s, es=es, nc=nc: es.enter_context(nc.psum_tensor(uname(n), s, F32))
        xn = sb('ssm_xn', [128, PB * NT], BF16)
        Xb = sb('ssm_X', [128, PB * NT], BF16)
        a_sb = sb('ssm_a', [128, PB], F32)
        adma = k.dma('sync', lambda y, a_sb=a_sb, W=W: y.dma_start(out=a_sb[:], in_=W['diagA']), slot='wa')
        emit_ln(k, x_hbm, consts, dst_sb=xn)
        pbu = [psum(f'ssm_pbu{i}', [128, 512]) for i in range(2)]
        py = [psum(f'ssm_py{i}', [128, 512]) for i in range(2)]
        es2 = ExitStack()
        sb2 = lambda n, s, d, es2=es2, nc=nc: es2.enter_context(nc.sbuf_tensor(uname(n), s, d))
        wB = sb2('ssm_wB', [128, PB * D], BF16)
        arep = sb2('ssm_arep', [128, T], F32)
        bu = sb2('ssm_bu', [128, T], F32)
        wl_b = load_weight_bf16(k, W['BT'], wB, D, D, slot='wb')
        k.wait('tensor', *wl_b)
        evac = {0: 0, 1: 0}
        for db in range(PB):
            k.wait('vector', *adma)
            k.op('vector', lambda v, arep=arep: v.memset(arep[:], 1.0))
            k.op('vector', lambda v, db=db, arep=arep, a_sb=a_sb: v.tensor_scalar(out=arep[:], in0=arep[:], scalar1=a_sb[:, db:db + 1], scalar2=None, op0=OP.mult))
            for b in range(BC):
                for q in range(T // 512):
                    tt = b * (T // 512) + q
                    par = tt % 2
                    k.wait('tensor', 'v', evac[par])
                    for mb in range(PB - 1):
                        k.raw('tensor', lambda t, mb=mb, db=db, tt=tt, par=par, pbu=pbu, wB=wB, xn=xn: t.matmul(pbu[par][:, :], wB[:, mb * D + db * 128:mb * D + db * 128 + 128], xn[:, mb * NT + tt * 512:mb * NT + tt * 512 + 512], start=mb == 0, stop=False))
                    mb = PB - 1
                    mbu = k.op('tensor', lambda t, mb=mb, db=db, tt=tt, par=par, pbu=pbu, wB=wB, xn=xn: t.matmul(pbu[par][:, :], wB[:, mb * D + db * 128:mb * D + db * 128 + 128], xn[:, mb * NT + tt * 512:mb * NT + tt * 512 + 512], start=False, stop=True))
                    k.wait('vector', 't', mbu)
                    evac[par] = k.op('vector', lambda v, q=q, par=par, bu=bu, pbu=pbu: v.tensor_copy(out=bu[:, q * 512:(q + 1) * 512], in_=pbu[par][:, :]))
                k.op('vector', lambda v, db=db, b=b, Xb=Xb, arep=arep, bu=bu: v.tensor_tensor_scan(out=Xb[:, db * NT + b * T:db * NT + (b + 1) * T], data0=arep[:], data1=bu[:, :], initial=0.0, op0=OP.mult, op1=OP.add))
        k.barrier()
        es2.close()
        es3 = ExitStack()
        sb3 = lambda n, s, d, es3=es3, nc=nc: es3.enter_context(nc.sbuf_tensor(uname(n), s, d))
        wC = sb3('ssm_wC', [128, PB * D], BF16)
        wD = sb3('ssm_wD', [128, PB * D], BF16)
        xcol = [sb3(f'ssm_xc{i}', [128, PB * 512], F32) for i in range(2)]
        ytile = [sb3(f'ssm_yt{i}', [128, 512], F32) for i in range(2)]
        wl_c = load_weight_bf16(k, W['CT'], wC, D, D, slot='wc')
        wl_d = load_weight_bf16(k, W['DT'], wD, D, D, slot='wd')
        k.wait('tensor', *wl_c)
        k.wait('tensor', *wl_d)
        ydone = {0: 0, 1: 0}
        st = {0: None, 1: None}
        tilev = [0] * NTT
        for tt in range(NTT):
            par = tt % 2
            if tt >= 2:
                k.wait('sync', 'v', tilev[tt - 2])
            xd = k.dma('sync', lambda y, tt=tt, par=par, xcol=xcol, x_hbm=x_hbm: y.dma_start(out=xcol[par][:, :], in_=xtile(x_hbm, tt)), slot=f'yx{par}')
            for db in range(PB):
                ppar = (tt * PB + db) % 2
                k.wait('tensor', 'v', ydone[ppar])
                for nb in range(PB):
                    k.raw('tensor', lambda t, nb=nb, db=db, tt=tt, ppar=ppar, py=py, wC=wC, Xb=Xb: t.matmul(py[ppar][:, :], wC[:, nb * D + db * 128:nb * D + db * 128 + 128], Xb[:, nb * NT + tt * 512:nb * NT + tt * 512 + 512], start=nb == 0, stop=False))
                for mb in range(PB - 1):
                    k.raw('tensor', lambda t, mb=mb, db=db, tt=tt, ppar=ppar, py=py, wD=wD, xn=xn: t.matmul(py[ppar][:, :], wD[:, mb * D + db * 128:mb * D + db * 128 + 128], xn[:, mb * NT + tt * 512:mb * NT + tt * 512 + 512], start=False, stop=False))
                mb = PB - 1
                my = k.op('tensor', lambda t, mb=mb, db=db, tt=tt, ppar=ppar, py=py, wD=wD, xn=xn: t.matmul(py[ppar][:, :], wD[:, mb * D + db * 128:mb * D + db * 128 + 128], xn[:, mb * NT + tt * 512:mb * NT + tt * 512 + 512], start=False, stop=True))
                k.wait('vector', 't', my)
                k.wait('vector', *xd)
                if st[ppar]:
                    k.wait('vector', *st[ppar])
                ydone[ppar] = k.op('vector', lambda v, db=db, par=par, ppar=ppar, ytile=ytile, py=py, xcol=xcol: v.tensor_tensor(out=ytile[ppar][:, :], in0=py[ppar][:, :], in1=xcol[par][:, db * 512:(db + 1) * 512], op=OP.add))
                k.wait('scalar', 'v', ydone[ppar])
                st[ppar] = k.dma('scalar', lambda s, db=db, tt=tt, ppar=ppar, out_hbm=out_hbm, ytile=ytile: s.dma_start(out=xchunk(out_hbm, db, tt * 512), in_=ytile[ppar][:, :]), slot=f'ys{ppar}')
            tilev[tt] = k.mark('v')
        k.barrier()
        es3.close()

def emit_gate_rows(k, logits_sb, wrows_hbm, consts, mode):
    nc = k.nc
    es2 = ExitStack()
    with es2:
        sb = lambda n, s, d, es2=es2, nc=nc: es2.enter_context(nc.sbuf_tensor(uname(n), s, d))
        pwt = es2.enter_context(nc.psum_tensor(uname('gt_pwt'), [8, 128], BF16))
        lg = sb('gt_lg', [128, 8], F32)
        cols = sb('gt_cols', [128, 10], F32)
        e_sb = sb('gt_e', [128, 8], F32)
        w_sb = sb('gt_w', [128, 8], F32)
        wb = sb('gt_wb', [128, 8], BF16)
        ident_bf = consts['ident_bf']
        wrows = sb('gt_wrows', [8, NT], BF16)
        prev_tr = 0
        for tt32 in range(NT // 128):
            cp = k.op('vector', lambda v, tt32=tt32, lg=lg, logits_sb=logits_sb: v.tensor_copy(out=lg[:], in_=logits_sb[:, tt32 * 8:(tt32 + 1) * 8]))
            m1 = cols[:, 0:1]
            nm1 = cols[:, 1:2]
            ssum = cols[:, 2:3]
            m1p = cols[:, 3:4]
            m2p = cols[:, 4:5]
            den = cols[:, 5:6]
            k.op('vector', lambda v, m1=m1, lg=lg: v.tensor_reduce(out=m1, in_=lg[:], axis=AX.X, op=OP.max))
            vm = k.op('vector', lambda v, nm1=nm1, m1=m1: v.tensor_scalar(out=nm1, in0=m1, scalar1=-1.0, scalar2=None, op0=OP.mult))
            if mode == 'renorm':
                k.wait('scalar', 'v', vm)
                se = k.op('scalar', lambda s, e_sb=e_sb, lg=lg, nm1=nm1, ssum=ssum: s.activation(out=e_sb[:], in_=lg[:], func=AF.Exp, bias=nm1, accum_out=ssum))
                k.wait('vector', 's', se)
                k.op('vector', lambda v, den=den, ssum=ssum: v.reciprocal(out=den, in_=ssum))
                k.op('vector', lambda v, w_sb=w_sb, e_sb=e_sb, den=den: v.tensor_scalar(out=w_sb[:], in0=e_sb[:], scalar1=den, scalar2=None, op0=OP.mult))
                k.op('vector', lambda v, m1p=m1p, w_sb=w_sb: v.tensor_reduce(out=m1p, in_=w_sb[:], axis=AX.X, op=OP.max))
                k.op('vector', lambda v, e_sb=e_sb, w_sb=w_sb, m1p=m1p: v.tensor_scalar(out=e_sb[:], in0=w_sb[:], scalar1=m1p, scalar2=None, op0=OP.is_lt))
                k.op('vector', lambda v, e_sb=e_sb, w_sb=w_sb: v.tensor_tensor(out=e_sb[:], in0=e_sb[:], in1=w_sb[:], op=OP.mult))
                k.op('vector', lambda v, m2p=m2p, e_sb=e_sb: v.tensor_reduce(out=m2p, in_=e_sb[:], axis=AX.X, op=OP.max))
                k.op('vector', lambda v, den=den, m1p=m1p, m2p=m2p: v.tensor_scalar(out=den, in0=m1p, scalar1=m2p, scalar2=EPS, op0=OP.add, op1=OP.add))
                k.op('vector', lambda v, den=den: v.reciprocal(out=den, in_=den))
                k.op('vector', lambda v, e_sb=e_sb, w_sb=w_sb, m2p=m2p: v.tensor_scalar(out=e_sb[:], in0=w_sb[:], scalar1=m2p, scalar2=None, op0=OP.is_ge))
                k.op('vector', lambda v, w_sb=w_sb, e_sb=e_sb: v.tensor_tensor(out=w_sb[:], in0=w_sb[:], in1=e_sb[:], op=OP.mult))
                k.op('vector', lambda v, w_sb=w_sb, den=den: v.tensor_scalar(out=w_sb[:], in0=w_sb[:], scalar1=den, scalar2=None, op0=OP.mult))
            else:
                k.op('vector', lambda v, e_sb=e_sb, lg=lg, m1=m1: v.tensor_scalar(out=e_sb[:], in0=lg[:], scalar1=m1, scalar2=None, op0=OP.is_ge))
                k.op('vector', lambda v, w_sb=w_sb, e_sb=e_sb, lg=lg: v.scalar_tensor_tensor(out=w_sb[:], in0=e_sb[:], scalar=-1e+30, in1=lg[:], op0=OP.mult, op1=OP.add))
                m2 = cols[:, 6:7]
                k.op('vector', lambda v, m2=m2, w_sb=w_sb: v.tensor_reduce(out=m2, in_=w_sb[:], axis=AX.X, op=OP.max))
                dcol = cols[:, 7:8]
                vd = k.op('vector', lambda v, dcol=dcol, m2=m2, m1=m1: v.tensor_scalar(out=dcol, in0=m2, scalar1=m1, scalar2=None, op0=OP.subtract))
                s1 = cols[:, 8:9]
                k.wait('scalar', 'v', vd)
                sv1 = k.op('scalar', lambda s, s1=s1, dcol=dcol: s.activation(out=s1, in_=dcol, func=AF.Sigmoid, scale=-1.0))
                k.wait('vector', 's', sv1)
                k.op('vector', lambda v, e_sb=e_sb, s1=s1: v.tensor_scalar(out=e_sb[:], in0=e_sb[:], scalar1=s1, scalar2=None, op0=OP.mult))
                k.op('vector', lambda v, w_sb=w_sb, lg=lg, m2=m2: v.tensor_scalar(out=w_sb[:], in0=lg[:], scalar1=m2, scalar2=None, op0=OP.is_ge))
                k.op('vector', lambda v, lg=lg, m1=m1: v.tensor_scalar(out=lg[:], in0=lg[:], scalar1=m1, scalar2=None, op0=OP.is_lt))
                k.op('vector', lambda v, w_sb=w_sb, lg=lg: v.tensor_tensor(out=w_sb[:], in0=w_sb[:], in1=lg[:], op=OP.mult))
                k.op('vector', lambda v, dcol=dcol, s1=s1: v.tensor_scalar(out=dcol, in0=s1, scalar1=-1.0, scalar2=1.0, op0=OP.mult, op1=OP.add))
                k.op('vector', lambda v, w_sb=w_sb, dcol=dcol: v.tensor_scalar(out=w_sb[:], in0=w_sb[:], scalar1=dcol, scalar2=None, op0=OP.mult))
                k.op('vector', lambda v, w_sb=w_sb, e_sb=e_sb: v.tensor_tensor(out=w_sb[:], in0=w_sb[:], in1=e_sb[:], op=OP.add))
            vw = k.op('vector', lambda v, wb=wb, w_sb=w_sb: v.tensor_copy(out=wb[:], in_=w_sb[:]))
            k.wait('tensor', 'v', vw)
            if prev_tr:
                k.wait('tensor', 'v', prev_tr)
            mtr = k.op('tensor', lambda t, pwt=pwt, wb=wb, ident_bf=ident_bf: t.transpose(pwt[:, :], wb[:, :], ident_bf[:]))
            k.wait('vector', 't', mtr)
            prev_tr = k.op('vector', lambda v, tt32=tt32, wrows=wrows, pwt=pwt: v.tensor_copy(out=wrows[:, tt32 * 128:(tt32 + 1) * 128], in_=pwt[:, :]))
        k.wait('sync', 'v', prev_tr)
        k.dma('sync', lambda y, wrows_hbm=wrows_hbm, wrows=wrows: y.dma_start(out=wrows_hbm, in_=wrows[:]), slot='gw')
        k.barrier()

def emit_moe_swiglu(k, x_hbm, out_hbm, W, consts, scr):
    nc = k.nc
    es = ExitStack()
    with es:
        sb = lambda n, s, d, es=es, nc=nc: es.enter_context(nc.sbuf_tensor(uname(n), s, d))
        lncols = sb('mo_lncols', [128, 2 * PB], F32)
        gwf = sb('mo_gwf', [128, PB * 8], F32)
        logits_sb = sb('mo_logits', [128, NT // 128 * 8], F32)
        cdma = k.dma('sync', lambda y, lncols=lncols, W=W: y.dma_start(out=lncols[:], in_=W['ln_cols']), slot='wa')
        gdma = k.dma('sync', lambda y, gwf=gwf, W=W: y.dma_start(out=gwf[:], in_=hview(W['gwT'], 0, [[8, 128], [128 * 8, PB], [1, 8]])), slot='wb')
        k.wait('vector', *cdma)
        k.wait('scalar', *cdma)
        k.wait('tensor', *gdma)
        emit_ln(k, x_hbm, consts, dst_hbm=scr['xn_hbm'], dual=dict(gamma_col=lncols[:, 0:PB], beta_col=lncols[:, PB:2 * PB], gw_sb=gwf, logits_sb=logits_sb))
        emit_gate_rows(k, logits_sb, scr['wrows_hbm'], consts, mode='renorm')
        k.barrier()
    es = ExitStack()
    with es:
        es_e = ExitStack()
        sb = lambda n, s, d, es_e=es_e, nc=nc: es_e.enter_context(nc.sbuf_tensor(uname(n), s, d))
        psum = lambda n, s, d=F32, es_e=es_e, nc=nc: es_e.enter_context(nc.psum_tensor(uname(n), s, d))
        I = 2048
        JB = I // 128
        wg_sb = sb('mo_wg', [128, PB * I], BF16)
        wu_sb = sb('mo_wu', [128, PB * I], BF16)
        wd_sb = sb('mo_wd', [128, JB * D], BF16)
        h_sb = sb('mo_h', [128, JB * 512], BF16)
        sg = sb('mo_sg', [128, 512], BF16)
        xnc = [sb(f'mo_xnc{i}', [128, PB * 512], BF16) for i in range(2)]
        wrow = sb('mo_wrow', [1, NT], BF16)
        wbc = sb('mo_wbc', [128, NT], F32)
        ytl = [sb(f'mo_ytl{i}', [128, 512], F32) for i in range(2)]
        pg = [psum(f'mo_pg{i}', [128, 512]) for i in range(2)]
        pu = [psum(f'mo_pu{i}', [128, 512]) for i in range(2)]
        pyy = [psum(f'mo_pyy{i}', [128, 512]) for i in range(2)]
        pwb = psum('mo_pwb', [128, 512])
        n_exp = len(W['experts'])
        for ei, ex in enumerate(W['experts']):
            shared = ex.get('shared', False)
            wlg = k.dma('gpsimd', lambda g, ex=ex, wg_sb=wg_sb: g.dma_start(out=wg_sb[:, :], in_=ex['wg']), slot='wg')
            wlu = k.dma('gpsimd', lambda g, ex=ex, wu_sb=wu_sb: g.dma_start(out=wu_sb[:, :], in_=ex['wu']), slot='wu')
            wld = k.dma('gpsimd', lambda g, ex=ex, wd_sb=wd_sb: g.dma_start(out=wd_sb[:, :], in_=ex['wd']), slot='wd')
            if not shared:
                wr = k.dma('sync', lambda y, ex=ex, wrow=wrow: y.dma_start(out=wrow[:, :], in_=ex['wrow']), slot='wr')
                prev = 0
                for q in range(NT // 512):
                    k.wait('tensor', *wr)
                    if prev:
                        k.wait('tensor', 'v', prev)
                    mb = k.op('tensor', lambda t, q=q, pwb=pwb, consts=consts, wrow=wrow: t.matmul(pwb[:, :], consts['ones1_bf'][0:1, :], wrow[0:1, q * 512:(q + 1) * 512], start=True, stop=True))
                    k.wait('vector', 't', mb)
                    prev = k.op('vector', lambda v, q=q, wbc=wbc, pwb=pwb: v.tensor_copy(out=wbc[:, q * 512:(q + 1) * 512], in_=pwb[:, :]))
            k.wait('tensor', *wlg)
            k.wait('tensor', *wlu)
            k.wait('tensor', *wld)
            xl = [None, None]
            hdone = 0
            sdone = {0: 0, 1: 0}
            vdone = {0: 0, 1: 0}
            ydone = {0: 0, 1: 0}
            st = {0: None, 1: None}
            mblk_t = [0] * NTT
            for mblk in range(NTT):
                par = mblk % 2
                if mblk >= 2:
                    k.wait('sync', 't', mblk_t[mblk - 2])
                xl[par] = k.dma('sync', lambda y, mblk=mblk, par=par, xnc=xnc, scr=scr: y.dma_start(out=xnc[par][:, :], in_=xtile(scr['xn_hbm'], mblk)), slot=f'xn{par}')
                for jb in range(JB):
                    jpar = jb % 2
                    k.wait('tensor', *xl[par])
                    k.wait('tensor', 's', sdone[jpar])
                    k.wait('tensor', 'v', vdone[jpar])
                    for mb in range(PB - 1):
                        k.raw('tensor', lambda t, mb=mb, jb=jb, par=par, jpar=jpar, pg=pg, wg_sb=wg_sb, I=I, xnc=xnc: t.matmul(pg[jpar][:, :], wg_sb[:, mb * I + jb * 128:mb * I + jb * 128 + 128], xnc[par][:, mb * 512:(mb + 1) * 512], start=mb == 0, stop=False))
                    mb = PB - 1
                    mg = k.op('tensor', lambda t, mb=mb, jb=jb, par=par, jpar=jpar, pg=pg, wg_sb=wg_sb, I=I, xnc=xnc: t.matmul(pg[jpar][:, :], wg_sb[:, mb * I + jb * 128:mb * I + jb * 128 + 128], xnc[par][:, mb * 512:(mb + 1) * 512], start=False, stop=True))
                    for mb in range(PB - 1):
                        k.raw('tensor', lambda t, mb=mb, jb=jb, par=par, jpar=jpar, pu=pu, wu_sb=wu_sb, I=I, xnc=xnc: t.matmul(pu[jpar][:, :], wu_sb[:, mb * I + jb * 128:mb * I + jb * 128 + 128], xnc[par][:, mb * 512:(mb + 1) * 512], start=mb == 0, stop=False))
                    mb = PB - 1
                    mu_ = k.op('tensor', lambda t, mb=mb, jb=jb, par=par, jpar=jpar, pu=pu, wu_sb=wu_sb, I=I, xnc=xnc: t.matmul(pu[jpar][:, :], wu_sb[:, mb * I + jb * 128:mb * I + jb * 128 + 128], xnc[par][:, mb * 512:(mb + 1) * 512], start=False, stop=True))
                    k.wait('scalar', 't', mg)
                    k.wait('scalar', 'v', hdone)
                    ssg = k.op('scalar', lambda s, jpar=jpar, sg=sg, pg=pg: s.activation(out=sg[:], in_=pg[jpar][:, :], func=AF.Silu))
                    sdone[jpar] = ssg
                    k.wait('vector', 't', mu_)
                    k.wait('vector', 's', ssg)
                    hdone = k.op('vector', lambda v, jb=jb, jpar=jpar, h_sb=h_sb, sg=sg, pu=pu: v.tensor_tensor(out=h_sb[:, jb * 512:(jb + 1) * 512], in0=sg[:], in1=pu[jpar][:, :], op=OP.mult))
                    vdone[jpar] = hdone
                for db in range(PB):
                    ypar = db % 2
                    k.wait('tensor', 'v', hdone)
                    k.wait('tensor', 'v', ydone[ypar])
                    for jb in range(JB - 1):
                        k.raw('tensor', lambda t, jb=jb, db=db, ypar=ypar, pyy=pyy, wd_sb=wd_sb, h_sb=h_sb: t.matmul(pyy[ypar][:, :], wd_sb[:, jb * D + db * 128:jb * D + db * 128 + 128], h_sb[:, jb * 512:(jb + 1) * 512], start=jb == 0, stop=False))
                    jb = JB - 1
                    my = k.op('tensor', lambda t, jb=jb, db=db, ypar=ypar, pyy=pyy, wd_sb=wd_sb, h_sb=h_sb: t.matmul(pyy[ypar][:, :], wd_sb[:, jb * D + db * 128:jb * D + db * 128 + 128], h_sb[:, jb * 512:(jb + 1) * 512], start=False, stop=True))
                    k.wait('vector', 't', my)
                    if st[ypar]:
                        k.wait('vector', *st[ypar])
                    if shared:
                        ydone[ypar] = k.op('vector', lambda v, ypar=ypar, ytl=ytl, pyy=pyy: v.tensor_copy(out=ytl[ypar][:], in_=pyy[ypar][:, :]))
                    else:
                        ydone[ypar] = k.op('vector', lambda v, ypar=ypar, mblk=mblk, ytl=ytl, pyy=pyy, wbc=wbc: v.tensor_tensor(out=ytl[ypar][:], in0=pyy[ypar][:, :], in1=wbc[:, mblk * 512:(mblk + 1) * 512], op=OP.mult))
                    k.wait('gpsimd', 'v', ydone[ypar])
                    dst = xchunk(scr['acc_hbm'], db, mblk * 512)
                    if shared:
                        st[ypar] = k.dma('gpsimd', lambda g, dst=dst, ypar=ypar, ytl=ytl: g.dma_start(out=dst, in_=ytl[ypar][:]), slot=f'ac{ypar}')
                    else:
                        st[ypar] = k.dma('gpsimd', lambda g, dst=dst, ypar=ypar, ytl=ytl: g.dma_start(out=dst, in_=ytl[ypar][:], accum_op=OP.add), slot=f'ac{ypar}')
                mblk_t[mblk] = k.mark('t')
            k.barrier()
        es_e.close()
        emit_add_hbm(k, x_hbm, scr['acc_hbm'], out_hbm, es)
        k.barrier()

def emit_add_hbm(k, a_hbm, b_hbm, out_hbm, es):
    nc = k.nc
    es2 = ExitStack()
    with es2:
        sb = lambda n, s, d, es2=es2, nc=nc: es2.enter_context(nc.sbuf_tensor(uname(n), s, d))
        at = [sb(f'ad_a{i}', [128, PB * 512], F32) for i in range(2)]
        bt = [sb(f'ad_b{i}', [128, PB * 512], F32) for i in range(2)]
        tilev = [0] * NTT
        stm = {0: None, 1: None}
        for tt in range(NTT):
            par = tt % 2
            if tt >= 2:
                k.wait('sync', 'v', tilev[tt - 2])
                if stm[par]:
                    k.wait('sync', *stm[par])
            da = k.dma('sync', lambda y, tt=tt, par=par, at=at, a_hbm=a_hbm: y.dma_start(out=at[par][:], in_=xtile(a_hbm, tt)), slot=f'aa{par}')
            db_ = k.dma('sync', lambda y, tt=tt, par=par, bt=bt, b_hbm=b_hbm: y.dma_start(out=bt[par][:], in_=xtile(b_hbm, tt)), slot=f'ab{par}')
            k.wait('vector', *da)
            k.wait('vector', *db_)
            if stm[par]:
                k.wait('vector', *stm[par])
            tilev[tt] = k.op('vector', lambda v, par=par, at=at, bt=bt: v.tensor_tensor(out=at[par][:], in0=at[par][:], in1=bt[par][:], op=OP.add))
            k.wait('scalar', 'v', tilev[tt])
            stm[par] = k.dma('scalar', lambda s, tt=tt, par=par, out_hbm=out_hbm, at=at: s.dma_start(out=xtile(out_hbm, tt), in_=at[par][:]), slot=f'ao{par}')
        k.barrier()

def emit_attn(k, x_hbm, out_hbm, W, consts, scr):
    nc = k.nc
    NH, HD, LAT, HG = (16, 64, 512, 4)
    NG = NH // HG
    LB = LAT // 128
    NT8 = T // 128
    NS4 = T // 512
    emit_ln(k, x_hbm, consts, dst_hbm=scr['xn_hbm'])
    es = ExitStack()
    with es:
        sb = lambda n, s, d, es=es, nc=nc: es.enter_context(nc.sbuf_tensor(uname(n), s, d))
        rope = sb('at_rope', [128, 4 * T], BF16)
        p64 = sb('at_p64', [128, 128], F32)
        m2 = sb('at_m2', [128, 4 * 512], BF16)
        rdma = k.dma('gpsimd', lambda g, rope=rope, W=W: g.dma_start(out=rope[:], in_=W['rope']), slot='wm2')
        pdma = k.dma('sync', lambda y, p64=p64, W=W: y.dma_start(out=p64[:], in_=W['p64']), slot='wa')
        mdma = k.dma('gpsimd', lambda g, m2=m2, W=W: g.dma_start(out=m2[:], in_=W['m2']), slot='wm')
        cosq, sinq = (rope[:, 0:T], rope[:, T:2 * T])
        cosk, sink = (rope[:, 2 * T:3 * T], rope[:, 3 * T:4 * T])
        xn_b = sb('at_xn', [128, PB * T], BF16)
        lat_sb = sb('at_lat', [128, LB * T], BF16)
        qpad = sb('at_qpad', [65, HG * T], BF16)
        kpad = sb('at_kpad', [65, HG * T], BF16)
        VW = NT8 * (HG * (HD + 1))
        vtok = sb('at_vtok', [128, VW], BF16)
        o_all = sb('at_oall', [128, PB * T], BF16)
        qraw = sb('at_qraw', [128, 512], F32)
        ropet = sb('at_ropet', [128, 512], F32)
        swt = sb('at_swt', [128, 512], F32)
        mx = sb('at_mx', [128, 2], F32)
        nmaxc = sb('at_nmax', [128, 1], BF16)
        pt_sb = sb('at_pt', [128, NT8 * 512], BF16)
        otmp = sb('at_otmp', [64, 512], F32)
        rrow = sb('at_rrow', [1, 512], BF16)
        rcp = sb('at_rcp', [1, 512], F32)
        wsl = sb('at_wsl', [128, PB * D], BF16)
        wv_sl = sb('at_wvsl', [128, LB * 256], BF16)
        xcol = [sb('at_xc0', [128, PB * 512], F32)]
        ytl = [sb(f'at_ytl{i}', [128, 512], F32) for i in range(2)]
        for b in range(BC):
            es2 = ExitStack()
            with es2:
                ps = lambda n, s, d=F32, es2=es2, nc=nc: es2.enter_context(nc.psum_tensor(uname(n), s, d))
                pp = [ps(f'at_pp{i}', [128, 512]) for i in range(2)]
                psw = [ps(f'at_psw{i}', [128, 512]) for i in range(2)]
                prow = ps('at_prow', [1, 128])
                pO = ps('at_pO', [65, 512])
                prc = ps('at_prc', [64, 512])
                for tq in range(4):
                    xbl = k.dma('sync', lambda y, b=b, tq=tq, xn_b=xn_b, scr=scr: y.dma_start(out=bass.AP(xn_b[:, :].tensor, xn_b[:, :].offset + tq * 512, [[PB * T, 128], [T, PB], [1, 512]]), in_=xtile(scr['xn_hbm'], b * 4 + tq)), slot='xb')
                k.wait('tensor', *xbl)
                wl = k.dma('gpsimd', lambda g, wsl=wsl, LAT=LAT, W=W: g.dma_start(out=wsl[:, 0:PB * LAT], in_=hview(W['wkvT'], 0, [[LAT, 128], [128 * LAT, PB], [1, LAT]])), slot='wg')
                k.wait('tensor', *wl)
                evac = {0: 0, 1: 0}
                for lb in range(LB):
                    for tt in range(NS4):
                        par = (lb * NS4 + tt) % 2
                        k.wait('tensor', 'v', evac[par])
                        for mb in range(PB):
                            fn = lambda t, mb=mb, lb=lb, tt=tt, par=par, pp=pp, wsl=wsl, LAT=LAT, xn_b=xn_b: t.matmul(pp[par][:, :], wsl[:, mb * LAT + lb * 128:mb * LAT + lb * 128 + 128], xn_b[:, mb * T + tt * 512:mb * T + tt * 512 + 512], start=mb == 0, stop=mb == PB - 1)
                            if mb < PB - 1:
                                k.raw('tensor', fn)
                            else:
                                mm = k.op('tensor', fn)
                        k.wait('vector', 't', mm)
                        evac[par] = k.op('vector', lambda v, lb=lb, tt=tt, par=par, lat_sb=lat_sb, pp=pp: v.tensor_copy(out=lat_sb[:, lb * T + tt * 512:lb * T + tt * 512 + 512], in_=pp[par][:, :]))
                msk = k.op('gpsimd', lambda g, kpad=kpad: g.memset(kpad[64:65, :], 1.0))
                msv = k.op('gpsimd', lambda g, vtok=vtok: g.memset(vtok[:], 1.0))
                k.wait('vector', *rdma)
                k.wait('vector', *pdma)
                k.wait('vector', *mdma)
                wsl_t = k.mark('t')
                wv_t = 0
                for grp in range(NG):
                    k.wait('gpsimd', 't', wsl_t)
                    wq = k.dma('gpsimd', lambda g, grp=grp, wsl=wsl, W=W, NH=NH, HD=HD: g.dma_start(out=wsl[:, 0:PB * 256], in_=hview(W['wqT'], grp * 256, [[NH * HD, 128], [128 * NH * HD, PB], [1, 256]])), slot='wg')
                    wk = k.dma('gpsimd', lambda g, grp=grp, wsl=wsl, LB=LB, W=W, NH=NH, HD=HD: g.dma_start(out=wsl[:, PB * 256:PB * 256 + LB * 256], in_=hview(W['wkupT'], grp * 256, [[NH * HD, 128], [128 * NH * HD, LB], [1, 256]])), slot='wu')
                    k.wait('gpsimd', 't', wv_t)
                    wv = k.dma('gpsimd', lambda g, grp=grp, wv_sl=wv_sl, LB=LB, W=W, NH=NH, HD=HD: g.dma_start(out=wv_sl[:, 0:LB * 256], in_=hview(W['wvupT'], grp * 256, [[NH * HD, 128], [128 * NH * HD, LB], [1, 256]])), slot='wd')
                    for qk in range(2):
                        k.wait('tensor', *(wq if qk == 0 else wk))
                        nin = PB if qk == 0 else LB
                        src = xn_b if qk == 0 else lat_sb
                        woff = 0 if qk == 0 else PB * 256
                        cos = cosq if qk == 0 else cosk
                        sin = sinq if qk == 0 else sink
                        dstpad = qpad if qk == 0 else kpad
                        for pt2 in range(2):
                            evq = k.mark('v')
                            for tt in range(NS4):
                                par = tt % 2
                                k.wait('tensor', 'v', evq)
                                for mb in range(nin):
                                    fn = lambda t, mb=mb, pt2=pt2, tt=tt, par=par, src=src, woff=woff, nin=nin, pp=pp, wsl=wsl: t.matmul(pp[par][:, :], wsl[:, woff + mb * 256 + pt2 * 128:woff + mb * 256 + pt2 * 128 + 128], src[:, mb * T + tt * 512:mb * T + tt * 512 + 512], start=mb == 0, stop=mb == nin - 1)
                                    if mb < nin - 1:
                                        k.raw('tensor', fn)
                                    else:
                                        mm = k.op('tensor', fn)
                                k.wait('vector', 't', mm)
                                evq = k.op('vector', lambda v, par=par, qraw=qraw, pp=pp: v.tensor_copy(out=qraw[:], in_=pp[par][:, :]))
                                k.wait('tensor', 'v', evq)
                                msw = k.op('tensor', lambda t, par=par, psw=psw, p64=p64, qraw=qraw: t.matmul(psw[par][:, :], p64[:, :], qraw[:, :], start=True, stop=True))
                                k.op('vector', lambda v, cos=cos, tt=tt, ropet=ropet, qraw=qraw: v.tensor_tensor(out=ropet[:], in0=qraw[:], in1=cos[:, tt * 512:(tt + 1) * 512], op=OP.mult))
                                k.wait('vector', 't', msw)
                                k.op('vector', lambda v, par=par, sin=sin, tt=tt, swt=swt, psw=psw: v.tensor_tensor(out=swt[:], in0=psw[par][:, :], in1=sin[:, tt * 512:(tt + 1) * 512], op=OP.mult))
                                for hh in range(2):
                                    hl = pt2 * 2 + hh
                                    base = hh * 64
                                    evq = k.op('vector', lambda v, dstpad=dstpad, hl=hl, tt=tt, base=base, ropet=ropet, swt=swt: v.tensor_tensor(out=dstpad[0:64, hl * T + tt * 512:hl * T + tt * 512 + 512], in0=ropet[base:base + 64, :], in1=swt[base:base + 64, :], op=OP.add))
                    wsl_t = k.mark('t')
                    k.wait('tensor', *wv)
                    k.wait('tensor', 'p', msv)
                    evv = {0: k.mark('v'), 1: k.mark('v')}
                    for tt8 in range(NT8):
                        par = tt8 % 2
                        k.wait('tensor', 'v', evv[par])
                        for lb in range(LB):
                            fn = lambda t, lb=lb, tt8=tt8, par=par, pp=pp, lat_sb=lat_sb, wv_sl=wv_sl, LB=LB: t.matmul(pp[par][:, 0:256], lat_sb[:, lb * T + tt8 * 128:lb * T + tt8 * 128 + 128], wv_sl[:, lb * 256:(lb + 1) * 256], start=lb == 0, stop=lb == LB - 1)
                            if lb < LB - 1:
                                k.raw('tensor', fn)
                            else:
                                mm = k.op('tensor', fn)
                        k.wait('vector', 't', mm)
                        dst = bass.AP(vtok, tt8 * (HG * (HD + 1)), [[VW, 128], [HD + 1, HG], [1, HD]])
                        evv[par] = k.op('vector', lambda v, dst=dst, par=par, pp=pp: v.tensor_copy(out=dst, in_=pp[par][:, 0:256]))
                    wv_t = k.mark('t')
                    k.wait('tensor', 'p', msk)
                    for hh in range(HG):
                        head = grp * HG + hh
                        hl = hh
                        hv = hh * (HD + 1)
                        rowv = k.mark('v')
                        for si in range(NT8):
                            ntj = si // 4 + 1
                            redv = 0
                            for tj in range(ntj):
                                par = tj % 2
                                k.wait('tensor', 'v', redv if tj >= 2 else rowv)
                                mm = k.op('tensor', lambda t, hl=hl, si=si, tj=tj, par=par, pp=pp, qpad=qpad, kpad=kpad: t.matmul(pp[par][:, :], qpad[0:64, hl * T + si * 128:hl * T + si * 128 + 128], kpad[0:64, hl * T + tj * 512:hl * T + tj * 512 + 512], start=True, stop=True))
                                k.wait('vector', 't', mm)
                                if tj == 0:
                                    redv = k.op('vector', lambda v, par=par, mx=mx, pp=pp: v.tensor_reduce(out=mx[:, 0:1], in_=pp[par][:, :], axis=AX.X, op=OP.max))
                                else:
                                    k.op('vector', lambda v, par=par, mx=mx, pp=pp: v.tensor_reduce(out=mx[:, 1:2], in_=pp[par][:, :], axis=AX.X, op=OP.max))
                                    redv = k.op('vector', lambda v, mx=mx: v.tensor_tensor(out=mx[:, 0:1], in0=mx[:, 0:1], in1=mx[:, 1:2], op=OP.max))
                            k.op('vector', lambda v, nmaxc=nmaxc, mx=mx: v.tensor_scalar(out=nmaxc[:], in0=mx[:, 0:1], scalar1=-1.0, scalar2=None, op0=OP.mult))
                            nv = k.mark('v')
                            k.wait('tensor', 'v', nv)
                            mrow = k.op('tensor', lambda t, prow=prow, nmaxc=nmaxc, consts=consts: t.matmul(prow[:, :], nmaxc[:, :], consts['ident_bf'][:, :], start=True, stop=True))
                            k.wait('vector', 't', mrow)
                            rowv = k.op('vector', lambda v, hl=hl, si=si, qpad=qpad, prow=prow: v.tensor_copy(out=qpad[64:65, hl * T + si * 128:hl * T + si * 128 + 128], in_=prow[:, :]))
                        for S4 in range(NS4):
                            nti = (S4 + 1) * 4
                            k.wait('tensor', 'v', rowv)
                            exv = {0: 0, 1: 0}
                            for ti in range(nti):
                                par = ti % 2
                                k.wait('tensor', 's', exv[par])
                                mm = k.op('tensor', lambda t, hl=hl, S4=S4, ti=ti, par=par, pp=pp, kpad=kpad, qpad=qpad: t.matmul(pp[par][:, :], kpad[0:65, hl * T + ti * 128:hl * T + ti * 128 + 128], qpad[0:65, hl * T + S4 * 512:hl * T + S4 * 512 + 512], start=True, stop=True))
                                k.wait('scalar', 't', mm)
                                exv[par] = k.op('scalar', lambda s, ti=ti, par=par, pt_sb=pt_sb, pp=pp: s.activation(out=pt_sb[:, ti * 512:(ti + 1) * 512], in_=pp[par][:, :], func=AF.Exp))
                                rel = ti * 128 - S4 * 512
                                if 0 <= rel < 512:
                                    k.wait('vector', 's', exv[par])
                                    k.op('vector', lambda v, ti=ti, rel=rel, pt_sb=pt_sb, m2=m2: v.tensor_tensor(out=pt_sb[:, ti * 512:(ti + 1) * 512], in0=pt_sb[:, ti * 512:(ti + 1) * 512], in1=m2[:, rel // 128 * 512:rel // 128 * 512 + 512], op=OP.mult))
                            mskv = k.mark('v')
                            sse = k.mark('s')
                            k.wait('tensor', 'v', mskv)
                            k.wait('tensor', 's', sse)
                            for ti in range(nti):
                                fn = lambda t, hv=hv, ti=ti, nti=nti, pO=pO, vtok=vtok, HG=HG, HD=HD, VW=VW, pt_sb=pt_sb: t.matmul(pO[0:65, :], bass.AP(vtok, ti * (HG * (HD + 1)) + hv, [[VW, 128], [1, HD + 1]]), pt_sb[:, ti * 512:(ti + 1) * 512], start=ti == 0, stop=ti == nti - 1)
                                if ti < nti - 1:
                                    k.raw('tensor', fn)
                                else:
                                    mmo = k.op('tensor', fn)
                            k.wait('vector', 't', mmo)
                            cpv = k.op('vector', lambda v, otmp=otmp, pO=pO: v.tensor_copy(out=otmp[:], in_=pO[0:64, :]))
                            k.op('vector', lambda v, rcp=rcp, pO=pO: v.reciprocal(out=rcp[:], in_=pO[64:65, :]))
                            rv = k.op('vector', lambda v, rrow=rrow, rcp=rcp: v.tensor_copy(out=rrow[:], in_=rcp[:]))
                            k.wait('tensor', 'v', rv)
                            mrb = k.op('tensor', lambda t, prc=prc, consts=consts, rrow=rrow: t.matmul(prc[:, :], consts['ones1_bf'][0:1, 0:64], rrow[0:1, :], start=True, stop=True))
                            k.wait('vector', 't', mrb)
                            db0, base = (head // 2, head % 2 * 64)
                            rowv = k.op('vector', lambda v, db0=db0, base=base, S4=S4, head=head, o_all=o_all, otmp=otmp, prc=prc: v.tensor_tensor(out=o_all[base:base + 64, db0 * T + S4 * 512:db0 * T + S4 * 512 + 512], in0=otmp[:], in1=prc[0:64, :], op=OP.mult))
                k.wait('gpsimd', 't', k.mark('t'))
                wo = k.dma('gpsimd', lambda g, wsl=wsl, W=W: g.dma_start(out=wsl[:, 0:PB * D], in_=hview(W['woT'], 0, [[D, 128], [128 * D, PB], [1, D]])), slot='wg')
                k.wait('tensor', *wo)
                allv = k.mark('v')
                k.wait('tensor', 'v', allv)
                ydone = {0: 0, 1: 0}
                st = {0: None, 1: None}
                for tt in range(NS4):
                    xpar = 0
                    if tt >= 1:
                        k.wait('sync', 'v', ydone[0])
                        k.wait('sync', 'v', ydone[1])
                    xd = k.dma('sync', lambda y, b=b, tt=tt, xpar=xpar, xcol=xcol, x_hbm=x_hbm, NS4=NS4: y.dma_start(out=xcol[xpar][:, :], in_=xtile(x_hbm, b * NS4 + tt)), slot=f'yx{xpar}')
                    for db in range(PB):
                        par = (tt * PB + db) % 2
                        k.wait('tensor', 'v', ydone[par])
                        for qb in range(PB):
                            fn = lambda t, qb=qb, db=db, tt=tt, par=par, pp=pp, wsl=wsl, o_all=o_all: t.matmul(pp[par][:, :], wsl[:, qb * D + db * 128:qb * D + db * 128 + 128], o_all[:, qb * T + tt * 512:qb * T + tt * 512 + 512], start=qb == 0, stop=qb == PB - 1)
                            if qb < PB - 1:
                                k.raw('tensor', fn)
                            else:
                                mm = k.op('tensor', fn)
                        k.wait('vector', 't', mm)
                        k.wait('vector', *xd)
                        if st[par]:
                            k.wait('vector', *st[par])
                        ydone[par] = k.op('vector', lambda v, db=db, par=par, xpar=xpar, ytl=ytl, pp=pp, xcol=xcol: v.tensor_tensor(out=ytl[par][:], in0=pp[par][:, :], in1=xcol[xpar][:, db * 512:(db + 1) * 512], op=OP.add))
                        k.wait('scalar', 'v', ydone[par])
                        st[par] = k.dma('scalar', lambda s, b=b, db=db, tt=tt, par=par, out_hbm=out_hbm, ytl=ytl: s.dma_start(out=xchunk(out_hbm, db, b * T + tt * 512), in_=ytl[par][:]), slot=f'ys{par}')
            k.barrier()
        k.barrier()

def emit_rwkv(k, x_hbm, out_hbm, W, consts, scr):
    nc = k.nc
    L, F = (4, 4096)
    TP = T + 1
    RW0, SRC0 = (0, 8 * NT)
    es = ExitStack()
    with es:
        sb = lambda n, s, d, es=es, nc=nc: es.enter_context(nc.sbuf_tensor(uname(n), s, d))
        cols = sb('rw_cols', [128, 368], F32)
        lamu = sb('rw_lamu', [128, 24], F32)
        zcol = sb('rw_zcol', [128, PB * BC], BF16)
        cdma = k.dma('sync', lambda y, cols=cols, W=W: y.dma_start(out=cols[:], in_=W['cols']), slot='wa')
        mz = k.op('gpsimd', lambda g, zcol=zcol: g.memset(zcol[:], 0.0))
        k.wait('gpsimd', 'p', mz)
        zd = k.dma('gpsimd', lambda g, scr=scr, TP=TP, zcol=zcol: g.dma_start(out=hview(scr['h_hbm'], 0, [[PB * (NT + 2), 128], [NT + 2, PB], [TP, BC]]), in_=zcol[:, :], allow_slow_non_contiguous=True), slot='zp')
        k.wait('vector', *cdma)
        k.wait('scalar', *cdma)
        emit_ln(k, x_hbm, consts, dual=dict(gamma_col=cols[:, 0:PB], beta_col=cols[:, PB:2 * PB], dst32_fn=lambda tt, scr=scr: xtile(scr['xr_hbm'], tt)))
        k.wait('sync', *zd)
        k.barrier()

        def hpad_fn(tt):
            b, q = (tt // 4, tt % 4)
            return hview(scr['h_hbm'], b * TP + 1 + q * 512, [[PB * (NT + 2), 128], [NT + 2, PB], [1, 512]])
        for l in range(L):
            lw = lambda base, cols=cols, l=l: cols[:, base + l * PB:base + (l + 1) * PB]
            sc1 = k.op('scalar', lambda s, lamu=lamu, lw=lw: s.activation(out=lamu[:, 16:24], in_=lw(304), func=AF.Exp))
            k.wait('vector', 's', sc1)
            vv1 = k.op('vector', lambda v, lamu=lamu: v.tensor_scalar(out=lamu[:, 16:24], in0=lamu[:, 16:24], scalar1=-1.0, scalar2=None, op0=OP.mult))
            k.wait('scalar', 'v', vv1)
            k.op('scalar', lambda s, lamu=lamu: s.activation(out=lamu[:, 0:8], in_=lamu[:, 16:24], func=AF.Exp))
            sc2 = k.op('scalar', lambda s, lamu=lamu, lw=lw: s.activation(out=lamu[:, 8:16], in_=lw(336), func=AF.Exp))
            emit_ln(k, scr['xr_hbm'], consts, gamma_col=lw(16), beta_col=lw(48), dst_hbm_fn=hpad_fn)
            for b in range(BC):
                es2 = ExitStack()
                sb2 = lambda n, s, d, es2=es2, nc=nc: es2.enter_context(nc.sbuf_tensor(uname(n), s, d))
                ps2 = lambda n, s, es2=es2, nc=nc: es2.enter_context(nc.psum_tensor(uname(n), s, F32))
                ek = sb2('rw_ek', [128, 8 * TP], BF16)
                ekv = sb2('rw_ekv', [128, 8 * TP], BF16)
                srb = sb2('rw_sr', [128, 8 * T], BF16)
                hcT = sb2('rw_hc', [128, PB * 512], BF16)
                hpT = sb2('rw_hp', [128, PB * 512], BF16)
                dT_ = sb2('rw_d', [128, PB * 512], BF16)
                xk_t = sb2('rw_xk', [128, PB * 512], BF16)
                xv_t = sb2('rw_xv', [128, PB * 512], BF16)
                xr_t = sb2('rw_xr', [128, PB * 512], BF16)
                vtmp = sb2('rw_vtmp', [128, 512], F32)
                rwt = sb2('rw_rwt', [128, T], BF16)
                ppT = [ps2(f'rw_pp{i}', [128, 512]) for i in range(2)]
                msz = 0
                for ob in range(8):
                    k.op('vector', lambda v, ob=ob, ek=ek, TP=TP: v.memset(ek[:, ob * TP:ob * TP + 1], 0.0))
                    msz = k.op('vector', lambda v, ob=ob, ekv=ekv, TP=TP: v.memset(ekv[:, ob * TP:ob * TP + 1], 0.0))
                es3 = ExitStack()
                sb3 = lambda n, s, d, es3=es3, nc=nc: es3.enter_context(nc.sbuf_tensor(uname(n), s, d))
                wk = sb3('rw_wk', [128, PB * D], BF16)
                wv = sb3('rw_wv', [128, PB * D], BF16)
                wr = sb3('rw_wr', [128, PB * D], BF16)
                wdk = k.dma('gpsimd', lambda g, l=l, wk=wk, W=W: g.dma_start(out=wk[:], in_=hview(W['wkT'], l * D * D, [[D, 128], [128 * D, PB], [1, D]])), slot='wg')
                wdv = k.dma('gpsimd', lambda g, l=l, wv=wv, W=W: g.dma_start(out=wv[:], in_=hview(W['wvT'], l * D * D, [[D, 128], [128 * D, PB], [1, D]])), slot='wu')
                wdr = k.dma('gpsimd', lambda g, l=l, wr=wr, W=W: g.dma_start(out=wr[:], in_=hview(W['wrT'], l * D * D, [[D, 128], [128 * D, PB], [1, D]])), slot='wd')
                k.wait('tensor', *wdk)
                k.wait('tensor', *wdv)
                k.wait('tensor', *wdr)
                evp = {0: ('v', 0), 1: ('v', 0)}
                lastt = 0
                tile_mixv = 0
                for q in range(4):
                    if q >= 1:
                        k.wait('sync', 'v', tile_mixv)
                    hcl = k.dma('sync', lambda y, b=b, q=q, hcT=hcT, scr=scr, TP=TP: y.dma_start(out=hcT[:], in_=hview(scr['h_hbm'], b * TP + 1 + q * 512, [[PB * (NT + 2), 128], [NT + 2, PB], [1, 512]])), slot='hc')
                    hpl = k.dma('sync', lambda y, b=b, q=q, hpT=hpT, scr=scr, TP=TP: y.dma_start(out=hpT[:], in_=hview(scr['h_hbm'], b * TP + q * 512, [[PB * (NT + 2), 128], [NT + 2, PB], [1, 512]])), slot='hp')
                    k.wait('vector', *hcl)
                    k.wait('vector', *hpl)
                    k.wait('vector', 't', lastt)
                    k.op('vector', lambda v, dT_=dT_, hcT=hcT, hpT=hpT: v.tensor_tensor(out=dT_[:], in0=hcT[:], in1=hpT[:], op=OP.subtract))
                    for db in range(PB):
                        sl = slice(db * 512, (db + 1) * 512)
                        k.op('vector', lambda v, sl=sl, db=db, l=l, xk_t=xk_t, dT_=dT_, cols=cols, hpT=hpT: v.scalar_tensor_tensor(out=xk_t[:, sl], in0=dT_[:, sl], scalar=cols[:, 144 + l * PB + db:144 + l * PB + db + 1], in1=hpT[:, sl], op0=OP.mult, op1=OP.add))
                        k.op('vector', lambda v, sl=sl, db=db, l=l, xv_t=xv_t, dT_=dT_, cols=cols, hpT=hpT: v.scalar_tensor_tensor(out=xv_t[:, sl], in0=dT_[:, sl], scalar=cols[:, 176 + l * PB + db:176 + l * PB + db + 1], in1=hpT[:, sl], op0=OP.mult, op1=OP.add))
                        k.op('vector', lambda v, sl=sl, db=db, l=l, xr_t=xr_t, dT_=dT_, cols=cols, hpT=hpT: v.scalar_tensor_tensor(out=xr_t[:, sl], in0=dT_[:, sl], scalar=cols[:, 208 + l * PB + db:208 + l * PB + db + 1], in1=hpT[:, sl], op0=OP.mult, op1=OP.add))
                    mixv = k.mark('v')
                    tile_mixv = mixv
                    k.wait('tensor', 'v', mixv)
                    for ob in range(8):
                        for which in range(3):
                            par = (ob * 3 + which) % 2
                            src = (xk_t, xv_t, xr_t)[which]
                            wmat = (wk, wv, wr)[which]
                            k.wait('tensor', evp[par][0], evp[par][1])
                            for mb in range(PB):
                                fn = lambda t, mb=mb, ob=ob, par=par, src=src, wmat=wmat, ppT=ppT: t.matmul(ppT[par][:, :], wmat[:, mb * D + ob * 128:mb * D + ob * 128 + 128], src[:, mb * 512:(mb + 1) * 512], start=mb == 0, stop=mb == PB - 1)
                                if mb < PB - 1:
                                    k.raw('tensor', fn)
                                else:
                                    mm = k.op('tensor', fn)
                            if which == 0:
                                k.wait('scalar', 't', mm)
                                evp[par] = ('s', k.op('scalar', lambda s, ob=ob, q=q, par=par, ek=ek, TP=TP, ppT=ppT: s.activation(out=ek[:, ob * TP + 1 + q * 512:ob * TP + 1 + q * 512 + 512], in_=ppT[par][:, :], func=AF.Exp)))
                            elif which == 1:
                                k.wait('vector', 't', mm)
                                k.op('vector', lambda v, par=par, vtmp=vtmp, ppT=ppT: v.tensor_copy(out=vtmp[:], in_=ppT[par][:, :]))
                                k.wait('vector', 's', k.mark('s'))
                                evp[par] = ('v', k.op('vector', lambda v, ob=ob, q=q, ekv=ekv, TP=TP, ek=ek, vtmp=vtmp: v.tensor_tensor(out=ekv[:, ob * TP + 1 + q * 512:ob * TP + 1 + q * 512 + 512], in0=ek[:, ob * TP + 1 + q * 512:ob * TP + 1 + q * 512 + 512], in1=vtmp[:], op=OP.mult)))
                            else:
                                k.wait('scalar', 't', mm)
                                evp[par] = ('s', k.op('scalar', lambda s, ob=ob, q=q, par=par, srb=srb, ppT=ppT: s.activation(out=srb[:, ob * T + q * 512:ob * T + q * 512 + 512], in_=ppT[par][:, :], func=AF.Sigmoid)))
                    lastt = k.mark('t')
                es3.close()
                es3b = ExitStack()
                sb3b = lambda n, s, d, es3b=es3b, nc=nc: es3b.enter_context(nc.sbuf_tensor(uname(n), s, d))
                lamrep = sb3b('rw_lamrep', [128, T], F32)
                nbuf = sb3b('rw_nbuf', [128, T], F32)
                dbuf = sb3b('rw_dbuf', [128, T], F32)
                rwst = None
                for ob in range(8):
                    k.op('vector', lambda v, lamrep=lamrep: v.memset(lamrep[:], 1.0))
                    k.op('vector', lambda v, ob=ob, lamrep=lamrep, lamu=lamu: v.tensor_scalar(out=lamrep[:], in0=lamrep[:], scalar1=lamu[:, ob:ob + 1], scalar2=None, op0=OP.mult))
                    k.wait('vector', 's', k.mark('s'))
                    k.op('vector', lambda v, ob=ob, nbuf=nbuf, lamrep=lamrep, ekv=ekv, TP=TP: v.tensor_tensor_scan(out=nbuf[:], data0=lamrep[:], data1=ekv[:, ob * TP:ob * TP + T], initial=0.0, op0=OP.mult, op1=OP.add))
                    k.op('vector', lambda v, ob=ob, dbuf=dbuf, lamrep=lamrep, ek=ek, TP=TP: v.tensor_tensor_scan(out=dbuf[:], data0=lamrep[:], data1=ek[:, ob * TP:ob * TP + T], initial=0.0, op0=OP.mult, op1=OP.add))
                    k.op('vector', lambda v, ob=ob, nbuf=nbuf, ekv=ekv, TP=TP, lamu=lamu: v.scalar_tensor_tensor(out=nbuf[:], in0=ekv[:, ob * TP + 1:ob * TP + 1 + T], scalar=lamu[:, 8 + ob:9 + ob], in1=nbuf[:], op0=OP.mult, op1=OP.add))
                    k.op('vector', lambda v, ob=ob, dbuf=dbuf, ek=ek, TP=TP, lamu=lamu: v.scalar_tensor_tensor(out=dbuf[:], in0=ek[:, ob * TP + 1:ob * TP + 1 + T], scalar=lamu[:, 8 + ob:9 + ob], in1=dbuf[:], op0=OP.mult, op1=OP.add))
                    k.op('vector', lambda v, dbuf=dbuf: v.reciprocal(out=dbuf[:], in_=dbuf[:]))
                    k.op('vector', lambda v, nbuf=nbuf, dbuf=dbuf: v.tensor_tensor(out=nbuf[:], in0=nbuf[:], in1=dbuf[:], op=OP.mult))
                    if rwst:
                        k.wait('vector', *rwst)
                    rwv = k.op('vector', lambda v, ob=ob, rwt=rwt, nbuf=nbuf, srb=srb: v.tensor_tensor(out=rwt[:], in0=nbuf[:], in1=srb[:, ob * T:(ob + 1) * T], op=OP.mult))
                    k.wait('scalar', 'v', rwv)
                    rwst = k.dma('scalar', lambda s, ob=ob, b=b, scr=scr, RW0=RW0, rwt=rwt: s.dma_start(out=hview(scr['aux_hbm'], RW0 + ob * NT + b * T, [[16 * NT, 128], [1, T]]), in_=rwt[:]), slot='rws')
                es3b.close()
                es2.close()
                es4 = ExitStack()
                sb4 = lambda n, s, d, es4=es4, nc=nc: es4.enter_context(nc.sbuf_tensor(uname(n), s, d))
                ps4 = lambda n, s, es4=es4, nc=nc: es4.enter_context(nc.psum_tensor(uname(n), s, F32))
                wo = sb4('rw_wo', [128, PB * D], BF16)
                rwtl = sb4('rw_rwtl', [128, 8 * 512], BF16)
                xcl = sb4('rw_xcl', [128, PB * 512], F32)
                ytl = [sb4(f'rw_ytl{i}', [128, 512], F32) for i in range(2)]
                pw = [ps4(f'rw_pw{i}', [128, 512]) for i in range(2)]
                k.wait('gpsimd', 't', k.mark('t'))
                k.wait('gpsimd', 'v', k.mark('v'))
                k.wait('gpsimd', 's', k.mark('s'))
                k.wait('gpsimd', 'c:rws', k.mark('c:rws'))
                wod = k.dma('gpsimd', lambda g, l=l, wo=wo, W=W: g.dma_start(out=wo[:], in_=hview(W['woT'], l * D * D, [[D, 128], [128 * D, PB], [1, D]])), slot='wg')
                k.wait('tensor', *wod)
                k.wait('sync', *rwst)
                ydone = {0: 0, 1: 0}
                st = {0: None, 1: None}
                for q in range(4):
                    tilev = k.mark('v')
                    if q >= 1:
                        k.wait('sync', 'v', tilev)
                    rwl = k.dma('sync', lambda y, b=b, q=q, rwtl=rwtl, scr=scr, RW0=RW0: y.dma_start(out=rwtl[:], in_=hview(scr['aux_hbm'], RW0 + b * T + q * 512, [[16 * NT, 128], [NT, 8], [1, 512]])), slot='hc')
                    xdl = k.dma('sync', lambda y, b=b, q=q, xcl=xcl, scr=scr: y.dma_start(out=xcl[:], in_=xtile(scr['xr_hbm'], b * 4 + q)), slot='hp')
                    k.wait('tensor', *rwl)
                    for db in range(PB):
                        par = db % 2
                        k.wait('tensor', 'v', ydone[par])
                        for ob in range(8):
                            fn = lambda t, ob=ob, db=db, par=par, pw=pw, wo=wo, rwtl=rwtl: t.matmul(pw[par][:, :], wo[:, ob * D + db * 128:ob * D + db * 128 + 128], rwtl[:, ob * 512:(ob + 1) * 512], start=ob == 0, stop=ob == 7)
                            if ob < 7:
                                k.raw('tensor', fn)
                            else:
                                mm = k.op('tensor', fn)
                        k.wait('vector', 't', mm)
                        k.wait('vector', *xdl)
                        if st[par]:
                            k.wait('vector', *st[par])
                        ydone[par] = k.op('vector', lambda v, db=db, par=par, ytl=ytl, pw=pw, xcl=xcl: v.tensor_tensor(out=ytl[par][:], in0=pw[par][:, :], in1=xcl[:, db * 512:(db + 1) * 512], op=OP.add))
                        k.wait('scalar', 'v', ydone[par])
                        st[par] = k.dma('scalar', lambda s, b=b, db=db, q=q, par=par, scr=scr, ytl=ytl: s.dma_start(out=xchunk(scr['xr_hbm'], db, b * T + q * 512), in_=ytl[par][:]), slot=f'ys{par}')
                    k.wait('sync', 'v', ydone[0])
                    k.wait('sync', 'v', ydone[1])
                es4.close()
                k.barrier()
            emit_ln(k, scr['xr_hbm'], consts, gamma_col=lw(80), beta_col=lw(112), dst_hbm_fn=hpad_fn)
            for b in range(BC):
                es2 = ExitStack()
                sb2 = lambda n, s, d, es2=es2, nc=nc: es2.enter_context(nc.sbuf_tensor(uname(n), s, d))
                ps2 = lambda n, s, es2=es2, nc=nc: es2.enter_context(nc.psum_tensor(uname(n), s, F32))
                hcC = sb2('cm_hc', [128, PB * 512], BF16)
                hpC = sb2('cm_hp', [128, PB * 512], BF16)
                dC_ = sb2('cm_d', [128, PB * 512], BF16)
                xk2 = sb2('cm_xk2', [128, PB * 512], BF16)
                xr2 = sb2('cm_xr2', [128, PB * 512], BF16)
                srct = sb2('cm_src', [128, 8 * 512], BF16)
                wcr = sb2('cm_wcr', [128, PB * D], BF16)
                ppC = [ps2(f'cm_pp{i}', [128, 512]) for i in range(2)]
                wdc = k.dma('gpsimd', lambda g, l=l, wcr=wcr, W=W: g.dma_start(out=wcr[:], in_=hview(W['wcrT'], l * D * D, [[D, 128], [128 * D, PB], [1, D]])), slot='wg')
                k.wait('tensor', *wdc)
                xst = None
                sst = None
                for q in range(4):
                    prevv = k.mark('v')
                    if q >= 1:
                        k.wait('sync', 'v', prevv)
                        k.wait('sync', 's', k.mark('s'))
                    hcl = k.dma('sync', lambda y, b=b, q=q, hcC=hcC, scr=scr, TP=TP: y.dma_start(out=hcC[:], in_=hview(scr['h_hbm'], b * TP + 1 + q * 512, [[PB * (NT + 2), 128], [NT + 2, PB], [1, 512]])), slot='hc')
                    hpl = k.dma('sync', lambda y, b=b, q=q, hpC=hpC, scr=scr, TP=TP: y.dma_start(out=hpC[:], in_=hview(scr['h_hbm'], b * TP + q * 512, [[PB * (NT + 2), 128], [NT + 2, PB], [1, 512]])), slot='hp')
                    k.wait('vector', *hcl)
                    k.wait('vector', *hpl)
                    if xst:
                        k.wait('vector', *xst)
                    k.wait('vector', 't', k.mark('t'))
                    k.op('vector', lambda v, dC_=dC_, hcC=hcC, hpC=hpC: v.tensor_tensor(out=dC_[:], in0=hcC[:], in1=hpC[:], op=OP.subtract))
                    for db in range(PB):
                        sl = slice(db * 512, (db + 1) * 512)
                        k.op('vector', lambda v, sl=sl, db=db, l=l, xk2=xk2, dC_=dC_, cols=cols, hpC=hpC: v.scalar_tensor_tensor(out=xk2[:, sl], in0=dC_[:, sl], scalar=cols[:, 240 + l * PB + db:240 + l * PB + db + 1], in1=hpC[:, sl], op0=OP.mult, op1=OP.add))
                        k.op('vector', lambda v, sl=sl, db=db, l=l, xr2=xr2, dC_=dC_, cols=cols, hpC=hpC: v.scalar_tensor_tensor(out=xr2[:, sl], in0=dC_[:, sl], scalar=cols[:, 272 + l * PB + db:272 + l * PB + db + 1], in1=hpC[:, sl], op0=OP.mult, op1=OP.add))
                    mixv = k.mark('v')
                    k.wait('scalar', 'v', mixv)
                    xst = k.dma('scalar', lambda s, b=b, q=q, scr=scr, xk2=xk2: s.dma_start(out=xtile(scr['xn_hbm'], b * 4 + q), in_=xk2[:]), slot='xks')
                    k.wait('tensor', 'v', mixv)
                    evs = {0: 0, 1: 0}
                    for ob in range(8):
                        par = ob % 2
                        k.wait('tensor', 's', evs[par])
                        for mb in range(PB):
                            fn = lambda t, mb=mb, ob=ob, par=par, ppC=ppC, wcr=wcr, xr2=xr2: t.matmul(ppC[par][:, :], wcr[:, mb * D + ob * 128:mb * D + ob * 128 + 128], xr2[:, mb * 512:(mb + 1) * 512], start=mb == 0, stop=mb == PB - 1)
                            if mb < PB - 1:
                                k.raw('tensor', fn)
                            else:
                                mm = k.op('tensor', fn)
                        k.wait('scalar', 't', mm)
                        if sst:
                            k.wait('scalar', *sst)
                        evs[par] = k.op('scalar', lambda s, ob=ob, par=par, srct=srct, ppC=ppC: s.activation(out=srct[:, ob * 512:(ob + 1) * 512], in_=ppC[par][:, :], func=AF.Sigmoid))
                    k.wait('scalar', 's', k.mark('s'))
                    sst = k.dma('scalar', lambda s, b=b, q=q, scr=scr, SRC0=SRC0, srct=srct: s.dma_start(out=hview(scr['aux_hbm'], SRC0 + b * T + q * 512, [[16 * NT, 128], [NT, 8], [1, 512]]), in_=srct[:]), slot='srs')
                k.wait('sync', *sst)
                k.wait('sync', *xst)
                es2.close()
                k.barrier()
                es2 = ExitStack()
                sb2 = lambda n, s, d, es2=es2, nc=nc: es2.enter_context(nc.sbuf_tensor(uname(n), s, d))
                ps2 = lambda n, s, es2=es2, nc=nc: es2.enter_context(nc.psum_tensor(uname(n), s, F32))
                wck = sb2('cm_wck', [128, PB * 2048], BF16)
                wcv = sb2('cm_wcv', [128, 16 * D], BF16)
                kf = sb2('cm_kf', [128, 16 * 512], BF16)
                xk2t = sb2('cm_xk2t', [128, PB * 512], BF16)
                rtmp = sb2('cm_rtmp', [128, 512], F32)
                kvtl = [sb2(f'cm_kvtl{i}', [128, 512], F32) for i in range(2)]
                pf = [ps2(f'cm_pf{i}', [128, 512]) for i in range(2)]
                pv = [ps2(f'cm_pv{i}', [128, 512]) for i in range(2)]
                for jh in range(2):
                    k.wait('gpsimd', 't', k.mark('t'))
                    wd1 = k.dma('gpsimd', lambda g, l=l, jh=jh, wck=wck, W=W, F=F: g.dma_start(out=wck[:], in_=hview(W['wckT'], l * D * F + jh * 2048, [[F, 128], [128 * F, PB], [1, 2048]])), slot='wg')
                    wd2 = k.dma('gpsimd', lambda g, l=l, jh=jh, wcv=wcv, W=W, F=F: g.dma_start(out=wcv[:], in_=hview(W['wcvT'], (l * F + jh * 2048) * D, [[D, 128], [128 * D, 16], [1, D]])), slot='wu')
                    k.wait('tensor', *wd1)
                    k.wait('tensor', *wd2)
                    xlt = None
                    kst = {0: None, 1: None}
                    for q in range(4):
                        if xlt:
                            k.wait('sync', 't', k.mark('t'))
                        xlt = k.dma('sync', lambda y, b=b, q=q, xk2t=xk2t, scr=scr: y.dma_start(out=xk2t[:], in_=xtile(scr['xn_hbm'], b * 4 + q)), slot='hc')
                        k.wait('tensor', *xlt)
                        evs = {0: 0, 1: 0}
                        evv = {0: 0, 1: 0}
                        sq_prev = 0
                        for jb in range(16):
                            par = jb % 2
                            k.wait('tensor', 's', evs[par])
                            k.wait('tensor', 'v', evv[par])
                            for mb in range(PB):
                                fn = lambda t, mb=mb, jb=jb, par=par, pf=pf, wck=wck, xk2t=xk2t: t.matmul(pf[par][:, :], wck[:, mb * 2048 + jb * 128:mb * 2048 + jb * 128 + 128], xk2t[:, mb * 512:(mb + 1) * 512], start=mb == 0, stop=mb == PB - 1)
                                if mb < PB - 1:
                                    k.raw('tensor', fn)
                                else:
                                    mm = k.op('tensor', fn)
                            k.wait('vector', 't', mm)
                            k.wait('vector', 's', sq_prev)
                            evv[par] = k.op('vector', lambda v, par=par, rtmp=rtmp, pf=pf: v.tensor_scalar(out=rtmp[:], in0=pf[par][:, :], scalar1=0.0, scalar2=None, op0=OP.max))
                            k.wait('scalar', 'v', evv[par])
                            evs[par] = k.op('scalar', lambda s, jb=jb, kf=kf, rtmp=rtmp: s.activation(out=kf[:, jb * 512:(jb + 1) * 512], in_=rtmp[:], func=AF.Square))
                            sq_prev = evs[par]
                        k.wait('tensor', 's', k.mark('s'))
                        kdone = {0: 0, 1: 0}
                        for db in range(PB):
                            par = db % 2
                            k.wait('tensor', 'v', kdone[par])
                            for jb in range(16):
                                fn = lambda t, jb=jb, db=db, par=par, pv=pv, wcv=wcv, kf=kf: t.matmul(pv[par][:, :], wcv[:, jb * D + db * 128:jb * D + db * 128 + 128], kf[:, jb * 512:(jb + 1) * 512], start=jb == 0, stop=jb == 15)
                                if jb < 15:
                                    k.raw('tensor', fn)
                                else:
                                    mm = k.op('tensor', fn)
                            k.wait('vector', 't', mm)
                            if kst[par]:
                                k.wait('vector', *kst[par])
                            kdone[par] = k.op('vector', lambda v, par=par, kvtl=kvtl, pv=pv: v.tensor_copy(out=kvtl[par][:], in_=pv[par][:, :]))
                            k.wait('gpsimd', 'v', kdone[par])
                            dst = xchunk(scr['acc_hbm'], db, b * T + q * 512)
                            if jh == 0:
                                kst[par] = k.dma('gpsimd', lambda g, dst=dst, par=par, kvtl=kvtl: g.dma_start(out=dst, in_=kvtl[par][:]), slot=f'ac{par}')
                            else:
                                kst[par] = k.dma('gpsimd', lambda g, dst=dst, par=par, kvtl=kvtl: g.dma_start(out=dst, in_=kvtl[par][:], accum_op=OP.add), slot=f'ac{par}')
                    k.barrier()
                es2.close()
                es2 = ExitStack()
                sb2 = lambda n, s, d, es2=es2, nc=nc: es2.enter_context(nc.sbuf_tensor(uname(n), s, d))
                kvt = sb2('cm_kvt', [128, PB * 512], F32)
                srt = sb2('cm_srt', [128, 8 * 512], BF16)
                xct = sb2('cm_xct', [128, PB * 512], F32)
                ytl2 = sb2('cm_ytl2', [128, PB * 512], F32)
                for q in range(4):
                    prevv = k.mark('v')
                    if q >= 1:
                        k.wait('sync', 'v', prevv)
                        k.wait('sync', 'c:ys0', k.mark('c:ys0'))
                    kvl = k.dma('sync', lambda y, b=b, q=q, kvt=kvt, scr=scr: y.dma_start(out=kvt[:], in_=xtile(scr['acc_hbm'], b * 4 + q)), slot='hc')
                    srl = k.dma('sync', lambda y, b=b, q=q, srt=srt, scr=scr, SRC0=SRC0: y.dma_start(out=srt[:], in_=hview(scr['aux_hbm'], SRC0 + b * T + q * 512, [[16 * NT, 128], [NT, 8], [1, 512]])), slot='hp')
                    xl2 = k.dma('sync', lambda y, b=b, q=q, xct=xct, scr=scr: y.dma_start(out=xct[:], in_=xtile(scr['xr_hbm'], b * 4 + q)), slot='wa')
                    k.wait('vector', *kvl)
                    k.wait('vector', *srl)
                    k.wait('vector', *xl2)
                    for db in range(PB):
                        k.op('vector', lambda v, db=db, kvt=kvt, srt=srt: v.tensor_tensor(out=kvt[:, db * 512:(db + 1) * 512], in0=kvt[:, db * 512:(db + 1) * 512], in1=srt[:, db * 512:(db + 1) * 512], op=OP.mult))
                        k.op('vector', lambda v, db=db, ytl2=ytl2, kvt=kvt, xct=xct: v.tensor_tensor(out=ytl2[:, db * 512:(db + 1) * 512], in0=kvt[:, db * 512:(db + 1) * 512], in1=xct[:, db * 512:(db + 1) * 512], op=OP.add))
                    wv_ = k.mark('v')
                    k.wait('scalar', 'v', wv_)
                    k.dma('scalar', lambda s, b=b, q=q, scr=scr, ytl2=ytl2: s.dma_start(out=xtile(scr['xr_hbm'], b * 4 + q), in_=ytl2[:]), slot='ys0')
                es2.close()
                k.barrier()
        emit_add_hbm(k, x_hbm, scr['xr_hbm'], out_hbm, es)
        k.barrier()

def emit_gla(k, x_hbm, out_hbm, W, consts, scr):
    nc = k.nc
    NH, HD2 = (8, 128)
    CH = 64
    NCH = T // CH
    NT8 = T // 128
    emit_ln(k, x_hbm, consts, dst_hbm=scr['xn_hbm'])
    es = ExitStack()
    with es:
        sb = lambda n, s, d, es=es, nc=nc: es.enter_context(nc.sbuf_tensor(uname(n), s, d))
        win = sb('gl_win', [128, PB * 3072], BF16)
        wg1 = sb('gl_wg1', [128, PB * 128], BF16)
        wg2 = sb('gl_wg2', [128, 1024], BF16)
        lnw = sb('gl_lnw', [128, PB], F32)
        glam = sb('gl_glam', [128, T], F32)
        ltm = sb('gl_ltm', [64, 64], BF16)
        xn_b = sb('gl_xn', [128, PB * T], BF16)
        g1f = sb('gl_g1f', [128, T], BF16)
        gate_h = sb('gl_gateh', [128, T], BF16)
        og_h = sb('gl_ogh', [128, T], BF16)
        ogt = sb('gl_ogt', [128, NH * 512], BF16)
        vtok = sb('gl_vtok', [64, NCH * 128], BF16)
        ktok = sb('gl_ktok', [64, NCH * 128], BF16)
        qsil = sb('gl_qsil', [128, T], BF16)
        qt = qsil
        khat = sb('gl_khat', [128, T], BF16)
        ktl = sb('gl_ktl', [128, T], BF16)
        lgb = sb('gl_lg', [128, T], F32)
        tmpa = sb('gl_tmpa', [128, T], F32)
        gbuf = sb('gl_g', [128, T], F32)
        cum = sb('gl_cum', [128, T], F32)
        lam = cum
        S_f = sb('gl_Sf', [128, 128], F32)
        S_b = sb('gl_Sb', [128, 128], BF16)
        amt = sb('gl_amt', [64, 64], BF16)
        rows = sb('gl_rows', [1, 512 * 6], F32)
        crow = sb('gl_crow', [1, 512], BF16)
        orow = sb('gl_orow', [1, 512], BF16)
        onesb = sb('gl_onesb', [128, 1], BF16)
        onorm = sb('gl_onorm', [128, NH * 512], BF16)
        ogsq = onorm
        xcol = [sb(f'gl_xc{i}', [128, 512], F32) for i in range(2)]
        ytl = [sb(f'gl_ytl{i}', [128, 512], F32) for i in range(2)]
        wd2 = k.dma('gpsimd', lambda g, wg1=wg1, W=W: g.dma_start(out=wg1[:], in_=hview(W['wg1T'], 0, [[128, 128], [128 * 128, PB], [1, 128]])), slot='wu')
        wd3 = k.dma('gpsimd', lambda g, wg2=wg2, W=W: g.dma_start(out=wg2[:], in_=hview(W['wg2T'], 0, [[1024, 128], [128 * 1024, 1], [1, 1024]])), slot='wd')
        wd4 = k.dma('gpsimd', lambda g, ltm=ltm, W=W: g.dma_start(out=ltm[:], in_=W['ltm']), slot='wm')
        cd1 = k.dma('sync', lambda y, lnw=lnw, W=W: y.dma_start(out=lnw[:], in_=W['lnw_col']), slot='wa')
        cd2 = k.dma('sync', lambda y, glam=glam, W=W: y.dma_start(out=glam[:], in_=W['glam']), slot='wb')
        ms1 = k.op('gpsimd', lambda g, onesb=onesb: g.memset(onesb[:], 1.0))
        for b in range(BC):
            es2 = ExitStack()
            with es2:
                ps = lambda n, s, d=F32, es2=es2, nc=nc: es2.enter_context(nc.psum_tensor(uname(n), s, d))
                pp = [ps(f'gl_pp{i}', [128, 512]) for i in range(2)]
                esA = ExitStack()
                psA = lambda n, s, d=F32, esA=esA, nc=nc: esA.enter_context(nc.psum_tensor(uname(n), s, d))
                pA = psA('gl_pA', [64, 64])
                pO = psA('gl_pO', [128, 64])
                pS = psA('gl_pS', [128, 128])
                ptr = esA.enter_context(nc.psum_tensor(uname('gl_ptr'), [128, 128], BF16))
                for tq in range(4):
                    xbl = k.dma('sync', lambda y, b=b, tq=tq, xn_b=xn_b, scr=scr: y.dma_start(out=bass.AP(xn_b[:, :].tensor, xn_b[:, :].offset + tq * 512, [[PB * T, 128], [T, PB], [1, 512]]), in_=xtile(scr['xn_hbm'], b * 4 + tq)), slot='xb')
                wd1 = k.dma('gpsimd', lambda g, win=win, W=W: g.dma_start(out=win[:], in_=hview(W['winT'], 0, [[3072, 128], [128 * 3072, PB], [1, 3072]])), slot='wg')
                k.wait('tensor', *xbl)
                k.wait('tensor', *wd1)
                k.wait('tensor', *wd2)
                evp = {0: k.mark('v'), 1: k.mark('v')}
                for tt in range(T // 512):
                    par = tt % 2
                    k.wait('tensor', 'v', evp[par])
                    for mb in range(PB):
                        fn = lambda t, mb=mb, tt=tt, par=par, pp=pp, wg1=wg1, xn_b=xn_b: t.matmul(pp[par][:, :], wg1[:, mb * 128:(mb + 1) * 128], xn_b[:, mb * T + tt * 512:mb * T + tt * 512 + 512], start=mb == 0, stop=mb == PB - 1)
                        if mb < PB - 1:
                            k.raw('tensor', fn)
                        else:
                            mm = k.op('tensor', fn)
                    k.wait('vector', 't', mm)
                    evp[par] = k.op('vector', lambda v, tt=tt, par=par, g1f=g1f, pp=pp: v.tensor_copy(out=g1f[:, tt * 512:(tt + 1) * 512], in_=pp[par][:, :]))
                evps = {0: ('v', 0), 1: ('v', 0)}
                for par in range(2):
                    evps[par] = ('v', evp[par])
                for h in range(NH):
                    for which, coff in (('q', 1024 + h * 128), ('lg', 2048 + h * 128)):
                        for tt in range(T // 512):
                            par = tt % 2
                            k.wait('tensor', *(evps[par][::-1] if False else (evps[par][0], evps[par][1])))
                            for mb in range(PB):
                                fn = lambda t, mb=mb, tt=tt, par=par, coff=coff, pp=pp, win=win, xn_b=xn_b: t.matmul(pp[par][:, :], win[:, mb * 3072 + coff:mb * 3072 + coff + 128], xn_b[:, mb * T + tt * 512:mb * T + tt * 512 + 512], start=mb == 0, stop=mb == PB - 1)
                                if mb < PB - 1:
                                    k.raw('tensor', fn)
                                else:
                                    mm = k.op('tensor', fn)
                            if which == 'q':
                                k.wait('scalar', 't', mm)
                                evs = k.op('scalar', lambda s, tt=tt, par=par, qsil=qsil, pp=pp: s.activation(out=qsil[:, tt * 512:(tt + 1) * 512], in_=pp[par][:, :], func=AF.Silu))
                                evps[par] = ('s', evs)
                            else:
                                k.wait('vector', 't', mm)
                                evps[par] = ('v', k.op('vector', lambda v, tt=tt, par=par, lgb=lgb, pp=pp: v.tensor_copy(out=lgb[:, tt * 512:(tt + 1) * 512], in_=pp[par][:, :])))
                    for par in range(2):
                        k.wait('tensor', evps[par][0], evps[par][1])
                    evv = {0: k.mark('v'), 1: k.mark('v')}
                    for tt8 in range(NT8):
                        par = tt8 % 2
                        k.wait('tensor', 'v', evv[par])
                        for mb in range(PB):
                            fn = lambda t, mb=mb, tt8=tt8, par=par, h=h, pp=pp, xn_b=xn_b, win=win: t.matmul(pp[par][:, 0:128], xn_b[:, mb * T + tt8 * 128:mb * T + tt8 * 128 + 128], win[:, mb * 3072 + h * 128:mb * 3072 + h * 128 + 128], start=mb == 0, stop=mb == PB - 1)
                            if mb < PB - 1:
                                k.raw('tensor', fn)
                            else:
                                mm = k.op('tensor', fn)
                        k.wait('vector', 't', mm)
                        k.op('vector', lambda v, tt8=tt8, par=par, vtok=vtok, pp=pp: v.tensor_copy(out=vtok[:, 2 * tt8 * 128:2 * tt8 * 128 + 128], in_=pp[par][0:64, 0:128]))
                        evv[par] = k.op('vector', lambda v, tt8=tt8, par=par, vtok=vtok, pp=pp: v.tensor_copy(out=vtok[:, (2 * tt8 + 1) * 128:(2 * tt8 + 1) * 128 + 128], in_=pp[par][64:128, 0:128]))
                    k.wait('tensor', 'v', k.mark('v'))
                    k.wait('tensor', *wd3)
                    evg = {0: k.mark('s'), 1: k.mark('s')}
                    for tt in range(T // 512):
                        par = tt % 2
                        k.wait('tensor', 's', evg[par])
                        mm = k.op('tensor', lambda t, tt=tt, par=par, h=h, pp=pp, wg2=wg2, g1f=g1f: t.matmul(pp[par][:, :], wg2[:, h * 128:(h + 1) * 128], g1f[:, tt * 512:(tt + 1) * 512], start=True, stop=True))
                        k.wait('scalar', 't', mm)
                        evg[par] = k.op('scalar', lambda s, tt=tt, par=par, gate_h=gate_h, pp=pp: s.activation(out=gate_h[:, tt * 512:(tt + 1) * 512], in_=pp[par][:, :], func=AF.Sigmoid))
                        evps[par] = ('s', evg[par])
                    k.wait('scalar', 'v', k.mark('v'))
                    k.op('scalar', lambda s, tmpa=tmpa, lgb=lgb: s.activation(out=tmpa[:], in_=lgb[:], func=AF.Abs))
                    k.op('scalar', lambda s, tmpa=tmpa: s.activation(out=tmpa[:], in_=tmpa[:], func=AF.Exp, scale=-1.0))
                    sln = k.op('scalar', lambda s, tmpa=tmpa: s.activation(out=tmpa[:], in_=tmpa[:], func=AF.Ln, bias=1.0))
                    k.wait('vector', 's', sln)
                    k.op('vector', lambda v, gbuf=gbuf, lgb=lgb: v.tensor_scalar(out=gbuf[:], in0=lgb[:], scalar1=-1.0, scalar2=None, op0=OP.mult))
                    k.op('vector', lambda v, gbuf=gbuf: v.tensor_scalar(out=gbuf[:], in0=gbuf[:], scalar1=0.0, scalar2=None, op0=OP.max))
                    vg = k.op('vector', lambda v, gbuf=gbuf, tmpa=tmpa: v.scalar_tensor_tensor(out=gbuf[:], in0=gbuf[:], scalar=-1.0, in1=tmpa[:], op0=OP.bypass, op1=OP.add))
                    k.op('vector', lambda v, gbuf=gbuf: v.tensor_scalar(out=gbuf[:], in0=gbuf[:], scalar1=-1.0, scalar2=None, op0=OP.mult))
                    k.wait('vector', *cd2)
                    scv = k.op('vector', lambda v, cum=cum, glam=glam, gbuf=gbuf: v.tensor_tensor_scan(out=cum[:], data0=glam[:], data1=gbuf[:], initial=0.0, op0=OP.mult, op1=OP.add))
                    k.wait('scalar', 'v', scv)
                    sl2 = k.op('scalar', lambda s, tmpa=tmpa, cum=cum: s.activation(out=tmpa[:], in_=cum[:], func=AF.Exp, scale=-1.0))
                    sl1 = k.op('scalar', lambda s, lam=lam, cum=cum: s.activation(out=lam[:], in_=cum[:], func=AF.Exp))
                    sl3 = k.op('scalar', lambda s, gbuf=gbuf, lgb=lgb: s.activation(out=gbuf[:], in_=lgb[:], func=AF.Sigmoid, scale=-1.0))
                    k.wait('vector', 's', sl3)
                    k.op('vector', lambda v, khat=khat, gbuf=gbuf, tmpa=tmpa: v.tensor_tensor(out=khat[:], in0=gbuf[:], in1=tmpa[:], op=OP.mult))
                    k.op('vector', lambda v, qt=qt, qsil=qsil, lam=lam: v.tensor_tensor(out=qt[:], in0=qsil[:], in1=lam[:], op=OP.mult))
                    for c in range(NCH):
                        k.op('vector', lambda v, c=c, ktl=ktl, CH=CH, khat=khat, lam=lam: v.tensor_scalar(out=ktl[:, c * CH:(c + 1) * CH], in0=khat[:, c * CH:(c + 1) * CH], scalar1=lam[:, c * CH + CH - 1:c * CH + CH], scalar2=None, op0=OP.mult))
                    ktv = k.mark('v')
                    k.wait('tensor', 'v', ktv)
                    prevtr = k.mark('v')
                    for tt8 in range(NT8):
                        k.wait('tensor', 'v', prevtr)
                        mtr = k.op('tensor', lambda t, tt8=tt8, ptr=ptr, ktl=ktl, consts=consts: t.transpose(ptr[:, :], ktl[:, tt8 * 128:(tt8 + 1) * 128], consts['ident_bf'][:]))
                        k.wait('vector', 't', mtr)
                        k.op('vector', lambda v, tt8=tt8, ktok=ktok, ptr=ptr: v.tensor_copy(out=ktok[:, 2 * tt8 * 128:2 * tt8 * 128 + 128], in_=ptr[0:64, :]))
                        prevtr = k.op('vector', lambda v, tt8=tt8, ktok=ktok, ptr=ptr: v.tensor_copy(out=ktok[:, (2 * tt8 + 1) * 128:(2 * tt8 + 1) * 128 + 128], in_=ptr[64:128, :]))
                    msS = k.op('vector', lambda v, S_f=S_f: v.memset(S_f[:], 0.0))
                    k.op('vector', lambda v, S_b=S_b: v.memset(S_b[:], 0.0))
                    k.wait('vector', *wd4)
                    prevA = k.mark('v')
                    prevO = k.mark('v')
                    prevS = k.mark('v')
                    for c in range(NCH):
                        cs = c * CH
                        k.wait('tensor', 'v', prevA)
                        mA = k.op('tensor', lambda t, cs=cs, pA=pA, khat=khat, CH=CH, qt=qt: t.matmul(pA[:, :], khat[:, cs:cs + CH], qt[:, cs:cs + CH], start=True, stop=True))
                        k.wait('vector', 't', mA)
                        prevA = k.op('vector', lambda v, amt=amt, pA=pA, ltm=ltm: v.tensor_tensor(out=amt[:], in0=pA[:, :], in1=ltm[:], op=OP.mult))
                        k.wait('tensor', 'v', prevA)
                        k.wait('tensor', 'v', prevO)
                        k.raw('tensor', lambda t, c=c, pO=pO, vtok=vtok, amt=amt: t.matmul(pO[:, :], vtok[:, c * 128:(c + 1) * 128], amt[:, :], start=True, stop=False))
                        mO = k.op('tensor', lambda t, cs=cs, pO=pO, S_b=S_b, qt=qt, CH=CH: t.matmul(pO[:, :], S_b[:, :], qt[:, cs:cs + CH], start=False, stop=True))
                        k.wait('tensor', 'v', prevS)
                        mS = k.op('tensor', lambda t, c=c, pS=pS, ktok=ktok, vtok=vtok: t.matmul(pS[:, :], ktok[:, c * 128:(c + 1) * 128], vtok[:, c * 128:(c + 1) * 128], start=True, stop=True))
                        k.wait('vector', 't', mO)
                        prevO = k.op('vector', lambda v, cs=cs, og_h=og_h, CH=CH, pO=pO, gate_h=gate_h: v.tensor_tensor(out=og_h[:, cs:cs + CH], in0=pO[:, :], in1=gate_h[:, cs:cs + CH], op=OP.mult))
                        k.wait('vector', 't', mS)
                        k.op('vector', lambda v, cs=cs, S_f=S_f, lam=lam, CH=CH: v.tensor_scalar(out=S_f[:], in0=S_f[:], scalar1=lam[:, cs + CH - 1:cs + CH], scalar2=None, op0=OP.mult))
                        k.op('vector', lambda v, S_f=S_f, pS=pS: v.tensor_tensor(out=S_f[:], in0=S_f[:], in1=pS[:, :], op=OP.add))
                        prevS = k.op('vector', lambda v, S_b=S_b, S_f=S_f: v.tensor_copy(out=S_b[:], in_=S_f[:]))
                    k.wait('scalar', 'v', prevO)
                    ogst = k.dma('scalar', lambda sc, h=h, b=b, scr=scr, NH=NH, og_h=og_h: sc.dma_start(out=hview(scr['og_hbm'], h * NT + b * T, [[NH * NT, 128], [1, T]]), in_=og_h[:, :]), slot='ogs')
                    k.wait('vector', *ogst)
                esA.close()
                esB = ExitStack()
                psB = lambda n, s, d=F32, esB=esB, nc=nc: esB.enter_context(nc.psum_tensor(uname(n), s, d))
                pst = [psB(f'gl_pst{i}', [1, 512]) for i in range(2)]
                pbc = psB('gl_pbc', [128, 512])
                pbo = psB('gl_pbo', [128, 512])
                k.wait('gpsimd', 't', k.mark('t'))
                wo = k.dma('gpsimd', lambda g, win=win, W=W: g.dma_start(out=win[:, 0:PB * D], in_=hview(W['woutT'], 0, [[D, 128], [128 * D, PB], [1, D]])), slot='wg')
                k.wait('vector', *cd1)
                ydone = {0: 0, 1: 0}
                st = {0: None, 1: None}
                k.wait('sync', *ogst)
                ogl = None
                for tt in range(T // 512):
                    allv = k.mark('v')
                    if ogl is not None:
                        k.wait('sync', 't', ogl_t)
                        k.wait('sync', 'v', allv)
                        k.wait('sync', 's', k.mark('s'))
                    ogl_t = k.mark('t')
                    ogl = k.dma('sync', lambda y, b=b, tt=tt, ogt=ogt, scr=scr, NH=NH: y.dma_start(out=ogt[:], in_=hview(scr['og_hbm'], b * T + tt * 512, [[NH * NT, 128], [NT, NH], [1, 512]])), slot='ogl')
                    k.wait('scalar', *ogl)
                    k.wait('scalar', 't', k.mark('t'))
                    sq = k.op('scalar', lambda s, tt=tt, ogsq=ogsq, ogt=ogt: s.activation(out=ogsq[:], in_=ogt[:], func=AF.Square))
                    k.wait('tensor', 's', sq)
                    k.wait('tensor', *ogl)
                    k.wait('tensor', 'p', ms1)
                    for hb in range(NH):
                        k.raw('tensor', lambda t, hb=hb, tt=tt, pst=pst, onesb=onesb, ogt=ogt, NH=NH: t.matmul(pst[0][:, :], onesb[:, :], ogt[:, hb * 512:(hb + 1) * 512], start=hb == 0, stop=hb == NH - 1))
                    for hb in range(NH - 1):
                        k.raw('tensor', lambda t, hb=hb, pst=pst, onesb=onesb, ogsq=ogsq: t.matmul(pst[1][:, :], onesb[:, :], ogsq[:, hb * 512:(hb + 1) * 512], start=hb == 0, stop=False))
                    mst = k.op('tensor', lambda t, pst=pst, onesb=onesb, ogsq=ogsq, NH=NH: t.matmul(pst[1][:, :], onesb[:, :], ogsq[:, (NH - 1) * 512:NH * 512], start=False, stop=True))
                    mu = rows[0:1, 0:512]
                    m2_ = rows[0:1, 512:1024]
                    var = rows[0:1, 1024:1536]
                    rst = rows[0:1, 1536:2048]
                    tmp = rows[0:1, 2048:2560]
                    k.wait('vector', 't', mst)
                    k.op('vector', lambda v, mu=mu, pst=pst: v.tensor_scalar(out=mu, in0=pst[0][:, :], scalar1=1.0 / D, scalar2=None, op0=OP.mult))
                    k.op('vector', lambda v, m2_=m2_, mu=mu: v.tensor_tensor(out=m2_, in0=mu, in1=mu, op=OP.mult))
                    vv = k.op('vector', lambda v, var=var, pst=pst, m2_=m2_: v.scalar_tensor_tensor(out=var, in0=pst[1][:, :], scalar=1.0 / D, in1=m2_, op0=OP.mult, op1=OP.subtract))
                    k.wait('scalar', 'v', vv)
                    sv = k.op('scalar', lambda s, tmp=tmp, var=var: s.activation(out=tmp, in_=var, func=AF.Sqrt, bias=EPS))
                    k.wait('vector', 's', sv)
                    k.op('vector', lambda v, rst=rst, tmp=tmp: v.reciprocal(out=rst, in_=tmp))
                    k.op('vector', lambda v, crow=crow, rst=rst: v.tensor_copy(out=crow[:], in_=rst))
                    rowv = k.op('vector', lambda v, orow=orow, mu=mu, rst=rst: v.scalar_tensor_tensor(out=orow[:], in0=mu, scalar=-1.0, in1=rst, op0=OP.mult, op1=OP.mult))
                    k.wait('tensor', 'v', rowv)
                    k.op('tensor', lambda t, pbc=pbc, consts=consts, crow=crow: t.matmul(pbc[:, :], consts['ones1_bf'][0:1, :], crow[0:1, :], start=True, stop=True))
                    mbc = k.op('tensor', lambda t, pbo=pbo, consts=consts, orow=orow: t.matmul(pbo[:, :], consts['ones1_bf'][0:1, :], orow[0:1, :], start=True, stop=True))
                    k.wait('vector', 't', mbc)
                    for hb in range(NH):
                        k.op('vector', lambda v, hb=hb, tt=tt, onorm=onorm, ogt=ogt, pbc=pbc: v.tensor_tensor(out=onorm[:, hb * 512:(hb + 1) * 512], in0=ogt[:, hb * 512:(hb + 1) * 512], in1=pbc[:, :], op=OP.mult))
                        k.op('vector', lambda v, hb=hb, onorm=onorm, pbo=pbo: v.tensor_tensor(out=onorm[:, hb * 512:(hb + 1) * 512], in0=onorm[:, hb * 512:(hb + 1) * 512], in1=pbo[:, :], op=OP.add))
                        k.op('vector', lambda v, hb=hb, onorm=onorm, lnw=lnw: v.tensor_scalar(out=onorm[:, hb * 512:(hb + 1) * 512], in0=onorm[:, hb * 512:(hb + 1) * 512], scalar1=lnw[:, hb:hb + 1], scalar2=None, op0=OP.mult))
                    nv = k.mark('v')
                    k.wait('tensor', 'v', nv)
                    k.wait('tensor', 's', k.mark('s'))
                    k.wait('tensor', *wo)
                    for db in range(PB):
                        par = db % 2
                        k.wait('sync', 'v', ydone[par])
                        xd = k.dma('sync', lambda y, b=b, tt=tt, db=db, par=par, xcol=xcol, x_hbm=x_hbm: y.dma_start(out=xcol[par][:, :], in_=xchunk(x_hbm, db, b * T + tt * 512)), slot=f'yx{par}')
                        k.wait('tensor', 'v', ydone[par])
                        for eb in range(NH):
                            fn = lambda t, eb=eb, db=db, par=par, pp=pp, win=win, onorm=onorm, NH=NH: t.matmul(pp[par][:, :], win[:, eb * D + db * 128:eb * D + db * 128 + 128], onorm[:, eb * 512:(eb + 1) * 512], start=eb == 0, stop=eb == NH - 1)
                            if eb < NH - 1:
                                k.raw('tensor', fn)
                            else:
                                mm = k.op('tensor', fn)
                        k.wait('vector', 't', mm)
                        k.wait('vector', *xd)
                        if st[par]:
                            k.wait('vector', *st[par])
                        ydone[par] = k.op('vector', lambda v, db=db, par=par, ytl=ytl, pp=pp, xcol=xcol: v.tensor_tensor(out=ytl[par][:], in0=pp[par][:, :], in1=xcol[par][:, :], op=OP.add))
                        k.wait('scalar', 'v', ydone[par])
                        st[par] = k.dma('scalar', lambda s, b=b, db=db, tt=tt, par=par, out_hbm=out_hbm, ytl=ytl: s.dma_start(out=xchunk(out_hbm, db, b * T + tt * 512), in_=ytl[par][:]), slot=f'ys{par}')
                    k.wait('sync', 'v', ydone[0])
                    k.wait('sync', 'v', ydone[1])
                esB.close()
            k.barrier()
        k.barrier()

def emit_moe_mlp(k, x_hbm, out_hbm, W, consts, scr):
    nc = k.nc
    I = 4096
    JB = I // 128
    emit_ln(k, x_hbm, consts, dst_hbm=scr['xn_hbm'])
    es = ExitStack()
    with es:
        sb = lambda n, s, d, es=es, nc=nc: es.enter_context(nc.sbuf_tensor(uname(n), s, d))
        psum = lambda n, s, d=F32, es=es, nc=nc: es.enter_context(nc.psum_tensor(uname(n), s, d))
        rw1 = sb('mr_rw1', [128, PB * I], BF16)
        rw2 = sb('mr_rw2', [128, JB * 8], F32)
        rcols = sb('mr_rcols', [128, JB + 8], F32)
        hr = sb('mr_hr', [128, JB * 512], F32)
        xnc = [sb(f'mr_xnc{i}', [128, PB * 512], BF16) for i in range(2)]
        logits_sb = sb('mr_logits', [128, NT // 128 * 8], F32)
        ph = [psum(f'mr_ph{i}', [128, 512]) for i in range(2)]
        plg = [psum(f'mr_plg{i}', [128, 8]) for i in range(2)]
        w1d = k.dma('gpsimd', lambda g, rw1=rw1, W=W: g.dma_start(out=rw1[:], in_=W['rw1T']), slot='wg')
        w2d = k.dma('sync', lambda y, rw2=rw2, W=W, JB=JB: y.dma_start(out=rw2[:], in_=hview(W['rw2'], 0, [[8, 128], [128 * 8, JB], [1, 8]])), slot='wb')
        cd = k.dma('sync', lambda y, rcols=rcols, JB=JB, W=W: y.dma_start(out=rcols[:, 0:JB], in_=W['rb1_cols']), slot='wa')
        cd2 = k.dma('sync', lambda y, rcols=rcols, JB=JB, W=W: y.dma_start(out=rcols[:, JB:JB + 8], in_=W['rb2_rep']), slot='wa')
        k.wait('tensor', *w1d)
        k.wait('tensor', *w2d)
        k.wait('scalar', *cd2)
        k.wait('vector', *cd2)
        sdone = {0: 0, 1: 0}
        lgv = {0: 0, 1: 0}
        tile_t = [0] * NTT
        for tt in range(NTT):
            par = tt % 2
            if tt >= 2:
                k.wait('sync', 't', tile_t[tt - 2])
            xl = k.dma('sync', lambda y, tt=tt, par=par, xnc=xnc, scr=scr: y.dma_start(out=xnc[par][:], in_=xtile(scr['xn_hbm'], tt)), slot=f'xn{par}')
            k.wait('tensor', *xl)
            for jb in range(JB):
                jpar = jb % 2
                k.wait('tensor', 's', sdone[jpar])
                for mb in range(PB - 1):
                    k.raw('tensor', lambda t, mb=mb, jb=jb, par=par, jpar=jpar, ph=ph, rw1=rw1, I=I, xnc=xnc: t.matmul(ph[jpar][:, :], rw1[:, mb * I + jb * 128:mb * I + jb * 128 + 128], xnc[par][:, mb * 512:(mb + 1) * 512], start=mb == 0, stop=False))
                mb = PB - 1
                mh = k.op('tensor', lambda t, mb=mb, jb=jb, par=par, jpar=jpar, ph=ph, rw1=rw1, I=I, xnc=xnc: t.matmul(ph[jpar][:, :], rw1[:, mb * I + jb * 128:mb * I + jb * 128 + 128], xnc[par][:, mb * 512:(mb + 1) * 512], start=False, stop=True))
                k.wait('scalar', 't', mh)
                sdone[jpar] = k.op('scalar', lambda s, jb=jb, jpar=jpar, hr=hr, ph=ph, rcols=rcols: s.activation(out=hr[:, jb * 512:(jb + 1) * 512], in_=ph[jpar][:, :], func=AF.Silu, bias=rcols[:, jb:jb + 1]))
            k.wait('tensor', 's', sdone[0])
            k.wait('tensor', 's', sdone[1])
            for ts4 in range(4):
                qpar = ts4 % 2
                k.wait('tensor', 'v', lgv[qpar])
                for jb in range(JB - 1):
                    k.raw('tensor', lambda t, jb=jb, ts4=ts4, qpar=qpar, plg=plg, hr=hr, rw2=rw2: t.matmul(plg[qpar][:, :], hr[:, jb * 512 + ts4 * 128:jb * 512 + ts4 * 128 + 128], rw2[:, jb * 8:(jb + 1) * 8], start=jb == 0, stop=False))
                jb = JB - 1
                mlg = k.op('tensor', lambda t, jb=jb, ts4=ts4, qpar=qpar, plg=plg, hr=hr, rw2=rw2: t.matmul(plg[qpar][:, :], hr[:, jb * 512 + ts4 * 128:jb * 512 + ts4 * 128 + 128], rw2[:, jb * 8:(jb + 1) * 8], start=False, stop=True))
                k.wait('vector', 't', mlg)
                lgv[qpar] = k.op('vector', lambda v, tt=tt, ts4=ts4, qpar=qpar, logits_sb=logits_sb, plg=plg, rcols=rcols, JB=JB: v.tensor_tensor(out=logits_sb[:, (tt * 4 + ts4) * 8:(tt * 4 + ts4) * 8 + 8], in0=plg[qpar][:, :], in1=rcols[:, JB:JB + 8], op=OP.add))
            tile_t[tt] = k.mark('t')
        k.barrier()
        emit_gate_rows(k, logits_sb, scr['wrows_hbm'], consts, mode='softmax2')
        k.barrier()
    es = ExitStack()
    with es:
        es_e = ExitStack()
        sb = lambda n, s, d, es_e=es_e, nc=nc: es_e.enter_context(nc.sbuf_tensor(uname(n), s, d))
        psum = lambda n, s, d=F32, es_e=es_e, nc=nc: es_e.enter_context(nc.psum_tensor(uname(n), s, d))
        w1_sb = sb('mr_w1', [128, PB * I], BF16)
        w2_sb = sb('mr_w2', [128, JB * D], BF16)
        h_sb = sb('mr_h', [128, JB * 512], BF16)
        bcols = sb('mr_bcols', [128, 8 * JB + 8 * PB], F32)
        xnc = [sb(f'mr_exc{i}', [128, PB * 512], BF16) for i in range(2)]
        wrow = sb('mr_wrow', [1, NT], BF16)
        wbc = sb('mr_wbc', [128, NT], F32)
        ytl = [sb(f'mr_ytl{i}', [128, 512], F32) for i in range(2)]
        pg = [psum(f'mr_pg{i}', [128, 512]) for i in range(2)]
        pyy = [psum(f'mr_pyy{i}', [128, 512]) for i in range(2)]
        pwb = psum('mr_pwb', [128, 512])
        bd1 = k.dma('sync', lambda y, bcols=bcols, JB=JB, W=W: y.dma_start(out=bcols[:, 0:8 * JB], in_=W['b1_cols']), slot='wa')
        bd2 = k.dma('sync', lambda y, bcols=bcols, JB=JB, W=W: y.dma_start(out=bcols[:, 8 * JB:8 * JB + 8 * PB], in_=W['b2_cols']), slot='wa')
        k.wait('scalar', *bd2)
        k.wait('vector', *bd2)
        for ei, ex in enumerate(W['experts']):
            b1c = bcols[:, ei * JB:(ei + 1) * JB]
            b2c = bcols[:, 8 * JB + ei * PB:8 * JB + (ei + 1) * PB]
            wl1 = k.dma('gpsimd', lambda g, ex=ex, w1_sb=w1_sb: g.dma_start(out=w1_sb[:, :], in_=ex['w1']), slot='wg')
            wl2 = k.dma('gpsimd', lambda g, ex=ex, w2_sb=w2_sb: g.dma_start(out=w2_sb[:, :], in_=ex['w2']), slot='wd')
            wr = k.dma('sync', lambda y, ex=ex, wrow=wrow: y.dma_start(out=wrow[:, :], in_=ex['wrow']), slot='wr')
            prev = 0
            for q in range(NT // 512):
                k.wait('tensor', *wr)
                if prev:
                    k.wait('tensor', 'v', prev)
                mb = k.op('tensor', lambda t, q=q, pwb=pwb, consts=consts, wrow=wrow: t.matmul(pwb[:, :], consts['ones1_bf'][0:1, :], wrow[0:1, q * 512:(q + 1) * 512], start=True, stop=True))
                k.wait('vector', 't', mb)
                prev = k.op('vector', lambda v, q=q, wbc=wbc, pwb=pwb: v.tensor_copy(out=wbc[:, q * 512:(q + 1) * 512], in_=pwb[:, :]))
            k.wait('tensor', *wl1)
            k.wait('tensor', *wl2)
            hdone = 0
            sdone = {0: 0, 1: 0}
            ydone = {0: 0, 1: 0}
            st = {0: None, 1: None}
            mblk_t = [0] * NTT
            for mblk in range(NTT):
                par = mblk % 2
                if mblk >= 2:
                    k.wait('sync', 't', mblk_t[mblk - 2])
                xl = k.dma('sync', lambda y, mblk=mblk, par=par, xnc=xnc, scr=scr: y.dma_start(out=xnc[par][:, :], in_=xtile(scr['xn_hbm'], mblk)), slot=f'xn{par}')
                for jb in range(JB):
                    jpar = jb % 2
                    k.wait('tensor', *xl)
                    k.wait('tensor', 's', sdone[jpar])
                    for mb in range(PB - 1):
                        k.raw('tensor', lambda t, mb=mb, jb=jb, par=par, jpar=jpar, pg=pg, w1_sb=w1_sb, I=I, xnc=xnc: t.matmul(pg[jpar][:, :], w1_sb[:, mb * I + jb * 128:mb * I + jb * 128 + 128], xnc[par][:, mb * 512:(mb + 1) * 512], start=mb == 0, stop=False))
                    mb = PB - 1
                    mg = k.op('tensor', lambda t, mb=mb, jb=jb, par=par, jpar=jpar, pg=pg, w1_sb=w1_sb, I=I, xnc=xnc: t.matmul(pg[jpar][:, :], w1_sb[:, mb * I + jb * 128:mb * I + jb * 128 + 128], xnc[par][:, mb * 512:(mb + 1) * 512], start=False, stop=True))
                    k.wait('scalar', 't', mg)
                    sdone[jpar] = k.op('scalar', lambda s, jb=jb, jpar=jpar, b1c=b1c, h_sb=h_sb, pg=pg: s.activation(out=h_sb[:, jb * 512:(jb + 1) * 512], in_=pg[jpar][:, :], func=AF.Silu, bias=b1c[:, jb:jb + 1]))
                hdone = sdone[(JB - 1) % 2]
                for db in range(PB):
                    ypar = db % 2
                    k.wait('tensor', 's', sdone[0])
                    k.wait('tensor', 's', sdone[1])
                    k.wait('tensor', 'v', ydone[ypar])
                    for jb in range(JB - 1):
                        k.raw('tensor', lambda t, jb=jb, db=db, ypar=ypar, pyy=pyy, w2_sb=w2_sb, h_sb=h_sb: t.matmul(pyy[ypar][:, :], w2_sb[:, jb * D + db * 128:jb * D + db * 128 + 128], h_sb[:, jb * 512:(jb + 1) * 512], start=jb == 0, stop=False))
                    jb = JB - 1
                    my = k.op('tensor', lambda t, jb=jb, db=db, ypar=ypar, pyy=pyy, w2_sb=w2_sb, h_sb=h_sb: t.matmul(pyy[ypar][:, :], w2_sb[:, jb * D + db * 128:jb * D + db * 128 + 128], h_sb[:, jb * 512:(jb + 1) * 512], start=False, stop=True))
                    k.wait('vector', 't', my)
                    if st[ypar]:
                        k.wait('vector', *st[ypar])
                    ydone[ypar] = k.op('vector', lambda v, ypar=ypar, db=db, mblk=mblk, b2c=b2c, ytl=ytl, pyy=pyy, wbc=wbc: v.scalar_tensor_tensor(out=ytl[ypar][:], in0=pyy[ypar][:, :], scalar=b2c[:, db:db + 1], in1=wbc[:, mblk * 512:(mblk + 1) * 512], op0=OP.add, op1=OP.mult))
                    k.wait('gpsimd', 'v', ydone[ypar])
                    dst = xchunk(scr['acc_hbm'], db, mblk * 512)
                    if ei == 0:
                        st[ypar] = k.dma('gpsimd', lambda g, dst=dst, ypar=ypar, ytl=ytl: g.dma_start(out=dst, in_=ytl[ypar][:]), slot=f'ac{ypar}')
                    else:
                        st[ypar] = k.dma('gpsimd', lambda g, dst=dst, ypar=ypar, ytl=ytl: g.dma_start(out=dst, in_=ytl[ypar][:], accum_op=OP.add), slot=f'ac{ypar}')
                mblk_t[mblk] = k.mark('t')
                if os.environ.get('MOE_PARANOID', '0') == '1':
                    k.barrier()
            k.barrier()
        es_e.close()
        emit_add_hbm(k, x_hbm, scr['acc_hbm'], out_hbm, es)
        k.barrier()

def pmaj(a2d):
    d, ntok = a2d.shape
    a = np.ascontiguousarray(a2d).reshape(PB, 128, ntok // 512, 512)
    return np.ascontiguousarray(a.transpose(1, 2, 0, 3).reshape(128, PB * ntok))

def unpmaj(a, d, ntok):
    a4 = np.asarray(a).reshape(128, ntok // 512, PB, 512)
    return np.ascontiguousarray(a4.transpose(2, 0, 1, 3).reshape(d, ntok))

def col_layout(vec):
    d = vec.shape[-1]
    nb = d // 128
    return np.ascontiguousarray(vec.reshape(-1, nb, 128).transpose(2, 0, 1).reshape(128, -1))


def _host_consts():
    dim, theta = 64, 10000.0
    inv = 1.0 / theta ** (np.arange(0, dim, 2) / dim)
    pos = np.arange(T, dtype=np.float64)
    fr = pos[None, :] * np.concatenate([inv, inv])[:, None]
    cos = np.cos(fr)
    sin = np.sin(fr)
    sgn = np.where(np.arange(64) < 32, -1.0, 1.0)[:, None]
    sins = sgn * sin
    tab = np.concatenate([cos / 8.0, sins / 8.0, cos, sins], axis=1)
    rope = np.concatenate([tab, tab], axis=0).astype(np.float32)
    p = np.zeros((64, 64), np.float32)
    p[(np.arange(64) + 32) % 64, np.arange(64)] = 1.0
    z = np.zeros((64, 64), np.float32)
    p64 = np.block([[p, z], [z, p]]).astype(np.float32)
    m2 = np.zeros((128, 4 * 512), np.float32)
    for r in range(4):
        for pp_ in range(128):
            m2[pp_, r * 512:(r + 1) * 512] = (r * 128 + pp_) <= np.arange(512)
    glam = np.ones((128, T), np.float32)
    glam[:, ::64] = 0.0
    ltm = np.zeros((64, 64), np.float32)
    for r in range(64):
        ltm[r, r:] = 1.0
    return dict(rope=rope, p64=p64, m2=m2, glam=glam, ltm=ltm,
                ident=np.eye(128, dtype=np.float32))


def _stackT(w):
    return np.ascontiguousarray(np.concatenate([w[l].T for l in range(w.shape[0])], axis=0))


def _host_prep(inputs):
    tr = lambda a: np.ascontiguousarray(np.asarray(a).T)
    g = dict(_host_consts())
    g["a_wqT"] = tr(inputs["L0_attn_a_wq"])
    g["a_wkvT"] = tr(inputs["L0_attn_a_w_kv_down"])
    g["a_wkupT"] = tr(inputs["L0_attn_a_w_k_up"])
    g["a_wvupT"] = tr(inputs["L0_attn_a_w_v_up"])
    g["a_woT"] = tr(inputs["L0_attn_a_wo"])
    g["f0_gwT"] = tr(inputs["L0_ffn_gate_w"])
    g["f0_lncols"] = np.concatenate(
        [col_layout(np.asarray(inputs["L0_ffn_ln_scale"])),
         col_layout(np.asarray(inputs["L0_ffn_ln_bias"]))], axis=1)
    g["f0_sWg"] = np.ascontiguousarray(inputs["L0_ffn_sWg"])
    g["f0_sWu"] = np.ascontiguousarray(inputs["L0_ffn_sWu"])
    g["f0_sWd"] = np.ascontiguousarray(inputs["L0_ffn_sWd"])
    g["f0_Wg"] = np.ascontiguousarray(np.asarray(inputs["L0_ffn_Wg"]).reshape(8 * D, 2048))
    g["f0_Wu"] = np.ascontiguousarray(np.asarray(inputs["L0_ffn_Wu"]).reshape(8 * D, 2048))
    g["f0_Wd"] = np.ascontiguousarray(np.asarray(inputs["L0_ffn_Wd"]).reshape(8 * 2048, D))
    g["s_BT"] = tr(inputs["L0_attn_b_B"])
    g["s_CT"] = tr(inputs["L0_attn_b_C"])
    g["s_DT"] = tr(inputs["L0_attn_b_D"])
    g["s_diagA"] = col_layout(np.ascontiguousarray(np.diagonal(np.asarray(inputs["L0_attn_b_A"]))))
    g["r_wkT"] = _stackT(np.asarray(inputs["L1_attn_a_Wk"]))
    g["r_wvT"] = _stackT(np.asarray(inputs["L1_attn_a_Wv"]))
    g["r_wrT"] = _stackT(np.asarray(inputs["L1_attn_a_Wr"]))
    g["r_woT"] = _stackT(np.asarray(inputs["L1_attn_a_Wo"]))
    g["r_wckT"] = _stackT(np.asarray(inputs["L1_attn_a_Wck"]))
    g["r_wcvT"] = _stackT(np.asarray(inputs["L1_attn_a_Wcv"]))
    g["r_wcrT"] = _stackT(np.asarray(inputs["L1_attn_a_Wcr"]))
    rc = np.zeros((128, 368), np.float32)
    rc[:, 0:8] = col_layout(np.asarray(inputs["L1_attn_a_ln0_w"]))
    rc[:, 8:16] = col_layout(np.asarray(inputs["L1_attn_a_ln0_b"]))
    for base, nm in ((16, "ln1_w"), (48, "ln1_b"), (80, "ln2_w"), (112, "ln2_b"),
                     (144, "tmk"), (176, "tmv"), (208, "tmr"), (240, "cmk"),
                     (272, "cmr"), (304, "time_decay"), (336, "time_first")):
        rc[:, base:base + 32] = col_layout(np.asarray(inputs["L1_attn_a_" + nm]))
    g["r_cols"] = rc
    g["f1_rw1"] = np.ascontiguousarray(inputs["L1_ffn_rw1"])
    g["f1_rw2"] = np.ascontiguousarray(inputs["L1_ffn_rw2"])
    g["f1_rb1c"] = col_layout(np.asarray(inputs["L1_ffn_rb1"]))
    g["f1_rb2r"] = np.ascontiguousarray(np.broadcast_to(np.asarray(inputs["L1_ffn_rb2"]), (128, 8)))
    g["f1_b1c"] = col_layout(np.asarray(inputs["L1_ffn_b1"]))
    g["f1_b2c"] = col_layout(np.asarray(inputs["L1_ffn_b2"]))
    g["f1_w1"] = np.ascontiguousarray(np.asarray(inputs["L1_ffn_w1"]).reshape(8 * D, 4096))
    g["f1_w2"] = np.ascontiguousarray(np.asarray(inputs["L1_ffn_w2"]).reshape(8 * 4096, D))
    g["g_winT"] = tr(inputs["L1_attn_b_W_in"])
    g["g_wg1T"] = tr(inputs["L1_attn_b_W_g1"])
    g["g_wg2T"] = tr(inputs["L1_attn_b_W_g2"])
    g["g_woutT"] = tr(inputs["L1_attn_b_W_out"])
    g["g_lnw"] = col_layout(np.asarray(inputs["L1_attn_b_ln_w"]))
    return g


def build_full():
    nc = bass.Bass("TRN2", target_bir_lowering=False)
    ap = lambda name, shape: nc.dram_tensor(name, shape, F32, kind="ExternalInput").ap()
    x_in = ap("x_in", [128, PB * NT])
    names = {
        "a_wqT": [D, 1024], "a_wkvT": [D, 512], "a_wkupT": [512, 1024],
        "a_wvupT": [512, 1024], "a_woT": [1024, D],
        "rope": [128, 4 * T], "p64": [128, 128], "m2": [128, 4 * 512],
        "glam": [128, T], "ltm": [64, 64], "ident": [128, 128],
        "f0_gwT": [D, 8], "f0_lncols": [128, 2 * PB],
        "f0_sWg": [D, 2048], "f0_sWu": [D, 2048], "f0_sWd": [2048, D],
        "f0_Wg": [8 * D, 2048], "f0_Wu": [8 * D, 2048], "f0_Wd": [8 * 2048, D],
        "s_BT": [D, D], "s_CT": [D, D], "s_DT": [D, D], "s_diagA": [128, PB],
        "r_wkT": [4 * D, D], "r_wvT": [4 * D, D], "r_wrT": [4 * D, D],
        "r_woT": [4 * D, D], "r_wckT": [4 * D, 4096], "r_wcvT": [4 * 4096, D],
        "r_wcrT": [4 * D, D], "r_cols": [128, 368],
        "f1_rw1": [D, 4096], "f1_rw2": [4096, 8], "f1_rb1c": [128, 32],
        "f1_rb2r": [128, 8], "f1_b1c": [128, 8 * 32], "f1_b2c": [128, 8 * PB],
        "f1_w1": [8 * D, 4096], "f1_w2": [8 * 4096, D],
        "g_winT": [D, 3072], "g_wg1T": [D, 128], "g_wg2T": [128, D],
        "g_woutT": [D, D], "g_lnw": [128, PB],
    }
    A = {n: ap(n, sh) for n, sh in names.items()}
    y = nc.dram_tensor("y", [128, PB * NT], F32, kind="ExternalOutput").ap()
    xs = nc.dram_tensor("xs_scr", [128, PB * NT], F32).ap()
    xn_hbm = nc.dram_tensor("xn_scr", [128, PB * NT], BF16).ap()
    acc_hbm = nc.dram_tensor("acc_scr", [128, PB * NT], F32).ap()
    wrows_hbm = nc.dram_tensor("wrows_scr", [8, NT], BF16).ap()
    og_hbm = nc.dram_tensor("og_scr", [128, 8 * NT], BF16).ap()
    xr_hbm = nc.dram_tensor("xr_scr", [128, PB * NT], F32).ap()
    h_hbm = nc.dram_tensor("h_scr", [128, PB * (NT + 2)], BF16).ap()
    aux_hbm = nc.dram_tensor("aux_scr", [128, 16 * NT], BF16).ap()

    def wap(base_ap, off, rows, cols_):
        return hview(base_ap, off, [[cols_, 128], [128 * cols_, rows // 128], [1, cols_]])

    es = ExitStack()
    with es:
        k = K(nc, es)
        consts = emit_consts(k, es, dict(ident=A["ident"]))
        scr = dict(xn_hbm=xn_hbm, acc_hbm=acc_hbm, wrows_hbm=wrows_hbm,
                   og_hbm=og_hbm, xr_hbm=xr_hbm, h_hbm=h_hbm, aux_hbm=aux_hbm)
        emit_attn(k, x_in, xs, dict(wqT=A["a_wqT"], wkvT=A["a_wkvT"],
                  wkupT=A["a_wkupT"], wvupT=A["a_wvupT"], woT=A["a_woT"],
                  rope=A["rope"], p64=A["p64"], m2=A["m2"]), consts, scr)
        I0 = 2048
        f0_experts = [dict(shared=True, wg=wap(A["f0_sWg"], 0, D, I0),
                           wu=wap(A["f0_sWu"], 0, D, I0), wd=wap(A["f0_sWd"], 0, I0, D))]
        for e in range(8):
            f0_experts.append(dict(
                wg=wap(A["f0_Wg"], e * D * I0, D, I0),
                wu=wap(A["f0_Wu"], e * D * I0, D, I0),
                wd=wap(A["f0_Wd"], e * I0 * D, I0, D),
                wrow=hview(wrows_hbm, e * NT, [[NT, 1], [1, NT]])))
        W_f0 = dict(gwT=A["f0_gwT"], ln_cols=A["f0_lncols"], experts=f0_experts)
        emit_moe_swiglu(k, xs, xs, W_f0, consts, scr)
        emit_ssm(k, xs, xs, dict(BT=A["s_BT"], CT=A["s_CT"], DT=A["s_DT"],
                                 diagA=A["s_diagA"]), consts)
        emit_moe_swiglu(k, xs, xs, W_f0, consts, scr)
        emit_rwkv(k, xs, xs, dict(wkT=A["r_wkT"], wvT=A["r_wvT"], wrT=A["r_wrT"],
                                  woT=A["r_woT"], wckT=A["r_wckT"], wcvT=A["r_wcvT"],
                                  wcrT=A["r_wcrT"], cols=A["r_cols"]), consts, scr)
        I1 = 4096
        f1_experts = []
        for e in range(8):
            f1_experts.append(dict(
                w1=wap(A["f1_w1"], e * D * I1, D, I1),
                w2=wap(A["f1_w2"], e * I1 * D, I1, D),
                wrow=hview(wrows_hbm, e * NT, [[NT, 1], [1, NT]])))
        W_f1 = dict(rw1T=wap(A["f1_rw1"], 0, D, I1), rw2=A["f1_rw2"],
                    rb1_cols=A["f1_rb1c"], rb2_rep=A["f1_rb2r"],
                    b1_cols=A["f1_b1c"], b2_cols=A["f1_b2c"], experts=f1_experts)
        emit_moe_mlp(k, xs, xs, W_f1, consts, scr)
        emit_gla(k, xs, xs, dict(winT=A["g_winT"], wg1T=A["g_wg1T"],
                                 wg2T=A["g_wg2T"], woutT=A["g_woutT"],
                                 lnw_col=A["g_lnw"], glam=A["glam"], ltm=A["ltm"]),
                 consts, scr)
        emit_moe_mlp(k, xs, y, W_f1, consts, scr)
        with nc.Block() as block:
            k.finalize(block)
    return nc


def kernel(**inputs):
    g = _host_prep(inputs)
    x = np.asarray(inputs["L0_attn_a_x"], np.float32)
    in_maps = []
    for c in range(N_CORES):
        xc = np.ascontiguousarray(x[2 * c: 2 * c + 2].transpose(2, 0, 1).reshape(D, NT))
        m = dict(g)
        m["x_in"] = pmaj(xc)
        in_maps.append(m)
    nc = build_full()
    res = run_bass_kernel_spmd(nc, in_maps, core_ids=list(range(N_CORES)))
    outs = []
    for c in range(N_CORES):
        yc = unpmaj(np.asarray(res.results[c]["y"], np.float32), D, NT)
        outs.append(yc.reshape(D, BC, T).transpose(1, 2, 0))
    return np.ascontiguousarray(np.concatenate(outs, axis=0))


# version 09 (final)
# speedup vs baseline: 1.0342x; 1.0342x over previous
import os
import numpy as np
import ml_dtypes
from contextlib import ExitStack
import concourse.bass as bass
import concourse.mybir as mybir
from concourse.bass_utils import run_bass_kernel_spmd
from concourse.alu_op_type import AluOpType as OP
F32 = mybir.dt.float32
BF16 = mybir.dt.bfloat16
AF = mybir.ActivationFunctionType
AX = mybir.AxisListType
D = 1024
T = 2048
BC = 2
NT = BC * T
PB = D // 128
NTT = NT // 512
N_CORES = 8
EPS = 1e-05
ENGINES = ('tensor', 'vector', 'scalar', 'gpsimd', 'sync')
_UID = [0]

def uname(base):
    _UID[0] += 1
    return f'{base}_u{_UID[0]}'

def hview(ap, free_off, dims):
    return bass.AP(ap.tensor, ap.offset + free_off, dims)

def xtile(ap, tt, width=512):
    return hview(ap, tt * PB * width, [[PB * NT, 128], [1, PB * width]])

def xchunk(ap, db, tok):
    return hview(ap, (tok // 512) * PB * 512 + db * 512, [[PB * NT, 128], [1, 512]])

class K:

    def __init__(self, nc, es):
        self.nc = nc
        self.es = es
        self.streams = {e: [] for e in ENGINES}
        self.sems = {}
        self.cnt = {}
        self.eng_sem = {'tensor': 't', 'vector': 'v', 'scalar': 's', 'gpsimd': 'p', 'sync': 'y'}
        self.selfser = ('vector', 'scalar', 'gpsimd')
        self._last = {e: 0 for e in self.selfser}

    def sem(self, name):
        if name not in self.sems:
            self.sems[name] = self.es.enter_context(self.nc.semaphore(name))
            self.cnt[name] = 0
        return self.sems[name]

    def mark(self, name):
        self.sem(name)
        return self.cnt[name]

    def raw(self, eng, build):
        self.streams[eng].append(lambda e, b=build: b(e))

    def op(self, eng, build, dn=1):
        name = self.eng_sem[eng]
        h = self.sem(name)
        if eng in self.selfser and self._last[eng] > 0:
            prev = self._last[eng]
            self.streams[eng].append(lambda e, h=h, prev=prev: e.wait_ge(h, prev))
        self.cnt[name] += dn
        tgt = self.cnt[name]
        if eng in self.selfser:
            self._last[eng] = tgt
        self.streams[eng].append(lambda e, b=build, h=h, dn=dn: b(e).then_inc(h, dn))
        return tgt

    def tmile(self):
        return self.op('tensor', lambda t: t.nop())

    def dma(self, eng, build, slot=None):
        q = {'sync': 'y', 'gpsimd': 'g', 'scalar': 'c'}[eng]
        name = q + ':' + (slot or 'q')
        h = self.sem(name)
        self.cnt[name] += 16
        tgt = self.cnt[name]
        self.streams[eng].append(lambda e, b=build, h=h: b(e).then_inc(h, 16))
        return (name, tgt)

    def wait(self, eng, name, val):
        if val and val > 0:
            h = self.sem(name)
            self.streams[eng].append(lambda e, h=h, val=val: e.wait_ge(h, val))

    def waits(self, eng, marks):
        for name, val in marks.items():
            self.wait(eng, name, val)

    def allmarks(self):
        return {n: self.cnt[n] for n in self.cnt if self.cnt[n] > 0}

    def barrier(self):
        marks = self.allmarks()
        for e in ENGINES:
            self.waits(e, marks)

    def finalize(self, block):
        for e in ENGINES:
            fns = self.streams[e]
            if fns:
                getattr(block, e)(lambda eng, fns=fns: [f(eng) for f in fns])

def emit_consts(k, es, cin):
    nc = k.nc
    c = {}
    c['ones1_bf'] = es.enter_context(nc.sbuf_tensor(uname('c_ones1_bf'), [1, 128], BF16))
    c['ones1_f'] = es.enter_context(nc.sbuf_tensor(uname('c_ones1_f'), [1, 128], F32))
    c['eps_col'] = es.enter_context(nc.sbuf_tensor(uname('c_eps'), [128, 1], F32))
    c['ident_bf'] = es.enter_context(nc.sbuf_tensor(uname('c_ident_bf'), [128, 128], BF16))
    k.op('gpsimd', lambda g, c=c: g.memset(c['ones1_bf'][:], 1.0))
    k.op('gpsimd', lambda g, c=c: g.memset(c['ones1_f'][:], 1.0))
    k.op('gpsimd', lambda g, c=c: g.memset(c['eps_col'][:], EPS))
    nc.const_aps.register(F32, EPS, c['eps_col'][:])
    c['eps2_col'] = es.enter_context(nc.sbuf_tensor(uname('c_eps2'), [128, 1], F32))
    k.op('gpsimd', lambda g, c=c: g.memset(c['eps2_col'][:], EPS * EPS))
    nc.const_aps.register(F32, EPS * EPS, c['eps2_col'][:])
    if 'ident' in cin:
        k.dma('gpsimd', lambda g, c=c, cin=cin: g.dma_start(out=c['ident_bf'][:], in_=cin['ident']), slot='ci')
    k.barrier()
    return c

def emit_ln(k, src_hbm, consts, dst_sb=None, dst_hbm=None, gamma_col=None, beta_col=None, double=False, x_sb_out=None, dual=None, dst_hbm_fn=None):
    nc = k.nc
    ones1_bf = consts['ones1_bf']
    es = ExitStack()
    with es:
        sb = lambda n, s, d, es=es, nc=nc: es.enter_context(nc.sbuf_tensor(uname(n), s, d))
        ps_ = lambda n, s, es=es, nc=nc: es.enter_context(nc.psum_tensor(uname(n), s, F32))
        need_f32 = dual is not None or x_sb_out is not None
        DEPTH = 2 if dual is not None else 4
        xcol = [sb(f'ln_xcol{i}', [128, PB * 512], F32) for i in range(2)] if need_f32 else [None, None]
        xsq = [sb(f'ln_xsq{i}', [128, PB * 512], BF16) for i in range(2)]
        xb = [sb(f'ln_xb{i}', [128, PB * 512], BF16) for i in range(DEPTH)]
        nrows = 4096 if dual is not None else (3584 if double else 2560)
        rows = sb('ln_rows', [1, nrows], F32)
        crow = [sb(f'ln_crow{i}', [1, 512], BF16) for i in range(2)]
        orow = [sb(f'ln_orow{i}', [1, 512], BF16) for i in range(2)]
        cb_sb = [sb(f'ln_cb{i}', [128, 512], F32) for i in range(2)]
        ob_sb = [sb(f'ln_ob{i}', [128, 512], F32) for i in range(2)]
        if dual is not None:
            xb2 = sb('ln_xb2', [128, PB * 512], F32)
            cb2_sb = [sb(f'ln_cb2{i}', [128, 512], F32) for i in range(2)]
            ob2_sb = [sb(f'ln_ob2{i}', [128, 512], F32) for i in range(2)]
            if 'gw_sb' in dual:
                plg = [ps_(f'ln_plg{i}', [128, 8]) for i in range(2)]
        onesb = sb('ln_onesb', [128, 1], BF16)
        ps = [ps_(f'ln_ps{i}', [1, 512]) for i in range(2)]
        ps2 = [ps_(f'ln_ps2{i}', [1, 512]) for i in range(2)]
        pbc = ps_('ln_pbc', [128, 512])
        pbo = ps_('ln_pbo', [128, 512])
        ms = k.op('gpsimd', lambda g, onesb=onesb: g.memset(onesb[:], 1.0))
        k.wait('tensor', 'p', ms)
        a_sq = [0] * NTT
        a_f32 = [None] * NTT
        a_sums = [0] * NTT
        m_mu = [0] * NTT
        m_vv = [0] * NTT
        b_rm = [0] * NTT
        b_mbc = [0] * NTT
        b_copy = [0] * NTT
        b_v = [0] * NTT
        b_g = [0] * NTT
        b_dv = [0] * NTT
        b_dg = [0] * NTT
        sd = [None] * NTT
        sd32 = [None] * NTT
        lg_t = [0] * NTT
        lgv = {0: 0, 1: 0}
        for tt in range(NTT + 1):
            if tt >= 1:
                bt = tt - 1
                par = bt % 2
                buf = xcol[par]
                mu = rows[0:1, 0:512]
                m2 = rows[0:1, 512:1024]
                var = rows[0:1, 1024:1536]
                s1 = rows[0:1, 1536:2048]
                s2 = rows[0:1, 2048:2560]
                k.wait('vector', 't', a_sums[bt])
                if bt >= 1:
                    k.wait('vector', 't', b_mbc[bt - 1])
                m_mu[bt] = k.op('vector', lambda v, mu=mu, ps=ps, par=par: v.tensor_scalar(out=mu, in0=ps[par][:, :], scalar1=1.0 / D, scalar2=None, op0=OP.mult))
                k.op('vector', lambda v, m2=m2, mu=mu: v.tensor_tensor(out=m2, in0=mu, in1=mu, op=OP.mult))
                vv = k.op('vector', lambda v, var=var, ps2=ps2, m2=m2, par=par: v.scalar_tensor_tensor(out=var, in0=ps2[par][:, :], scalar=1.0 / D, in1=m2, op0=OP.mult, op1=OP.subtract))
                m_vv[bt] = vv
                k.wait('scalar', 'v', vv)
                if bt >= 1:
                    k.wait('scalar', 't', b_mbc[bt - 1])
                if double:
                    rowv = k.op('scalar', lambda s, s1=s1, var=var: s.activation(out=s1, in_=var, func=AF.Sqrt, scale=1.0 + EPS, bias=EPS * EPS))
                else:
                    rowv = k.op('scalar', lambda s, s1=s1, var=var: s.activation(out=s1, in_=var, func=AF.Sqrt, bias=EPS))
                if dual is not None:
                    rowv = k.op('scalar', lambda s, s2=s2, var=var: s.activation(out=s2, in_=var, func=AF.Sqrt, scale=1.0 + EPS, bias=EPS * EPS))
                b_rm[bt] = rowv
            if tt < NTT:
                apar = tt % 2
                ax = tt % DEPTH
                if tt >= DEPTH:
                    k.wait('gpsimd', 'v', b_v[tt - DEPTH])
                    k.wait('gpsimd', 'p', b_g[tt - DEPTH])
                    k.wait('gpsimd', 't', a_sums[tt - DEPTH])
                    k.wait('gpsimd', 's', a_sq[tt - DEPTH])
                    if sd[tt - DEPTH]:
                        k.wait('gpsimd', *sd[tt - DEPTH])
                gmark = k.dma('gpsimd', lambda g, tt=tt, ax=ax, xb=xb, src_hbm=src_hbm: g.dma_start(out=xb[ax][:], in_=xtile(src_hbm, tt)), slot=f'lnb{ax}')
                if need_f32:
                    abuf = xcol[apar]
                    if tt >= 2:
                        k.wait('sync', 'v', b_v[tt - 2])
                        k.wait('sync', 'p', b_g[tt - 2])
                    a_f32[tt] = k.dma('sync', lambda y, tt=tt, buf=abuf, src_hbm=src_hbm: y.dma_start(out=buf[:], in_=xtile(src_hbm, tt)), slot=f'lnx{apar}')
                k.wait('scalar', *gmark)
                if tt >= 2:
                    k.wait('scalar', 't', a_sums[tt - 2])
                sq = k.op('scalar', lambda s, ax=ax, par=apar, xb=xb, xsq=xsq: s.activation(out=xsq[par][:], in_=xb[ax][:], func=AF.Square))
                a_sq[tt] = sq
                k.wait('tensor', *gmark)
                k.wait('tensor', 's', sq)
                if tt >= 2:
                    k.wait('tensor', 'v', m_vv[tt - 2])
                for db in range(PB - 1):
                    k.raw('tensor', lambda t, db=db, ps=ps, onesb=onesb, xb=xb, par=apar, ax=ax: t.matmul(ps[par][:, :], onesb[:, :], xb[ax][:, db * 512:(db + 1) * 512], start=db == 0, stop=False))
                k.raw('tensor', lambda t, ps=ps, onesb=onesb, xb=xb, par=apar, ax=ax: t.matmul(ps[par][:, :], onesb[:, :], xb[ax][:, (PB - 1) * 512:PB * 512], start=False, stop=True))
                for db in range(PB - 1):
                    k.raw('tensor', lambda t, db=db, ps2=ps2, onesb=onesb, xsq=xsq, par=apar: t.matmul(ps2[par][:, :], onesb[:, :], xsq[par][:, db * 512:(db + 1) * 512], start=db == 0, stop=False))
                a_sums[tt] = k.op('tensor', lambda t, ps2=ps2, onesb=onesb, xsq=xsq, par=apar: t.matmul(ps2[par][:, :], onesb[:, :], xsq[par][:, (PB - 1) * 512:PB * 512], start=False, stop=True))
            if tt >= 1:
                bt = tt - 1
                par = bt % 2
                buf = xcol[par]
                rowv = b_rm[bt]
                mu = rows[0:1, 0:512]
                s1 = rows[0:1, 1536:2048]
                s2 = rows[0:1, 2048:2560]
                k.wait('tensor', 's', rowv)
                k.wait('tensor', 'v', m_vv[bt])
                if bt >= 1:
                    k.wait('tensor', 'v', b_copy[bt - 1])
                    if dual is not None:
                        k.wait('tensor', 'v', b_dv[bt - 1])
                        k.wait('tensor', 'p', b_dg[bt - 1])
                k.raw('tensor', lambda t, pbc=pbc, consts=consts, s1=s1: t.matmul(pbc[:, :], consts['ones1_f'][0:1, :], s1, start=True, stop=True))
                mbc = k.op('tensor', lambda t, pbo=pbo, consts=consts, mu=mu: t.matmul(pbo[:, :], consts['ones1_f'][0:1, :], mu, start=True, stop=True))
                b_mbc[bt] = mbc
                k.wait('vector', 't', mbc)
                if bt >= 2:
                    k.wait('vector', 'p', b_g[bt - 2])
                rcp = k.op('vector', lambda v, par=par, cb_sb=cb_sb, pbc=pbc: v.reciprocal(out=cb_sb[par][:], in_=pbc[:, :]))
                if dual is not None:
                    vob = k.op('vector', lambda v, par=par, ob_sb=ob_sb, pbo=pbo: v.tensor_copy(out=ob_sb[par][:], in_=pbo[:, :]))
                    k.wait('tensor', 'v', vob)
                    mbc2 = k.op('tensor', lambda t, pbc=pbc, consts=consts, s2=s2: t.matmul(pbc[:, :], consts['ones1_f'][0:1, :], s2, start=True, stop=True))
                    dv = 0
                    dg = 0
                    if bt >= 1 and sd32[bt - 1]:
                        k.wait('vector', *sd32[bt - 1])
                        k.wait('gpsimd', *sd32[bt - 1])
                    if bt >= 1 and 'gw_sb' in dual:
                        k.wait('gpsimd', 't', lg_t[bt - 1])
                    k.wait('vector', 't', mbc2)
                    k.op('vector', lambda v, par=par, cb2_sb=cb2_sb, pbc=pbc: v.reciprocal(out=cb2_sb[par][:], in_=pbc[:, :]))
                    vd2 = k.op('vector', lambda v, par=par, ob2_sb=ob2_sb, pbo=pbo: v.tensor_copy(out=ob2_sb[par][:], in_=pbo[:, :]))
                    k.wait('gpsimd', 'v', vd2)
                    k.wait('vector', *a_f32[bt])
                    k.wait('gpsimd', *a_f32[bt])
                    for db in range(PB):
                        eng = 'vector' if db < 6 else 'gpsimd'
                        x2s = xb2[:, db * 512:(db + 1) * 512]
                        xcs = buf[:, db * 512:(db + 1) * 512]
                        k.op(eng, lambda v, par=par, xcs=xcs, x2s=x2s, ob2_sb=ob2_sb: v.tensor_tensor(out=x2s, in0=xcs, in1=ob2_sb[par][:], op=OP.subtract))
                        k.op(eng, lambda v, par=par, x2s=x2s, cb2_sb=cb2_sb: v.tensor_tensor(out=x2s, in0=x2s, in1=cb2_sb[par][:], op=OP.mult))
                        dm = k.op(eng, lambda v, x2s=x2s, db=db, dual=dual: v.tensor_scalar(out=x2s, in0=x2s, scalar1=dual['gamma_col'][:, db:db + 1], scalar2=dual['beta_col'][:, db:db + 1], op0=OP.mult, op1=OP.add))
                        if eng == 'vector':
                            dv = dm
                        else:
                            dg = dm
                    b_dv[bt] = dv
                    b_dg[bt] = dg
                if bt >= 2 and sd[bt - 2]:
                    k.wait('vector', *sd[bt - 2])
                    k.wait('gpsimd', *sd[bt - 2])
                apl = 0
                bx = bt % DEPTH
                for db in range(PB):
                    eng = 'vector' if dual is None or db < 6 else 'gpsimd'
                    xbs = xb[bx][:, db * 512:(db + 1) * 512]
                    if eng == 'gpsimd':
                        k.wait(eng, 'v', rcp)
                        if dual is not None:
                            k.wait(eng, 'v', vob)
                        k.op(eng, lambda v, par=par, xbs=xbs, ob_sb=ob_sb: v.tensor_tensor(out=xbs, in0=xbs, in1=ob_sb[par][:], op=OP.subtract))
                    else:
                        k.op(eng, lambda v, xbs=xbs, pbo=pbo: v.tensor_tensor(out=xbs, in0=xbs, in1=pbo[:, :], op=OP.subtract))
                    m_ = k.op(eng, lambda v, par=par, xbs=xbs, cb_sb=cb_sb: v.tensor_tensor(out=xbs, in0=xbs, in1=cb_sb[par][:], op=OP.mult))
                    if gamma_col is not None:
                        m_ = k.op(eng, lambda v, xbs=xbs, db=db, gamma_col=gamma_col, beta_col=beta_col: v.tensor_scalar(out=xbs, in0=xbs, scalar1=gamma_col[:, db:db + 1], scalar2=beta_col[:, db:db + 1], op0=OP.mult, op1=OP.add))
                    if eng == 'vector':
                        apl = m_
                    if dst_sb is not None:
                        ceng = 'vector' if db < 4 else 'gpsimd'
                        if ceng == 'gpsimd':
                            k.wait(ceng, 'v', m_)
                        k.op(ceng, lambda v, xbs=xbs, db=db, bt=bt, dst_sb=dst_sb: v.tensor_copy(out=dst_sb[:, db * NT + bt * 512:db * NT + bt * 512 + 512], in_=xbs))
                b_copy[bt] = apl
                if x_sb_out is not None:
                    k.wait('vector', *a_f32[bt])
                    k.wait('gpsimd', *a_f32[bt])
                    for db in range(PB):
                        eng = 'vector' if db < 4 else 'gpsimd'
                        xcs = buf[:, db * 512:(db + 1) * 512]
                        k.op(eng, lambda v, xcs=xcs, db=db, bt=bt, x_sb_out=x_sb_out: v.tensor_copy(out=x_sb_out[:, db * NT + bt * 512:db * NT + bt * 512 + 512], in_=xcs))
                b_v[bt] = k.mark('v')
                b_g[bt] = k.mark('p')
                if dual is not None and 'dst32_fn' in dual:
                    k.wait('scalar', 'v', b_dv[bt])
                    k.wait('scalar', 'p', b_dg[bt])
                    if bt >= 1 and sd32[bt - 1]:
                        k.wait('scalar', *sd32[bt - 1])
                    sd32[bt] = k.dma('scalar', lambda s, bt=bt, dual=dual, xb2=xb2: s.dma_start(out=dual['dst32_fn'](bt), in_=xb2[:]), slot='ln32')
                if dual is not None and 'gw_sb' in dual:
                    k.wait('tensor', 'v', b_dv[bt])
                    k.wait('tensor', 'p', b_dg[bt])
                    for ts4 in range(4):
                        qpar = ts4 % 2
                        k.wait('tensor', 'v', lgv[qpar])
                        for db in range(PB - 1):
                            k.raw('tensor', lambda t, db=db, ts4=ts4, qpar=qpar, plg=plg, xb2=xb2, dual=dual: t.matmul(plg[qpar][:, :], xb2[:, db * 512 + ts4 * 128:db * 512 + ts4 * 128 + 128], dual['gw_sb'][:, db * 8:(db + 1) * 8], start=db == 0, stop=False))
                        db = PB - 1
                        mlg = k.op('tensor', lambda t, db=db, ts4=ts4, qpar=qpar, plg=plg, xb2=xb2, dual=dual: t.matmul(plg[qpar][:, :], xb2[:, db * 512 + ts4 * 128:db * 512 + ts4 * 128 + 128], dual['gw_sb'][:, db * 8:(db + 1) * 8], start=False, stop=True))
                        k.wait('vector', 't', mlg)
                        lgv[qpar] = k.op('vector', lambda v, bt=bt, ts4=ts4, qpar=qpar, dual=dual, plg=plg: v.tensor_copy(out=dual['logits_sb'][:, (bt * 4 + ts4) * 8:(bt * 4 + ts4) * 8 + 8], in_=plg[qpar][:, :]))
                        lg_t[bt] = mlg
                if dst_hbm is not None or dst_hbm_fn is not None:
                    k.wait('scalar', 'v', b_v[bt])
                    k.wait('scalar', 'p', b_g[bt])
                    if bt >= 2 and sd[bt - 2]:
                        k.wait('scalar', *sd[bt - 2])
                    dstap = dst_hbm_fn(bt) if dst_hbm_fn is not None else xtile(dst_hbm, bt)
                    sd[bt] = k.dma('scalar', lambda s, dstap=dstap, bx=bx, xb=xb: s.dma_start(out=dstap, in_=xb[bx][:]), slot=f'lns{bt % 2}')
        k.barrier()

def load_weight_bf16(k, w_hbm, dst_sb, rows, cols, slot='w'):
    nb = rows // 128
    src = bass.AP(w_hbm.tensor, w_hbm.offset, [[cols, 128], [128 * cols, nb], [1, cols]])
    return k.dma('gpsimd', lambda g: g.dma_start(out=dst_sb[:, 0:nb * cols], in_=src), slot=slot)

def emit_ssm(k, x_hbm, out_hbm, W, consts):
    nc = k.nc
    es = ExitStack()
    with es:
        sb = lambda n, s, d, es=es, nc=nc: es.enter_context(nc.sbuf_tensor(uname(n), s, d))
        psum = lambda n, s, es=es, nc=nc: es.enter_context(nc.psum_tensor(uname(n), s, F32))
        xn = sb('ssm_xn', [128, PB * NT], BF16)
        Xb = sb('ssm_X', [128, PB * NT], BF16)
        a_sb = sb('ssm_a', [128, PB], F32)
        adma = k.dma('sync', lambda y, a_sb=a_sb, W=W: y.dma_start(out=a_sb[:], in_=W['diagA']), slot='wa')
        emit_ln(k, x_hbm, consts, dst_sb=xn)
        pbu = [psum(f'ssm_pbu{i}', [128, 512]) for i in range(2)]
        py = [psum(f'ssm_py{i}', [128, 512]) for i in range(2)]
        es2 = ExitStack()
        sb2 = lambda n, s, d, es2=es2, nc=nc: es2.enter_context(nc.sbuf_tensor(uname(n), s, d))
        wB = sb2('ssm_wB', [128, PB * D], BF16)
        arep = sb2('ssm_arep', [128, T], F32)
        bu = sb2('ssm_bu', [128, T], F32)
        wl_b = load_weight_bf16(k, W['BT'], wB, D, D, slot='wb')
        k.wait('tensor', *wl_b)
        evac = {0: 0, 1: 0}
        for db in range(PB):
            k.wait('vector', *adma)
            k.op('vector', lambda v, arep=arep: v.memset(arep[:], 1.0))
            k.op('vector', lambda v, db=db, arep=arep, a_sb=a_sb: v.tensor_scalar(out=arep[:], in0=arep[:], scalar1=a_sb[:, db:db + 1], scalar2=None, op0=OP.mult))
            for b in range(BC):
                for q in range(T // 512):
                    tt = b * (T // 512) + q
                    par = tt % 2
                    k.wait('tensor', 'v', evac[par])
                    for mb in range(PB - 1):
                        k.raw('tensor', lambda t, mb=mb, db=db, tt=tt, par=par, pbu=pbu, wB=wB, xn=xn: t.matmul(pbu[par][:, :], wB[:, mb * D + db * 128:mb * D + db * 128 + 128], xn[:, mb * NT + tt * 512:mb * NT + tt * 512 + 512], start=mb == 0, stop=False))
                    mb = PB - 1
                    mbu = k.op('tensor', lambda t, mb=mb, db=db, tt=tt, par=par, pbu=pbu, wB=wB, xn=xn: t.matmul(pbu[par][:, :], wB[:, mb * D + db * 128:mb * D + db * 128 + 128], xn[:, mb * NT + tt * 512:mb * NT + tt * 512 + 512], start=False, stop=True))
                    k.wait('vector', 't', mbu)
                    evac[par] = k.op('vector', lambda v, q=q, par=par, bu=bu, pbu=pbu: v.tensor_copy(out=bu[:, q * 512:(q + 1) * 512], in_=pbu[par][:, :]))
                k.op('vector', lambda v, db=db, b=b, Xb=Xb, arep=arep, bu=bu: v.tensor_tensor_scan(out=Xb[:, db * NT + b * T:db * NT + (b + 1) * T], data0=arep[:], data1=bu[:, :], initial=0.0, op0=OP.mult, op1=OP.add))
        k.barrier()
        es2.close()
        es3 = ExitStack()
        sb3 = lambda n, s, d, es3=es3, nc=nc: es3.enter_context(nc.sbuf_tensor(uname(n), s, d))
        wC = sb3('ssm_wC', [128, PB * D], BF16)
        wD = sb3('ssm_wD', [128, PB * D], BF16)
        xcol = [sb3(f'ssm_xc{i}', [128, PB * 512], F32) for i in range(2)]
        ytile = [sb3(f'ssm_yt{i}', [128, 512], F32) for i in range(2)]
        wl_c = load_weight_bf16(k, W['CT'], wC, D, D, slot='wc')
        wl_d = load_weight_bf16(k, W['DT'], wD, D, D, slot='wd')
        k.wait('tensor', *wl_c)
        k.wait('tensor', *wl_d)
        ydone = {0: 0, 1: 0}
        st = {0: None, 1: None}
        tilev = [0] * NTT
        for tt in range(NTT):
            par = tt % 2
            if tt >= 2:
                k.wait('sync', 'v', tilev[tt - 2])
            xd = k.dma('sync', lambda y, tt=tt, par=par, xcol=xcol, x_hbm=x_hbm: y.dma_start(out=xcol[par][:, :], in_=xtile(x_hbm, tt)), slot=f'yx{par}')
            for db in range(PB):
                ppar = (tt * PB + db) % 2
                k.wait('tensor', 'v', ydone[ppar])
                for nb in range(PB):
                    k.raw('tensor', lambda t, nb=nb, db=db, tt=tt, ppar=ppar, py=py, wC=wC, Xb=Xb: t.matmul(py[ppar][:, :], wC[:, nb * D + db * 128:nb * D + db * 128 + 128], Xb[:, nb * NT + tt * 512:nb * NT + tt * 512 + 512], start=nb == 0, stop=False))
                for mb in range(PB - 1):
                    k.raw('tensor', lambda t, mb=mb, db=db, tt=tt, ppar=ppar, py=py, wD=wD, xn=xn: t.matmul(py[ppar][:, :], wD[:, mb * D + db * 128:mb * D + db * 128 + 128], xn[:, mb * NT + tt * 512:mb * NT + tt * 512 + 512], start=False, stop=False))
                mb = PB - 1
                my = k.op('tensor', lambda t, mb=mb, db=db, tt=tt, ppar=ppar, py=py, wD=wD, xn=xn: t.matmul(py[ppar][:, :], wD[:, mb * D + db * 128:mb * D + db * 128 + 128], xn[:, mb * NT + tt * 512:mb * NT + tt * 512 + 512], start=False, stop=True))
                k.wait('vector', 't', my)
                k.wait('vector', *xd)
                if st[ppar]:
                    k.wait('vector', *st[ppar])
                ydone[ppar] = k.op('vector', lambda v, db=db, par=par, ppar=ppar, ytile=ytile, py=py, xcol=xcol: v.tensor_tensor(out=ytile[ppar][:, :], in0=py[ppar][:, :], in1=xcol[par][:, db * 512:(db + 1) * 512], op=OP.add))
                k.wait('scalar', 'v', ydone[ppar])
                st[ppar] = k.dma('scalar', lambda s, db=db, tt=tt, ppar=ppar, out_hbm=out_hbm, ytile=ytile: s.dma_start(out=xchunk(out_hbm, db, tt * 512), in_=ytile[ppar][:, :]), slot=f'ys{ppar}')
            tilev[tt] = k.mark('v')
        k.barrier()
        es3.close()

def emit_gate_rows(k, logits_sb, wrows_hbm, consts, mode):
    nc = k.nc
    es2 = ExitStack()
    with es2:
        sb = lambda n, s, d, es2=es2, nc=nc: es2.enter_context(nc.sbuf_tensor(uname(n), s, d))
        pwt = es2.enter_context(nc.psum_tensor(uname('gt_pwt'), [8, 128], BF16))
        lg = sb('gt_lg', [128, 8], F32)
        cols = sb('gt_cols', [128, 10], F32)
        e_sb = sb('gt_e', [128, 8], F32)
        w_sb = sb('gt_w', [128, 8], F32)
        wb = sb('gt_wb', [128, 8], BF16)
        ident_bf = consts['ident_bf']
        wrows = sb('gt_wrows', [8, NT], BF16)
        prev_tr = 0
        for tt32 in range(NT // 128):
            cp = k.op('vector', lambda v, tt32=tt32, lg=lg, logits_sb=logits_sb: v.tensor_copy(out=lg[:], in_=logits_sb[:, tt32 * 8:(tt32 + 1) * 8]))
            m1 = cols[:, 0:1]
            nm1 = cols[:, 1:2]
            ssum = cols[:, 2:3]
            m1p = cols[:, 3:4]
            m2p = cols[:, 4:5]
            den = cols[:, 5:6]
            k.op('vector', lambda v, m1=m1, lg=lg: v.tensor_reduce(out=m1, in_=lg[:], axis=AX.X, op=OP.max))
            vm = k.op('vector', lambda v, nm1=nm1, m1=m1: v.tensor_scalar(out=nm1, in0=m1, scalar1=-1.0, scalar2=None, op0=OP.mult))
            if mode == 'renorm':
                k.wait('scalar', 'v', vm)
                se = k.op('scalar', lambda s, e_sb=e_sb, lg=lg, nm1=nm1, ssum=ssum: s.activation(out=e_sb[:], in_=lg[:], func=AF.Exp, bias=nm1, accum_out=ssum))
                k.wait('vector', 's', se)
                k.op('vector', lambda v, den=den, ssum=ssum: v.reciprocal(out=den, in_=ssum))
                k.op('vector', lambda v, w_sb=w_sb, e_sb=e_sb, den=den: v.tensor_scalar(out=w_sb[:], in0=e_sb[:], scalar1=den, scalar2=None, op0=OP.mult))
                k.op('vector', lambda v, m1p=m1p, w_sb=w_sb: v.tensor_reduce(out=m1p, in_=w_sb[:], axis=AX.X, op=OP.max))
                k.op('vector', lambda v, e_sb=e_sb, w_sb=w_sb, m1p=m1p: v.tensor_scalar(out=e_sb[:], in0=w_sb[:], scalar1=m1p, scalar2=None, op0=OP.is_lt))
                k.op('vector', lambda v, e_sb=e_sb, w_sb=w_sb: v.tensor_tensor(out=e_sb[:], in0=e_sb[:], in1=w_sb[:], op=OP.mult))
                k.op('vector', lambda v, m2p=m2p, e_sb=e_sb: v.tensor_reduce(out=m2p, in_=e_sb[:], axis=AX.X, op=OP.max))
                k.op('vector', lambda v, den=den, m1p=m1p, m2p=m2p: v.tensor_scalar(out=den, in0=m1p, scalar1=m2p, scalar2=EPS, op0=OP.add, op1=OP.add))
                k.op('vector', lambda v, den=den: v.reciprocal(out=den, in_=den))
                k.op('vector', lambda v, e_sb=e_sb, w_sb=w_sb, m2p=m2p: v.tensor_scalar(out=e_sb[:], in0=w_sb[:], scalar1=m2p, scalar2=None, op0=OP.is_ge))
                k.op('vector', lambda v, w_sb=w_sb, e_sb=e_sb: v.tensor_tensor(out=w_sb[:], in0=w_sb[:], in1=e_sb[:], op=OP.mult))
                k.op('vector', lambda v, w_sb=w_sb, den=den: v.tensor_scalar(out=w_sb[:], in0=w_sb[:], scalar1=den, scalar2=None, op0=OP.mult))
            else:
                k.op('vector', lambda v, e_sb=e_sb, lg=lg, m1=m1: v.tensor_scalar(out=e_sb[:], in0=lg[:], scalar1=m1, scalar2=None, op0=OP.is_ge))
                k.op('vector', lambda v, w_sb=w_sb, e_sb=e_sb, lg=lg: v.scalar_tensor_tensor(out=w_sb[:], in0=e_sb[:], scalar=-1e+30, in1=lg[:], op0=OP.mult, op1=OP.add))
                m2 = cols[:, 6:7]
                k.op('vector', lambda v, m2=m2, w_sb=w_sb: v.tensor_reduce(out=m2, in_=w_sb[:], axis=AX.X, op=OP.max))
                dcol = cols[:, 7:8]
                vd = k.op('vector', lambda v, dcol=dcol, m2=m2, m1=m1: v.tensor_scalar(out=dcol, in0=m2, scalar1=m1, scalar2=None, op0=OP.subtract))
                s1 = cols[:, 8:9]
                k.wait('scalar', 'v', vd)
                sv1 = k.op('scalar', lambda s, s1=s1, dcol=dcol: s.activation(out=s1, in_=dcol, func=AF.Sigmoid, scale=-1.0))
                k.wait('vector', 's', sv1)
                k.op('vector', lambda v, e_sb=e_sb, s1=s1: v.tensor_scalar(out=e_sb[:], in0=e_sb[:], scalar1=s1, scalar2=None, op0=OP.mult))
                k.op('vector', lambda v, w_sb=w_sb, lg=lg, m2=m2: v.tensor_scalar(out=w_sb[:], in0=lg[:], scalar1=m2, scalar2=None, op0=OP.is_ge))
                k.op('vector', lambda v, lg=lg, m1=m1: v.tensor_scalar(out=lg[:], in0=lg[:], scalar1=m1, scalar2=None, op0=OP.is_lt))
                k.op('vector', lambda v, w_sb=w_sb, lg=lg: v.tensor_tensor(out=w_sb[:], in0=w_sb[:], in1=lg[:], op=OP.mult))
                k.op('vector', lambda v, dcol=dcol, s1=s1: v.tensor_scalar(out=dcol, in0=s1, scalar1=-1.0, scalar2=1.0, op0=OP.mult, op1=OP.add))
                k.op('vector', lambda v, w_sb=w_sb, dcol=dcol: v.tensor_scalar(out=w_sb[:], in0=w_sb[:], scalar1=dcol, scalar2=None, op0=OP.mult))
                k.op('vector', lambda v, w_sb=w_sb, e_sb=e_sb: v.tensor_tensor(out=w_sb[:], in0=w_sb[:], in1=e_sb[:], op=OP.add))
            vw = k.op('vector', lambda v, wb=wb, w_sb=w_sb: v.tensor_copy(out=wb[:], in_=w_sb[:]))
            k.wait('tensor', 'v', vw)
            if prev_tr:
                k.wait('tensor', 'v', prev_tr)
            mtr = k.op('tensor', lambda t, pwt=pwt, wb=wb, ident_bf=ident_bf: t.transpose(pwt[:, :], wb[:, :], ident_bf[:]))
            k.wait('vector', 't', mtr)
            prev_tr = k.op('vector', lambda v, tt32=tt32, wrows=wrows, pwt=pwt: v.tensor_copy(out=wrows[:, tt32 * 128:(tt32 + 1) * 128], in_=pwt[:, :]))
        k.wait('sync', 'v', prev_tr)
        k.dma('sync', lambda y, wrows_hbm=wrows_hbm, wrows=wrows: y.dma_start(out=wrows_hbm, in_=wrows[:]), slot='gw')
        k.barrier()

def emit_moe_swiglu(k, x_hbm, out_hbm, W, consts, scr):
    nc = k.nc
    es = ExitStack()
    with es:
        sb = lambda n, s, d, es=es, nc=nc: es.enter_context(nc.sbuf_tensor(uname(n), s, d))
        lncols = sb('mo_lncols', [128, 2 * PB], F32)
        gwf = sb('mo_gwf', [128, PB * 8], F32)
        logits_sb = sb('mo_logits', [128, NT // 128 * 8], F32)
        cdma = k.dma('sync', lambda y, lncols=lncols, W=W: y.dma_start(out=lncols[:], in_=W['ln_cols']), slot='wa')
        gdma = k.dma('sync', lambda y, gwf=gwf, W=W: y.dma_start(out=gwf[:], in_=hview(W['gwT'], 0, [[8, 128], [128 * 8, PB], [1, 8]])), slot='wb')
        k.wait('vector', *cdma)
        k.wait('scalar', *cdma)
        k.wait('tensor', *gdma)
        emit_ln(k, x_hbm, consts, dst_hbm=scr['xn_hbm'], dual=dict(gamma_col=lncols[:, 0:PB], beta_col=lncols[:, PB:2 * PB], gw_sb=gwf, logits_sb=logits_sb))
        emit_gate_rows(k, logits_sb, scr['wrows_hbm'], consts, mode='renorm')
        k.barrier()
    es = ExitStack()
    with es:
        es_e = ExitStack()
        sb = lambda n, s, d, es_e=es_e, nc=nc: es_e.enter_context(nc.sbuf_tensor(uname(n), s, d))
        psum = lambda n, s, d=F32, es_e=es_e, nc=nc: es_e.enter_context(nc.psum_tensor(uname(n), s, d))
        I = 2048
        JB = I // 128
        wg_sb = sb('mo_wg', [128, PB * I], BF16)
        wu_sb = sb('mo_wu', [128, PB * I], BF16)
        wd_sb = sb('mo_wd', [128, JB * D], BF16)
        h_sb = sb('mo_h', [128, JB * 512], BF16)
        sg = sb('mo_sg', [128, 512], BF16)
        xnc = [sb(f'mo_xnc{i}', [128, PB * 512], BF16) for i in range(2)]
        wrow = sb('mo_wrow', [1, NT], BF16)
        wbc = sb('mo_wbc', [128, NT], F32)
        ytl = [sb(f'mo_ytl{i}', [128, 512], F32) for i in range(2)]
        pg = [psum(f'mo_pg{i}', [128, 512]) for i in range(2)]
        pu = [psum(f'mo_pu{i}', [128, 512]) for i in range(2)]
        pyy = [psum(f'mo_pyy{i}', [128, 512]) for i in range(2)]
        pwb = psum('mo_pwb', [128, 512])
        n_exp = len(W['experts'])
        for ei, ex in enumerate(W['experts']):
            shared = ex.get('shared', False)
            wlg = k.dma('gpsimd', lambda g, ex=ex, wg_sb=wg_sb: g.dma_start(out=wg_sb[:, :], in_=ex['wg']), slot='wg')
            wlu = k.dma('gpsimd', lambda g, ex=ex, wu_sb=wu_sb: g.dma_start(out=wu_sb[:, :], in_=ex['wu']), slot='wu')
            wld = k.dma('gpsimd', lambda g, ex=ex, wd_sb=wd_sb: g.dma_start(out=wd_sb[:, :], in_=ex['wd']), slot='wd')
            if not shared:
                wr = k.dma('sync', lambda y, ex=ex, wrow=wrow: y.dma_start(out=wrow[:, :], in_=ex['wrow']), slot='wr')
                prev = 0
                for q in range(NT // 512):
                    k.wait('tensor', *wr)
                    if prev:
                        k.wait('tensor', 'v', prev)
                    mb = k.op('tensor', lambda t, q=q, pwb=pwb, consts=consts, wrow=wrow: t.matmul(pwb[:, :], consts['ones1_bf'][0:1, :], wrow[0:1, q * 512:(q + 1) * 512], start=True, stop=True))
                    k.wait('vector', 't', mb)
                    prev = k.op('vector', lambda v, q=q, wbc=wbc, pwb=pwb: v.tensor_copy(out=wbc[:, q * 512:(q + 1) * 512], in_=pwb[:, :]))
            k.wait('tensor', *wlg)
            k.wait('tensor', *wlu)
            k.wait('tensor', *wld)
            xl = [None, None]
            hdone = 0
            sdone = {0: 0, 1: 0}
            vdone = {0: 0, 1: 0}
            ydone = {0: 0, 1: 0}
            st = {0: None, 1: None}
            mblk_t = [0] * NTT
            for mblk in range(NTT):
                par = mblk % 2
                if mblk >= 2:
                    k.wait('sync', 't', mblk_t[mblk - 2])
                xl[par] = k.dma('sync', lambda y, mblk=mblk, par=par, xnc=xnc, scr=scr: y.dma_start(out=xnc[par][:, :], in_=xtile(scr['xn_hbm'], mblk)), slot=f'xn{par}')
                for jb in range(JB):
                    jpar = jb % 2
                    k.wait('tensor', *xl[par])
                    k.wait('tensor', 's', sdone[jpar])
                    k.wait('tensor', 'v', vdone[jpar])
                    for mb in range(PB - 1):
                        k.raw('tensor', lambda t, mb=mb, jb=jb, par=par, jpar=jpar, pg=pg, wg_sb=wg_sb, I=I, xnc=xnc: t.matmul(pg[jpar][:, :], wg_sb[:, mb * I + jb * 128:mb * I + jb * 128 + 128], xnc[par][:, mb * 512:(mb + 1) * 512], start=mb == 0, stop=False))
                    mb = PB - 1
                    mg = k.op('tensor', lambda t, mb=mb, jb=jb, par=par, jpar=jpar, pg=pg, wg_sb=wg_sb, I=I, xnc=xnc: t.matmul(pg[jpar][:, :], wg_sb[:, mb * I + jb * 128:mb * I + jb * 128 + 128], xnc[par][:, mb * 512:(mb + 1) * 512], start=False, stop=True))
                    for mb in range(PB - 1):
                        k.raw('tensor', lambda t, mb=mb, jb=jb, par=par, jpar=jpar, pu=pu, wu_sb=wu_sb, I=I, xnc=xnc: t.matmul(pu[jpar][:, :], wu_sb[:, mb * I + jb * 128:mb * I + jb * 128 + 128], xnc[par][:, mb * 512:(mb + 1) * 512], start=mb == 0, stop=False))
                    mb = PB - 1
                    mu_ = k.op('tensor', lambda t, mb=mb, jb=jb, par=par, jpar=jpar, pu=pu, wu_sb=wu_sb, I=I, xnc=xnc: t.matmul(pu[jpar][:, :], wu_sb[:, mb * I + jb * 128:mb * I + jb * 128 + 128], xnc[par][:, mb * 512:(mb + 1) * 512], start=False, stop=True))
                    k.wait('scalar', 't', mg)
                    k.wait('scalar', 'v', hdone)
                    ssg = k.op('scalar', lambda s, jpar=jpar, sg=sg, pg=pg: s.activation(out=sg[:], in_=pg[jpar][:, :], func=AF.Silu))
                    sdone[jpar] = ssg
                    k.wait('vector', 't', mu_)
                    k.wait('vector', 's', ssg)
                    hdone = k.op('vector', lambda v, jb=jb, jpar=jpar, h_sb=h_sb, sg=sg, pu=pu: v.tensor_tensor(out=h_sb[:, jb * 512:(jb + 1) * 512], in0=sg[:], in1=pu[jpar][:, :], op=OP.mult))
                    vdone[jpar] = hdone
                for db in range(PB):
                    ypar = db % 2
                    k.wait('tensor', 'v', hdone)
                    k.wait('tensor', 'v', ydone[ypar])
                    for jb in range(JB - 1):
                        k.raw('tensor', lambda t, jb=jb, db=db, ypar=ypar, pyy=pyy, wd_sb=wd_sb, h_sb=h_sb: t.matmul(pyy[ypar][:, :], wd_sb[:, jb * D + db * 128:jb * D + db * 128 + 128], h_sb[:, jb * 512:(jb + 1) * 512], start=jb == 0, stop=False))
                    jb = JB - 1
                    my = k.op('tensor', lambda t, jb=jb, db=db, ypar=ypar, pyy=pyy, wd_sb=wd_sb, h_sb=h_sb: t.matmul(pyy[ypar][:, :], wd_sb[:, jb * D + db * 128:jb * D + db * 128 + 128], h_sb[:, jb * 512:(jb + 1) * 512], start=False, stop=True))
                    k.wait('vector', 't', my)
                    if st[ypar]:
                        k.wait('vector', *st[ypar])
                    if shared:
                        ydone[ypar] = k.op('vector', lambda v, ypar=ypar, ytl=ytl, pyy=pyy: v.tensor_copy(out=ytl[ypar][:], in_=pyy[ypar][:, :]))
                    else:
                        ydone[ypar] = k.op('vector', lambda v, ypar=ypar, mblk=mblk, ytl=ytl, pyy=pyy, wbc=wbc: v.tensor_tensor(out=ytl[ypar][:], in0=pyy[ypar][:, :], in1=wbc[:, mblk * 512:(mblk + 1) * 512], op=OP.mult))
                    k.wait('gpsimd', 'v', ydone[ypar])
                    dst = xchunk(scr['acc_hbm'], db, mblk * 512)
                    if shared:
                        st[ypar] = k.dma('gpsimd', lambda g, dst=dst, ypar=ypar, ytl=ytl: g.dma_start(out=dst, in_=ytl[ypar][:]), slot=f'ac{ypar}')
                    else:
                        st[ypar] = k.dma('gpsimd', lambda g, dst=dst, ypar=ypar, ytl=ytl: g.dma_start(out=dst, in_=ytl[ypar][:], accum_op=OP.add), slot=f'ac{ypar}')
                mblk_t[mblk] = k.mark('t')
            k.barrier()
        es_e.close()
        emit_add_hbm(k, x_hbm, scr['acc_hbm'], out_hbm, es)
        k.barrier()

def emit_add_hbm(k, a_hbm, b_hbm, out_hbm, es):
    nc = k.nc
    es2 = ExitStack()
    with es2:
        sb = lambda n, s, d, es2=es2, nc=nc: es2.enter_context(nc.sbuf_tensor(uname(n), s, d))
        at = [sb(f'ad_a{i}', [128, PB * 512], F32) for i in range(2)]
        bt = [sb(f'ad_b{i}', [128, PB * 512], F32) for i in range(2)]
        tilev = [0] * NTT
        stm = {0: None, 1: None}
        for tt in range(NTT):
            par = tt % 2
            if tt >= 2:
                k.wait('sync', 'v', tilev[tt - 2])
                if stm[par]:
                    k.wait('sync', *stm[par])
            da = k.dma('sync', lambda y, tt=tt, par=par, at=at, a_hbm=a_hbm: y.dma_start(out=at[par][:], in_=xtile(a_hbm, tt)), slot=f'aa{par}')
            db_ = k.dma('sync', lambda y, tt=tt, par=par, bt=bt, b_hbm=b_hbm: y.dma_start(out=bt[par][:], in_=xtile(b_hbm, tt)), slot=f'ab{par}')
            k.wait('vector', *da)
            k.wait('vector', *db_)
            if stm[par]:
                k.wait('vector', *stm[par])
            tilev[tt] = k.op('vector', lambda v, par=par, at=at, bt=bt: v.tensor_tensor(out=at[par][:], in0=at[par][:], in1=bt[par][:], op=OP.add))
            k.wait('scalar', 'v', tilev[tt])
            stm[par] = k.dma('scalar', lambda s, tt=tt, par=par, out_hbm=out_hbm, at=at: s.dma_start(out=xtile(out_hbm, tt), in_=at[par][:]), slot=f'ao{par}')
        k.barrier()

def emit_attn(k, x_hbm, out_hbm, W, consts, scr):
    nc = k.nc
    NH, HD, LAT, HG = (16, 64, 512, 4)
    NG = NH // HG
    LB = LAT // 128
    NT8 = T // 128
    NS4 = T // 512
    emit_ln(k, x_hbm, consts, dst_hbm=scr['xn_hbm'])
    es = ExitStack()
    with es:
        sb = lambda n, s, d, es=es, nc=nc: es.enter_context(nc.sbuf_tensor(uname(n), s, d))
        rope = sb('at_rope', [128, 4 * T], BF16)
        p64 = sb('at_p64', [128, 128], F32)
        m2 = sb('at_m2', [128, 4 * 512], BF16)
        rdma = k.dma('gpsimd', lambda g, rope=rope, W=W: g.dma_start(out=rope[:], in_=W['rope']), slot='wm2')
        pdma = k.dma('sync', lambda y, p64=p64, W=W: y.dma_start(out=p64[:], in_=W['p64']), slot='wa')
        mdma = k.dma('gpsimd', lambda g, m2=m2, W=W: g.dma_start(out=m2[:], in_=W['m2']), slot='wm')
        cosq, sinq = (rope[:, 0:T], rope[:, T:2 * T])
        cosk, sink = (rope[:, 2 * T:3 * T], rope[:, 3 * T:4 * T])
        xn_b = sb('at_xn', [128, PB * T], BF16)
        lat_sb = sb('at_lat', [128, LB * T], BF16)
        qpad = sb('at_qpad', [65, HG * T], BF16)
        kpad = sb('at_kpad', [65, HG * T], BF16)
        VW = NT8 * (HG * (HD + 1))
        vtok = sb('at_vtok', [128, VW], BF16)
        o_all = sb('at_oall', [128, PB * T], BF16)
        qraw = sb('at_qraw', [128, 512], F32)
        ropet = sb('at_ropet', [128, 512], F32)
        swt = sb('at_swt', [128, 512], F32)
        mx = sb('at_mx', [128, 2], F32)
        nmaxc = sb('at_nmax', [128, 1], BF16)
        pt_sb = sb('at_pt', [128, NT8 * 512], BF16)
        otmp = sb('at_otmp', [64, 512], F32)
        rrow = sb('at_rrow', [1, 512], BF16)
        rcp = sb('at_rcp', [1, 512], F32)
        wsl = sb('at_wsl', [128, PB * D], BF16)
        wv_sl = sb('at_wvsl', [128, LB * 256], BF16)
        xcol = [sb('at_xc0', [128, PB * 512], F32)]
        ytl = [sb(f'at_ytl{i}', [128, 512], F32) for i in range(2)]
        for b in range(BC):
            es2 = ExitStack()
            with es2:
                ps = lambda n, s, d=F32, es2=es2, nc=nc: es2.enter_context(nc.psum_tensor(uname(n), s, d))
                pp = [ps(f'at_pp{i}', [128, 512]) for i in range(2)]
                psw = [ps(f'at_psw{i}', [128, 512]) for i in range(2)]
                prow = ps('at_prow', [1, 128])
                pO = ps('at_pO', [65, 512])
                prc = ps('at_prc', [64, 512])
                for tq in range(4):
                    xbl = k.dma('sync', lambda y, b=b, tq=tq, xn_b=xn_b, scr=scr: y.dma_start(out=bass.AP(xn_b[:, :].tensor, xn_b[:, :].offset + tq * 512, [[PB * T, 128], [T, PB], [1, 512]]), in_=xtile(scr['xn_hbm'], b * 4 + tq)), slot='xb')
                k.wait('tensor', *xbl)
                wl = k.dma('gpsimd', lambda g, wsl=wsl, LAT=LAT, W=W: g.dma_start(out=wsl[:, 0:PB * LAT], in_=hview(W['wkvT'], 0, [[LAT, 128], [128 * LAT, PB], [1, LAT]])), slot='wg')
                k.wait('tensor', *wl)
                evac = {0: 0, 1: 0}
                for lb in range(LB):
                    for tt in range(NS4):
                        par = (lb * NS4 + tt) % 2
                        k.wait('tensor', 'v', evac[par])
                        for mb in range(PB):
                            fn = lambda t, mb=mb, lb=lb, tt=tt, par=par, pp=pp, wsl=wsl, LAT=LAT, xn_b=xn_b: t.matmul(pp[par][:, :], wsl[:, mb * LAT + lb * 128:mb * LAT + lb * 128 + 128], xn_b[:, mb * T + tt * 512:mb * T + tt * 512 + 512], start=mb == 0, stop=mb == PB - 1)
                            if mb < PB - 1:
                                k.raw('tensor', fn)
                            else:
                                mm = k.op('tensor', fn)
                        k.wait('vector', 't', mm)
                        evac[par] = k.op('vector', lambda v, lb=lb, tt=tt, par=par, lat_sb=lat_sb, pp=pp: v.tensor_copy(out=lat_sb[:, lb * T + tt * 512:lb * T + tt * 512 + 512], in_=pp[par][:, :]))
                msk = k.op('gpsimd', lambda g, kpad=kpad: g.memset(kpad[64:65, :], 1.0))
                msv = k.op('gpsimd', lambda g, vtok=vtok: g.memset(vtok[:], 1.0))
                k.wait('vector', *rdma)
                k.wait('vector', *pdma)
                k.wait('vector', *mdma)
                wsl_t = k.mark('t')
                wv_t = 0
                for grp in range(NG):
                    k.wait('gpsimd', 't', wsl_t)
                    wq = k.dma('gpsimd', lambda g, grp=grp, wsl=wsl, W=W, NH=NH, HD=HD: g.dma_start(out=wsl[:, 0:PB * 256], in_=hview(W['wqT'], grp * 256, [[NH * HD, 128], [128 * NH * HD, PB], [1, 256]])), slot='wg')
                    wk = k.dma('gpsimd', lambda g, grp=grp, wsl=wsl, LB=LB, W=W, NH=NH, HD=HD: g.dma_start(out=wsl[:, PB * 256:PB * 256 + LB * 256], in_=hview(W['wkupT'], grp * 256, [[NH * HD, 128], [128 * NH * HD, LB], [1, 256]])), slot='wu')
                    k.wait('gpsimd', 't', wv_t)
                    wv = k.dma('gpsimd', lambda g, grp=grp, wv_sl=wv_sl, LB=LB, W=W, NH=NH, HD=HD: g.dma_start(out=wv_sl[:, 0:LB * 256], in_=hview(W['wvupT'], grp * 256, [[NH * HD, 128], [128 * NH * HD, LB], [1, 256]])), slot='wd')
                    for qk in range(2):
                        k.wait('tensor', *(wq if qk == 0 else wk))
                        nin = PB if qk == 0 else LB
                        src = xn_b if qk == 0 else lat_sb
                        woff = 0 if qk == 0 else PB * 256
                        cos = cosq if qk == 0 else cosk
                        sin = sinq if qk == 0 else sink
                        dstpad = qpad if qk == 0 else kpad
                        for pt2 in range(2):
                            evq = k.mark('v')
                            for tt in range(NS4):
                                par = tt % 2
                                k.wait('tensor', 'v', evq)
                                for mb in range(nin):
                                    fn = lambda t, mb=mb, pt2=pt2, tt=tt, par=par, src=src, woff=woff, nin=nin, pp=pp, wsl=wsl: t.matmul(pp[par][:, :], wsl[:, woff + mb * 256 + pt2 * 128:woff + mb * 256 + pt2 * 128 + 128], src[:, mb * T + tt * 512:mb * T + tt * 512 + 512], start=mb == 0, stop=mb == nin - 1)
                                    if mb < nin - 1:
                                        k.raw('tensor', fn)
                                    else:
                                        mm = k.op('tensor', fn)
                                k.wait('vector', 't', mm)
                                evq = k.op('vector', lambda v, par=par, qraw=qraw, pp=pp: v.tensor_copy(out=qraw[:], in_=pp[par][:, :]))
                                k.wait('tensor', 'v', evq)
                                msw = k.op('tensor', lambda t, par=par, psw=psw, p64=p64, qraw=qraw: t.matmul(psw[par][:, :], p64[:, :], qraw[:, :], start=True, stop=True))
                                k.op('vector', lambda v, cos=cos, tt=tt, ropet=ropet, qraw=qraw: v.tensor_tensor(out=ropet[:], in0=qraw[:], in1=cos[:, tt * 512:(tt + 1) * 512], op=OP.mult))
                                k.wait('vector', 't', msw)
                                k.op('vector', lambda v, par=par, sin=sin, tt=tt, swt=swt, psw=psw: v.tensor_tensor(out=swt[:], in0=psw[par][:, :], in1=sin[:, tt * 512:(tt + 1) * 512], op=OP.mult))
                                for hh in range(2):
                                    hl = pt2 * 2 + hh
                                    base = hh * 64
                                    evq = k.op('vector', lambda v, dstpad=dstpad, hl=hl, tt=tt, base=base, ropet=ropet, swt=swt: v.tensor_tensor(out=dstpad[0:64, hl * T + tt * 512:hl * T + tt * 512 + 512], in0=ropet[base:base + 64, :], in1=swt[base:base + 64, :], op=OP.add))
                    wsl_t = k.mark('t')
                    k.wait('tensor', *wv)
                    k.wait('tensor', 'p', msv)
                    evv = {0: k.mark('v'), 1: k.mark('v')}
                    for tt8 in range(NT8):
                        par = tt8 % 2
                        k.wait('tensor', 'v', evv[par])
                        for lb in range(LB):
                            fn = lambda t, lb=lb, tt8=tt8, par=par, pp=pp, lat_sb=lat_sb, wv_sl=wv_sl, LB=LB: t.matmul(pp[par][:, 0:256], lat_sb[:, lb * T + tt8 * 128:lb * T + tt8 * 128 + 128], wv_sl[:, lb * 256:(lb + 1) * 256], start=lb == 0, stop=lb == LB - 1)
                            if lb < LB - 1:
                                k.raw('tensor', fn)
                            else:
                                mm = k.op('tensor', fn)
                        k.wait('vector', 't', mm)
                        dst = bass.AP(vtok, tt8 * (HG * (HD + 1)), [[VW, 128], [HD + 1, HG], [1, HD]])
                        evv[par] = k.op('vector', lambda v, dst=dst, par=par, pp=pp: v.tensor_copy(out=dst, in_=pp[par][:, 0:256]))
                    wv_t = k.mark('t')
                    k.wait('tensor', 'p', msk)
                    for hh in range(HG):
                        head = grp * HG + hh
                        hl = hh
                        hv = hh * (HD + 1)
                        rowv = k.mark('v')
                        for si in range(NT8):
                            ntj = si // 4 + 1
                            redv = 0
                            for tj in range(ntj):
                                par = tj % 2
                                k.wait('tensor', 'v', redv if tj >= 2 else rowv)
                                mm = k.op('tensor', lambda t, hl=hl, si=si, tj=tj, par=par, pp=pp, qpad=qpad, kpad=kpad: t.matmul(pp[par][:, :], qpad[0:64, hl * T + si * 128:hl * T + si * 128 + 128], kpad[0:64, hl * T + tj * 512:hl * T + tj * 512 + 512], start=True, stop=True))
                                k.wait('vector', 't', mm)
                                if tj == 0:
                                    redv = k.op('vector', lambda v, par=par, mx=mx, pp=pp: v.tensor_reduce(out=mx[:, 0:1], in_=pp[par][:, :], axis=AX.X, op=OP.max))
                                else:
                                    k.op('vector', lambda v, par=par, mx=mx, pp=pp: v.tensor_reduce(out=mx[:, 1:2], in_=pp[par][:, :], axis=AX.X, op=OP.max))
                                    redv = k.op('vector', lambda v, mx=mx: v.tensor_tensor(out=mx[:, 0:1], in0=mx[:, 0:1], in1=mx[:, 1:2], op=OP.max))
                            k.op('vector', lambda v, nmaxc=nmaxc, mx=mx: v.tensor_scalar(out=nmaxc[:], in0=mx[:, 0:1], scalar1=-1.0, scalar2=None, op0=OP.mult))
                            nv = k.mark('v')
                            k.wait('tensor', 'v', nv)
                            mrow = k.op('tensor', lambda t, prow=prow, nmaxc=nmaxc, consts=consts: t.matmul(prow[:, :], nmaxc[:, :], consts['ident_bf'][:, :], start=True, stop=True))
                            k.wait('vector', 't', mrow)
                            rowv = k.op('vector', lambda v, hl=hl, si=si, qpad=qpad, prow=prow: v.tensor_copy(out=qpad[64:65, hl * T + si * 128:hl * T + si * 128 + 128], in_=prow[:, :]))
                        for S4 in range(NS4):
                            nti = (S4 + 1) * 4
                            k.wait('tensor', 'v', rowv)
                            exv = {0: 0, 1: 0}
                            for ti in range(nti):
                                par = ti % 2
                                k.wait('tensor', 's', exv[par])
                                mm = k.op('tensor', lambda t, hl=hl, S4=S4, ti=ti, par=par, pp=pp, kpad=kpad, qpad=qpad: t.matmul(pp[par][:, :], kpad[0:65, hl * T + ti * 128:hl * T + ti * 128 + 128], qpad[0:65, hl * T + S4 * 512:hl * T + S4 * 512 + 512], start=True, stop=True))
                                k.wait('scalar', 't', mm)
                                exv[par] = k.op('scalar', lambda s, ti=ti, par=par, pt_sb=pt_sb, pp=pp: s.activation(out=pt_sb[:, ti * 512:(ti + 1) * 512], in_=pp[par][:, :], func=AF.Exp))
                                rel = ti * 128 - S4 * 512
                                if 0 <= rel < 512:
                                    k.wait('vector', 's', exv[par])
                                    k.op('vector', lambda v, ti=ti, rel=rel, pt_sb=pt_sb, m2=m2: v.tensor_tensor(out=pt_sb[:, ti * 512:(ti + 1) * 512], in0=pt_sb[:, ti * 512:(ti + 1) * 512], in1=m2[:, rel // 128 * 512:rel // 128 * 512 + 512], op=OP.mult))
                            mskv = k.mark('v')
                            sse = k.mark('s')
                            k.wait('tensor', 'v', mskv)
                            k.wait('tensor', 's', sse)
                            for ti in range(nti):
                                fn = lambda t, hv=hv, ti=ti, nti=nti, pO=pO, vtok=vtok, HG=HG, HD=HD, VW=VW, pt_sb=pt_sb: t.matmul(pO[0:65, :], bass.AP(vtok, ti * (HG * (HD + 1)) + hv, [[VW, 128], [1, HD + 1]]), pt_sb[:, ti * 512:(ti + 1) * 512], start=ti == 0, stop=ti == nti - 1)
                                if ti < nti - 1:
                                    k.raw('tensor', fn)
                                else:
                                    mmo = k.op('tensor', fn)
                            k.wait('vector', 't', mmo)
                            cpv = k.op('vector', lambda v, otmp=otmp, pO=pO: v.tensor_copy(out=otmp[:], in_=pO[0:64, :]))
                            k.op('vector', lambda v, rcp=rcp, pO=pO: v.reciprocal(out=rcp[:], in_=pO[64:65, :]))
                            rv = k.op('vector', lambda v, rrow=rrow, rcp=rcp: v.tensor_copy(out=rrow[:], in_=rcp[:]))
                            k.wait('tensor', 'v', rv)
                            mrb = k.op('tensor', lambda t, prc=prc, consts=consts, rrow=rrow: t.matmul(prc[:, :], consts['ones1_bf'][0:1, 0:64], rrow[0:1, :], start=True, stop=True))
                            k.wait('vector', 't', mrb)
                            db0, base = (head // 2, head % 2 * 64)
                            rowv = k.op('vector', lambda v, db0=db0, base=base, S4=S4, head=head, o_all=o_all, otmp=otmp, prc=prc: v.tensor_tensor(out=o_all[base:base + 64, db0 * T + S4 * 512:db0 * T + S4 * 512 + 512], in0=otmp[:], in1=prc[0:64, :], op=OP.mult))
                k.wait('gpsimd', 't', k.mark('t'))
                wo = k.dma('gpsimd', lambda g, wsl=wsl, W=W: g.dma_start(out=wsl[:, 0:PB * D], in_=hview(W['woT'], 0, [[D, 128], [128 * D, PB], [1, D]])), slot='wg')
                k.wait('tensor', *wo)
                allv = k.mark('v')
                k.wait('tensor', 'v', allv)
                ydone = {0: 0, 1: 0}
                st = {0: None, 1: None}
                for tt in range(NS4):
                    xpar = 0
                    if tt >= 1:
                        k.wait('sync', 'v', ydone[0])
                        k.wait('sync', 'v', ydone[1])
                    xd = k.dma('sync', lambda y, b=b, tt=tt, xpar=xpar, xcol=xcol, x_hbm=x_hbm, NS4=NS4: y.dma_start(out=xcol[xpar][:, :], in_=xtile(x_hbm, b * NS4 + tt)), slot=f'yx{xpar}')
                    for db in range(PB):
                        par = (tt * PB + db) % 2
                        k.wait('tensor', 'v', ydone[par])
                        for qb in range(PB):
                            fn = lambda t, qb=qb, db=db, tt=tt, par=par, pp=pp, wsl=wsl, o_all=o_all: t.matmul(pp[par][:, :], wsl[:, qb * D + db * 128:qb * D + db * 128 + 128], o_all[:, qb * T + tt * 512:qb * T + tt * 512 + 512], start=qb == 0, stop=qb == PB - 1)
                            if qb < PB - 1:
                                k.raw('tensor', fn)
                            else:
                                mm = k.op('tensor', fn)
                        k.wait('vector', 't', mm)
                        k.wait('vector', *xd)
                        if st[par]:
                            k.wait('vector', *st[par])
                        ydone[par] = k.op('vector', lambda v, db=db, par=par, xpar=xpar, ytl=ytl, pp=pp, xcol=xcol: v.tensor_tensor(out=ytl[par][:], in0=pp[par][:, :], in1=xcol[xpar][:, db * 512:(db + 1) * 512], op=OP.add))
                        k.wait('scalar', 'v', ydone[par])
                        st[par] = k.dma('scalar', lambda s, b=b, db=db, tt=tt, par=par, out_hbm=out_hbm, ytl=ytl: s.dma_start(out=xchunk(out_hbm, db, b * T + tt * 512), in_=ytl[par][:]), slot=f'ys{par}')
            k.barrier()
        k.barrier()

def emit_rwkv(k, x_hbm, out_hbm, W, consts, scr):
    nc = k.nc
    L, F = (4, 4096)
    TP = T + 1
    RW0, SRC0 = (0, 8 * NT)
    es = ExitStack()
    with es:
        sb = lambda n, s, d, es=es, nc=nc: es.enter_context(nc.sbuf_tensor(uname(n), s, d))
        cols = sb('rw_cols', [128, 368], F32)
        lamu = sb('rw_lamu', [128, 24], F32)
        zcol = sb('rw_zcol', [128, PB * BC], BF16)
        cdma = k.dma('sync', lambda y, cols=cols, W=W: y.dma_start(out=cols[:], in_=W['cols']), slot='wa')
        mz = k.op('gpsimd', lambda g, zcol=zcol: g.memset(zcol[:], 0.0))
        k.wait('gpsimd', 'p', mz)
        zd = k.dma('gpsimd', lambda g, scr=scr, TP=TP, zcol=zcol: g.dma_start(out=hview(scr['h_hbm'], 0, [[PB * (NT + 2), 128], [NT + 2, PB], [TP, BC]]), in_=zcol[:, :], allow_slow_non_contiguous=True), slot='zp')
        k.wait('vector', *cdma)
        k.wait('scalar', *cdma)
        emit_ln(k, x_hbm, consts, dual=dict(gamma_col=cols[:, 0:PB], beta_col=cols[:, PB:2 * PB], dst32_fn=lambda tt, scr=scr: xtile(scr['xr_hbm'], tt)))
        k.wait('sync', *zd)
        k.barrier()

        def hpad_fn(tt):
            b, q = (tt // 4, tt % 4)
            return hview(scr['h_hbm'], b * TP + 1 + q * 512, [[PB * (NT + 2), 128], [NT + 2, PB], [1, 512]])
        for l in range(L):
            lw = lambda base, cols=cols, l=l: cols[:, base + l * PB:base + (l + 1) * PB]
            sc1 = k.op('scalar', lambda s, lamu=lamu, lw=lw: s.activation(out=lamu[:, 16:24], in_=lw(304), func=AF.Exp))
            k.wait('vector', 's', sc1)
            vv1 = k.op('vector', lambda v, lamu=lamu: v.tensor_scalar(out=lamu[:, 16:24], in0=lamu[:, 16:24], scalar1=-1.0, scalar2=None, op0=OP.mult))
            k.wait('scalar', 'v', vv1)
            k.op('scalar', lambda s, lamu=lamu: s.activation(out=lamu[:, 0:8], in_=lamu[:, 16:24], func=AF.Exp))
            sc2 = k.op('scalar', lambda s, lamu=lamu, lw=lw: s.activation(out=lamu[:, 8:16], in_=lw(336), func=AF.Exp))
            emit_ln(k, scr['xr_hbm'], consts, gamma_col=lw(16), beta_col=lw(48), dst_hbm_fn=hpad_fn)
            for b in range(BC):
                es2 = ExitStack()
                sb2 = lambda n, s, d, es2=es2, nc=nc: es2.enter_context(nc.sbuf_tensor(uname(n), s, d))
                ps2 = lambda n, s, es2=es2, nc=nc: es2.enter_context(nc.psum_tensor(uname(n), s, F32))
                ek = sb2('rw_ek', [128, 8 * TP], BF16)
                ekv = sb2('rw_ekv', [128, 8 * TP], BF16)
                srb = sb2('rw_sr', [128, 8 * T], BF16)
                hcT = sb2('rw_hc', [128, PB * 512], BF16)
                hpT = sb2('rw_hp', [128, PB * 512], BF16)
                dT_ = sb2('rw_d', [128, PB * 512], BF16)
                xk_t = sb2('rw_xk', [128, PB * 512], BF16)
                xv_t = sb2('rw_xv', [128, PB * 512], BF16)
                xr_t = sb2('rw_xr', [128, PB * 512], BF16)
                vtmp = sb2('rw_vtmp', [128, 512], F32)
                rwt = sb2('rw_rwt', [128, T], BF16)
                ppT = [ps2(f'rw_pp{i}', [128, 512]) for i in range(2)]
                msz = 0
                for ob in range(8):
                    k.op('vector', lambda v, ob=ob, ek=ek, TP=TP: v.memset(ek[:, ob * TP:ob * TP + 1], 0.0))
                    msz = k.op('vector', lambda v, ob=ob, ekv=ekv, TP=TP: v.memset(ekv[:, ob * TP:ob * TP + 1], 0.0))
                es3 = ExitStack()
                sb3 = lambda n, s, d, es3=es3, nc=nc: es3.enter_context(nc.sbuf_tensor(uname(n), s, d))
                wk = sb3('rw_wk', [128, PB * D], BF16)
                wv = sb3('rw_wv', [128, PB * D], BF16)
                wr = sb3('rw_wr', [128, PB * D], BF16)
                wdk = k.dma('gpsimd', lambda g, l=l, wk=wk, W=W: g.dma_start(out=wk[:], in_=hview(W['wkT'], l * D * D, [[D, 128], [128 * D, PB], [1, D]])), slot='wg')
                wdv = k.dma('gpsimd', lambda g, l=l, wv=wv, W=W: g.dma_start(out=wv[:], in_=hview(W['wvT'], l * D * D, [[D, 128], [128 * D, PB], [1, D]])), slot='wu')
                wdr = k.dma('gpsimd', lambda g, l=l, wr=wr, W=W: g.dma_start(out=wr[:], in_=hview(W['wrT'], l * D * D, [[D, 128], [128 * D, PB], [1, D]])), slot='wd')
                k.wait('tensor', *wdk)
                k.wait('tensor', *wdv)
                k.wait('tensor', *wdr)
                evp = {0: ('v', 0), 1: ('v', 0)}
                lastt = 0
                tile_mixv = 0
                for q in range(4):
                    if q >= 1:
                        k.wait('sync', 'v', tile_mixv)
                    hcl = k.dma('sync', lambda y, b=b, q=q, hcT=hcT, scr=scr, TP=TP: y.dma_start(out=hcT[:], in_=hview(scr['h_hbm'], b * TP + 1 + q * 512, [[PB * (NT + 2), 128], [NT + 2, PB], [1, 512]])), slot='hc')
                    hpl = k.dma('sync', lambda y, b=b, q=q, hpT=hpT, scr=scr, TP=TP: y.dma_start(out=hpT[:], in_=hview(scr['h_hbm'], b * TP + q * 512, [[PB * (NT + 2), 128], [NT + 2, PB], [1, 512]])), slot='hp')
                    k.wait('vector', *hcl)
                    k.wait('vector', *hpl)
                    k.wait('vector', 't', lastt)
                    k.op('vector', lambda v, dT_=dT_, hcT=hcT, hpT=hpT: v.tensor_tensor(out=dT_[:], in0=hcT[:], in1=hpT[:], op=OP.subtract))
                    for db in range(PB):
                        sl = slice(db * 512, (db + 1) * 512)
                        k.op('vector', lambda v, sl=sl, db=db, l=l, xk_t=xk_t, dT_=dT_, cols=cols, hpT=hpT: v.scalar_tensor_tensor(out=xk_t[:, sl], in0=dT_[:, sl], scalar=cols[:, 144 + l * PB + db:144 + l * PB + db + 1], in1=hpT[:, sl], op0=OP.mult, op1=OP.add))
                        k.op('vector', lambda v, sl=sl, db=db, l=l, xv_t=xv_t, dT_=dT_, cols=cols, hpT=hpT: v.scalar_tensor_tensor(out=xv_t[:, sl], in0=dT_[:, sl], scalar=cols[:, 176 + l * PB + db:176 + l * PB + db + 1], in1=hpT[:, sl], op0=OP.mult, op1=OP.add))
                        k.op('vector', lambda v, sl=sl, db=db, l=l, xr_t=xr_t, dT_=dT_, cols=cols, hpT=hpT: v.scalar_tensor_tensor(out=xr_t[:, sl], in0=dT_[:, sl], scalar=cols[:, 208 + l * PB + db:208 + l * PB + db + 1], in1=hpT[:, sl], op0=OP.mult, op1=OP.add))
                    mixv = k.mark('v')
                    tile_mixv = mixv
                    k.wait('tensor', 'v', mixv)
                    for ob in range(8):
                        for which in range(3):
                            par = (ob * 3 + which) % 2
                            src = (xk_t, xv_t, xr_t)[which]
                            wmat = (wk, wv, wr)[which]
                            k.wait('tensor', evp[par][0], evp[par][1])
                            for mb in range(PB):
                                fn = lambda t, mb=mb, ob=ob, par=par, src=src, wmat=wmat, ppT=ppT: t.matmul(ppT[par][:, :], wmat[:, mb * D + ob * 128:mb * D + ob * 128 + 128], src[:, mb * 512:(mb + 1) * 512], start=mb == 0, stop=mb == PB - 1)
                                if mb < PB - 1:
                                    k.raw('tensor', fn)
                                else:
                                    mm = k.op('tensor', fn)
                            if which == 0:
                                k.wait('scalar', 't', mm)
                                evp[par] = ('s', k.op('scalar', lambda s, ob=ob, q=q, par=par, ek=ek, TP=TP, ppT=ppT: s.activation(out=ek[:, ob * TP + 1 + q * 512:ob * TP + 1 + q * 512 + 512], in_=ppT[par][:, :], func=AF.Exp)))
                            elif which == 1:
                                k.wait('vector', 't', mm)
                                k.op('vector', lambda v, par=par, vtmp=vtmp, ppT=ppT: v.tensor_copy(out=vtmp[:], in_=ppT[par][:, :]))
                                k.wait('vector', 's', k.mark('s'))
                                evp[par] = ('v', k.op('vector', lambda v, ob=ob, q=q, ekv=ekv, TP=TP, ek=ek, vtmp=vtmp: v.tensor_tensor(out=ekv[:, ob * TP + 1 + q * 512:ob * TP + 1 + q * 512 + 512], in0=ek[:, ob * TP + 1 + q * 512:ob * TP + 1 + q * 512 + 512], in1=vtmp[:], op=OP.mult)))
                            else:
                                k.wait('scalar', 't', mm)
                                evp[par] = ('s', k.op('scalar', lambda s, ob=ob, q=q, par=par, srb=srb, ppT=ppT: s.activation(out=srb[:, ob * T + q * 512:ob * T + q * 512 + 512], in_=ppT[par][:, :], func=AF.Sigmoid)))
                    lastt = k.mark('t')
                es3.close()
                es3b = ExitStack()
                sb3b = lambda n, s, d, es3b=es3b, nc=nc: es3b.enter_context(nc.sbuf_tensor(uname(n), s, d))
                lamrep = sb3b('rw_lamrep', [128, T], F32)
                nbuf = sb3b('rw_nbuf', [128, T], F32)
                dbuf = sb3b('rw_dbuf', [128, T], F32)
                rwst = None
                for ob in range(8):
                    k.op('vector', lambda v, lamrep=lamrep: v.memset(lamrep[:], 1.0))
                    k.op('vector', lambda v, ob=ob, lamrep=lamrep, lamu=lamu: v.tensor_scalar(out=lamrep[:], in0=lamrep[:], scalar1=lamu[:, ob:ob + 1], scalar2=None, op0=OP.mult))
                    k.wait('vector', 's', k.mark('s'))
                    k.op('vector', lambda v, ob=ob, nbuf=nbuf, lamrep=lamrep, ekv=ekv, TP=TP: v.tensor_tensor_scan(out=nbuf[:], data0=lamrep[:], data1=ekv[:, ob * TP:ob * TP + T], initial=0.0, op0=OP.mult, op1=OP.add))
                    k.op('vector', lambda v, ob=ob, dbuf=dbuf, lamrep=lamrep, ek=ek, TP=TP: v.tensor_tensor_scan(out=dbuf[:], data0=lamrep[:], data1=ek[:, ob * TP:ob * TP + T], initial=0.0, op0=OP.mult, op1=OP.add))
                    k.op('vector', lambda v, ob=ob, nbuf=nbuf, ekv=ekv, TP=TP, lamu=lamu: v.scalar_tensor_tensor(out=nbuf[:], in0=ekv[:, ob * TP + 1:ob * TP + 1 + T], scalar=lamu[:, 8 + ob:9 + ob], in1=nbuf[:], op0=OP.mult, op1=OP.add))
                    k.op('vector', lambda v, ob=ob, dbuf=dbuf, ek=ek, TP=TP, lamu=lamu: v.scalar_tensor_tensor(out=dbuf[:], in0=ek[:, ob * TP + 1:ob * TP + 1 + T], scalar=lamu[:, 8 + ob:9 + ob], in1=dbuf[:], op0=OP.mult, op1=OP.add))
                    k.op('vector', lambda v, dbuf=dbuf: v.reciprocal(out=dbuf[:], in_=dbuf[:]))
                    k.op('vector', lambda v, nbuf=nbuf, dbuf=dbuf: v.tensor_tensor(out=nbuf[:], in0=nbuf[:], in1=dbuf[:], op=OP.mult))
                    if rwst:
                        k.wait('vector', *rwst)
                    rwv = k.op('vector', lambda v, ob=ob, rwt=rwt, nbuf=nbuf, srb=srb: v.tensor_tensor(out=rwt[:], in0=nbuf[:], in1=srb[:, ob * T:(ob + 1) * T], op=OP.mult))
                    k.wait('scalar', 'v', rwv)
                    rwst = k.dma('scalar', lambda s, ob=ob, b=b, scr=scr, RW0=RW0, rwt=rwt: s.dma_start(out=hview(scr['aux_hbm'], RW0 + ob * NT + b * T, [[16 * NT, 128], [1, T]]), in_=rwt[:]), slot='rws')
                es3b.close()
                es2.close()
                es4 = ExitStack()
                sb4 = lambda n, s, d, es4=es4, nc=nc: es4.enter_context(nc.sbuf_tensor(uname(n), s, d))
                ps4 = lambda n, s, es4=es4, nc=nc: es4.enter_context(nc.psum_tensor(uname(n), s, F32))
                wo = sb4('rw_wo', [128, PB * D], BF16)
                rwtl = sb4('rw_rwtl', [128, 8 * 512], BF16)
                xcl = sb4('rw_xcl', [128, PB * 512], F32)
                ytl = [sb4(f'rw_ytl{i}', [128, 512], F32) for i in range(2)]
                pw = [ps4(f'rw_pw{i}', [128, 512]) for i in range(2)]
                k.wait('gpsimd', 't', k.mark('t'))
                k.wait('gpsimd', 'v', k.mark('v'))
                k.wait('gpsimd', 's', k.mark('s'))
                k.wait('gpsimd', 'c:rws', k.mark('c:rws'))
                wod = k.dma('gpsimd', lambda g, l=l, wo=wo, W=W: g.dma_start(out=wo[:], in_=hview(W['woT'], l * D * D, [[D, 128], [128 * D, PB], [1, D]])), slot='wg')
                k.wait('tensor', *wod)
                k.wait('sync', *rwst)
                ydone = {0: 0, 1: 0}
                st = {0: None, 1: None}
                for q in range(4):
                    tilev = k.mark('v')
                    if q >= 1:
                        k.wait('sync', 'v', tilev)
                    rwl = k.dma('sync', lambda y, b=b, q=q, rwtl=rwtl, scr=scr, RW0=RW0: y.dma_start(out=rwtl[:], in_=hview(scr['aux_hbm'], RW0 + b * T + q * 512, [[16 * NT, 128], [NT, 8], [1, 512]])), slot='hc')
                    xdl = k.dma('sync', lambda y, b=b, q=q, xcl=xcl, scr=scr: y.dma_start(out=xcl[:], in_=xtile(scr['xr_hbm'], b * 4 + q)), slot='hp')
                    k.wait('tensor', *rwl)
                    for db in range(PB):
                        par = db % 2
                        k.wait('tensor', 'v', ydone[par])
                        for ob in range(8):
                            fn = lambda t, ob=ob, db=db, par=par, pw=pw, wo=wo, rwtl=rwtl: t.matmul(pw[par][:, :], wo[:, ob * D + db * 128:ob * D + db * 128 + 128], rwtl[:, ob * 512:(ob + 1) * 512], start=ob == 0, stop=ob == 7)
                            if ob < 7:
                                k.raw('tensor', fn)
                            else:
                                mm = k.op('tensor', fn)
                        k.wait('vector', 't', mm)
                        k.wait('vector', *xdl)
                        if st[par]:
                            k.wait('vector', *st[par])
                        ydone[par] = k.op('vector', lambda v, db=db, par=par, ytl=ytl, pw=pw, xcl=xcl: v.tensor_tensor(out=ytl[par][:], in0=pw[par][:, :], in1=xcl[:, db * 512:(db + 1) * 512], op=OP.add))
                        k.wait('scalar', 'v', ydone[par])
                        st[par] = k.dma('scalar', lambda s, b=b, db=db, q=q, par=par, scr=scr, ytl=ytl: s.dma_start(out=xchunk(scr['xr_hbm'], db, b * T + q * 512), in_=ytl[par][:]), slot=f'ys{par}')
                    k.wait('sync', 'v', ydone[0])
                    k.wait('sync', 'v', ydone[1])
                es4.close()
                k.barrier()
            emit_ln(k, scr['xr_hbm'], consts, gamma_col=lw(80), beta_col=lw(112), dst_hbm_fn=hpad_fn)
            for b in range(BC):
                es2 = ExitStack()
                sb2 = lambda n, s, d, es2=es2, nc=nc: es2.enter_context(nc.sbuf_tensor(uname(n), s, d))
                ps2 = lambda n, s, es2=es2, nc=nc: es2.enter_context(nc.psum_tensor(uname(n), s, F32))
                hcC = sb2('cm_hc', [128, PB * 512], BF16)
                hpC = sb2('cm_hp', [128, PB * 512], BF16)
                dC_ = sb2('cm_d', [128, PB * 512], BF16)
                xk2 = sb2('cm_xk2', [128, PB * 512], BF16)
                xr2 = sb2('cm_xr2', [128, PB * 512], BF16)
                srct = sb2('cm_src', [128, 8 * 512], BF16)
                wcr = sb2('cm_wcr', [128, PB * D], BF16)
                ppC = [ps2(f'cm_pp{i}', [128, 512]) for i in range(2)]
                wdc = k.dma('gpsimd', lambda g, l=l, wcr=wcr, W=W: g.dma_start(out=wcr[:], in_=hview(W['wcrT'], l * D * D, [[D, 128], [128 * D, PB], [1, D]])), slot='wg')
                k.wait('tensor', *wdc)
                xst = None
                sst = None
                for q in range(4):
                    prevv = k.mark('v')
                    if q >= 1:
                        k.wait('sync', 'v', prevv)
                        k.wait('sync', 's', k.mark('s'))
                    hcl = k.dma('sync', lambda y, b=b, q=q, hcC=hcC, scr=scr, TP=TP: y.dma_start(out=hcC[:], in_=hview(scr['h_hbm'], b * TP + 1 + q * 512, [[PB * (NT + 2), 128], [NT + 2, PB], [1, 512]])), slot='hc')
                    hpl = k.dma('sync', lambda y, b=b, q=q, hpC=hpC, scr=scr, TP=TP: y.dma_start(out=hpC[:], in_=hview(scr['h_hbm'], b * TP + q * 512, [[PB * (NT + 2), 128], [NT + 2, PB], [1, 512]])), slot='hp')
                    k.wait('vector', *hcl)
                    k.wait('vector', *hpl)
                    if xst:
                        k.wait('vector', *xst)
                    k.wait('vector', 't', k.mark('t'))
                    k.op('vector', lambda v, dC_=dC_, hcC=hcC, hpC=hpC: v.tensor_tensor(out=dC_[:], in0=hcC[:], in1=hpC[:], op=OP.subtract))
                    for db in range(PB):
                        sl = slice(db * 512, (db + 1) * 512)
                        k.op('vector', lambda v, sl=sl, db=db, l=l, xk2=xk2, dC_=dC_, cols=cols, hpC=hpC: v.scalar_tensor_tensor(out=xk2[:, sl], in0=dC_[:, sl], scalar=cols[:, 240 + l * PB + db:240 + l * PB + db + 1], in1=hpC[:, sl], op0=OP.mult, op1=OP.add))
                        k.op('vector', lambda v, sl=sl, db=db, l=l, xr2=xr2, dC_=dC_, cols=cols, hpC=hpC: v.scalar_tensor_tensor(out=xr2[:, sl], in0=dC_[:, sl], scalar=cols[:, 272 + l * PB + db:272 + l * PB + db + 1], in1=hpC[:, sl], op0=OP.mult, op1=OP.add))
                    mixv = k.mark('v')
                    k.wait('scalar', 'v', mixv)
                    xst = k.dma('scalar', lambda s, b=b, q=q, scr=scr, xk2=xk2: s.dma_start(out=xtile(scr['xn_hbm'], b * 4 + q), in_=xk2[:]), slot='xks')
                    k.wait('tensor', 'v', mixv)
                    evs = {0: 0, 1: 0}
                    for ob in range(8):
                        par = ob % 2
                        k.wait('tensor', 's', evs[par])
                        for mb in range(PB):
                            fn = lambda t, mb=mb, ob=ob, par=par, ppC=ppC, wcr=wcr, xr2=xr2: t.matmul(ppC[par][:, :], wcr[:, mb * D + ob * 128:mb * D + ob * 128 + 128], xr2[:, mb * 512:(mb + 1) * 512], start=mb == 0, stop=mb == PB - 1)
                            if mb < PB - 1:
                                k.raw('tensor', fn)
                            else:
                                mm = k.op('tensor', fn)
                        k.wait('scalar', 't', mm)
                        if sst:
                            k.wait('scalar', *sst)
                        evs[par] = k.op('scalar', lambda s, ob=ob, par=par, srct=srct, ppC=ppC: s.activation(out=srct[:, ob * 512:(ob + 1) * 512], in_=ppC[par][:, :], func=AF.Sigmoid))
                    k.wait('scalar', 's', k.mark('s'))
                    sst = k.dma('scalar', lambda s, b=b, q=q, scr=scr, SRC0=SRC0, srct=srct: s.dma_start(out=hview(scr['aux_hbm'], SRC0 + b * T + q * 512, [[16 * NT, 128], [NT, 8], [1, 512]]), in_=srct[:]), slot='srs')
                k.wait('sync', *sst)
                k.wait('sync', *xst)
                es2.close()
                k.barrier()
                es2 = ExitStack()
                sb2 = lambda n, s, d, es2=es2, nc=nc: es2.enter_context(nc.sbuf_tensor(uname(n), s, d))
                ps2 = lambda n, s, es2=es2, nc=nc: es2.enter_context(nc.psum_tensor(uname(n), s, F32))
                wck = sb2('cm_wck', [128, PB * 2048], BF16)
                wcv = sb2('cm_wcv', [128, 16 * D], BF16)
                kf = sb2('cm_kf', [128, 16 * 512], BF16)
                xk2t = sb2('cm_xk2t', [128, PB * 512], BF16)
                rtmp = sb2('cm_rtmp', [128, 512], F32)
                kvtl = [sb2(f'cm_kvtl{i}', [128, 512], F32) for i in range(2)]
                pf = [ps2(f'cm_pf{i}', [128, 512]) for i in range(2)]
                pv = [ps2(f'cm_pv{i}', [128, 512]) for i in range(2)]
                for jh in range(2):
                    k.wait('gpsimd', 't', k.mark('t'))
                    wd1 = k.dma('gpsimd', lambda g, l=l, jh=jh, wck=wck, W=W, F=F: g.dma_start(out=wck[:], in_=hview(W['wckT'], l * D * F + jh * 2048, [[F, 128], [128 * F, PB], [1, 2048]])), slot='wg')
                    wd2 = k.dma('gpsimd', lambda g, l=l, jh=jh, wcv=wcv, W=W, F=F: g.dma_start(out=wcv[:], in_=hview(W['wcvT'], (l * F + jh * 2048) * D, [[D, 128], [128 * D, 16], [1, D]])), slot='wu')
                    k.wait('tensor', *wd1)
                    k.wait('tensor', *wd2)
                    xlt = None
                    kst = {0: None, 1: None}
                    for q in range(4):
                        if xlt:
                            k.wait('sync', 't', k.mark('t'))
                        xlt = k.dma('sync', lambda y, b=b, q=q, xk2t=xk2t, scr=scr: y.dma_start(out=xk2t[:], in_=xtile(scr['xn_hbm'], b * 4 + q)), slot='hc')
                        k.wait('tensor', *xlt)
                        evs = {0: 0, 1: 0}
                        evv = {0: 0, 1: 0}
                        sq_prev = 0
                        for jb in range(16):
                            par = jb % 2
                            k.wait('tensor', 's', evs[par])
                            k.wait('tensor', 'v', evv[par])
                            for mb in range(PB):
                                fn = lambda t, mb=mb, jb=jb, par=par, pf=pf, wck=wck, xk2t=xk2t: t.matmul(pf[par][:, :], wck[:, mb * 2048 + jb * 128:mb * 2048 + jb * 128 + 128], xk2t[:, mb * 512:(mb + 1) * 512], start=mb == 0, stop=mb == PB - 1)
                                if mb < PB - 1:
                                    k.raw('tensor', fn)
                                else:
                                    mm = k.op('tensor', fn)
                            k.wait('vector', 't', mm)
                            k.wait('vector', 's', sq_prev)
                            evv[par] = k.op('vector', lambda v, par=par, rtmp=rtmp, pf=pf: v.tensor_scalar(out=rtmp[:], in0=pf[par][:, :], scalar1=0.0, scalar2=None, op0=OP.max))
                            k.wait('scalar', 'v', evv[par])
                            evs[par] = k.op('scalar', lambda s, jb=jb, kf=kf, rtmp=rtmp: s.activation(out=kf[:, jb * 512:(jb + 1) * 512], in_=rtmp[:], func=AF.Square))
                            sq_prev = evs[par]
                        k.wait('tensor', 's', k.mark('s'))
                        kdone = {0: 0, 1: 0}
                        for db in range(PB):
                            par = db % 2
                            k.wait('tensor', 'v', kdone[par])
                            for jb in range(16):
                                fn = lambda t, jb=jb, db=db, par=par, pv=pv, wcv=wcv, kf=kf: t.matmul(pv[par][:, :], wcv[:, jb * D + db * 128:jb * D + db * 128 + 128], kf[:, jb * 512:(jb + 1) * 512], start=jb == 0, stop=jb == 15)
                                if jb < 15:
                                    k.raw('tensor', fn)
                                else:
                                    mm = k.op('tensor', fn)
                            k.wait('vector', 't', mm)
                            if kst[par]:
                                k.wait('vector', *kst[par])
                            kdone[par] = k.op('vector', lambda v, par=par, kvtl=kvtl, pv=pv: v.tensor_copy(out=kvtl[par][:], in_=pv[par][:, :]))
                            k.wait('gpsimd', 'v', kdone[par])
                            dst = xchunk(scr['acc_hbm'], db, b * T + q * 512)
                            if jh == 0:
                                kst[par] = k.dma('gpsimd', lambda g, dst=dst, par=par, kvtl=kvtl: g.dma_start(out=dst, in_=kvtl[par][:]), slot=f'ac{par}')
                            else:
                                kst[par] = k.dma('gpsimd', lambda g, dst=dst, par=par, kvtl=kvtl: g.dma_start(out=dst, in_=kvtl[par][:], accum_op=OP.add), slot=f'ac{par}')
                    k.barrier()
                es2.close()
                es2 = ExitStack()
                sb2 = lambda n, s, d, es2=es2, nc=nc: es2.enter_context(nc.sbuf_tensor(uname(n), s, d))
                kvt = sb2('cm_kvt', [128, PB * 512], F32)
                srt = sb2('cm_srt', [128, 8 * 512], BF16)
                xct = sb2('cm_xct', [128, PB * 512], F32)
                ytl2 = sb2('cm_ytl2', [128, PB * 512], F32)
                for q in range(4):
                    prevv = k.mark('v')
                    if q >= 1:
                        k.wait('sync', 'v', prevv)
                        k.wait('sync', 'c:ys0', k.mark('c:ys0'))
                    kvl = k.dma('sync', lambda y, b=b, q=q, kvt=kvt, scr=scr: y.dma_start(out=kvt[:], in_=xtile(scr['acc_hbm'], b * 4 + q)), slot='hc')
                    srl = k.dma('sync', lambda y, b=b, q=q, srt=srt, scr=scr, SRC0=SRC0: y.dma_start(out=srt[:], in_=hview(scr['aux_hbm'], SRC0 + b * T + q * 512, [[16 * NT, 128], [NT, 8], [1, 512]])), slot='hp')
                    xl2 = k.dma('sync', lambda y, b=b, q=q, xct=xct, scr=scr: y.dma_start(out=xct[:], in_=xtile(scr['xr_hbm'], b * 4 + q)), slot='wa')
                    k.wait('vector', *kvl)
                    k.wait('vector', *srl)
                    k.wait('vector', *xl2)
                    for db in range(PB):
                        k.op('vector', lambda v, db=db, kvt=kvt, srt=srt: v.tensor_tensor(out=kvt[:, db * 512:(db + 1) * 512], in0=kvt[:, db * 512:(db + 1) * 512], in1=srt[:, db * 512:(db + 1) * 512], op=OP.mult))
                        k.op('vector', lambda v, db=db, ytl2=ytl2, kvt=kvt, xct=xct: v.tensor_tensor(out=ytl2[:, db * 512:(db + 1) * 512], in0=kvt[:, db * 512:(db + 1) * 512], in1=xct[:, db * 512:(db + 1) * 512], op=OP.add))
                    wv_ = k.mark('v')
                    k.wait('scalar', 'v', wv_)
                    k.dma('scalar', lambda s, b=b, q=q, scr=scr, ytl2=ytl2: s.dma_start(out=xtile(scr['xr_hbm'], b * 4 + q), in_=ytl2[:]), slot='ys0')
                es2.close()
                k.barrier()
        emit_add_hbm(k, x_hbm, scr['xr_hbm'], out_hbm, es)
        k.barrier()

def emit_gla(k, x_hbm, out_hbm, W, consts, scr):
    nc = k.nc
    NH, HD2 = (8, 128)
    CH = 64
    NCH = T // CH
    NT8 = T // 128
    emit_ln(k, x_hbm, consts, dst_hbm=scr['xn_hbm'])
    es = ExitStack()
    with es:
        sb = lambda n, s, d, es=es, nc=nc: es.enter_context(nc.sbuf_tensor(uname(n), s, d))
        win = sb('gl_win', [128, PB * 3072], BF16)
        wg1 = sb('gl_wg1', [128, PB * 128], BF16)
        wg2 = sb('gl_wg2', [128, 1024], BF16)
        lnw = sb('gl_lnw', [128, PB], F32)
        glam = sb('gl_glam', [128, T], F32)
        ltm = sb('gl_ltm', [64, 64], BF16)
        xn_b = sb('gl_xn', [128, PB * T], BF16)
        g1f = sb('gl_g1f', [128, T], BF16)
        gate_h = sb('gl_gateh', [128, T], BF16)
        og_h = sb('gl_ogh', [128, T], BF16)
        ogt = sb('gl_ogt', [128, NH * 512], BF16)
        vtok = sb('gl_vtok', [64, NCH * 128], BF16)
        ktok = sb('gl_ktok', [64, NCH * 128], BF16)
        qsil = sb('gl_qsil', [128, T], BF16)
        qt = qsil
        khat = sb('gl_khat', [128, T], BF16)
        ktl = sb('gl_ktl', [128, T], BF16)
        lgb = sb('gl_lg', [128, T], F32)
        tmpa = sb('gl_tmpa', [128, T], F32)
        gbuf = sb('gl_g', [128, T], F32)
        cum = sb('gl_cum', [128, T], F32)
        lam = cum
        S_f = sb('gl_Sf', [128, 128], F32)
        S_b = sb('gl_Sb', [128, 128], BF16)
        amt = sb('gl_amt', [64, 64], BF16)
        rows = sb('gl_rows', [1, 512 * 6], F32)
        crow = sb('gl_crow', [1, 512], BF16)
        orow = sb('gl_orow', [1, 512], BF16)
        onesb = sb('gl_onesb', [128, 1], BF16)
        onorm = sb('gl_onorm', [128, NH * 512], BF16)
        ogsq = onorm
        xcol = [sb(f'gl_xc{i}', [128, 512], F32) for i in range(2)]
        ytl = [sb(f'gl_ytl{i}', [128, 512], F32) for i in range(2)]
        wd2 = k.dma('gpsimd', lambda g, wg1=wg1, W=W: g.dma_start(out=wg1[:], in_=hview(W['wg1T'], 0, [[128, 128], [128 * 128, PB], [1, 128]])), slot='wu')
        wd3 = k.dma('gpsimd', lambda g, wg2=wg2, W=W: g.dma_start(out=wg2[:], in_=hview(W['wg2T'], 0, [[1024, 128], [128 * 1024, 1], [1, 1024]])), slot='wd')
        wd4 = k.dma('gpsimd', lambda g, ltm=ltm, W=W: g.dma_start(out=ltm[:], in_=W['ltm']), slot='wm')
        cd1 = k.dma('sync', lambda y, lnw=lnw, W=W: y.dma_start(out=lnw[:], in_=W['lnw_col']), slot='wa')
        cd2 = k.dma('sync', lambda y, glam=glam, W=W: y.dma_start(out=glam[:], in_=W['glam']), slot='wb')
        ms1 = k.op('gpsimd', lambda g, onesb=onesb: g.memset(onesb[:], 1.0))
        for b in range(BC):
            es2 = ExitStack()
            with es2:
                ps = lambda n, s, d=F32, es2=es2, nc=nc: es2.enter_context(nc.psum_tensor(uname(n), s, d))
                pp = [ps(f'gl_pp{i}', [128, 512]) for i in range(2)]
                esA = ExitStack()
                psA = lambda n, s, d=F32, esA=esA, nc=nc: esA.enter_context(nc.psum_tensor(uname(n), s, d))
                pA = psA('gl_pA', [64, 64])
                pO = psA('gl_pO', [128, 64])
                pS = psA('gl_pS', [128, 128])
                ptr = esA.enter_context(nc.psum_tensor(uname('gl_ptr'), [128, 128], BF16))
                for tq in range(4):
                    xbl = k.dma('sync', lambda y, b=b, tq=tq, xn_b=xn_b, scr=scr: y.dma_start(out=bass.AP(xn_b[:, :].tensor, xn_b[:, :].offset + tq * 512, [[PB * T, 128], [T, PB], [1, 512]]), in_=xtile(scr['xn_hbm'], b * 4 + tq)), slot='xb')
                wd1 = k.dma('gpsimd', lambda g, win=win, W=W: g.dma_start(out=win[:], in_=hview(W['winT'], 0, [[3072, 128], [128 * 3072, PB], [1, 3072]])), slot='wg')
                k.wait('tensor', *xbl)
                k.wait('tensor', *wd1)
                k.wait('tensor', *wd2)
                evp = {0: k.mark('v'), 1: k.mark('v')}
                for tt in range(T // 512):
                    par = tt % 2
                    k.wait('tensor', 'v', evp[par])
                    for mb in range(PB):
                        fn = lambda t, mb=mb, tt=tt, par=par, pp=pp, wg1=wg1, xn_b=xn_b: t.matmul(pp[par][:, :], wg1[:, mb * 128:(mb + 1) * 128], xn_b[:, mb * T + tt * 512:mb * T + tt * 512 + 512], start=mb == 0, stop=mb == PB - 1)
                        if mb < PB - 1:
                            k.raw('tensor', fn)
                        else:
                            mm = k.op('tensor', fn)
                    k.wait('vector', 't', mm)
                    evp[par] = k.op('vector', lambda v, tt=tt, par=par, g1f=g1f, pp=pp: v.tensor_copy(out=g1f[:, tt * 512:(tt + 1) * 512], in_=pp[par][:, :]))
                evps = {0: ('v', 0), 1: ('v', 0)}
                for par in range(2):
                    evps[par] = ('v', evp[par])
                for h in range(NH):
                    for which, coff in (('q', 1024 + h * 128), ('lg', 2048 + h * 128)):
                        for tt in range(T // 512):
                            par = tt % 2
                            k.wait('tensor', *(evps[par][::-1] if False else (evps[par][0], evps[par][1])))
                            for mb in range(PB):
                                fn = lambda t, mb=mb, tt=tt, par=par, coff=coff, pp=pp, win=win, xn_b=xn_b: t.matmul(pp[par][:, :], win[:, mb * 3072 + coff:mb * 3072 + coff + 128], xn_b[:, mb * T + tt * 512:mb * T + tt * 512 + 512], start=mb == 0, stop=mb == PB - 1)
                                if mb < PB - 1:
                                    k.raw('tensor', fn)
                                else:
                                    mm = k.op('tensor', fn)
                            if which == 'q':
                                k.wait('scalar', 't', mm)
                                evs = k.op('scalar', lambda s, tt=tt, par=par, qsil=qsil, pp=pp: s.activation(out=qsil[:, tt * 512:(tt + 1) * 512], in_=pp[par][:, :], func=AF.Silu))
                                evps[par] = ('s', evs)
                            else:
                                k.wait('vector', 't', mm)
                                evps[par] = ('v', k.op('vector', lambda v, tt=tt, par=par, lgb=lgb, pp=pp: v.tensor_copy(out=lgb[:, tt * 512:(tt + 1) * 512], in_=pp[par][:, :])))
                    for par in range(2):
                        k.wait('tensor', evps[par][0], evps[par][1])
                    evv = {0: k.mark('v'), 1: k.mark('v')}
                    for tt8 in range(NT8):
                        par = tt8 % 2
                        k.wait('tensor', 'v', evv[par])
                        for mb in range(PB):
                            fn = lambda t, mb=mb, tt8=tt8, par=par, h=h, pp=pp, xn_b=xn_b, win=win: t.matmul(pp[par][:, 0:128], xn_b[:, mb * T + tt8 * 128:mb * T + tt8 * 128 + 128], win[:, mb * 3072 + h * 128:mb * 3072 + h * 128 + 128], start=mb == 0, stop=mb == PB - 1)
                            if mb < PB - 1:
                                k.raw('tensor', fn)
                            else:
                                mm = k.op('tensor', fn)
                        k.wait('vector', 't', mm)
                        k.op('vector', lambda v, tt8=tt8, par=par, vtok=vtok, pp=pp: v.tensor_copy(out=vtok[:, 2 * tt8 * 128:2 * tt8 * 128 + 128], in_=pp[par][0:64, 0:128]))
                        evv[par] = k.op('vector', lambda v, tt8=tt8, par=par, vtok=vtok, pp=pp: v.tensor_copy(out=vtok[:, (2 * tt8 + 1) * 128:(2 * tt8 + 1) * 128 + 128], in_=pp[par][64:128, 0:128]))
                    k.wait('tensor', 'v', k.mark('v'))
                    k.wait('tensor', *wd3)
                    evg = {0: k.mark('s'), 1: k.mark('s')}
                    for tt in range(T // 512):
                        par = tt % 2
                        k.wait('tensor', 's', evg[par])
                        mm = k.op('tensor', lambda t, tt=tt, par=par, h=h, pp=pp, wg2=wg2, g1f=g1f: t.matmul(pp[par][:, :], wg2[:, h * 128:(h + 1) * 128], g1f[:, tt * 512:(tt + 1) * 512], start=True, stop=True))
                        k.wait('scalar', 't', mm)
                        evg[par] = k.op('scalar', lambda s, tt=tt, par=par, gate_h=gate_h, pp=pp: s.activation(out=gate_h[:, tt * 512:(tt + 1) * 512], in_=pp[par][:, :], func=AF.Sigmoid))
                        evps[par] = ('s', evg[par])
                    k.wait('scalar', 'v', k.mark('v'))
                    k.op('scalar', lambda s, tmpa=tmpa, lgb=lgb: s.activation(out=tmpa[:], in_=lgb[:], func=AF.Abs))
                    k.op('scalar', lambda s, tmpa=tmpa: s.activation(out=tmpa[:], in_=tmpa[:], func=AF.Exp, scale=-1.0))
                    sln = k.op('scalar', lambda s, tmpa=tmpa: s.activation(out=tmpa[:], in_=tmpa[:], func=AF.Ln, bias=1.0))
                    k.wait('vector', 's', sln)
                    k.op('vector', lambda v, gbuf=gbuf, lgb=lgb: v.tensor_scalar(out=gbuf[:], in0=lgb[:], scalar1=-1.0, scalar2=None, op0=OP.mult))
                    k.op('vector', lambda v, gbuf=gbuf: v.tensor_scalar(out=gbuf[:], in0=gbuf[:], scalar1=0.0, scalar2=None, op0=OP.max))
                    vg = k.op('vector', lambda v, gbuf=gbuf, tmpa=tmpa: v.scalar_tensor_tensor(out=gbuf[:], in0=gbuf[:], scalar=-1.0, in1=tmpa[:], op0=OP.bypass, op1=OP.add))
                    k.op('vector', lambda v, gbuf=gbuf: v.tensor_scalar(out=gbuf[:], in0=gbuf[:], scalar1=-1.0, scalar2=None, op0=OP.mult))
                    k.wait('vector', *cd2)
                    scv = k.op('vector', lambda v, cum=cum, glam=glam, gbuf=gbuf: v.tensor_tensor_scan(out=cum[:], data0=glam[:], data1=gbuf[:], initial=0.0, op0=OP.mult, op1=OP.add))
                    k.wait('scalar', 'v', scv)
                    sl2 = k.op('scalar', lambda s, tmpa=tmpa, cum=cum: s.activation(out=tmpa[:], in_=cum[:], func=AF.Exp, scale=-1.0))
                    sl1 = k.op('scalar', lambda s, lam=lam, cum=cum: s.activation(out=lam[:], in_=cum[:], func=AF.Exp))
                    sl3 = k.op('scalar', lambda s, gbuf=gbuf, lgb=lgb: s.activation(out=gbuf[:], in_=lgb[:], func=AF.Sigmoid, scale=-1.0))
                    k.wait('vector', 's', sl3)
                    k.op('vector', lambda v, khat=khat, gbuf=gbuf, tmpa=tmpa: v.tensor_tensor(out=khat[:], in0=gbuf[:], in1=tmpa[:], op=OP.mult))
                    k.op('vector', lambda v, qt=qt, qsil=qsil, lam=lam: v.tensor_tensor(out=qt[:], in0=qsil[:], in1=lam[:], op=OP.mult))
                    for c in range(NCH):
                        k.op('vector', lambda v, c=c, ktl=ktl, CH=CH, khat=khat, lam=lam: v.tensor_scalar(out=ktl[:, c * CH:(c + 1) * CH], in0=khat[:, c * CH:(c + 1) * CH], scalar1=lam[:, c * CH + CH - 1:c * CH + CH], scalar2=None, op0=OP.mult))
                    ktv = k.mark('v')
                    k.wait('tensor', 'v', ktv)
                    prevtr = k.mark('v')
                    for tt8 in range(NT8):
                        k.wait('tensor', 'v', prevtr)
                        mtr = k.op('tensor', lambda t, tt8=tt8, ptr=ptr, ktl=ktl, consts=consts: t.transpose(ptr[:, :], ktl[:, tt8 * 128:(tt8 + 1) * 128], consts['ident_bf'][:]))
                        k.wait('vector', 't', mtr)
                        k.op('vector', lambda v, tt8=tt8, ktok=ktok, ptr=ptr: v.tensor_copy(out=ktok[:, 2 * tt8 * 128:2 * tt8 * 128 + 128], in_=ptr[0:64, :]))
                        prevtr = k.op('vector', lambda v, tt8=tt8, ktok=ktok, ptr=ptr: v.tensor_copy(out=ktok[:, (2 * tt8 + 1) * 128:(2 * tt8 + 1) * 128 + 128], in_=ptr[64:128, :]))
                    msS = k.op('vector', lambda v, S_f=S_f: v.memset(S_f[:], 0.0))
                    k.op('vector', lambda v, S_b=S_b: v.memset(S_b[:], 0.0))
                    k.wait('vector', *wd4)
                    prevA = k.mark('v')
                    prevO = k.mark('v')
                    prevS = k.mark('v')
                    for c in range(NCH):
                        cs = c * CH
                        k.wait('tensor', 'v', prevA)
                        mA = k.op('tensor', lambda t, cs=cs, pA=pA, khat=khat, CH=CH, qt=qt: t.matmul(pA[:, :], khat[:, cs:cs + CH], qt[:, cs:cs + CH], start=True, stop=True))
                        k.wait('vector', 't', mA)
                        prevA = k.op('vector', lambda v, amt=amt, pA=pA, ltm=ltm: v.tensor_tensor(out=amt[:], in0=pA[:, :], in1=ltm[:], op=OP.mult))
                        k.wait('tensor', 'v', prevA)
                        k.wait('tensor', 'v', prevO)
                        k.raw('tensor', lambda t, c=c, pO=pO, vtok=vtok, amt=amt: t.matmul(pO[:, :], vtok[:, c * 128:(c + 1) * 128], amt[:, :], start=True, stop=False))
                        mO = k.op('tensor', lambda t, cs=cs, pO=pO, S_b=S_b, qt=qt, CH=CH: t.matmul(pO[:, :], S_b[:, :], qt[:, cs:cs + CH], start=False, stop=True))
                        k.wait('tensor', 'v', prevS)
                        mS = k.op('tensor', lambda t, c=c, pS=pS, ktok=ktok, vtok=vtok: t.matmul(pS[:, :], ktok[:, c * 128:(c + 1) * 128], vtok[:, c * 128:(c + 1) * 128], start=True, stop=True))
                        k.wait('vector', 't', mO)
                        prevO = k.op('vector', lambda v, cs=cs, og_h=og_h, CH=CH, pO=pO, gate_h=gate_h: v.tensor_tensor(out=og_h[:, cs:cs + CH], in0=pO[:, :], in1=gate_h[:, cs:cs + CH], op=OP.mult))
                        k.wait('vector', 't', mS)
                        k.op('vector', lambda v, cs=cs, S_f=S_f, lam=lam, CH=CH: v.tensor_scalar(out=S_f[:], in0=S_f[:], scalar1=lam[:, cs + CH - 1:cs + CH], scalar2=None, op0=OP.mult))
                        k.op('vector', lambda v, S_f=S_f, pS=pS: v.tensor_tensor(out=S_f[:], in0=S_f[:], in1=pS[:, :], op=OP.add))
                        prevS = k.op('vector', lambda v, S_b=S_b, S_f=S_f: v.tensor_copy(out=S_b[:], in_=S_f[:]))
                    k.wait('scalar', 'v', prevO)
                    ogst = k.dma('scalar', lambda sc, h=h, b=b, scr=scr, NH=NH, og_h=og_h: sc.dma_start(out=hview(scr['og_hbm'], h * NT + b * T, [[NH * NT, 128], [1, T]]), in_=og_h[:, :]), slot='ogs')
                    k.wait('vector', *ogst)
                esA.close()
                esB = ExitStack()
                psB = lambda n, s, d=F32, esB=esB, nc=nc: esB.enter_context(nc.psum_tensor(uname(n), s, d))
                pst = [psB(f'gl_pst{i}', [1, 512]) for i in range(2)]
                pbc = psB('gl_pbc', [128, 512])
                pbo = psB('gl_pbo', [128, 512])
                k.wait('gpsimd', 't', k.mark('t'))
                wo = k.dma('gpsimd', lambda g, win=win, W=W: g.dma_start(out=win[:, 0:PB * D], in_=hview(W['woutT'], 0, [[D, 128], [128 * D, PB], [1, D]])), slot='wg')
                k.wait('vector', *cd1)
                ydone = {0: 0, 1: 0}
                st = {0: None, 1: None}
                k.wait('sync', *ogst)
                ogl = None
                for tt in range(T // 512):
                    allv = k.mark('v')
                    if ogl is not None:
                        k.wait('sync', 't', ogl_t)
                        k.wait('sync', 'v', allv)
                        k.wait('sync', 's', k.mark('s'))
                    ogl_t = k.mark('t')
                    ogl = k.dma('sync', lambda y, b=b, tt=tt, ogt=ogt, scr=scr, NH=NH: y.dma_start(out=ogt[:], in_=hview(scr['og_hbm'], b * T + tt * 512, [[NH * NT, 128], [NT, NH], [1, 512]])), slot='ogl')
                    k.wait('scalar', *ogl)
                    k.wait('scalar', 't', k.mark('t'))
                    sq = k.op('scalar', lambda s, tt=tt, ogsq=ogsq, ogt=ogt: s.activation(out=ogsq[:], in_=ogt[:], func=AF.Square))
                    k.wait('tensor', 's', sq)
                    k.wait('tensor', *ogl)
                    k.wait('tensor', 'p', ms1)
                    for hb in range(NH):
                        k.raw('tensor', lambda t, hb=hb, tt=tt, pst=pst, onesb=onesb, ogt=ogt, NH=NH: t.matmul(pst[0][:, :], onesb[:, :], ogt[:, hb * 512:(hb + 1) * 512], start=hb == 0, stop=hb == NH - 1))
                    for hb in range(NH - 1):
                        k.raw('tensor', lambda t, hb=hb, pst=pst, onesb=onesb, ogsq=ogsq: t.matmul(pst[1][:, :], onesb[:, :], ogsq[:, hb * 512:(hb + 1) * 512], start=hb == 0, stop=False))
                    mst = k.op('tensor', lambda t, pst=pst, onesb=onesb, ogsq=ogsq, NH=NH: t.matmul(pst[1][:, :], onesb[:, :], ogsq[:, (NH - 1) * 512:NH * 512], start=False, stop=True))
                    mu = rows[0:1, 0:512]
                    m2_ = rows[0:1, 512:1024]
                    var = rows[0:1, 1024:1536]
                    rst = rows[0:1, 1536:2048]
                    tmp = rows[0:1, 2048:2560]
                    k.wait('vector', 't', mst)
                    k.op('vector', lambda v, mu=mu, pst=pst: v.tensor_scalar(out=mu, in0=pst[0][:, :], scalar1=1.0 / D, scalar2=None, op0=OP.mult))
                    k.op('vector', lambda v, m2_=m2_, mu=mu: v.tensor_tensor(out=m2_, in0=mu, in1=mu, op=OP.mult))
                    vv = k.op('vector', lambda v, var=var, pst=pst, m2_=m2_: v.scalar_tensor_tensor(out=var, in0=pst[1][:, :], scalar=1.0 / D, in1=m2_, op0=OP.mult, op1=OP.subtract))
                    k.wait('scalar', 'v', vv)
                    sv = k.op('scalar', lambda s, tmp=tmp, var=var: s.activation(out=tmp, in_=var, func=AF.Sqrt, bias=EPS))
                    k.wait('vector', 's', sv)
                    k.op('vector', lambda v, rst=rst, tmp=tmp: v.reciprocal(out=rst, in_=tmp))
                    k.op('vector', lambda v, crow=crow, rst=rst: v.tensor_copy(out=crow[:], in_=rst))
                    rowv = k.op('vector', lambda v, orow=orow, mu=mu, rst=rst: v.scalar_tensor_tensor(out=orow[:], in0=mu, scalar=-1.0, in1=rst, op0=OP.mult, op1=OP.mult))
                    k.wait('tensor', 'v', rowv)
                    k.op('tensor', lambda t, pbc=pbc, consts=consts, crow=crow: t.matmul(pbc[:, :], consts['ones1_bf'][0:1, :], crow[0:1, :], start=True, stop=True))
                    mbc = k.op('tensor', lambda t, pbo=pbo, consts=consts, orow=orow: t.matmul(pbo[:, :], consts['ones1_bf'][0:1, :], orow[0:1, :], start=True, stop=True))
                    k.wait('vector', 't', mbc)
                    for hb in range(NH):
                        k.op('vector', lambda v, hb=hb, tt=tt, onorm=onorm, ogt=ogt, pbc=pbc: v.tensor_tensor(out=onorm[:, hb * 512:(hb + 1) * 512], in0=ogt[:, hb * 512:(hb + 1) * 512], in1=pbc[:, :], op=OP.mult))
                        k.op('vector', lambda v, hb=hb, onorm=onorm, pbo=pbo: v.tensor_tensor(out=onorm[:, hb * 512:(hb + 1) * 512], in0=onorm[:, hb * 512:(hb + 1) * 512], in1=pbo[:, :], op=OP.add))
                        k.op('vector', lambda v, hb=hb, onorm=onorm, lnw=lnw: v.tensor_scalar(out=onorm[:, hb * 512:(hb + 1) * 512], in0=onorm[:, hb * 512:(hb + 1) * 512], scalar1=lnw[:, hb:hb + 1], scalar2=None, op0=OP.mult))
                    nv = k.mark('v')
                    k.wait('tensor', 'v', nv)
                    k.wait('tensor', 's', k.mark('s'))
                    k.wait('tensor', *wo)
                    for db in range(PB):
                        par = db % 2
                        k.wait('sync', 'v', ydone[par])
                        xd = k.dma('sync', lambda y, b=b, tt=tt, db=db, par=par, xcol=xcol, x_hbm=x_hbm: y.dma_start(out=xcol[par][:, :], in_=xchunk(x_hbm, db, b * T + tt * 512)), slot=f'yx{par}')
                        k.wait('tensor', 'v', ydone[par])
                        for eb in range(NH):
                            fn = lambda t, eb=eb, db=db, par=par, pp=pp, win=win, onorm=onorm, NH=NH: t.matmul(pp[par][:, :], win[:, eb * D + db * 128:eb * D + db * 128 + 128], onorm[:, eb * 512:(eb + 1) * 512], start=eb == 0, stop=eb == NH - 1)
                            if eb < NH - 1:
                                k.raw('tensor', fn)
                            else:
                                mm = k.op('tensor', fn)
                        k.wait('vector', 't', mm)
                        k.wait('vector', *xd)
                        if st[par]:
                            k.wait('vector', *st[par])
                        ydone[par] = k.op('vector', lambda v, db=db, par=par, ytl=ytl, pp=pp, xcol=xcol: v.tensor_tensor(out=ytl[par][:], in0=pp[par][:, :], in1=xcol[par][:, :], op=OP.add))
                        k.wait('scalar', 'v', ydone[par])
                        st[par] = k.dma('scalar', lambda s, b=b, db=db, tt=tt, par=par, out_hbm=out_hbm, ytl=ytl: s.dma_start(out=xchunk(out_hbm, db, b * T + tt * 512), in_=ytl[par][:]), slot=f'ys{par}')
                    k.wait('sync', 'v', ydone[0])
                    k.wait('sync', 'v', ydone[1])
                esB.close()
            k.barrier()
        k.barrier()

def emit_moe_mlp(k, x_hbm, out_hbm, W, consts, scr):
    nc = k.nc
    I = 4096
    JB = I // 128
    emit_ln(k, x_hbm, consts, dst_hbm=scr['xn_hbm'])
    es = ExitStack()
    with es:
        sb = lambda n, s, d, es=es, nc=nc: es.enter_context(nc.sbuf_tensor(uname(n), s, d))
        psum = lambda n, s, d=F32, es=es, nc=nc: es.enter_context(nc.psum_tensor(uname(n), s, d))
        rw1 = sb('mr_rw1', [128, PB * I], BF16)
        rw2 = sb('mr_rw2', [128, JB * 8], F32)
        rcols = sb('mr_rcols', [128, JB + 8], F32)
        hr = sb('mr_hr', [128, JB * 512], F32)
        xnc = [sb(f'mr_xnc{i}', [128, PB * 512], BF16) for i in range(2)]
        logits_sb = sb('mr_logits', [128, NT // 128 * 8], F32)
        ph = [psum(f'mr_ph{i}', [128, 512]) for i in range(2)]
        plg = [psum(f'mr_plg{i}', [128, 8]) for i in range(2)]
        w1d = k.dma('gpsimd', lambda g, rw1=rw1, W=W: g.dma_start(out=rw1[:], in_=W['rw1T']), slot='wg')
        w2d = k.dma('sync', lambda y, rw2=rw2, W=W, JB=JB: y.dma_start(out=rw2[:], in_=hview(W['rw2'], 0, [[8, 128], [128 * 8, JB], [1, 8]])), slot='wb')
        cd = k.dma('sync', lambda y, rcols=rcols, JB=JB, W=W: y.dma_start(out=rcols[:, 0:JB], in_=W['rb1_cols']), slot='wa')
        cd2 = k.dma('sync', lambda y, rcols=rcols, JB=JB, W=W: y.dma_start(out=rcols[:, JB:JB + 8], in_=W['rb2_rep']), slot='wa')
        k.wait('tensor', *w1d)
        k.wait('tensor', *w2d)
        k.wait('scalar', *cd2)
        k.wait('vector', *cd2)
        sdone = {0: 0, 1: 0}
        lgv = {0: 0, 1: 0}
        tile_t = [0] * NTT
        for tt in range(NTT):
            par = tt % 2
            if tt >= 2:
                k.wait('sync', 't', tile_t[tt - 2])
            xl = k.dma('sync', lambda y, tt=tt, par=par, xnc=xnc, scr=scr: y.dma_start(out=xnc[par][:], in_=xtile(scr['xn_hbm'], tt)), slot=f'xn{par}')
            k.wait('tensor', *xl)
            for jb in range(JB):
                jpar = jb % 2
                k.wait('tensor', 's', sdone[jpar])
                for mb in range(PB - 1):
                    k.raw('tensor', lambda t, mb=mb, jb=jb, par=par, jpar=jpar, ph=ph, rw1=rw1, I=I, xnc=xnc: t.matmul(ph[jpar][:, :], rw1[:, mb * I + jb * 128:mb * I + jb * 128 + 128], xnc[par][:, mb * 512:(mb + 1) * 512], start=mb == 0, stop=False))
                mb = PB - 1
                mh = k.op('tensor', lambda t, mb=mb, jb=jb, par=par, jpar=jpar, ph=ph, rw1=rw1, I=I, xnc=xnc: t.matmul(ph[jpar][:, :], rw1[:, mb * I + jb * 128:mb * I + jb * 128 + 128], xnc[par][:, mb * 512:(mb + 1) * 512], start=False, stop=True))
                k.wait('scalar', 't', mh)
                sdone[jpar] = k.op('scalar', lambda s, jb=jb, jpar=jpar, hr=hr, ph=ph, rcols=rcols: s.activation(out=hr[:, jb * 512:(jb + 1) * 512], in_=ph[jpar][:, :], func=AF.Silu, bias=rcols[:, jb:jb + 1]))
            k.wait('tensor', 's', sdone[0])
            k.wait('tensor', 's', sdone[1])
            for ts4 in range(4):
                qpar = ts4 % 2
                k.wait('tensor', 'v', lgv[qpar])
                for jb in range(JB - 1):
                    k.raw('tensor', lambda t, jb=jb, ts4=ts4, qpar=qpar, plg=plg, hr=hr, rw2=rw2: t.matmul(plg[qpar][:, :], hr[:, jb * 512 + ts4 * 128:jb * 512 + ts4 * 128 + 128], rw2[:, jb * 8:(jb + 1) * 8], start=jb == 0, stop=False))
                jb = JB - 1
                mlg = k.op('tensor', lambda t, jb=jb, ts4=ts4, qpar=qpar, plg=plg, hr=hr, rw2=rw2: t.matmul(plg[qpar][:, :], hr[:, jb * 512 + ts4 * 128:jb * 512 + ts4 * 128 + 128], rw2[:, jb * 8:(jb + 1) * 8], start=False, stop=True))
                k.wait('vector', 't', mlg)
                lgv[qpar] = k.op('vector', lambda v, tt=tt, ts4=ts4, qpar=qpar, logits_sb=logits_sb, plg=plg, rcols=rcols, JB=JB: v.tensor_tensor(out=logits_sb[:, (tt * 4 + ts4) * 8:(tt * 4 + ts4) * 8 + 8], in0=plg[qpar][:, :], in1=rcols[:, JB:JB + 8], op=OP.add))
            tile_t[tt] = k.mark('t')
        k.barrier()
        emit_gate_rows(k, logits_sb, scr['wrows_hbm'], consts, mode='softmax2')
        k.barrier()
    es = ExitStack()
    with es:
        es_e = ExitStack()
        sb = lambda n, s, d, es_e=es_e, nc=nc: es_e.enter_context(nc.sbuf_tensor(uname(n), s, d))
        psum = lambda n, s, d=F32, es_e=es_e, nc=nc: es_e.enter_context(nc.psum_tensor(uname(n), s, d))
        w1_sb = sb('mr_w1', [128, PB * I], BF16)
        w2_sb = sb('mr_w2', [128, JB * D], BF16)
        h_sb = sb('mr_h', [128, JB * 512], BF16)
        bcols = sb('mr_bcols', [128, 8 * JB + 8 * PB], F32)
        xnc = [sb(f'mr_exc{i}', [128, PB * 512], BF16) for i in range(2)]
        wrow = sb('mr_wrow', [1, NT], BF16)
        wbc = sb('mr_wbc', [128, NT], F32)
        ytl = [sb(f'mr_ytl{i}', [128, 512], F32) for i in range(2)]
        pg = [psum(f'mr_pg{i}', [128, 512]) for i in range(2)]
        pyy = [psum(f'mr_pyy{i}', [128, 512]) for i in range(2)]
        pwb = psum('mr_pwb', [128, 512])
        bd1 = k.dma('sync', lambda y, bcols=bcols, JB=JB, W=W: y.dma_start(out=bcols[:, 0:8 * JB], in_=W['b1_cols']), slot='wa')
        bd2 = k.dma('sync', lambda y, bcols=bcols, JB=JB, W=W: y.dma_start(out=bcols[:, 8 * JB:8 * JB + 8 * PB], in_=W['b2_cols']), slot='wa')
        k.wait('scalar', *bd2)
        k.wait('vector', *bd2)
        for ei, ex in enumerate(W['experts']):
            b1c = bcols[:, ei * JB:(ei + 1) * JB]
            b2c = bcols[:, 8 * JB + ei * PB:8 * JB + (ei + 1) * PB]
            wl1 = k.dma('gpsimd', lambda g, ex=ex, w1_sb=w1_sb: g.dma_start(out=w1_sb[:, :], in_=ex['w1']), slot='wg')
            wl2 = k.dma('gpsimd', lambda g, ex=ex, w2_sb=w2_sb: g.dma_start(out=w2_sb[:, :], in_=ex['w2']), slot='wd')
            wr = k.dma('sync', lambda y, ex=ex, wrow=wrow: y.dma_start(out=wrow[:, :], in_=ex['wrow']), slot='wr')
            prev = 0
            for q in range(NT // 512):
                k.wait('tensor', *wr)
                if prev:
                    k.wait('tensor', 'v', prev)
                mb = k.op('tensor', lambda t, q=q, pwb=pwb, consts=consts, wrow=wrow: t.matmul(pwb[:, :], consts['ones1_bf'][0:1, :], wrow[0:1, q * 512:(q + 1) * 512], start=True, stop=True))
                k.wait('vector', 't', mb)
                prev = k.op('vector', lambda v, q=q, wbc=wbc, pwb=pwb: v.tensor_copy(out=wbc[:, q * 512:(q + 1) * 512], in_=pwb[:, :]))
            k.wait('tensor', *wl1)
            k.wait('tensor', *wl2)
            hdone = 0
            sdone = {0: 0, 1: 0}
            ydone = {0: 0, 1: 0}
            st = {0: None, 1: None}
            mblk_t = [0] * NTT
            for mblk in range(NTT):
                par = mblk % 2
                if mblk >= 2:
                    k.wait('sync', 't', mblk_t[mblk - 2])
                xl = k.dma('sync', lambda y, mblk=mblk, par=par, xnc=xnc, scr=scr: y.dma_start(out=xnc[par][:, :], in_=xtile(scr['xn_hbm'], mblk)), slot=f'xn{par}')
                for jb in range(JB):
                    jpar = jb % 2
                    k.wait('tensor', *xl)
                    k.wait('tensor', 's', sdone[jpar])
                    for mb in range(PB - 1):
                        k.raw('tensor', lambda t, mb=mb, jb=jb, par=par, jpar=jpar, pg=pg, w1_sb=w1_sb, I=I, xnc=xnc: t.matmul(pg[jpar][:, :], w1_sb[:, mb * I + jb * 128:mb * I + jb * 128 + 128], xnc[par][:, mb * 512:(mb + 1) * 512], start=mb == 0, stop=False))
                    mb = PB - 1
                    mg = k.op('tensor', lambda t, mb=mb, jb=jb, par=par, jpar=jpar, pg=pg, w1_sb=w1_sb, I=I, xnc=xnc: t.matmul(pg[jpar][:, :], w1_sb[:, mb * I + jb * 128:mb * I + jb * 128 + 128], xnc[par][:, mb * 512:(mb + 1) * 512], start=False, stop=True))
                    k.wait('scalar', 't', mg)
                    sdone[jpar] = k.op('scalar', lambda s, jb=jb, jpar=jpar, b1c=b1c, h_sb=h_sb, pg=pg: s.activation(out=h_sb[:, jb * 512:(jb + 1) * 512], in_=pg[jpar][:, :], func=AF.Silu, bias=b1c[:, jb:jb + 1]))
                hdone = sdone[(JB - 1) % 2]
                for db in range(PB):
                    ypar = db % 2
                    k.wait('tensor', 's', sdone[0])
                    k.wait('tensor', 's', sdone[1])
                    k.wait('tensor', 'v', ydone[ypar])
                    for jb in range(JB - 1):
                        k.raw('tensor', lambda t, jb=jb, db=db, ypar=ypar, pyy=pyy, w2_sb=w2_sb, h_sb=h_sb: t.matmul(pyy[ypar][:, :], w2_sb[:, jb * D + db * 128:jb * D + db * 128 + 128], h_sb[:, jb * 512:(jb + 1) * 512], start=jb == 0, stop=False))
                    jb = JB - 1
                    my = k.op('tensor', lambda t, jb=jb, db=db, ypar=ypar, pyy=pyy, w2_sb=w2_sb, h_sb=h_sb: t.matmul(pyy[ypar][:, :], w2_sb[:, jb * D + db * 128:jb * D + db * 128 + 128], h_sb[:, jb * 512:(jb + 1) * 512], start=False, stop=True))
                    k.wait('vector', 't', my)
                    if st[ypar]:
                        k.wait('vector', *st[ypar])
                    ydone[ypar] = k.op('vector', lambda v, ypar=ypar, db=db, mblk=mblk, b2c=b2c, ytl=ytl, pyy=pyy, wbc=wbc: v.scalar_tensor_tensor(out=ytl[ypar][:], in0=pyy[ypar][:, :], scalar=b2c[:, db:db + 1], in1=wbc[:, mblk * 512:(mblk + 1) * 512], op0=OP.add, op1=OP.mult))
                    k.wait('gpsimd', 'v', ydone[ypar])
                    dst = xchunk(scr['acc_hbm'], db, mblk * 512)
                    if ei == 0:
                        st[ypar] = k.dma('gpsimd', lambda g, dst=dst, ypar=ypar, ytl=ytl: g.dma_start(out=dst, in_=ytl[ypar][:]), slot=f'ac{ypar}')
                    else:
                        st[ypar] = k.dma('gpsimd', lambda g, dst=dst, ypar=ypar, ytl=ytl: g.dma_start(out=dst, in_=ytl[ypar][:], accum_op=OP.add), slot=f'ac{ypar}')
                mblk_t[mblk] = k.mark('t')
                if os.environ.get('MOE_PARANOID', '0') == '1':
                    k.barrier()
            k.barrier()
        es_e.close()
        emit_add_hbm(k, x_hbm, scr['acc_hbm'], out_hbm, es)
        k.barrier()

def pmaj(a2d):
    d, ntok = a2d.shape
    a = np.ascontiguousarray(a2d).reshape(PB, 128, ntok // 512, 512)
    return np.ascontiguousarray(a.transpose(1, 2, 0, 3).reshape(128, PB * ntok))

def unpmaj(a, d, ntok):
    a4 = np.asarray(a).reshape(128, ntok // 512, PB, 512)
    return np.ascontiguousarray(a4.transpose(2, 0, 1, 3).reshape(d, ntok))

def col_layout(vec):
    d = vec.shape[-1]
    nb = d // 128
    return np.ascontiguousarray(vec.reshape(-1, nb, 128).transpose(2, 0, 1).reshape(128, -1))


def _host_consts():
    dim, theta = 64, 10000.0
    inv = 1.0 / theta ** (np.arange(0, dim, 2) / dim)
    pos = np.arange(T, dtype=np.float64)
    fr = pos[None, :] * np.concatenate([inv, inv])[:, None]
    cos = np.cos(fr)
    sin = np.sin(fr)
    sgn = np.where(np.arange(64) < 32, -1.0, 1.0)[:, None]
    sins = sgn * sin
    tab = np.concatenate([cos / 8.0, sins / 8.0, cos, sins], axis=1)
    rope = np.concatenate([tab, tab], axis=0).astype(np.float32)
    p = np.zeros((64, 64), np.float32)
    p[(np.arange(64) + 32) % 64, np.arange(64)] = 1.0
    z = np.zeros((64, 64), np.float32)
    p64 = np.block([[p, z], [z, p]]).astype(np.float32)
    m2 = np.zeros((128, 4 * 512), np.float32)
    for r in range(4):
        for pp_ in range(128):
            m2[pp_, r * 512:(r + 1) * 512] = (r * 128 + pp_) <= np.arange(512)
    glam = np.ones((128, T), np.float32)
    glam[:, ::64] = 0.0
    ltm = np.zeros((64, 64), np.float32)
    for r in range(64):
        ltm[r, r:] = 1.0
    return dict(rope=rope, p64=p64, m2=m2, glam=glam, ltm=ltm,
                ident=np.eye(128, dtype=np.float32))


def _stackT(w):
    return np.ascontiguousarray(np.concatenate([w[l].T for l in range(w.shape[0])], axis=0))


def _host_prep(inputs):
    tr = lambda a: np.ascontiguousarray(np.asarray(a).T)
    g = dict(_host_consts())
    g["a_wqT"] = tr(inputs["L0_attn_a_wq"])
    g["a_wkvT"] = tr(inputs["L0_attn_a_w_kv_down"])
    g["a_wkupT"] = tr(inputs["L0_attn_a_w_k_up"])
    g["a_wvupT"] = tr(inputs["L0_attn_a_w_v_up"])
    g["a_woT"] = tr(inputs["L0_attn_a_wo"])
    g["f0_gwT"] = tr(inputs["L0_ffn_gate_w"])
    g["f0_lncols"] = np.concatenate(
        [col_layout(np.asarray(inputs["L0_ffn_ln_scale"])),
         col_layout(np.asarray(inputs["L0_ffn_ln_bias"]))], axis=1)
    g["f0_sWg"] = np.ascontiguousarray(inputs["L0_ffn_sWg"])
    g["f0_sWu"] = np.ascontiguousarray(inputs["L0_ffn_sWu"])
    g["f0_sWd"] = np.ascontiguousarray(inputs["L0_ffn_sWd"])
    g["f0_Wg"] = np.ascontiguousarray(np.asarray(inputs["L0_ffn_Wg"]).reshape(8 * D, 2048))
    g["f0_Wu"] = np.ascontiguousarray(np.asarray(inputs["L0_ffn_Wu"]).reshape(8 * D, 2048))
    g["f0_Wd"] = np.ascontiguousarray(np.asarray(inputs["L0_ffn_Wd"]).reshape(8 * 2048, D))
    g["s_BT"] = tr(inputs["L0_attn_b_B"])
    g["s_CT"] = tr(inputs["L0_attn_b_C"])
    g["s_DT"] = tr(inputs["L0_attn_b_D"])
    g["s_diagA"] = col_layout(np.ascontiguousarray(np.diagonal(np.asarray(inputs["L0_attn_b_A"]))))
    g["r_wkT"] = _stackT(np.asarray(inputs["L1_attn_a_Wk"]))
    g["r_wvT"] = _stackT(np.asarray(inputs["L1_attn_a_Wv"]))
    g["r_wrT"] = _stackT(np.asarray(inputs["L1_attn_a_Wr"]))
    g["r_woT"] = _stackT(np.asarray(inputs["L1_attn_a_Wo"]))
    g["r_wckT"] = _stackT(np.asarray(inputs["L1_attn_a_Wck"]))
    g["r_wcvT"] = _stackT(np.asarray(inputs["L1_attn_a_Wcv"]))
    g["r_wcrT"] = _stackT(np.asarray(inputs["L1_attn_a_Wcr"]))
    rc = np.zeros((128, 368), np.float32)
    rc[:, 0:8] = col_layout(np.asarray(inputs["L1_attn_a_ln0_w"]))
    rc[:, 8:16] = col_layout(np.asarray(inputs["L1_attn_a_ln0_b"]))
    for base, nm in ((16, "ln1_w"), (48, "ln1_b"), (80, "ln2_w"), (112, "ln2_b"),
                     (144, "tmk"), (176, "tmv"), (208, "tmr"), (240, "cmk"),
                     (272, "cmr"), (304, "time_decay"), (336, "time_first")):
        rc[:, base:base + 32] = col_layout(np.asarray(inputs["L1_attn_a_" + nm]))
    g["r_cols"] = rc
    g["f1_rw1"] = np.ascontiguousarray(inputs["L1_ffn_rw1"])
    g["f1_rw2"] = np.ascontiguousarray(inputs["L1_ffn_rw2"])
    g["f1_rb1c"] = col_layout(np.asarray(inputs["L1_ffn_rb1"]))
    g["f1_rb2r"] = np.ascontiguousarray(np.broadcast_to(np.asarray(inputs["L1_ffn_rb2"]), (128, 8)))
    g["f1_b1c"] = col_layout(np.asarray(inputs["L1_ffn_b1"]))
    g["f1_b2c"] = col_layout(np.asarray(inputs["L1_ffn_b2"]))
    g["f1_w1"] = np.ascontiguousarray(np.asarray(inputs["L1_ffn_w1"]).reshape(8 * D, 4096))
    g["f1_w2"] = np.ascontiguousarray(np.asarray(inputs["L1_ffn_w2"]).reshape(8 * 4096, D))
    g["g_winT"] = tr(inputs["L1_attn_b_W_in"])
    g["g_wg1T"] = tr(inputs["L1_attn_b_W_g1"])
    g["g_wg2T"] = tr(inputs["L1_attn_b_W_g2"])
    g["g_woutT"] = tr(inputs["L1_attn_b_W_out"])
    g["g_lnw"] = col_layout(np.asarray(inputs["L1_attn_b_ln_w"]))
    return g


def build_full():
    nc = bass.Bass("TRN2", target_bir_lowering=False)
    ap = lambda name, shape: nc.dram_tensor(name, shape, F32, kind="ExternalInput").ap()
    x_in = ap("x_in", [128, PB * NT])
    names = {
        "a_wqT": [D, 1024], "a_wkvT": [D, 512], "a_wkupT": [512, 1024],
        "a_wvupT": [512, 1024], "a_woT": [1024, D],
        "rope": [128, 4 * T], "p64": [128, 128], "m2": [128, 4 * 512],
        "glam": [128, T], "ltm": [64, 64], "ident": [128, 128],
        "f0_gwT": [D, 8], "f0_lncols": [128, 2 * PB],
        "f0_sWg": [D, 2048], "f0_sWu": [D, 2048], "f0_sWd": [2048, D],
        "f0_Wg": [8 * D, 2048], "f0_Wu": [8 * D, 2048], "f0_Wd": [8 * 2048, D],
        "s_BT": [D, D], "s_CT": [D, D], "s_DT": [D, D], "s_diagA": [128, PB],
        "r_wkT": [4 * D, D], "r_wvT": [4 * D, D], "r_wrT": [4 * D, D],
        "r_woT": [4 * D, D], "r_wckT": [4 * D, 4096], "r_wcvT": [4 * 4096, D],
        "r_wcrT": [4 * D, D], "r_cols": [128, 368],
        "f1_rw1": [D, 4096], "f1_rw2": [4096, 8], "f1_rb1c": [128, 32],
        "f1_rb2r": [128, 8], "f1_b1c": [128, 8 * 32], "f1_b2c": [128, 8 * PB],
        "f1_w1": [8 * D, 4096], "f1_w2": [8 * 4096, D],
        "g_winT": [D, 3072], "g_wg1T": [D, 128], "g_wg2T": [128, D],
        "g_woutT": [D, D], "g_lnw": [128, PB],
    }
    A = {n: ap(n, sh) for n, sh in names.items()}
    y = nc.dram_tensor("y", [128, PB * NT], F32, kind="ExternalOutput").ap()
    xs = nc.dram_tensor("xs_scr", [128, PB * NT], F32).ap()
    xn_hbm = nc.dram_tensor("xn_scr", [128, PB * NT], BF16).ap()
    acc_hbm = nc.dram_tensor("acc_scr", [128, PB * NT], F32).ap()
    wrows_hbm = nc.dram_tensor("wrows_scr", [8, NT], BF16).ap()
    og_hbm = nc.dram_tensor("og_scr", [128, 8 * NT], BF16).ap()
    xr_hbm = nc.dram_tensor("xr_scr", [128, PB * NT], F32).ap()
    h_hbm = nc.dram_tensor("h_scr", [128, PB * (NT + 2)], BF16).ap()
    aux_hbm = nc.dram_tensor("aux_scr", [128, 16 * NT], BF16).ap()

    def wap(base_ap, off, rows, cols_):
        return hview(base_ap, off, [[cols_, 128], [128 * cols_, rows // 128], [1, cols_]])

    es = ExitStack()
    with es:
        k = K(nc, es)
        consts = emit_consts(k, es, dict(ident=A["ident"]))
        scr = dict(xn_hbm=xn_hbm, acc_hbm=acc_hbm, wrows_hbm=wrows_hbm,
                   og_hbm=og_hbm, xr_hbm=xr_hbm, h_hbm=h_hbm, aux_hbm=aux_hbm)
        emit_attn(k, x_in, xs, dict(wqT=A["a_wqT"], wkvT=A["a_wkvT"],
                  wkupT=A["a_wkupT"], wvupT=A["a_wvupT"], woT=A["a_woT"],
                  rope=A["rope"], p64=A["p64"], m2=A["m2"]), consts, scr)
        I0 = 2048
        f0_experts = [dict(shared=True, wg=wap(A["f0_sWg"], 0, D, I0),
                           wu=wap(A["f0_sWu"], 0, D, I0), wd=wap(A["f0_sWd"], 0, I0, D))]
        for e in range(8):
            f0_experts.append(dict(
                wg=wap(A["f0_Wg"], e * D * I0, D, I0),
                wu=wap(A["f0_Wu"], e * D * I0, D, I0),
                wd=wap(A["f0_Wd"], e * I0 * D, I0, D),
                wrow=hview(wrows_hbm, e * NT, [[NT, 1], [1, NT]])))
        W_f0 = dict(gwT=A["f0_gwT"], ln_cols=A["f0_lncols"], experts=f0_experts)
        emit_moe_swiglu(k, xs, xs, W_f0, consts, scr)
        emit_ssm(k, xs, xs, dict(BT=A["s_BT"], CT=A["s_CT"], DT=A["s_DT"],
                                 diagA=A["s_diagA"]), consts)
        emit_moe_swiglu(k, xs, xs, W_f0, consts, scr)
        emit_rwkv(k, xs, xs, dict(wkT=A["r_wkT"], wvT=A["r_wvT"], wrT=A["r_wrT"],
                                  woT=A["r_woT"], wckT=A["r_wckT"], wcvT=A["r_wcvT"],
                                  wcrT=A["r_wcrT"], cols=A["r_cols"]), consts, scr)
        I1 = 4096
        f1_experts = []
        for e in range(8):
            f1_experts.append(dict(
                w1=wap(A["f1_w1"], e * D * I1, D, I1),
                w2=wap(A["f1_w2"], e * I1 * D, I1, D),
                wrow=hview(wrows_hbm, e * NT, [[NT, 1], [1, NT]])))
        W_f1 = dict(rw1T=wap(A["f1_rw1"], 0, D, I1), rw2=A["f1_rw2"],
                    rb1_cols=A["f1_rb1c"], rb2_rep=A["f1_rb2r"],
                    b1_cols=A["f1_b1c"], b2_cols=A["f1_b2c"], experts=f1_experts)
        emit_moe_mlp(k, xs, xs, W_f1, consts, scr)
        emit_gla(k, xs, xs, dict(winT=A["g_winT"], wg1T=A["g_wg1T"],
                                 wg2T=A["g_wg2T"], woutT=A["g_woutT"],
                                 lnw_col=A["g_lnw"], glam=A["glam"], ltm=A["ltm"]),
                 consts, scr)
        emit_moe_mlp(k, xs, y, W_f1, consts, scr)
        with nc.Block() as block:
            k.finalize(block)
    return nc


def kernel(**inputs):
    g = _host_prep(inputs)
    x = np.asarray(inputs["L0_attn_a_x"], np.float32)
    in_maps = []
    for c in range(N_CORES):
        xc = np.ascontiguousarray(x[2 * c: 2 * c + 2].transpose(2, 0, 1).reshape(D, NT))
        m = dict(g)
        m["x_in"] = pmaj(xc)
        in_maps.append(m)
    nc = build_full()
    res = run_bass_kernel_spmd(nc, in_maps, core_ids=list(range(N_CORES)))
    outs = []
    for c in range(N_CORES):
        yc = unpmaj(np.asarray(res.results[c]["y"], np.float32), D, NT)
        outs.append(yc.reshape(D, BC, T).transpose(1, 2, 0))
    return np.ascontiguousarray(np.concatenate(outs, axis=0))


# version 10
# speedup vs baseline: 1.0453x; 1.0108x over previous
import os
import numpy as np
import ml_dtypes
from contextlib import ExitStack
import concourse.bass as bass
import concourse.mybir as mybir
from concourse.bass_utils import run_bass_kernel_spmd
from concourse.alu_op_type import AluOpType as OP
F32 = mybir.dt.float32
BF16 = mybir.dt.bfloat16
AF = mybir.ActivationFunctionType
AX = mybir.AxisListType
D = 1024
T = 2048
BC = 2
NT = BC * T
PB = D // 128
NTT = NT // 512
N_CORES = 8
EPS = 1e-05
ENGINES = ('tensor', 'vector', 'scalar', 'gpsimd', 'sync')
_UID = [0]

def uname(base):
    _UID[0] += 1
    return f'{base}_u{_UID[0]}'

def hview(ap, free_off, dims):
    return bass.AP(ap.tensor, ap.offset + free_off, dims)

def xtile(ap, tt, width=512):
    return hview(ap, tt * PB * width, [[PB * NT, 128], [1, PB * width]])

def xchunk(ap, db, tok):
    return hview(ap, (tok // 512) * PB * 512 + db * 512, [[PB * NT, 128], [1, 512]])

class K:

    def __init__(self, nc, es):
        self.nc = nc
        self.es = es
        self.streams = {e: [] for e in ENGINES}
        self.sems = {}
        self.cnt = {}
        self.eng_sem = {'tensor': 't', 'vector': 'v', 'scalar': 's', 'gpsimd': 'p', 'sync': 'y'}
        self.selfser = ('vector', 'scalar', 'gpsimd')
        self._last = {e: 0 for e in self.selfser}

    def sem(self, name):
        if name not in self.sems:
            self.sems[name] = self.es.enter_context(self.nc.semaphore(name))
            self.cnt[name] = 0
        return self.sems[name]

    def mark(self, name):
        self.sem(name)
        return self.cnt[name]

    def raw(self, eng, build):
        self.streams[eng].append(lambda e, b=build: b(e))

    def op(self, eng, build, dn=1):
        name = self.eng_sem[eng]
        h = self.sem(name)
        if eng in self.selfser and self._last[eng] > 0:
            prev = self._last[eng]
            self.streams[eng].append(lambda e, h=h, prev=prev: e.wait_ge(h, prev))
        self.cnt[name] += dn
        tgt = self.cnt[name]
        if eng in self.selfser:
            self._last[eng] = tgt
        self.streams[eng].append(lambda e, b=build, h=h, dn=dn: b(e).then_inc(h, dn))
        return tgt

    def tmile(self):
        return self.op('tensor', lambda t: t.nop())

    def dma(self, eng, build, slot=None):
        q = {'sync': 'y', 'gpsimd': 'g', 'scalar': 'c'}[eng]
        name = q + ':' + (slot or 'q')
        h = self.sem(name)
        self.cnt[name] += 16
        tgt = self.cnt[name]
        self.streams[eng].append(lambda e, b=build, h=h: b(e).then_inc(h, 16))
        return (name, tgt)

    def wait(self, eng, name, val):
        if val and val > 0:
            h = self.sem(name)
            self.streams[eng].append(lambda e, h=h, val=val: e.wait_ge(h, val))

    def waits(self, eng, marks):
        for name, val in marks.items():
            self.wait(eng, name, val)

    def allmarks(self):
        return {n: self.cnt[n] for n in self.cnt if self.cnt[n] > 0}

    def barrier(self):
        marks = self.allmarks()
        for e in ENGINES:
            self.waits(e, marks)

    def finalize(self, block):
        for e in ENGINES:
            fns = self.streams[e]
            if fns:
                getattr(block, e)(lambda eng, fns=fns: [f(eng) for f in fns])

def emit_consts(k, es, cin):
    nc = k.nc
    c = {}
    c['ones1_bf'] = es.enter_context(nc.sbuf_tensor(uname('c_ones1_bf'), [1, 128], BF16))
    c['ones1_f'] = es.enter_context(nc.sbuf_tensor(uname('c_ones1_f'), [1, 128], F32))
    c['eps_col'] = es.enter_context(nc.sbuf_tensor(uname('c_eps'), [128, 1], F32))
    c['ident_bf'] = es.enter_context(nc.sbuf_tensor(uname('c_ident_bf'), [128, 128], BF16))
    k.op('gpsimd', lambda g, c=c: g.memset(c['ones1_bf'][:], 1.0))
    k.op('gpsimd', lambda g, c=c: g.memset(c['ones1_f'][:], 1.0))
    k.op('gpsimd', lambda g, c=c: g.memset(c['eps_col'][:], EPS))
    nc.const_aps.register(F32, EPS, c['eps_col'][:])
    c['eps2_col'] = es.enter_context(nc.sbuf_tensor(uname('c_eps2'), [128, 1], F32))
    k.op('gpsimd', lambda g, c=c: g.memset(c['eps2_col'][:], EPS * EPS))
    nc.const_aps.register(F32, EPS * EPS, c['eps2_col'][:])
    if 'ident' in cin:
        k.dma('gpsimd', lambda g, c=c, cin=cin: g.dma_start(out=c['ident_bf'][:], in_=cin['ident']), slot='ci')
    k.barrier()
    return c

def emit_ln(k, src_hbm, consts, dst_sb=None, dst_hbm=None, gamma_col=None, beta_col=None, double=False, x_sb_out=None, dual=None, dst_hbm_fn=None):
    nc = k.nc
    ones1_bf = consts['ones1_bf']
    es = ExitStack()
    with es:
        sb = lambda n, s, d, es=es, nc=nc: es.enter_context(nc.sbuf_tensor(uname(n), s, d))
        ps_ = lambda n, s, es=es, nc=nc: es.enter_context(nc.psum_tensor(uname(n), s, F32))
        need_f32 = dual is not None or x_sb_out is not None
        DEPTH = 2 if dual is not None else 4
        xcol = [sb(f'ln_xcol{i}', [128, PB * 512], F32) for i in range(2)] if need_f32 else [None, None]
        xsq = [sb(f'ln_xsq{i}', [128, PB * 512], BF16) for i in range(2)]
        xb = [sb(f'ln_xb{i}', [128, PB * 512], BF16) for i in range(DEPTH)]
        nrows = 4096 if dual is not None else (3584 if double else 2560)
        rows = sb('ln_rows', [1, nrows], F32)
        crow = [sb(f'ln_crow{i}', [1, 512], BF16) for i in range(2)]
        orow = [sb(f'ln_orow{i}', [1, 512], BF16) for i in range(2)]
        cb_sb = [sb(f'ln_cb{i}', [128, 512], F32) for i in range(2)]
        ob_sb = [sb(f'ln_ob{i}', [128, 512], F32) for i in range(2)]
        if dual is not None:
            xb2 = sb('ln_xb2', [128, PB * 512], F32)
            cb2_sb = [sb(f'ln_cb2{i}', [128, 512], F32) for i in range(2)]
            ob2_sb = [sb(f'ln_ob2{i}', [128, 512], F32) for i in range(2)]
            if 'gw_sb' in dual:
                plg = [ps_(f'ln_plg{i}', [128, 8]) for i in range(2)]
        onesb = sb('ln_onesb', [128, 1], BF16)
        ps = [ps_(f'ln_ps{i}', [1, 512]) for i in range(2)]
        ps2 = [ps_(f'ln_ps2{i}', [1, 512]) for i in range(2)]
        pbc = ps_('ln_pbc', [128, 512])
        pbo = ps_('ln_pbo', [128, 512])
        ms = k.op('gpsimd', lambda g, onesb=onesb: g.memset(onesb[:], 1.0))
        k.wait('tensor', 'p', ms)
        a_sq = [0] * NTT
        a_f32 = [None] * NTT
        a_sums = [0] * NTT
        m_mu = [0] * NTT
        m_vv = [0] * NTT
        b_rm = [0] * NTT
        b_mbc = [0] * NTT
        b_copy = [0] * NTT
        b_v = [0] * NTT
        b_g = [0] * NTT
        b_dv = [0] * NTT
        b_dg = [0] * NTT
        sd = [None] * NTT
        sd32 = [None] * NTT
        lg_t = [0] * NTT
        lgv = {0: 0, 1: 0}
        for tt in range(NTT + 1):
            if tt >= 1:
                bt = tt - 1
                par = bt % 2
                buf = xcol[par]
                mu = rows[0:1, 0:512]
                m2 = rows[0:1, 512:1024]
                var = rows[0:1, 1024:1536]
                s1 = rows[0:1, 1536:2048]
                s2 = rows[0:1, 2048:2560]
                k.wait('vector', 't', a_sums[bt])
                if bt >= 1:
                    k.wait('vector', 't', b_mbc[bt - 1])
                m_mu[bt] = k.op('vector', lambda v, mu=mu, ps=ps, par=par: v.tensor_scalar(out=mu, in0=ps[par][:, :], scalar1=1.0 / D, scalar2=None, op0=OP.mult))
                k.op('vector', lambda v, m2=m2, mu=mu: v.tensor_tensor(out=m2, in0=mu, in1=mu, op=OP.mult))
                vv = k.op('vector', lambda v, var=var, ps2=ps2, m2=m2, par=par: v.scalar_tensor_tensor(out=var, in0=ps2[par][:, :], scalar=1.0 / D, in1=m2, op0=OP.mult, op1=OP.subtract))
                m_vv[bt] = vv
                k.wait('scalar', 'v', vv)
                if bt >= 1:
                    k.wait('scalar', 't', b_mbc[bt - 1])
                if double:
                    rowv = k.op('scalar', lambda s, s1=s1, var=var: s.activation(out=s1, in_=var, func=AF.Sqrt, scale=1.0 + EPS, bias=EPS * EPS))
                else:
                    rowv = k.op('scalar', lambda s, s1=s1, var=var: s.activation(out=s1, in_=var, func=AF.Sqrt, bias=EPS))
                if dual is not None:
                    rowv = k.op('scalar', lambda s, s2=s2, var=var: s.activation(out=s2, in_=var, func=AF.Sqrt, scale=1.0 + EPS, bias=EPS * EPS))
                b_rm[bt] = rowv
            if tt < NTT:
                apar = tt % 2
                ax = tt % DEPTH
                if tt >= DEPTH:
                    k.wait('gpsimd', 'v', b_v[tt - DEPTH])
                    k.wait('gpsimd', 'p', b_g[tt - DEPTH])
                    k.wait('gpsimd', 't', a_sums[tt - DEPTH])
                    k.wait('gpsimd', 's', a_sq[tt - DEPTH])
                    if sd[tt - DEPTH]:
                        k.wait('gpsimd', *sd[tt - DEPTH])
                gmark = k.dma('gpsimd', lambda g, tt=tt, ax=ax, xb=xb, src_hbm=src_hbm: g.dma_start(out=xb[ax][:], in_=xtile(src_hbm, tt)), slot=f'lnb{ax}')
                if need_f32:
                    abuf = xcol[apar]
                    if tt >= 2:
                        k.wait('sync', 'v', b_v[tt - 2])
                        k.wait('sync', 'p', b_g[tt - 2])
                    a_f32[tt] = k.dma('sync', lambda y, tt=tt, buf=abuf, src_hbm=src_hbm: y.dma_start(out=buf[:], in_=xtile(src_hbm, tt)), slot=f'lnx{apar}')
                k.wait('scalar', *gmark)
                if tt >= 2:
                    k.wait('scalar', 't', a_sums[tt - 2])
                sq = k.op('scalar', lambda s, ax=ax, par=apar, xb=xb, xsq=xsq: s.activation(out=xsq[par][:], in_=xb[ax][:], func=AF.Square))
                a_sq[tt] = sq
                k.wait('tensor', *gmark)
                k.wait('tensor', 's', sq)
                if tt >= 2:
                    k.wait('tensor', 'v', m_vv[tt - 2])
                for db in range(PB - 1):
                    k.raw('tensor', lambda t, db=db, ps=ps, onesb=onesb, xb=xb, par=apar, ax=ax: t.matmul(ps[par][:, :], onesb[:, :], xb[ax][:, db * 512:(db + 1) * 512], start=db == 0, stop=False))
                k.raw('tensor', lambda t, ps=ps, onesb=onesb, xb=xb, par=apar, ax=ax: t.matmul(ps[par][:, :], onesb[:, :], xb[ax][:, (PB - 1) * 512:PB * 512], start=False, stop=True))
                for db in range(PB - 1):
                    k.raw('tensor', lambda t, db=db, ps2=ps2, onesb=onesb, xsq=xsq, par=apar: t.matmul(ps2[par][:, :], onesb[:, :], xsq[par][:, db * 512:(db + 1) * 512], start=db == 0, stop=False))
                a_sums[tt] = k.op('tensor', lambda t, ps2=ps2, onesb=onesb, xsq=xsq, par=apar: t.matmul(ps2[par][:, :], onesb[:, :], xsq[par][:, (PB - 1) * 512:PB * 512], start=False, stop=True))
            if tt >= 1:
                bt = tt - 1
                par = bt % 2
                buf = xcol[par]
                rowv = b_rm[bt]
                mu = rows[0:1, 0:512]
                s1 = rows[0:1, 1536:2048]
                s2 = rows[0:1, 2048:2560]
                k.wait('tensor', 's', rowv)
                k.wait('tensor', 'v', m_vv[bt])
                if bt >= 1:
                    k.wait('tensor', 'v', b_copy[bt - 1])
                    if dual is not None:
                        k.wait('tensor', 'v', b_dv[bt - 1])
                        k.wait('tensor', 'p', b_dg[bt - 1])
                k.raw('tensor', lambda t, pbc=pbc, consts=consts, s1=s1: t.matmul(pbc[:, :], consts['ones1_f'][0:1, :], s1, start=True, stop=True))
                mbc = k.op('tensor', lambda t, pbo=pbo, consts=consts, mu=mu: t.matmul(pbo[:, :], consts['ones1_f'][0:1, :], mu, start=True, stop=True))
                b_mbc[bt] = mbc
                k.wait('vector', 't', mbc)
                if bt >= 2:
                    k.wait('vector', 'p', b_g[bt - 2])
                rcp = k.op('vector', lambda v, par=par, cb_sb=cb_sb, pbc=pbc: v.reciprocal(out=cb_sb[par][:], in_=pbc[:, :]))
                if dual is not None:
                    vob = k.op('vector', lambda v, par=par, ob_sb=ob_sb, pbo=pbo: v.tensor_copy(out=ob_sb[par][:], in_=pbo[:, :]))
                    k.wait('tensor', 'v', vob)
                    mbc2 = k.op('tensor', lambda t, pbc=pbc, consts=consts, s2=s2: t.matmul(pbc[:, :], consts['ones1_f'][0:1, :], s2, start=True, stop=True))
                    dv = 0
                    dg = 0
                    if bt >= 1 and sd32[bt - 1]:
                        k.wait('vector', *sd32[bt - 1])
                        k.wait('gpsimd', *sd32[bt - 1])
                    if bt >= 1 and 'gw_sb' in dual:
                        k.wait('gpsimd', 't', lg_t[bt - 1])
                    k.wait('vector', 't', mbc2)
                    k.op('vector', lambda v, par=par, cb2_sb=cb2_sb, pbc=pbc: v.reciprocal(out=cb2_sb[par][:], in_=pbc[:, :]))
                    vd2 = k.op('vector', lambda v, par=par, ob2_sb=ob2_sb, pbo=pbo: v.tensor_copy(out=ob2_sb[par][:], in_=pbo[:, :]))
                    k.wait('gpsimd', 'v', vd2)
                    k.wait('vector', *a_f32[bt])
                    k.wait('gpsimd', *a_f32[bt])
                    for db in range(PB):
                        eng = 'vector' if db < 6 else 'gpsimd'
                        x2s = xb2[:, db * 512:(db + 1) * 512]
                        xcs = buf[:, db * 512:(db + 1) * 512]
                        k.op(eng, lambda v, par=par, xcs=xcs, x2s=x2s, ob2_sb=ob2_sb: v.tensor_tensor(out=x2s, in0=xcs, in1=ob2_sb[par][:], op=OP.subtract))
                        k.op(eng, lambda v, par=par, x2s=x2s, cb2_sb=cb2_sb: v.tensor_tensor(out=x2s, in0=x2s, in1=cb2_sb[par][:], op=OP.mult))
                        dm = k.op(eng, lambda v, x2s=x2s, db=db, dual=dual: v.tensor_scalar(out=x2s, in0=x2s, scalar1=dual['gamma_col'][:, db:db + 1], scalar2=dual['beta_col'][:, db:db + 1], op0=OP.mult, op1=OP.add))
                        if eng == 'vector':
                            dv = dm
                        else:
                            dg = dm
                    b_dv[bt] = dv
                    b_dg[bt] = dg
                if bt >= 2 and sd[bt - 2]:
                    k.wait('vector', *sd[bt - 2])
                    k.wait('gpsimd', *sd[bt - 2])
                apl = 0
                bx = bt % DEPTH
                for db in range(PB):
                    eng = 'vector' if dual is None or db < 6 else 'gpsimd'
                    xbs = xb[bx][:, db * 512:(db + 1) * 512]
                    if eng == 'gpsimd':
                        k.wait(eng, 'v', rcp)
                        if dual is not None:
                            k.wait(eng, 'v', vob)
                        k.op(eng, lambda v, par=par, xbs=xbs, ob_sb=ob_sb: v.tensor_tensor(out=xbs, in0=xbs, in1=ob_sb[par][:], op=OP.subtract))
                    else:
                        k.op(eng, lambda v, xbs=xbs, pbo=pbo: v.tensor_tensor(out=xbs, in0=xbs, in1=pbo[:, :], op=OP.subtract))
                    m_ = k.op(eng, lambda v, par=par, xbs=xbs, cb_sb=cb_sb: v.tensor_tensor(out=xbs, in0=xbs, in1=cb_sb[par][:], op=OP.mult))
                    if gamma_col is not None:
                        m_ = k.op(eng, lambda v, xbs=xbs, db=db, gamma_col=gamma_col, beta_col=beta_col: v.tensor_scalar(out=xbs, in0=xbs, scalar1=gamma_col[:, db:db + 1], scalar2=beta_col[:, db:db + 1], op0=OP.mult, op1=OP.add))
                    if eng == 'vector':
                        apl = m_
                    if dst_sb is not None:
                        ceng = 'vector' if db < 4 else 'gpsimd'
                        if ceng == 'gpsimd':
                            k.wait(ceng, 'v', m_)
                        k.op(ceng, lambda v, xbs=xbs, db=db, bt=bt, dst_sb=dst_sb: v.tensor_copy(out=dst_sb[:, db * NT + bt * 512:db * NT + bt * 512 + 512], in_=xbs))
                b_copy[bt] = apl
                if x_sb_out is not None:
                    k.wait('vector', *a_f32[bt])
                    k.wait('gpsimd', *a_f32[bt])
                    for db in range(PB):
                        eng = 'vector' if db < 4 else 'gpsimd'
                        xcs = buf[:, db * 512:(db + 1) * 512]
                        k.op(eng, lambda v, xcs=xcs, db=db, bt=bt, x_sb_out=x_sb_out: v.tensor_copy(out=x_sb_out[:, db * NT + bt * 512:db * NT + bt * 512 + 512], in_=xcs))
                b_v[bt] = k.mark('v')
                b_g[bt] = k.mark('p')
                if dual is not None and 'dst32_fn' in dual:
                    k.wait('scalar', 'v', b_dv[bt])
                    k.wait('scalar', 'p', b_dg[bt])
                    if bt >= 1 and sd32[bt - 1]:
                        k.wait('scalar', *sd32[bt - 1])
                    sd32[bt] = k.dma('scalar', lambda s, bt=bt, dual=dual, xb2=xb2: s.dma_start(out=dual['dst32_fn'](bt), in_=xb2[:]), slot='ln32')
                if dual is not None and 'gw_sb' in dual:
                    k.wait('tensor', 'v', b_dv[bt])
                    k.wait('tensor', 'p', b_dg[bt])
                    for ts4 in range(4):
                        qpar = ts4 % 2
                        k.wait('tensor', 'v', lgv[qpar])
                        for db in range(PB - 1):
                            k.raw('tensor', lambda t, db=db, ts4=ts4, qpar=qpar, plg=plg, xb2=xb2, dual=dual: t.matmul(plg[qpar][:, :], xb2[:, db * 512 + ts4 * 128:db * 512 + ts4 * 128 + 128], dual['gw_sb'][:, db * 8:(db + 1) * 8], start=db == 0, stop=False))
                        db = PB - 1
                        mlg = k.op('tensor', lambda t, db=db, ts4=ts4, qpar=qpar, plg=plg, xb2=xb2, dual=dual: t.matmul(plg[qpar][:, :], xb2[:, db * 512 + ts4 * 128:db * 512 + ts4 * 128 + 128], dual['gw_sb'][:, db * 8:(db + 1) * 8], start=False, stop=True))
                        k.wait('vector', 't', mlg)
                        lgv[qpar] = k.op('vector', lambda v, bt=bt, ts4=ts4, qpar=qpar, dual=dual, plg=plg: v.tensor_copy(out=dual['logits_sb'][:, (bt * 4 + ts4) * 8:(bt * 4 + ts4) * 8 + 8], in_=plg[qpar][:, :]))
                        lg_t[bt] = mlg
                if dst_hbm is not None or dst_hbm_fn is not None:
                    k.wait('scalar', 'v', b_v[bt])
                    k.wait('scalar', 'p', b_g[bt])
                    if bt >= 2 and sd[bt - 2]:
                        k.wait('scalar', *sd[bt - 2])
                    dstap = dst_hbm_fn(bt) if dst_hbm_fn is not None else xtile(dst_hbm, bt)
                    sd[bt] = k.dma('scalar', lambda s, dstap=dstap, bx=bx, xb=xb: s.dma_start(out=dstap, in_=xb[bx][:]), slot=f'lns{bt % 2}')
        k.barrier()

def load_weight_bf16(k, w_hbm, dst_sb, rows, cols, slot='w'):
    nb = rows // 128
    src = bass.AP(w_hbm.tensor, w_hbm.offset, [[cols, 128], [128 * cols, nb], [1, cols]])
    return k.dma('gpsimd', lambda g: g.dma_start(out=dst_sb[:, 0:nb * cols], in_=src), slot=slot)

def emit_ssm(k, x_hbm, out_hbm, W, consts):
    nc = k.nc
    es = ExitStack()
    with es:
        sb = lambda n, s, d, es=es, nc=nc: es.enter_context(nc.sbuf_tensor(uname(n), s, d))
        psum = lambda n, s, es=es, nc=nc: es.enter_context(nc.psum_tensor(uname(n), s, F32))
        xn = sb('ssm_xn', [128, PB * NT], BF16)
        Xb = sb('ssm_X', [128, PB * NT], BF16)
        a_sb = sb('ssm_a', [128, PB], F32)
        adma = k.dma('sync', lambda y, a_sb=a_sb, W=W: y.dma_start(out=a_sb[:], in_=W['diagA']), slot='wa')
        emit_ln(k, x_hbm, consts, dst_sb=xn)
        pbu = [psum(f'ssm_pbu{i}', [128, 512]) for i in range(2)]
        py = [psum(f'ssm_py{i}', [128, 512]) for i in range(2)]
        es2 = ExitStack()
        sb2 = lambda n, s, d, es2=es2, nc=nc: es2.enter_context(nc.sbuf_tensor(uname(n), s, d))
        wB = sb2('ssm_wB', [128, PB * D], BF16)
        arep = sb2('ssm_arep', [128, T], F32)
        bu = sb2('ssm_bu', [128, T], F32)
        wl_b = load_weight_bf16(k, W['BT'], wB, D, D, slot='wb')
        k.wait('tensor', *wl_b)
        evac = {0: 0, 1: 0}
        for db in range(PB):
            k.wait('vector', *adma)
            k.op('vector', lambda v, arep=arep: v.memset(arep[:], 1.0))
            k.op('vector', lambda v, db=db, arep=arep, a_sb=a_sb: v.tensor_scalar(out=arep[:], in0=arep[:], scalar1=a_sb[:, db:db + 1], scalar2=None, op0=OP.mult))
            for b in range(BC):
                for q in range(T // 512):
                    tt = b * (T // 512) + q
                    par = tt % 2
                    k.wait('tensor', 'v', evac[par])
                    for mb in range(PB - 1):
                        k.raw('tensor', lambda t, mb=mb, db=db, tt=tt, par=par, pbu=pbu, wB=wB, xn=xn: t.matmul(pbu[par][:, :], wB[:, mb * D + db * 128:mb * D + db * 128 + 128], xn[:, mb * NT + tt * 512:mb * NT + tt * 512 + 512], start=mb == 0, stop=False))
                    mb = PB - 1
                    mbu = k.op('tensor', lambda t, mb=mb, db=db, tt=tt, par=par, pbu=pbu, wB=wB, xn=xn: t.matmul(pbu[par][:, :], wB[:, mb * D + db * 128:mb * D + db * 128 + 128], xn[:, mb * NT + tt * 512:mb * NT + tt * 512 + 512], start=False, stop=True))
                    k.wait('vector', 't', mbu)
                    evac[par] = k.op('vector', lambda v, q=q, par=par, bu=bu, pbu=pbu: v.tensor_copy(out=bu[:, q * 512:(q + 1) * 512], in_=pbu[par][:, :]))
                k.op('vector', lambda v, db=db, b=b, Xb=Xb, arep=arep, bu=bu: v.tensor_tensor_scan(out=Xb[:, db * NT + b * T:db * NT + (b + 1) * T], data0=arep[:], data1=bu[:, :], initial=0.0, op0=OP.mult, op1=OP.add))
        k.barrier()
        es2.close()
        es3 = ExitStack()
        sb3 = lambda n, s, d, es3=es3, nc=nc: es3.enter_context(nc.sbuf_tensor(uname(n), s, d))
        wC = sb3('ssm_wC', [128, PB * D], BF16)
        wD = sb3('ssm_wD', [128, PB * D], BF16)
        xcol = [sb3(f'ssm_xc{i}', [128, PB * 512], F32) for i in range(2)]
        ytile = [sb3(f'ssm_yt{i}', [128, 512], F32) for i in range(2)]
        wl_c = load_weight_bf16(k, W['CT'], wC, D, D, slot='wc')
        wl_d = load_weight_bf16(k, W['DT'], wD, D, D, slot='wd')
        k.wait('tensor', *wl_c)
        k.wait('tensor', *wl_d)
        ydone = {0: 0, 1: 0}
        st = {0: None, 1: None}
        tilev = [0] * NTT
        for tt in range(NTT):
            par = tt % 2
            if tt >= 2:
                k.wait('sync', 'v', tilev[tt - 2])
            xd = k.dma('sync', lambda y, tt=tt, par=par, xcol=xcol, x_hbm=x_hbm: y.dma_start(out=xcol[par][:, :], in_=xtile(x_hbm, tt)), slot=f'yx{par}')
            for db in range(PB):
                ppar = (tt * PB + db) % 2
                k.wait('tensor', 'v', ydone[ppar])
                for nb in range(PB):
                    k.raw('tensor', lambda t, nb=nb, db=db, tt=tt, ppar=ppar, py=py, wC=wC, Xb=Xb: t.matmul(py[ppar][:, :], wC[:, nb * D + db * 128:nb * D + db * 128 + 128], Xb[:, nb * NT + tt * 512:nb * NT + tt * 512 + 512], start=nb == 0, stop=False))
                for mb in range(PB - 1):
                    k.raw('tensor', lambda t, mb=mb, db=db, tt=tt, ppar=ppar, py=py, wD=wD, xn=xn: t.matmul(py[ppar][:, :], wD[:, mb * D + db * 128:mb * D + db * 128 + 128], xn[:, mb * NT + tt * 512:mb * NT + tt * 512 + 512], start=False, stop=False))
                mb = PB - 1
                my = k.op('tensor', lambda t, mb=mb, db=db, tt=tt, ppar=ppar, py=py, wD=wD, xn=xn: t.matmul(py[ppar][:, :], wD[:, mb * D + db * 128:mb * D + db * 128 + 128], xn[:, mb * NT + tt * 512:mb * NT + tt * 512 + 512], start=False, stop=True))
                k.wait('vector', 't', my)
                k.wait('vector', *xd)
                if st[ppar]:
                    k.wait('vector', *st[ppar])
                ydone[ppar] = k.op('vector', lambda v, db=db, par=par, ppar=ppar, ytile=ytile, py=py, xcol=xcol: v.tensor_tensor(out=ytile[ppar][:, :], in0=py[ppar][:, :], in1=xcol[par][:, db * 512:(db + 1) * 512], op=OP.add))
                k.wait('scalar', 'v', ydone[ppar])
                st[ppar] = k.dma('scalar', lambda s, db=db, tt=tt, ppar=ppar, out_hbm=out_hbm, ytile=ytile: s.dma_start(out=xchunk(out_hbm, db, tt * 512), in_=ytile[ppar][:, :]), slot=f'ys{ppar}')
            tilev[tt] = k.mark('v')
        k.barrier()
        es3.close()

def emit_gate_rows(k, logits_sb, wrows_hbm, consts, mode):
    nc = k.nc
    es2 = ExitStack()
    with es2:
        sb = lambda n, s, d, es2=es2, nc=nc: es2.enter_context(nc.sbuf_tensor(uname(n), s, d))
        pwt = es2.enter_context(nc.psum_tensor(uname('gt_pwt'), [8, 128], BF16))
        lg = sb('gt_lg', [128, 8], F32)
        cols = sb('gt_cols', [128, 10], F32)
        e_sb = sb('gt_e', [128, 8], F32)
        w_sb = sb('gt_w', [128, 8], F32)
        wb = sb('gt_wb', [128, 8], BF16)
        ident_bf = consts['ident_bf']
        wrows = sb('gt_wrows', [8, NT], BF16)
        prev_tr = 0
        for tt32 in range(NT // 128):
            cp = k.op('vector', lambda v, tt32=tt32, lg=lg, logits_sb=logits_sb: v.tensor_copy(out=lg[:], in_=logits_sb[:, tt32 * 8:(tt32 + 1) * 8]))
            m1 = cols[:, 0:1]
            nm1 = cols[:, 1:2]
            ssum = cols[:, 2:3]
            m1p = cols[:, 3:4]
            m2p = cols[:, 4:5]
            den = cols[:, 5:6]
            k.op('vector', lambda v, m1=m1, lg=lg: v.tensor_reduce(out=m1, in_=lg[:], axis=AX.X, op=OP.max))
            vm = k.op('vector', lambda v, nm1=nm1, m1=m1: v.tensor_scalar(out=nm1, in0=m1, scalar1=-1.0, scalar2=None, op0=OP.mult))
            if mode == 'renorm':
                k.wait('scalar', 'v', vm)
                se = k.op('scalar', lambda s, e_sb=e_sb, lg=lg, nm1=nm1, ssum=ssum: s.activation(out=e_sb[:], in_=lg[:], func=AF.Exp, bias=nm1, accum_out=ssum))
                k.wait('vector', 's', se)
                k.op('vector', lambda v, den=den, ssum=ssum: v.reciprocal(out=den, in_=ssum))
                k.op('vector', lambda v, w_sb=w_sb, e_sb=e_sb, den=den: v.tensor_scalar(out=w_sb[:], in0=e_sb[:], scalar1=den, scalar2=None, op0=OP.mult))
                k.op('vector', lambda v, m1p=m1p, w_sb=w_sb: v.tensor_reduce(out=m1p, in_=w_sb[:], axis=AX.X, op=OP.max))
                k.op('vector', lambda v, e_sb=e_sb, w_sb=w_sb, m1p=m1p: v.tensor_scalar(out=e_sb[:], in0=w_sb[:], scalar1=m1p, scalar2=None, op0=OP.is_lt))
                k.op('vector', lambda v, e_sb=e_sb, w_sb=w_sb: v.tensor_tensor(out=e_sb[:], in0=e_sb[:], in1=w_sb[:], op=OP.mult))
                k.op('vector', lambda v, m2p=m2p, e_sb=e_sb: v.tensor_reduce(out=m2p, in_=e_sb[:], axis=AX.X, op=OP.max))
                k.op('vector', lambda v, den=den, m1p=m1p, m2p=m2p: v.tensor_scalar(out=den, in0=m1p, scalar1=m2p, scalar2=EPS, op0=OP.add, op1=OP.add))
                k.op('vector', lambda v, den=den: v.reciprocal(out=den, in_=den))
                k.op('vector', lambda v, e_sb=e_sb, w_sb=w_sb, m2p=m2p: v.tensor_scalar(out=e_sb[:], in0=w_sb[:], scalar1=m2p, scalar2=None, op0=OP.is_ge))
                k.op('vector', lambda v, w_sb=w_sb, e_sb=e_sb: v.tensor_tensor(out=w_sb[:], in0=w_sb[:], in1=e_sb[:], op=OP.mult))
                k.op('vector', lambda v, w_sb=w_sb, den=den: v.tensor_scalar(out=w_sb[:], in0=w_sb[:], scalar1=den, scalar2=None, op0=OP.mult))
            else:
                k.op('vector', lambda v, e_sb=e_sb, lg=lg, m1=m1: v.tensor_scalar(out=e_sb[:], in0=lg[:], scalar1=m1, scalar2=None, op0=OP.is_ge))
                k.op('vector', lambda v, w_sb=w_sb, e_sb=e_sb, lg=lg: v.scalar_tensor_tensor(out=w_sb[:], in0=e_sb[:], scalar=-1e+30, in1=lg[:], op0=OP.mult, op1=OP.add))
                m2 = cols[:, 6:7]
                k.op('vector', lambda v, m2=m2, w_sb=w_sb: v.tensor_reduce(out=m2, in_=w_sb[:], axis=AX.X, op=OP.max))
                dcol = cols[:, 7:8]
                vd = k.op('vector', lambda v, dcol=dcol, m2=m2, m1=m1: v.tensor_scalar(out=dcol, in0=m2, scalar1=m1, scalar2=None, op0=OP.subtract))
                s1 = cols[:, 8:9]
                k.wait('scalar', 'v', vd)
                sv1 = k.op('scalar', lambda s, s1=s1, dcol=dcol: s.activation(out=s1, in_=dcol, func=AF.Sigmoid, scale=-1.0))
                k.wait('vector', 's', sv1)
                k.op('vector', lambda v, e_sb=e_sb, s1=s1: v.tensor_scalar(out=e_sb[:], in0=e_sb[:], scalar1=s1, scalar2=None, op0=OP.mult))
                k.op('vector', lambda v, w_sb=w_sb, lg=lg, m2=m2: v.tensor_scalar(out=w_sb[:], in0=lg[:], scalar1=m2, scalar2=None, op0=OP.is_ge))
                k.op('vector', lambda v, lg=lg, m1=m1: v.tensor_scalar(out=lg[:], in0=lg[:], scalar1=m1, scalar2=None, op0=OP.is_lt))
                k.op('vector', lambda v, w_sb=w_sb, lg=lg: v.tensor_tensor(out=w_sb[:], in0=w_sb[:], in1=lg[:], op=OP.mult))
                k.op('vector', lambda v, dcol=dcol, s1=s1: v.tensor_scalar(out=dcol, in0=s1, scalar1=-1.0, scalar2=1.0, op0=OP.mult, op1=OP.add))
                k.op('vector', lambda v, w_sb=w_sb, dcol=dcol: v.tensor_scalar(out=w_sb[:], in0=w_sb[:], scalar1=dcol, scalar2=None, op0=OP.mult))
                k.op('vector', lambda v, w_sb=w_sb, e_sb=e_sb: v.tensor_tensor(out=w_sb[:], in0=w_sb[:], in1=e_sb[:], op=OP.add))
            vw = k.op('vector', lambda v, wb=wb, w_sb=w_sb: v.tensor_copy(out=wb[:], in_=w_sb[:]))
            k.wait('tensor', 'v', vw)
            if prev_tr:
                k.wait('tensor', 'v', prev_tr)
            mtr = k.op('tensor', lambda t, pwt=pwt, wb=wb, ident_bf=ident_bf: t.transpose(pwt[:, :], wb[:, :], ident_bf[:]))
            k.wait('vector', 't', mtr)
            prev_tr = k.op('vector', lambda v, tt32=tt32, wrows=wrows, pwt=pwt: v.tensor_copy(out=wrows[:, tt32 * 128:(tt32 + 1) * 128], in_=pwt[:, :]))
        k.wait('sync', 'v', prev_tr)
        k.dma('sync', lambda y, wrows_hbm=wrows_hbm, wrows=wrows: y.dma_start(out=wrows_hbm, in_=wrows[:]), slot='gw')
        k.barrier()

def emit_moe_swiglu(k, x_hbm, out_hbm, W, consts, scr):
    nc = k.nc
    es = ExitStack()
    with es:
        sb = lambda n, s, d, es=es, nc=nc: es.enter_context(nc.sbuf_tensor(uname(n), s, d))
        lncols = sb('mo_lncols', [128, 2 * PB], F32)
        gwf = sb('mo_gwf', [128, PB * 8], F32)
        logits_sb = sb('mo_logits', [128, NT // 128 * 8], F32)
        cdma = k.dma('sync', lambda y, lncols=lncols, W=W: y.dma_start(out=lncols[:], in_=W['ln_cols']), slot='wa')
        gdma = k.dma('sync', lambda y, gwf=gwf, W=W: y.dma_start(out=gwf[:], in_=hview(W['gwT'], 0, [[8, 128], [128 * 8, PB], [1, 8]])), slot='wb')
        k.wait('vector', *cdma)
        k.wait('scalar', *cdma)
        k.wait('tensor', *gdma)
        emit_ln(k, x_hbm, consts, dst_hbm=scr['xn_hbm'], dual=dict(gamma_col=lncols[:, 0:PB], beta_col=lncols[:, PB:2 * PB], gw_sb=gwf, logits_sb=logits_sb))
        emit_gate_rows(k, logits_sb, scr['wrows_hbm'], consts, mode='renorm')
        k.barrier()
    es = ExitStack()
    with es:
        es_e = ExitStack()
        sb = lambda n, s, d, es_e=es_e, nc=nc: es_e.enter_context(nc.sbuf_tensor(uname(n), s, d))
        psum = lambda n, s, d=F32, es_e=es_e, nc=nc: es_e.enter_context(nc.psum_tensor(uname(n), s, d))
        I = 2048
        JB = I // 128
        wg_sb = sb('mo_wg', [128, PB * I], BF16)
        wu_sb = sb('mo_wu', [128, PB * I], BF16)
        wd_sb = sb('mo_wd', [128, JB * D], BF16)
        h_sb = sb('mo_h', [128, JB * 512], BF16)
        sg = sb('mo_sg', [128, 512], BF16)
        xnc = [sb(f'mo_xnc{i}', [128, PB * 512], BF16) for i in range(2)]
        wrow = sb('mo_wrow', [1, NT], BF16)
        wbc = sb('mo_wbc', [128, NT], F32)
        ytl = [sb(f'mo_ytl{i}', [128, 512], F32) for i in range(2)]
        pg = [psum(f'mo_pg{i}', [128, 512]) for i in range(2)]
        pu = [psum(f'mo_pu{i}', [128, 512]) for i in range(2)]
        pyy = [psum(f'mo_pyy{i}', [128, 512]) for i in range(2)]
        pwb = psum('mo_pwb', [128, 512])
        n_exp = len(W['experts'])
        for ei, ex in enumerate(W['experts']):
            shared = ex.get('shared', False)
            wlg = k.dma('gpsimd', lambda g, ex=ex, wg_sb=wg_sb: g.dma_start(out=wg_sb[:, :], in_=ex['wg']), slot='wg')
            wlu = k.dma('gpsimd', lambda g, ex=ex, wu_sb=wu_sb: g.dma_start(out=wu_sb[:, :], in_=ex['wu']), slot='wu')
            wld = k.dma('gpsimd', lambda g, ex=ex, wd_sb=wd_sb: g.dma_start(out=wd_sb[:, :], in_=ex['wd']), slot='wd')
            if not shared:
                wr = k.dma('sync', lambda y, ex=ex, wrow=wrow: y.dma_start(out=wrow[:, :], in_=ex['wrow']), slot='wr')
                prev = 0
                for q in range(NT // 512):
                    k.wait('tensor', *wr)
                    if prev:
                        k.wait('tensor', 'v', prev)
                    mb = k.op('tensor', lambda t, q=q, pwb=pwb, consts=consts, wrow=wrow: t.matmul(pwb[:, :], consts['ones1_bf'][0:1, :], wrow[0:1, q * 512:(q + 1) * 512], start=True, stop=True))
                    k.wait('vector', 't', mb)
                    prev = k.op('vector', lambda v, q=q, wbc=wbc, pwb=pwb: v.tensor_copy(out=wbc[:, q * 512:(q + 1) * 512], in_=pwb[:, :]))
            k.wait('tensor', *wlg)
            k.wait('tensor', *wlu)
            k.wait('tensor', *wld)
            xl = [None, None]
            hdone = 0
            sdone = {0: 0, 1: 0}
            vdone = {0: 0, 1: 0}
            ydone = {0: 0, 1: 0}
            st = {0: None, 1: None}
            mblk_t = [0] * NTT
            for mblk in range(NTT):
                par = mblk % 2
                if mblk >= 2:
                    k.wait('sync', 't', mblk_t[mblk - 2])
                xl[par] = k.dma('sync', lambda y, mblk=mblk, par=par, xnc=xnc, scr=scr: y.dma_start(out=xnc[par][:, :], in_=xtile(scr['xn_hbm'], mblk)), slot=f'xn{par}')
                for jb in range(JB):
                    jpar = jb % 2
                    k.wait('tensor', *xl[par])
                    k.wait('tensor', 's', sdone[jpar])
                    k.wait('tensor', 'v', vdone[jpar])
                    for mb in range(PB - 1):
                        k.raw('tensor', lambda t, mb=mb, jb=jb, par=par, jpar=jpar, pg=pg, wg_sb=wg_sb, I=I, xnc=xnc: t.matmul(pg[jpar][:, :], wg_sb[:, mb * I + jb * 128:mb * I + jb * 128 + 128], xnc[par][:, mb * 512:(mb + 1) * 512], start=mb == 0, stop=False))
                    mb = PB - 1
                    mg = k.op('tensor', lambda t, mb=mb, jb=jb, par=par, jpar=jpar, pg=pg, wg_sb=wg_sb, I=I, xnc=xnc: t.matmul(pg[jpar][:, :], wg_sb[:, mb * I + jb * 128:mb * I + jb * 128 + 128], xnc[par][:, mb * 512:(mb + 1) * 512], start=False, stop=True))
                    for mb in range(PB - 1):
                        k.raw('tensor', lambda t, mb=mb, jb=jb, par=par, jpar=jpar, pu=pu, wu_sb=wu_sb, I=I, xnc=xnc: t.matmul(pu[jpar][:, :], wu_sb[:, mb * I + jb * 128:mb * I + jb * 128 + 128], xnc[par][:, mb * 512:(mb + 1) * 512], start=mb == 0, stop=False))
                    mb = PB - 1
                    mu_ = k.op('tensor', lambda t, mb=mb, jb=jb, par=par, jpar=jpar, pu=pu, wu_sb=wu_sb, I=I, xnc=xnc: t.matmul(pu[jpar][:, :], wu_sb[:, mb * I + jb * 128:mb * I + jb * 128 + 128], xnc[par][:, mb * 512:(mb + 1) * 512], start=False, stop=True))
                    k.wait('scalar', 't', mg)
                    k.wait('scalar', 'v', hdone)
                    ssg = k.op('scalar', lambda s, jpar=jpar, sg=sg, pg=pg: s.activation(out=sg[:], in_=pg[jpar][:, :], func=AF.Silu))
                    sdone[jpar] = ssg
                    k.wait('vector', 't', mu_)
                    k.wait('vector', 's', ssg)
                    hdone = k.op('vector', lambda v, jb=jb, jpar=jpar, h_sb=h_sb, sg=sg, pu=pu: v.tensor_tensor(out=h_sb[:, jb * 512:(jb + 1) * 512], in0=sg[:], in1=pu[jpar][:, :], op=OP.mult))
                    vdone[jpar] = hdone
                for db in range(PB):
                    ypar = db % 2
                    k.wait('tensor', 'v', hdone)
                    k.wait('tensor', 'v', ydone[ypar])
                    for jb in range(JB - 1):
                        k.raw('tensor', lambda t, jb=jb, db=db, ypar=ypar, pyy=pyy, wd_sb=wd_sb, h_sb=h_sb: t.matmul(pyy[ypar][:, :], wd_sb[:, jb * D + db * 128:jb * D + db * 128 + 128], h_sb[:, jb * 512:(jb + 1) * 512], start=jb == 0, stop=False))
                    jb = JB - 1
                    my = k.op('tensor', lambda t, jb=jb, db=db, ypar=ypar, pyy=pyy, wd_sb=wd_sb, h_sb=h_sb: t.matmul(pyy[ypar][:, :], wd_sb[:, jb * D + db * 128:jb * D + db * 128 + 128], h_sb[:, jb * 512:(jb + 1) * 512], start=False, stop=True))
                    k.wait('vector', 't', my)
                    if st[ypar]:
                        k.wait('vector', *st[ypar])
                    if shared:
                        ydone[ypar] = k.op('vector', lambda v, ypar=ypar, ytl=ytl, pyy=pyy: v.tensor_copy(out=ytl[ypar][:], in_=pyy[ypar][:, :]))
                    else:
                        ydone[ypar] = k.op('vector', lambda v, ypar=ypar, mblk=mblk, ytl=ytl, pyy=pyy, wbc=wbc: v.tensor_tensor(out=ytl[ypar][:], in0=pyy[ypar][:, :], in1=wbc[:, mblk * 512:(mblk + 1) * 512], op=OP.mult))
                    k.wait('gpsimd', 'v', ydone[ypar])
                    dst = xchunk(scr['acc_hbm'], db, mblk * 512)
                    if shared:
                        st[ypar] = k.dma('gpsimd', lambda g, dst=dst, ypar=ypar, ytl=ytl: g.dma_start(out=dst, in_=ytl[ypar][:]), slot=f'ac{ypar}')
                    else:
                        st[ypar] = k.dma('gpsimd', lambda g, dst=dst, ypar=ypar, ytl=ytl: g.dma_start(out=dst, in_=ytl[ypar][:], accum_op=OP.add), slot=f'ac{ypar}')
                mblk_t[mblk] = k.mark('t')
            k.barrier()
        es_e.close()
        emit_add_hbm(k, x_hbm, scr['acc_hbm'], out_hbm, es)
        k.barrier()

def emit_add_hbm(k, a_hbm, b_hbm, out_hbm, es):
    nc = k.nc
    es2 = ExitStack()
    with es2:
        sb = lambda n, s, d, es2=es2, nc=nc: es2.enter_context(nc.sbuf_tensor(uname(n), s, d))
        at = [sb(f'ad_a{i}', [128, PB * 512], F32) for i in range(2)]
        bt = [sb(f'ad_b{i}', [128, PB * 512], F32) for i in range(2)]
        tilev = [0] * NTT
        stm = {0: None, 1: None}
        for tt in range(NTT):
            par = tt % 2
            if tt >= 2:
                k.wait('sync', 'v', tilev[tt - 2])
                if stm[par]:
                    k.wait('sync', *stm[par])
            da = k.dma('sync', lambda y, tt=tt, par=par, at=at, a_hbm=a_hbm: y.dma_start(out=at[par][:], in_=xtile(a_hbm, tt)), slot=f'aa{par}')
            db_ = k.dma('sync', lambda y, tt=tt, par=par, bt=bt, b_hbm=b_hbm: y.dma_start(out=bt[par][:], in_=xtile(b_hbm, tt)), slot=f'ab{par}')
            k.wait('vector', *da)
            k.wait('vector', *db_)
            if stm[par]:
                k.wait('vector', *stm[par])
            tilev[tt] = k.op('vector', lambda v, par=par, at=at, bt=bt: v.tensor_tensor(out=at[par][:], in0=at[par][:], in1=bt[par][:], op=OP.add))
            k.wait('scalar', 'v', tilev[tt])
            stm[par] = k.dma('scalar', lambda s, tt=tt, par=par, out_hbm=out_hbm, at=at: s.dma_start(out=xtile(out_hbm, tt), in_=at[par][:]), slot=f'ao{par}')
        k.barrier()

def emit_attn(k, x_hbm, out_hbm, W, consts, scr):
    nc = k.nc
    NH, HD, LAT, HG = (16, 64, 512, 4)
    NG = NH // HG
    LB = LAT // 128
    NT8 = T // 128
    NS4 = T // 512
    emit_ln(k, x_hbm, consts, dst_hbm=scr['xn_hbm'])
    es = ExitStack()
    with es:
        sb = lambda n, s, d, es=es, nc=nc: es.enter_context(nc.sbuf_tensor(uname(n), s, d))
        rope = sb('at_rope', [128, 4 * T], BF16)
        p64 = sb('at_p64', [128, 128], F32)
        m2 = sb('at_m2', [128, 4 * 512], BF16)
        rdma = k.dma('gpsimd', lambda g, rope=rope, W=W: g.dma_start(out=rope[:], in_=W['rope']), slot='wm2')
        pdma = k.dma('sync', lambda y, p64=p64, W=W: y.dma_start(out=p64[:], in_=W['p64']), slot='wa')
        mdma = k.dma('gpsimd', lambda g, m2=m2, W=W: g.dma_start(out=m2[:], in_=W['m2']), slot='wm')
        cosq, sinq = (rope[:, 0:T], rope[:, T:2 * T])
        cosk, sink = (rope[:, 2 * T:3 * T], rope[:, 3 * T:4 * T])
        xn_b = sb('at_xn', [128, PB * T], BF16)
        lat_sb = sb('at_lat', [128, LB * T], BF16)
        qpad = sb('at_qpad', [65, HG * T], BF16)
        kpad = sb('at_kpad', [65, HG * T], BF16)
        VW = NT8 * (HG * (HD + 1))
        vtok = sb('at_vtok', [128, VW], BF16)
        o_all = sb('at_oall', [128, PB * T], BF16)
        qraw = sb('at_qraw', [128, 512], F32)
        ropet = sb('at_ropet', [128, 512], F32)
        swt = sb('at_swt', [128, 512], F32)
        mx = sb('at_mx', [128, 2], F32)
        nmaxc = sb('at_nmax', [128, 1], BF16)
        pt_sb = sb('at_pt', [128, NT8 * 512], BF16)
        otmp = sb('at_otmp', [64, 512], F32)
        rrow = sb('at_rrow', [1, 512], BF16)
        rcp = sb('at_rcp', [1, 512], F32)
        wsl = sb('at_wsl', [128, PB * D], BF16)
        wv_sl = sb('at_wvsl', [128, LB * 256], BF16)
        xcol = [sb('at_xc0', [128, PB * 512], F32)]
        ytl = [sb(f'at_ytl{i}', [128, 512], F32) for i in range(2)]
        for b in range(BC):
            es2 = ExitStack()
            with es2:
                ps = lambda n, s, d=F32, es2=es2, nc=nc: es2.enter_context(nc.psum_tensor(uname(n), s, d))
                pp = [ps(f'at_pp{i}', [128, 512]) for i in range(2)]
                psw = [ps(f'at_psw{i}', [128, 512]) for i in range(2)]
                prow = ps('at_prow', [1, 128])
                pO = ps('at_pO', [65, 512])
                prc = ps('at_prc', [64, 512])
                for tq in range(4):
                    xbl = k.dma('sync', lambda y, b=b, tq=tq, xn_b=xn_b, scr=scr: y.dma_start(out=bass.AP(xn_b[:, :].tensor, xn_b[:, :].offset + tq * 512, [[PB * T, 128], [T, PB], [1, 512]]), in_=xtile(scr['xn_hbm'], b * 4 + tq)), slot='xb')
                k.wait('tensor', *xbl)
                wl = k.dma('gpsimd', lambda g, wsl=wsl, LAT=LAT, W=W: g.dma_start(out=wsl[:, 0:PB * LAT], in_=hview(W['wkvT'], 0, [[LAT, 128], [128 * LAT, PB], [1, LAT]])), slot='wg')
                k.wait('tensor', *wl)
                evac = {0: 0, 1: 0}
                for lb in range(LB):
                    for tt in range(NS4):
                        par = (lb * NS4 + tt) % 2
                        k.wait('tensor', 'v', evac[par])
                        for mb in range(PB):
                            fn = lambda t, mb=mb, lb=lb, tt=tt, par=par, pp=pp, wsl=wsl, LAT=LAT, xn_b=xn_b: t.matmul(pp[par][:, :], wsl[:, mb * LAT + lb * 128:mb * LAT + lb * 128 + 128], xn_b[:, mb * T + tt * 512:mb * T + tt * 512 + 512], start=mb == 0, stop=mb == PB - 1)
                            if mb < PB - 1:
                                k.raw('tensor', fn)
                            else:
                                mm = k.op('tensor', fn)
                        k.wait('vector', 't', mm)
                        evac[par] = k.op('vector', lambda v, lb=lb, tt=tt, par=par, lat_sb=lat_sb, pp=pp: v.tensor_copy(out=lat_sb[:, lb * T + tt * 512:lb * T + tt * 512 + 512], in_=pp[par][:, :]))
                msk = k.op('gpsimd', lambda g, kpad=kpad: g.memset(kpad[64:65, :], 1.0))
                msv = k.op('gpsimd', lambda g, vtok=vtok: g.memset(vtok[:], 1.0))
                k.wait('vector', *rdma)
                k.wait('vector', *pdma)
                k.wait('vector', *mdma)
                wsl_t = k.mark('t')
                wv_t = 0
                for grp in range(NG):
                    k.wait('gpsimd', 't', wsl_t)
                    wq = k.dma('gpsimd', lambda g, grp=grp, wsl=wsl, W=W, NH=NH, HD=HD: g.dma_start(out=wsl[:, 0:PB * 256], in_=hview(W['wqT'], grp * 256, [[NH * HD, 128], [128 * NH * HD, PB], [1, 256]])), slot='wg')
                    wk = k.dma('gpsimd', lambda g, grp=grp, wsl=wsl, LB=LB, W=W, NH=NH, HD=HD: g.dma_start(out=wsl[:, PB * 256:PB * 256 + LB * 256], in_=hview(W['wkupT'], grp * 256, [[NH * HD, 128], [128 * NH * HD, LB], [1, 256]])), slot='wu')
                    k.wait('gpsimd', 't', wv_t)
                    wv = k.dma('gpsimd', lambda g, grp=grp, wv_sl=wv_sl, LB=LB, W=W, NH=NH, HD=HD: g.dma_start(out=wv_sl[:, 0:LB * 256], in_=hview(W['wvupT'], grp * 256, [[NH * HD, 128], [128 * NH * HD, LB], [1, 256]])), slot='wd')
                    for qk in range(2):
                        k.wait('tensor', *(wq if qk == 0 else wk))
                        nin = PB if qk == 0 else LB
                        src = xn_b if qk == 0 else lat_sb
                        woff = 0 if qk == 0 else PB * 256
                        cos = cosq if qk == 0 else cosk
                        sin = sinq if qk == 0 else sink
                        dstpad = qpad if qk == 0 else kpad
                        for pt2 in range(2):
                            evq = k.mark('v')
                            for tt in range(NS4):
                                par = tt % 2
                                k.wait('tensor', 'v', evq)
                                for mb in range(nin):
                                    fn = lambda t, mb=mb, pt2=pt2, tt=tt, par=par, src=src, woff=woff, nin=nin, pp=pp, wsl=wsl: t.matmul(pp[par][:, :], wsl[:, woff + mb * 256 + pt2 * 128:woff + mb * 256 + pt2 * 128 + 128], src[:, mb * T + tt * 512:mb * T + tt * 512 + 512], start=mb == 0, stop=mb == nin - 1)
                                    if mb < nin - 1:
                                        k.raw('tensor', fn)
                                    else:
                                        mm = k.op('tensor', fn)
                                k.wait('vector', 't', mm)
                                evq = k.op('vector', lambda v, par=par, qraw=qraw, pp=pp: v.tensor_copy(out=qraw[:], in_=pp[par][:, :]))
                                k.wait('tensor', 'v', evq)
                                msw = k.op('tensor', lambda t, par=par, psw=psw, p64=p64, qraw=qraw: t.matmul(psw[par][:, :], p64[:, :], qraw[:, :], start=True, stop=True))
                                k.op('vector', lambda v, cos=cos, tt=tt, ropet=ropet, qraw=qraw: v.tensor_tensor(out=ropet[:], in0=qraw[:], in1=cos[:, tt * 512:(tt + 1) * 512], op=OP.mult))
                                k.wait('vector', 't', msw)
                                k.op('vector', lambda v, par=par, sin=sin, tt=tt, swt=swt, psw=psw: v.tensor_tensor(out=swt[:], in0=psw[par][:, :], in1=sin[:, tt * 512:(tt + 1) * 512], op=OP.mult))
                                for hh in range(2):
                                    hl = pt2 * 2 + hh
                                    base = hh * 64
                                    evq = k.op('vector', lambda v, dstpad=dstpad, hl=hl, tt=tt, base=base, ropet=ropet, swt=swt: v.tensor_tensor(out=dstpad[0:64, hl * T + tt * 512:hl * T + tt * 512 + 512], in0=ropet[base:base + 64, :], in1=swt[base:base + 64, :], op=OP.add))
                    wsl_t = k.mark('t')
                    k.wait('tensor', *wv)
                    k.wait('tensor', 'p', msv)
                    evv = {0: k.mark('v'), 1: k.mark('v')}
                    for tt8 in range(NT8):
                        par = tt8 % 2
                        k.wait('tensor', 'v', evv[par])
                        for lb in range(LB):
                            fn = lambda t, lb=lb, tt8=tt8, par=par, pp=pp, lat_sb=lat_sb, wv_sl=wv_sl, LB=LB: t.matmul(pp[par][:, 0:256], lat_sb[:, lb * T + tt8 * 128:lb * T + tt8 * 128 + 128], wv_sl[:, lb * 256:(lb + 1) * 256], start=lb == 0, stop=lb == LB - 1)
                            if lb < LB - 1:
                                k.raw('tensor', fn)
                            else:
                                mm = k.op('tensor', fn)
                        k.wait('vector', 't', mm)
                        dst = bass.AP(vtok, tt8 * (HG * (HD + 1)), [[VW, 128], [HD + 1, HG], [1, HD]])
                        evv[par] = k.op('vector', lambda v, dst=dst, par=par, pp=pp: v.tensor_copy(out=dst, in_=pp[par][:, 0:256]))
                    wv_t = k.mark('t')
                    k.wait('tensor', 'p', msk)
                    for hh in range(HG):
                        head = grp * HG + hh
                        hl = hh
                        hv = hh * (HD + 1)
                        rowv = k.mark('v')
                        for si in range(NT8):
                            ntj = si // 4 + 1
                            redv = 0
                            for tj in range(ntj):
                                par = tj % 2
                                k.wait('tensor', 'v', redv if tj >= 2 else rowv)
                                mm = k.op('tensor', lambda t, hl=hl, si=si, tj=tj, par=par, pp=pp, qpad=qpad, kpad=kpad: t.matmul(pp[par][:, :], qpad[0:64, hl * T + si * 128:hl * T + si * 128 + 128], kpad[0:64, hl * T + tj * 512:hl * T + tj * 512 + 512], start=True, stop=True))
                                k.wait('vector', 't', mm)
                                if tj == 0:
                                    redv = k.op('vector', lambda v, par=par, mx=mx, pp=pp: v.tensor_reduce(out=mx[:, 0:1], in_=pp[par][:, :], axis=AX.X, op=OP.max))
                                else:
                                    k.op('vector', lambda v, par=par, mx=mx, pp=pp: v.tensor_reduce(out=mx[:, 1:2], in_=pp[par][:, :], axis=AX.X, op=OP.max))
                                    redv = k.op('vector', lambda v, mx=mx: v.tensor_tensor(out=mx[:, 0:1], in0=mx[:, 0:1], in1=mx[:, 1:2], op=OP.max))
                            k.op('vector', lambda v, nmaxc=nmaxc, mx=mx: v.tensor_scalar(out=nmaxc[:], in0=mx[:, 0:1], scalar1=-1.0, scalar2=None, op0=OP.mult))
                            nv = k.mark('v')
                            k.wait('tensor', 'v', nv)
                            mrow = k.op('tensor', lambda t, prow=prow, nmaxc=nmaxc, consts=consts: t.matmul(prow[:, :], nmaxc[:, :], consts['ident_bf'][:, :], start=True, stop=True))
                            k.wait('vector', 't', mrow)
                            rowv = k.op('vector', lambda v, hl=hl, si=si, qpad=qpad, prow=prow: v.tensor_copy(out=qpad[64:65, hl * T + si * 128:hl * T + si * 128 + 128], in_=prow[:, :]))
                        for S4 in range(NS4):
                            nti = (S4 + 1) * 4
                            k.wait('tensor', 'v', rowv)
                            exv = {0: 0, 1: 0}
                            for ti in range(nti):
                                par = ti % 2
                                k.wait('tensor', 's', exv[par])
                                mm = k.op('tensor', lambda t, hl=hl, S4=S4, ti=ti, par=par, pp=pp, kpad=kpad, qpad=qpad: t.matmul(pp[par][:, :], kpad[0:65, hl * T + ti * 128:hl * T + ti * 128 + 128], qpad[0:65, hl * T + S4 * 512:hl * T + S4 * 512 + 512], start=True, stop=True))
                                k.wait('scalar', 't', mm)
                                exv[par] = k.op('scalar', lambda s, ti=ti, par=par, pt_sb=pt_sb, pp=pp: s.activation(out=pt_sb[:, ti * 512:(ti + 1) * 512], in_=pp[par][:, :], func=AF.Exp))
                                rel = ti * 128 - S4 * 512
                                if 0 <= rel < 512:
                                    k.wait('vector', 's', exv[par])
                                    k.op('vector', lambda v, ti=ti, rel=rel, pt_sb=pt_sb, m2=m2: v.tensor_tensor(out=pt_sb[:, ti * 512:(ti + 1) * 512], in0=pt_sb[:, ti * 512:(ti + 1) * 512], in1=m2[:, rel // 128 * 512:rel // 128 * 512 + 512], op=OP.mult))
                            mskv = k.mark('v')
                            sse = k.mark('s')
                            k.wait('tensor', 'v', mskv)
                            k.wait('tensor', 's', sse)
                            for ti in range(nti):
                                fn = lambda t, hv=hv, ti=ti, nti=nti, pO=pO, vtok=vtok, HG=HG, HD=HD, VW=VW, pt_sb=pt_sb: t.matmul(pO[0:65, :], bass.AP(vtok, ti * (HG * (HD + 1)) + hv, [[VW, 128], [1, HD + 1]]), pt_sb[:, ti * 512:(ti + 1) * 512], start=ti == 0, stop=ti == nti - 1)
                                if ti < nti - 1:
                                    k.raw('tensor', fn)
                                else:
                                    mmo = k.op('tensor', fn)
                            k.wait('vector', 't', mmo)
                            cpv = k.op('vector', lambda v, otmp=otmp, pO=pO: v.tensor_copy(out=otmp[:], in_=pO[0:64, :]))
                            k.op('vector', lambda v, rcp=rcp, pO=pO: v.reciprocal(out=rcp[:], in_=pO[64:65, :]))
                            rv = k.op('vector', lambda v, rrow=rrow, rcp=rcp: v.tensor_copy(out=rrow[:], in_=rcp[:]))
                            k.wait('tensor', 'v', rv)
                            mrb = k.op('tensor', lambda t, prc=prc, consts=consts, rrow=rrow: t.matmul(prc[:, :], consts['ones1_bf'][0:1, 0:64], rrow[0:1, :], start=True, stop=True))
                            k.wait('vector', 't', mrb)
                            db0, base = (head // 2, head % 2 * 64)
                            rowv = k.op('vector', lambda v, db0=db0, base=base, S4=S4, head=head, o_all=o_all, otmp=otmp, prc=prc: v.tensor_tensor(out=o_all[base:base + 64, db0 * T + S4 * 512:db0 * T + S4 * 512 + 512], in0=otmp[:], in1=prc[0:64, :], op=OP.mult))
                k.wait('gpsimd', 't', k.mark('t'))
                wo = k.dma('gpsimd', lambda g, wsl=wsl, W=W: g.dma_start(out=wsl[:, 0:PB * D], in_=hview(W['woT'], 0, [[D, 128], [128 * D, PB], [1, D]])), slot='wg')
                k.wait('tensor', *wo)
                allv = k.mark('v')
                k.wait('tensor', 'v', allv)
                ydone = {0: 0, 1: 0}
                st = {0: None, 1: None}
                for tt in range(NS4):
                    xpar = 0
                    if tt >= 1:
                        k.wait('sync', 'v', ydone[0])
                        k.wait('sync', 'v', ydone[1])
                    xd = k.dma('sync', lambda y, b=b, tt=tt, xpar=xpar, xcol=xcol, x_hbm=x_hbm, NS4=NS4: y.dma_start(out=xcol[xpar][:, :], in_=xtile(x_hbm, b * NS4 + tt)), slot=f'yx{xpar}')
                    for db in range(PB):
                        par = (tt * PB + db) % 2
                        k.wait('tensor', 'v', ydone[par])
                        for qb in range(PB):
                            fn = lambda t, qb=qb, db=db, tt=tt, par=par, pp=pp, wsl=wsl, o_all=o_all: t.matmul(pp[par][:, :], wsl[:, qb * D + db * 128:qb * D + db * 128 + 128], o_all[:, qb * T + tt * 512:qb * T + tt * 512 + 512], start=qb == 0, stop=qb == PB - 1)
                            if qb < PB - 1:
                                k.raw('tensor', fn)
                            else:
                                mm = k.op('tensor', fn)
                        k.wait('vector', 't', mm)
                        k.wait('vector', *xd)
                        if st[par]:
                            k.wait('vector', *st[par])
                        ydone[par] = k.op('vector', lambda v, db=db, par=par, xpar=xpar, ytl=ytl, pp=pp, xcol=xcol: v.tensor_tensor(out=ytl[par][:], in0=pp[par][:, :], in1=xcol[xpar][:, db * 512:(db + 1) * 512], op=OP.add))
                        k.wait('scalar', 'v', ydone[par])
                        st[par] = k.dma('scalar', lambda s, b=b, db=db, tt=tt, par=par, out_hbm=out_hbm, ytl=ytl: s.dma_start(out=xchunk(out_hbm, db, b * T + tt * 512), in_=ytl[par][:]), slot=f'ys{par}')
            k.barrier()
        k.barrier()

def emit_rwkv(k, x_hbm, out_hbm, W, consts, scr):
    nc = k.nc
    L, F = (4, 4096)
    TP = T + 1
    RW0, SRC0 = (0, 8 * NT)
    es = ExitStack()
    with es:
        sb = lambda n, s, d, es=es, nc=nc: es.enter_context(nc.sbuf_tensor(uname(n), s, d))
        cols = sb('rw_cols', [128, 368], F32)
        lamu = sb('rw_lamu', [128, 24], F32)
        zcol = sb('rw_zcol', [128, PB * BC], BF16)
        cdma = k.dma('sync', lambda y, cols=cols, W=W: y.dma_start(out=cols[:], in_=W['cols']), slot='wa')
        mz = k.op('gpsimd', lambda g, zcol=zcol: g.memset(zcol[:], 0.0))
        k.wait('gpsimd', 'p', mz)
        zd = k.dma('gpsimd', lambda g, scr=scr, TP=TP, zcol=zcol: g.dma_start(out=hview(scr['h_hbm'], 0, [[PB * (NT + 2), 128], [NT + 2, PB], [TP, BC]]), in_=zcol[:, :], allow_slow_non_contiguous=True), slot='zp')
        k.wait('vector', *cdma)
        k.wait('scalar', *cdma)
        emit_ln(k, x_hbm, consts, dual=dict(gamma_col=cols[:, 0:PB], beta_col=cols[:, PB:2 * PB], dst32_fn=lambda tt, scr=scr: xtile(scr['xr_hbm'], tt)))
        k.wait('sync', *zd)
        k.barrier()

        def hpad_fn(tt):
            b, q = (tt // 4, tt % 4)
            return hview(scr['h_hbm'], b * TP + 1 + q * 512, [[PB * (NT + 2), 128], [NT + 2, PB], [1, 512]])
        for l in range(L):
            lw = lambda base, cols=cols, l=l: cols[:, base + l * PB:base + (l + 1) * PB]
            es_l = ExitStack()
            wo = es_l.enter_context(nc.sbuf_tensor(uname('rw_wo'), [128, PB * D], BF16))
            wod = k.dma('gpsimd', lambda g, l=l, wo=wo, W=W: g.dma_start(out=wo[:], in_=hview(W['woT'], l * D * D, [[D, 128], [128 * D, PB], [1, D]])), slot='wo')
            sc1 = k.op('scalar', lambda s, lamu=lamu, lw=lw: s.activation(out=lamu[:, 16:24], in_=lw(304), func=AF.Exp))
            k.wait('vector', 's', sc1)
            vv1 = k.op('vector', lambda v, lamu=lamu: v.tensor_scalar(out=lamu[:, 16:24], in0=lamu[:, 16:24], scalar1=-1.0, scalar2=None, op0=OP.mult))
            k.wait('scalar', 'v', vv1)
            k.op('scalar', lambda s, lamu=lamu: s.activation(out=lamu[:, 0:8], in_=lamu[:, 16:24], func=AF.Exp))
            sc2 = k.op('scalar', lambda s, lamu=lamu, lw=lw: s.activation(out=lamu[:, 8:16], in_=lw(336), func=AF.Exp))
            emit_ln(k, scr['xr_hbm'], consts, gamma_col=lw(16), beta_col=lw(48), dst_hbm_fn=hpad_fn)
            for b in range(BC):
                es_s = ExitStack()
                srb = es_s.enter_context(nc.sbuf_tensor(uname('rw_sr'), [128, 8 * T], BF16))
                es2 = ExitStack()
                sb2 = lambda n, s, d, es2=es2, nc=nc: es2.enter_context(nc.sbuf_tensor(uname(n), s, d))
                ps2 = lambda n, s, es2=es2, nc=nc: es2.enter_context(nc.psum_tensor(uname(n), s, F32))
                ek = sb2('rw_ek', [128, 8 * TP], BF16)
                ekv = sb2('rw_ekv', [128, 8 * TP], BF16)
                hcT = sb2('rw_hc', [128, PB * 512], BF16)
                hpT = sb2('rw_hp', [128, PB * 512], BF16)
                xk_t = sb2('rw_xk', [128, PB * 512], BF16)
                xv_t = sb2('rw_xv', [128, PB * 512], BF16)
                xr_t = sb2('rw_xr', [128, PB * 512], BF16)
                vtmp = sb2('rw_vtmp', [128, 512], F32)
                ppT = [ps2(f'rw_pp{i}', [128, 512]) for i in range(2)]
                msz = 0
                for ob in range(8):
                    k.op('vector', lambda v, ob=ob, ek=ek, TP=TP: v.memset(ek[:, ob * TP:ob * TP + 1], 0.0))
                    msz = k.op('vector', lambda v, ob=ob, ekv=ekv, TP=TP: v.memset(ekv[:, ob * TP:ob * TP + 1], 0.0))
                es3 = ExitStack()
                sb3 = lambda n, s, d, es3=es3, nc=nc: es3.enter_context(nc.sbuf_tensor(uname(n), s, d))
                wk = sb3('rw_wk', [128, PB * D], BF16)
                wv = sb3('rw_wv', [128, PB * D], BF16)
                wr = sb3('rw_wr', [128, PB * D], BF16)
                wdk = k.dma('gpsimd', lambda g, l=l, wk=wk, W=W: g.dma_start(out=wk[:], in_=hview(W['wkT'], l * D * D, [[D, 128], [128 * D, PB], [1, D]])), slot='wg')
                wdv = k.dma('gpsimd', lambda g, l=l, wv=wv, W=W: g.dma_start(out=wv[:], in_=hview(W['wvT'], l * D * D, [[D, 128], [128 * D, PB], [1, D]])), slot='wu')
                wdr = k.dma('gpsimd', lambda g, l=l, wr=wr, W=W: g.dma_start(out=wr[:], in_=hview(W['wrT'], l * D * D, [[D, 128], [128 * D, PB], [1, D]])), slot='wd')
                k.wait('tensor', *wdk)
                k.wait('tensor', *wdv)
                k.wait('tensor', *wdr)
                evp = {0: ('v', 0), 1: ('v', 0)}
                lastt = 0
                tile_mixv = 0
                for q in range(4):
                    if q >= 1:
                        k.wait('sync', 'v', tile_mixv)
                    hcl = k.dma('sync', lambda y, b=b, q=q, hcT=hcT, scr=scr, TP=TP: y.dma_start(out=hcT[:], in_=hview(scr['h_hbm'], b * TP + 1 + q * 512, [[PB * (NT + 2), 128], [NT + 2, PB], [1, 512]])), slot='hc')
                    hpl = k.dma('sync', lambda y, b=b, q=q, hpT=hpT, scr=scr, TP=TP: y.dma_start(out=hpT[:], in_=hview(scr['h_hbm'], b * TP + q * 512, [[PB * (NT + 2), 128], [NT + 2, PB], [1, 512]])), slot='hp')
                    k.wait('vector', *hcl)
                    k.wait('vector', *hpl)
                    k.wait('vector', 't', lastt)
                    k.op('vector', lambda v, xr_t=xr_t, hcT=hcT, hpT=hpT: v.tensor_tensor(out=xr_t[:], in0=hcT[:], in1=hpT[:], op=OP.subtract))
                    for db in range(PB):
                        sl = slice(db * 512, (db + 1) * 512)
                        k.op('vector', lambda v, sl=sl, db=db, l=l, xk_t=xk_t, xr_t=xr_t, cols=cols, hpT=hpT: v.scalar_tensor_tensor(out=xk_t[:, sl], in0=xr_t[:, sl], scalar=cols[:, 144 + l * PB + db:144 + l * PB + db + 1], in1=hpT[:, sl], op0=OP.mult, op1=OP.add))
                        k.op('vector', lambda v, sl=sl, db=db, l=l, xv_t=xv_t, xr_t=xr_t, cols=cols, hpT=hpT: v.scalar_tensor_tensor(out=xv_t[:, sl], in0=xr_t[:, sl], scalar=cols[:, 176 + l * PB + db:176 + l * PB + db + 1], in1=hpT[:, sl], op0=OP.mult, op1=OP.add))
                        k.op('vector', lambda v, sl=sl, db=db, l=l, xr_t=xr_t, cols=cols, hpT=hpT: v.scalar_tensor_tensor(out=xr_t[:, sl], in0=xr_t[:, sl], scalar=cols[:, 208 + l * PB + db:208 + l * PB + db + 1], in1=hpT[:, sl], op0=OP.mult, op1=OP.add))
                    mixv = k.mark('v')
                    tile_mixv = mixv
                    k.wait('tensor', 'v', mixv)
                    for ob in range(8):
                        for which in range(3):
                            par = (ob * 3 + which) % 2
                            src = (xk_t, xv_t, xr_t)[which]
                            wmat = (wk, wv, wr)[which]
                            k.wait('tensor', evp[par][0], evp[par][1])
                            for mb in range(PB):
                                fn = lambda t, mb=mb, ob=ob, par=par, src=src, wmat=wmat, ppT=ppT: t.matmul(ppT[par][:, :], wmat[:, mb * D + ob * 128:mb * D + ob * 128 + 128], src[:, mb * 512:(mb + 1) * 512], start=mb == 0, stop=mb == PB - 1)
                                if mb < PB - 1:
                                    k.raw('tensor', fn)
                                else:
                                    mm = k.op('tensor', fn)
                            if which == 0:
                                k.wait('scalar', 't', mm)
                                evp[par] = ('s', k.op('scalar', lambda s, ob=ob, q=q, par=par, ek=ek, TP=TP, ppT=ppT: s.activation(out=ek[:, ob * TP + 1 + q * 512:ob * TP + 1 + q * 512 + 512], in_=ppT[par][:, :], func=AF.Exp)))
                            elif which == 1:
                                k.wait('vector', 't', mm)
                                k.op('vector', lambda v, par=par, vtmp=vtmp, ppT=ppT: v.tensor_copy(out=vtmp[:], in_=ppT[par][:, :]))
                                k.wait('vector', 's', k.mark('s'))
                                evp[par] = ('v', k.op('vector', lambda v, ob=ob, q=q, ekv=ekv, TP=TP, ek=ek, vtmp=vtmp: v.tensor_tensor(out=ekv[:, ob * TP + 1 + q * 512:ob * TP + 1 + q * 512 + 512], in0=ek[:, ob * TP + 1 + q * 512:ob * TP + 1 + q * 512 + 512], in1=vtmp[:], op=OP.mult)))
                            else:
                                k.wait('scalar', 't', mm)
                                evp[par] = ('s', k.op('scalar', lambda s, ob=ob, q=q, par=par, srb=srb, ppT=ppT: s.activation(out=srb[:, ob * T + q * 512:ob * T + q * 512 + 512], in_=ppT[par][:, :], func=AF.Sigmoid)))
                    lastt = k.mark('t')
                es3.close()
                es3b = ExitStack()
                sb3b = lambda n, s, d, es3b=es3b, nc=nc: es3b.enter_context(nc.sbuf_tensor(uname(n), s, d))
                lamrep = sb3b('rw_lamrep', [128, T], F32)
                nbuf = sb3b('rw_nbuf', [128, T], F32)
                dbuf = sb3b('rw_dbuf', [128, T], F32)
                rwv = 0
                for ob in range(8):
                    k.op('vector', lambda v, lamrep=lamrep: v.memset(lamrep[:], 1.0))
                    k.op('vector', lambda v, ob=ob, lamrep=lamrep, lamu=lamu: v.tensor_scalar(out=lamrep[:], in0=lamrep[:], scalar1=lamu[:, ob:ob + 1], scalar2=None, op0=OP.mult))
                    k.wait('vector', 's', k.mark('s'))
                    k.op('vector', lambda v, ob=ob, nbuf=nbuf, lamrep=lamrep, ekv=ekv, TP=TP: v.tensor_tensor_scan(out=nbuf[:], data0=lamrep[:], data1=ekv[:, ob * TP:ob * TP + T], initial=0.0, op0=OP.mult, op1=OP.add))
                    k.op('vector', lambda v, ob=ob, dbuf=dbuf, lamrep=lamrep, ek=ek, TP=TP: v.tensor_tensor_scan(out=dbuf[:], data0=lamrep[:], data1=ek[:, ob * TP:ob * TP + T], initial=0.0, op0=OP.mult, op1=OP.add))
                    k.op('vector', lambda v, ob=ob, nbuf=nbuf, ekv=ekv, TP=TP, lamu=lamu: v.scalar_tensor_tensor(out=nbuf[:], in0=ekv[:, ob * TP + 1:ob * TP + 1 + T], scalar=lamu[:, 8 + ob:9 + ob], in1=nbuf[:], op0=OP.mult, op1=OP.add))
                    k.op('vector', lambda v, ob=ob, dbuf=dbuf, ek=ek, TP=TP, lamu=lamu: v.scalar_tensor_tensor(out=dbuf[:], in0=ek[:, ob * TP + 1:ob * TP + 1 + T], scalar=lamu[:, 8 + ob:9 + ob], in1=dbuf[:], op0=OP.mult, op1=OP.add))
                    k.op('vector', lambda v, dbuf=dbuf: v.reciprocal(out=dbuf[:], in_=dbuf[:]))
                    k.op('vector', lambda v, nbuf=nbuf, dbuf=dbuf: v.tensor_tensor(out=nbuf[:], in0=nbuf[:], in1=dbuf[:], op=OP.mult))
                    rwv = k.op('vector', lambda v, ob=ob, nbuf=nbuf, srb=srb: v.tensor_tensor(out=srb[:, ob * T:(ob + 1) * T], in0=nbuf[:], in1=srb[:, ob * T:(ob + 1) * T], op=OP.mult))
                es3b.close()
                es2.close()
                es4 = ExitStack()
                sb4 = lambda n, s, d, es4=es4, nc=nc: es4.enter_context(nc.sbuf_tensor(uname(n), s, d))
                ps4 = lambda n, s, es4=es4, nc=nc: es4.enter_context(nc.psum_tensor(uname(n), s, F32))
                xcl = sb4('rw_xcl', [128, PB * 512], F32)
                ytl = [sb4(f'rw_ytl{i}', [128, 512], F32) for i in range(2)]
                pw = [ps4(f'rw_pw{i}', [128, 512]) for i in range(2)]
                k.wait('sync', 'v', rwv)
                k.wait('tensor', *wod)
                k.wait('tensor', 'v', rwv)
                ydone = {0: 0, 1: 0}
                st = {0: None, 1: None}
                for q in range(4):
                    tilev = k.mark('v')
                    if q >= 1:
                        k.wait('sync', 'v', tilev)
                    xdl = k.dma('sync', lambda y, b=b, q=q, xcl=xcl, scr=scr: y.dma_start(out=xcl[:], in_=xtile(scr['xr_hbm'], b * 4 + q)), slot='hp')
                    for db in range(PB):
                        par = db % 2
                        k.wait('tensor', 'v', ydone[par])
                        for ob in range(8):
                            fn = lambda t, ob=ob, db=db, q=q, par=par, pw=pw, wo=wo, srb=srb: t.matmul(pw[par][:, :], wo[:, ob * D + db * 128:ob * D + db * 128 + 128], srb[:, ob * T + q * 512:ob * T + q * 512 + 512], start=ob == 0, stop=ob == 7)
                            if ob < 7:
                                k.raw('tensor', fn)
                            else:
                                mm = k.op('tensor', fn)
                        k.wait('vector', 't', mm)
                        k.wait('vector', *xdl)
                        if st[par]:
                            k.wait('vector', *st[par])
                        ydone[par] = k.op('vector', lambda v, db=db, par=par, ytl=ytl, pw=pw, xcl=xcl: v.tensor_tensor(out=ytl[par][:], in0=pw[par][:, :], in1=xcl[:, db * 512:(db + 1) * 512], op=OP.add))
                        k.wait('scalar', 'v', ydone[par])
                        st[par] = k.dma('scalar', lambda s, b=b, db=db, q=q, par=par, scr=scr, ytl=ytl: s.dma_start(out=xchunk(scr['xr_hbm'], db, b * T + q * 512), in_=ytl[par][:]), slot=f'ys{par}')
                    k.wait('sync', 'v', ydone[0])
                    k.wait('sync', 'v', ydone[1])
                es4.close()
                es_s.close()
                k.barrier()
            emit_ln(k, scr['xr_hbm'], consts, gamma_col=lw(80), beta_col=lw(112), dst_hbm_fn=hpad_fn)
            for b in range(BC):
                es2 = ExitStack()
                sb2 = lambda n, s, d, es2=es2, nc=nc: es2.enter_context(nc.sbuf_tensor(uname(n), s, d))
                ps2 = lambda n, s, es2=es2, nc=nc: es2.enter_context(nc.psum_tensor(uname(n), s, F32))
                hcC = sb2('cm_hc', [128, PB * 512], BF16)
                hpC = sb2('cm_hp', [128, PB * 512], BF16)
                dC_ = sb2('cm_d', [128, PB * 512], BF16)
                xk2 = sb2('cm_xk2', [128, PB * 512], BF16)
                xr2 = sb2('cm_xr2', [128, PB * 512], BF16)
                srct = sb2('cm_src', [128, 8 * 512], BF16)
                wcr = sb2('cm_wcr', [128, PB * D], BF16)
                ppC = [ps2(f'cm_pp{i}', [128, 512]) for i in range(2)]
                wdc = k.dma('gpsimd', lambda g, l=l, wcr=wcr, W=W: g.dma_start(out=wcr[:], in_=hview(W['wcrT'], l * D * D, [[D, 128], [128 * D, PB], [1, D]])), slot='wg')
                k.wait('tensor', *wdc)
                xst = None
                sst = None
                for q in range(4):
                    prevv = k.mark('v')
                    if q >= 1:
                        k.wait('sync', 'v', prevv)
                        k.wait('sync', 's', k.mark('s'))
                    hcl = k.dma('sync', lambda y, b=b, q=q, hcC=hcC, scr=scr, TP=TP: y.dma_start(out=hcC[:], in_=hview(scr['h_hbm'], b * TP + 1 + q * 512, [[PB * (NT + 2), 128], [NT + 2, PB], [1, 512]])), slot='hc')
                    hpl = k.dma('sync', lambda y, b=b, q=q, hpC=hpC, scr=scr, TP=TP: y.dma_start(out=hpC[:], in_=hview(scr['h_hbm'], b * TP + q * 512, [[PB * (NT + 2), 128], [NT + 2, PB], [1, 512]])), slot='hp')
                    k.wait('vector', *hcl)
                    k.wait('vector', *hpl)
                    if xst:
                        k.wait('vector', *xst)
                    k.wait('vector', 't', k.mark('t'))
                    k.op('vector', lambda v, dC_=dC_, hcC=hcC, hpC=hpC: v.tensor_tensor(out=dC_[:], in0=hcC[:], in1=hpC[:], op=OP.subtract))
                    for db in range(PB):
                        sl = slice(db * 512, (db + 1) * 512)
                        k.op('vector', lambda v, sl=sl, db=db, l=l, xk2=xk2, dC_=dC_, cols=cols, hpC=hpC: v.scalar_tensor_tensor(out=xk2[:, sl], in0=dC_[:, sl], scalar=cols[:, 240 + l * PB + db:240 + l * PB + db + 1], in1=hpC[:, sl], op0=OP.mult, op1=OP.add))
                        k.op('vector', lambda v, sl=sl, db=db, l=l, xr2=xr2, dC_=dC_, cols=cols, hpC=hpC: v.scalar_tensor_tensor(out=xr2[:, sl], in0=dC_[:, sl], scalar=cols[:, 272 + l * PB + db:272 + l * PB + db + 1], in1=hpC[:, sl], op0=OP.mult, op1=OP.add))
                    mixv = k.mark('v')
                    k.wait('scalar', 'v', mixv)
                    xst = k.dma('scalar', lambda s, b=b, q=q, scr=scr, xk2=xk2: s.dma_start(out=xtile(scr['xn_hbm'], b * 4 + q), in_=xk2[:]), slot='xks')
                    k.wait('tensor', 'v', mixv)
                    evs = {0: 0, 1: 0}
                    for ob in range(8):
                        par = ob % 2
                        k.wait('tensor', 's', evs[par])
                        for mb in range(PB):
                            fn = lambda t, mb=mb, ob=ob, par=par, ppC=ppC, wcr=wcr, xr2=xr2: t.matmul(ppC[par][:, :], wcr[:, mb * D + ob * 128:mb * D + ob * 128 + 128], xr2[:, mb * 512:(mb + 1) * 512], start=mb == 0, stop=mb == PB - 1)
                            if mb < PB - 1:
                                k.raw('tensor', fn)
                            else:
                                mm = k.op('tensor', fn)
                        k.wait('scalar', 't', mm)
                        if sst:
                            k.wait('scalar', *sst)
                        evs[par] = k.op('scalar', lambda s, ob=ob, par=par, srct=srct, ppC=ppC: s.activation(out=srct[:, ob * 512:(ob + 1) * 512], in_=ppC[par][:, :], func=AF.Sigmoid))
                    k.wait('scalar', 's', k.mark('s'))
                    sst = k.dma('scalar', lambda s, b=b, q=q, scr=scr, SRC0=SRC0, srct=srct: s.dma_start(out=hview(scr['aux_hbm'], SRC0 + b * T + q * 512, [[16 * NT, 128], [NT, 8], [1, 512]]), in_=srct[:]), slot='srs')
                k.wait('sync', *sst)
                k.wait('sync', *xst)
                es2.close()
                k.barrier()
                es2 = ExitStack()
                sb2 = lambda n, s, d, es2=es2, nc=nc: es2.enter_context(nc.sbuf_tensor(uname(n), s, d))
                ps2 = lambda n, s, es2=es2, nc=nc: es2.enter_context(nc.psum_tensor(uname(n), s, F32))
                wck = sb2('cm_wck', [128, PB * 2048], BF16)
                wcv = sb2('cm_wcv', [128, 16 * D], BF16)
                kf = sb2('cm_kf', [128, 16 * 512], BF16)
                xk2t = sb2('cm_xk2t', [128, PB * 512], BF16)
                rtmp = sb2('cm_rtmp', [128, 512], F32)
                kvtl = [sb2(f'cm_kvtl{i}', [128, 512], F32) for i in range(2)]
                pf = [ps2(f'cm_pf{i}', [128, 512]) for i in range(2)]
                pv = [ps2(f'cm_pv{i}', [128, 512]) for i in range(2)]
                for jh in range(2):
                    k.wait('gpsimd', 't', k.mark('t'))
                    wd1 = k.dma('gpsimd', lambda g, l=l, jh=jh, wck=wck, W=W, F=F: g.dma_start(out=wck[:], in_=hview(W['wckT'], l * D * F + jh * 2048, [[F, 128], [128 * F, PB], [1, 2048]])), slot='wg')
                    wd2 = k.dma('gpsimd', lambda g, l=l, jh=jh, wcv=wcv, W=W, F=F: g.dma_start(out=wcv[:], in_=hview(W['wcvT'], (l * F + jh * 2048) * D, [[D, 128], [128 * D, 16], [1, D]])), slot='wu')
                    k.wait('tensor', *wd1)
                    k.wait('tensor', *wd2)
                    xlt = None
                    kst = {0: None, 1: None}
                    for q in range(4):
                        if xlt:
                            k.wait('sync', 't', k.mark('t'))
                        xlt = k.dma('sync', lambda y, b=b, q=q, xk2t=xk2t, scr=scr: y.dma_start(out=xk2t[:], in_=xtile(scr['xn_hbm'], b * 4 + q)), slot='hc')
                        k.wait('tensor', *xlt)
                        evs = {0: 0, 1: 0}
                        evv = {0: 0, 1: 0}
                        sq_prev = 0
                        for jb in range(16):
                            par = jb % 2
                            k.wait('tensor', 's', evs[par])
                            k.wait('tensor', 'v', evv[par])
                            for mb in range(PB):
                                fn = lambda t, mb=mb, jb=jb, par=par, pf=pf, wck=wck, xk2t=xk2t: t.matmul(pf[par][:, :], wck[:, mb * 2048 + jb * 128:mb * 2048 + jb * 128 + 128], xk2t[:, mb * 512:(mb + 1) * 512], start=mb == 0, stop=mb == PB - 1)
                                if mb < PB - 1:
                                    k.raw('tensor', fn)
                                else:
                                    mm = k.op('tensor', fn)
                            k.wait('vector', 't', mm)
                            k.wait('vector', 's', sq_prev)
                            evv[par] = k.op('vector', lambda v, par=par, rtmp=rtmp, pf=pf: v.tensor_scalar(out=rtmp[:], in0=pf[par][:, :], scalar1=0.0, scalar2=None, op0=OP.max))
                            k.wait('scalar', 'v', evv[par])
                            evs[par] = k.op('scalar', lambda s, jb=jb, kf=kf, rtmp=rtmp: s.activation(out=kf[:, jb * 512:(jb + 1) * 512], in_=rtmp[:], func=AF.Square))
                            sq_prev = evs[par]
                        k.wait('tensor', 's', k.mark('s'))
                        kdone = {0: 0, 1: 0}
                        for db in range(PB):
                            par = db % 2
                            k.wait('tensor', 'v', kdone[par])
                            for jb in range(16):
                                fn = lambda t, jb=jb, db=db, par=par, pv=pv, wcv=wcv, kf=kf: t.matmul(pv[par][:, :], wcv[:, jb * D + db * 128:jb * D + db * 128 + 128], kf[:, jb * 512:(jb + 1) * 512], start=jb == 0, stop=jb == 15)
                                if jb < 15:
                                    k.raw('tensor', fn)
                                else:
                                    mm = k.op('tensor', fn)
                            k.wait('vector', 't', mm)
                            if kst[par]:
                                k.wait('vector', *kst[par])
                            kdone[par] = k.op('vector', lambda v, par=par, kvtl=kvtl, pv=pv: v.tensor_copy(out=kvtl[par][:], in_=pv[par][:, :]))
                            k.wait('gpsimd', 'v', kdone[par])
                            dst = xchunk(scr['acc_hbm'], db, b * T + q * 512)
                            if jh == 0:
                                kst[par] = k.dma('gpsimd', lambda g, dst=dst, par=par, kvtl=kvtl: g.dma_start(out=dst, in_=kvtl[par][:]), slot=f'ac{par}')
                            else:
                                kst[par] = k.dma('gpsimd', lambda g, dst=dst, par=par, kvtl=kvtl: g.dma_start(out=dst, in_=kvtl[par][:], accum_op=OP.add), slot=f'ac{par}')
                    k.barrier()
                es2.close()
                es2 = ExitStack()
                sb2 = lambda n, s, d, es2=es2, nc=nc: es2.enter_context(nc.sbuf_tensor(uname(n), s, d))
                kvt = sb2('cm_kvt', [128, PB * 512], F32)
                srt = sb2('cm_srt', [128, 8 * 512], BF16)
                xct = sb2('cm_xct', [128, PB * 512], F32)
                ytl2 = sb2('cm_ytl2', [128, PB * 512], F32)
                for q in range(4):
                    prevv = k.mark('v')
                    if q >= 1:
                        k.wait('sync', 'v', prevv)
                        k.wait('sync', 'c:ys0', k.mark('c:ys0'))
                    kvl = k.dma('sync', lambda y, b=b, q=q, kvt=kvt, scr=scr: y.dma_start(out=kvt[:], in_=xtile(scr['acc_hbm'], b * 4 + q)), slot='hc')
                    srl = k.dma('sync', lambda y, b=b, q=q, srt=srt, scr=scr, SRC0=SRC0: y.dma_start(out=srt[:], in_=hview(scr['aux_hbm'], SRC0 + b * T + q * 512, [[16 * NT, 128], [NT, 8], [1, 512]])), slot='hp')
                    xl2 = k.dma('sync', lambda y, b=b, q=q, xct=xct, scr=scr: y.dma_start(out=xct[:], in_=xtile(scr['xr_hbm'], b * 4 + q)), slot='wa')
                    k.wait('vector', *kvl)
                    k.wait('vector', *srl)
                    k.wait('vector', *xl2)
                    for db in range(PB):
                        k.op('vector', lambda v, db=db, kvt=kvt, srt=srt: v.tensor_tensor(out=kvt[:, db * 512:(db + 1) * 512], in0=kvt[:, db * 512:(db + 1) * 512], in1=srt[:, db * 512:(db + 1) * 512], op=OP.mult))
                        k.op('vector', lambda v, db=db, ytl2=ytl2, kvt=kvt, xct=xct: v.tensor_tensor(out=ytl2[:, db * 512:(db + 1) * 512], in0=kvt[:, db * 512:(db + 1) * 512], in1=xct[:, db * 512:(db + 1) * 512], op=OP.add))
                    wv_ = k.mark('v')
                    k.wait('scalar', 'v', wv_)
                    k.dma('scalar', lambda s, b=b, q=q, scr=scr, ytl2=ytl2: s.dma_start(out=xtile(scr['xr_hbm'], b * 4 + q), in_=ytl2[:]), slot='ys0')
                es2.close()
                k.barrier()
            es_l.close()
        emit_add_hbm(k, x_hbm, scr['xr_hbm'], out_hbm, es)
        k.barrier()

def emit_gla(k, x_hbm, out_hbm, W, consts, scr):
    nc = k.nc
    NH, HD2 = (8, 128)
    CH = 64
    NCH = T // CH
    NT8 = T // 128
    emit_ln(k, x_hbm, consts, dst_hbm=scr['xn_hbm'])
    es = ExitStack()
    with es:
        sb = lambda n, s, d, es=es, nc=nc: es.enter_context(nc.sbuf_tensor(uname(n), s, d))
        win = sb('gl_win', [128, PB * 3072], BF16)
        wg1 = sb('gl_wg1', [128, PB * 128], BF16)
        wg2 = sb('gl_wg2', [128, 1024], BF16)
        lnw = sb('gl_lnw', [128, PB], F32)
        glam = sb('gl_glam', [128, T], F32)
        ltm = sb('gl_ltm', [64, 64], BF16)
        xn_b = sb('gl_xn', [128, PB * T], BF16)
        g1f = sb('gl_g1f', [128, T], BF16)
        gate_h = sb('gl_gateh', [128, T], BF16)
        og_h = sb('gl_ogh', [128, T], BF16)
        ogt = sb('gl_ogt', [128, NH * 512], BF16)
        vtok = sb('gl_vtok', [64, NCH * 128], BF16)
        ktok = sb('gl_ktok', [64, NCH * 128], BF16)
        qsil = sb('gl_qsil', [128, T], BF16)
        qt = qsil
        khat = sb('gl_khat', [128, T], BF16)
        ktl = sb('gl_ktl', [128, T], BF16)
        lgb = sb('gl_lg', [128, T], F32)
        tmpa = sb('gl_tmpa', [128, T], F32)
        gbuf = sb('gl_g', [128, T], F32)
        cum = sb('gl_cum', [128, T], F32)
        lam = cum
        S_f = sb('gl_Sf', [128, 128], F32)
        S_b = sb('gl_Sb', [128, 128], BF16)
        amt = sb('gl_amt', [64, 64], BF16)
        rows = sb('gl_rows', [1, 512 * 6], F32)
        crow = sb('gl_crow', [1, 512], BF16)
        orow = sb('gl_orow', [1, 512], BF16)
        onesb = sb('gl_onesb', [128, 1], BF16)
        onorm = sb('gl_onorm', [128, NH * 512], BF16)
        ogsq = onorm
        xcol = [sb(f'gl_xc{i}', [128, 512], F32) for i in range(2)]
        ytl = [sb(f'gl_ytl{i}', [128, 512], F32) for i in range(2)]
        wd2 = k.dma('gpsimd', lambda g, wg1=wg1, W=W: g.dma_start(out=wg1[:], in_=hview(W['wg1T'], 0, [[128, 128], [128 * 128, PB], [1, 128]])), slot='wu')
        wd3 = k.dma('gpsimd', lambda g, wg2=wg2, W=W: g.dma_start(out=wg2[:], in_=hview(W['wg2T'], 0, [[1024, 128], [128 * 1024, 1], [1, 1024]])), slot='wd')
        wd4 = k.dma('gpsimd', lambda g, ltm=ltm, W=W: g.dma_start(out=ltm[:], in_=W['ltm']), slot='wm')
        cd1 = k.dma('sync', lambda y, lnw=lnw, W=W: y.dma_start(out=lnw[:], in_=W['lnw_col']), slot='wa')
        cd2 = k.dma('sync', lambda y, glam=glam, W=W: y.dma_start(out=glam[:], in_=W['glam']), slot='wb')
        ms1 = k.op('gpsimd', lambda g, onesb=onesb: g.memset(onesb[:], 1.0))
        for b in range(BC):
            es2 = ExitStack()
            with es2:
                ps = lambda n, s, d=F32, es2=es2, nc=nc: es2.enter_context(nc.psum_tensor(uname(n), s, d))
                pp = [ps(f'gl_pp{i}', [128, 512]) for i in range(2)]
                esA = ExitStack()
                psA = lambda n, s, d=F32, esA=esA, nc=nc: esA.enter_context(nc.psum_tensor(uname(n), s, d))
                pA = psA('gl_pA', [64, 64])
                pO = psA('gl_pO', [128, 64])
                pS = psA('gl_pS', [128, 128])
                ptr = esA.enter_context(nc.psum_tensor(uname('gl_ptr'), [128, 128], BF16))
                for tq in range(4):
                    xbl = k.dma('sync', lambda y, b=b, tq=tq, xn_b=xn_b, scr=scr: y.dma_start(out=bass.AP(xn_b[:, :].tensor, xn_b[:, :].offset + tq * 512, [[PB * T, 128], [T, PB], [1, 512]]), in_=xtile(scr['xn_hbm'], b * 4 + tq)), slot='xb')
                wd1 = k.dma('gpsimd', lambda g, win=win, W=W: g.dma_start(out=win[:], in_=hview(W['winT'], 0, [[3072, 128], [128 * 3072, PB], [1, 3072]])), slot='wg')
                k.wait('tensor', *xbl)
                k.wait('tensor', *wd1)
                k.wait('tensor', *wd2)
                evp = {0: k.mark('v'), 1: k.mark('v')}
                for tt in range(T // 512):
                    par = tt % 2
                    k.wait('tensor', 'v', evp[par])
                    for mb in range(PB):
                        fn = lambda t, mb=mb, tt=tt, par=par, pp=pp, wg1=wg1, xn_b=xn_b: t.matmul(pp[par][:, :], wg1[:, mb * 128:(mb + 1) * 128], xn_b[:, mb * T + tt * 512:mb * T + tt * 512 + 512], start=mb == 0, stop=mb == PB - 1)
                        if mb < PB - 1:
                            k.raw('tensor', fn)
                        else:
                            mm = k.op('tensor', fn)
                    k.wait('vector', 't', mm)
                    evp[par] = k.op('vector', lambda v, tt=tt, par=par, g1f=g1f, pp=pp: v.tensor_copy(out=g1f[:, tt * 512:(tt + 1) * 512], in_=pp[par][:, :]))
                evps = {0: ('v', 0), 1: ('v', 0)}
                for par in range(2):
                    evps[par] = ('v', evp[par])
                for h in range(NH):
                    for which, coff in (('q', 1024 + h * 128), ('lg', 2048 + h * 128)):
                        for tt in range(T // 512):
                            par = tt % 2
                            k.wait('tensor', *(evps[par][::-1] if False else (evps[par][0], evps[par][1])))
                            for mb in range(PB):
                                fn = lambda t, mb=mb, tt=tt, par=par, coff=coff, pp=pp, win=win, xn_b=xn_b: t.matmul(pp[par][:, :], win[:, mb * 3072 + coff:mb * 3072 + coff + 128], xn_b[:, mb * T + tt * 512:mb * T + tt * 512 + 512], start=mb == 0, stop=mb == PB - 1)
                                if mb < PB - 1:
                                    k.raw('tensor', fn)
                                else:
                                    mm = k.op('tensor', fn)
                            if which == 'q':
                                k.wait('scalar', 't', mm)
                                evs = k.op('scalar', lambda s, tt=tt, par=par, qsil=qsil, pp=pp: s.activation(out=qsil[:, tt * 512:(tt + 1) * 512], in_=pp[par][:, :], func=AF.Silu))
                                evps[par] = ('s', evs)
                            else:
                                k.wait('vector', 't', mm)
                                evps[par] = ('v', k.op('vector', lambda v, tt=tt, par=par, lgb=lgb, pp=pp: v.tensor_copy(out=lgb[:, tt * 512:(tt + 1) * 512], in_=pp[par][:, :])))
                    for par in range(2):
                        k.wait('tensor', evps[par][0], evps[par][1])
                    evv = {0: k.mark('v'), 1: k.mark('v')}
                    for tt8 in range(NT8):
                        par = tt8 % 2
                        k.wait('tensor', 'v', evv[par])
                        for mb in range(PB):
                            fn = lambda t, mb=mb, tt8=tt8, par=par, h=h, pp=pp, xn_b=xn_b, win=win: t.matmul(pp[par][:, 0:128], xn_b[:, mb * T + tt8 * 128:mb * T + tt8 * 128 + 128], win[:, mb * 3072 + h * 128:mb * 3072 + h * 128 + 128], start=mb == 0, stop=mb == PB - 1)
                            if mb < PB - 1:
                                k.raw('tensor', fn)
                            else:
                                mm = k.op('tensor', fn)
                        k.wait('vector', 't', mm)
                        k.op('vector', lambda v, tt8=tt8, par=par, vtok=vtok, pp=pp: v.tensor_copy(out=vtok[:, 2 * tt8 * 128:2 * tt8 * 128 + 128], in_=pp[par][0:64, 0:128]))
                        evv[par] = k.op('vector', lambda v, tt8=tt8, par=par, vtok=vtok, pp=pp: v.tensor_copy(out=vtok[:, (2 * tt8 + 1) * 128:(2 * tt8 + 1) * 128 + 128], in_=pp[par][64:128, 0:128]))
                    k.wait('tensor', 'v', k.mark('v'))
                    k.wait('tensor', *wd3)
                    evg = {0: k.mark('s'), 1: k.mark('s')}
                    for tt in range(T // 512):
                        par = tt % 2
                        k.wait('tensor', 's', evg[par])
                        mm = k.op('tensor', lambda t, tt=tt, par=par, h=h, pp=pp, wg2=wg2, g1f=g1f: t.matmul(pp[par][:, :], wg2[:, h * 128:(h + 1) * 128], g1f[:, tt * 512:(tt + 1) * 512], start=True, stop=True))
                        k.wait('scalar', 't', mm)
                        evg[par] = k.op('scalar', lambda s, tt=tt, par=par, gate_h=gate_h, pp=pp: s.activation(out=gate_h[:, tt * 512:(tt + 1) * 512], in_=pp[par][:, :], func=AF.Sigmoid))
                        evps[par] = ('s', evg[par])
                    k.wait('scalar', 'v', k.mark('v'))
                    k.op('scalar', lambda s, tmpa=tmpa, lgb=lgb: s.activation(out=tmpa[:], in_=lgb[:], func=AF.Abs))
                    k.op('scalar', lambda s, tmpa=tmpa: s.activation(out=tmpa[:], in_=tmpa[:], func=AF.Exp, scale=-1.0))
                    sln = k.op('scalar', lambda s, tmpa=tmpa: s.activation(out=tmpa[:], in_=tmpa[:], func=AF.Ln, bias=1.0))
                    k.wait('vector', 's', sln)
                    k.op('vector', lambda v, gbuf=gbuf, lgb=lgb: v.tensor_scalar(out=gbuf[:], in0=lgb[:], scalar1=-1.0, scalar2=None, op0=OP.mult))
                    k.op('vector', lambda v, gbuf=gbuf: v.tensor_scalar(out=gbuf[:], in0=gbuf[:], scalar1=0.0, scalar2=None, op0=OP.max))
                    vg = k.op('vector', lambda v, gbuf=gbuf, tmpa=tmpa: v.scalar_tensor_tensor(out=gbuf[:], in0=gbuf[:], scalar=-1.0, in1=tmpa[:], op0=OP.bypass, op1=OP.add))
                    k.op('vector', lambda v, gbuf=gbuf: v.tensor_scalar(out=gbuf[:], in0=gbuf[:], scalar1=-1.0, scalar2=None, op0=OP.mult))
                    k.wait('vector', *cd2)
                    scv = k.op('vector', lambda v, cum=cum, glam=glam, gbuf=gbuf: v.tensor_tensor_scan(out=cum[:], data0=glam[:], data1=gbuf[:], initial=0.0, op0=OP.mult, op1=OP.add))
                    k.wait('scalar', 'v', scv)
                    sl2 = k.op('scalar', lambda s, tmpa=tmpa, cum=cum: s.activation(out=tmpa[:], in_=cum[:], func=AF.Exp, scale=-1.0))
                    sl1 = k.op('scalar', lambda s, lam=lam, cum=cum: s.activation(out=lam[:], in_=cum[:], func=AF.Exp))
                    sl3 = k.op('scalar', lambda s, gbuf=gbuf, lgb=lgb: s.activation(out=gbuf[:], in_=lgb[:], func=AF.Sigmoid, scale=-1.0))
                    k.wait('vector', 's', sl3)
                    k.op('vector', lambda v, khat=khat, gbuf=gbuf, tmpa=tmpa: v.tensor_tensor(out=khat[:], in0=gbuf[:], in1=tmpa[:], op=OP.mult))
                    k.op('vector', lambda v, qt=qt, qsil=qsil, lam=lam: v.tensor_tensor(out=qt[:], in0=qsil[:], in1=lam[:], op=OP.mult))
                    for c in range(NCH):
                        k.op('vector', lambda v, c=c, ktl=ktl, CH=CH, khat=khat, lam=lam: v.tensor_scalar(out=ktl[:, c * CH:(c + 1) * CH], in0=khat[:, c * CH:(c + 1) * CH], scalar1=lam[:, c * CH + CH - 1:c * CH + CH], scalar2=None, op0=OP.mult))
                    ktv = k.mark('v')
                    k.wait('tensor', 'v', ktv)
                    prevtr = k.mark('v')
                    for tt8 in range(NT8):
                        k.wait('tensor', 'v', prevtr)
                        mtr = k.op('tensor', lambda t, tt8=tt8, ptr=ptr, ktl=ktl, consts=consts: t.transpose(ptr[:, :], ktl[:, tt8 * 128:(tt8 + 1) * 128], consts['ident_bf'][:]))
                        k.wait('vector', 't', mtr)
                        k.op('vector', lambda v, tt8=tt8, ktok=ktok, ptr=ptr: v.tensor_copy(out=ktok[:, 2 * tt8 * 128:2 * tt8 * 128 + 128], in_=ptr[0:64, :]))
                        prevtr = k.op('vector', lambda v, tt8=tt8, ktok=ktok, ptr=ptr: v.tensor_copy(out=ktok[:, (2 * tt8 + 1) * 128:(2 * tt8 + 1) * 128 + 128], in_=ptr[64:128, :]))
                    msS = k.op('vector', lambda v, S_f=S_f: v.memset(S_f[:], 0.0))
                    k.op('vector', lambda v, S_b=S_b: v.memset(S_b[:], 0.0))
                    k.wait('vector', *wd4)
                    prevA = k.mark('v')
                    prevO = k.mark('v')
                    prevS = k.mark('v')
                    for c in range(NCH):
                        cs = c * CH
                        k.wait('tensor', 'v', prevA)
                        mA = k.op('tensor', lambda t, cs=cs, pA=pA, khat=khat, CH=CH, qt=qt: t.matmul(pA[:, :], khat[:, cs:cs + CH], qt[:, cs:cs + CH], start=True, stop=True))
                        k.wait('vector', 't', mA)
                        prevA = k.op('vector', lambda v, amt=amt, pA=pA, ltm=ltm: v.tensor_tensor(out=amt[:], in0=pA[:, :], in1=ltm[:], op=OP.mult))
                        k.wait('tensor', 'v', prevA)
                        k.wait('tensor', 'v', prevO)
                        k.raw('tensor', lambda t, c=c, pO=pO, vtok=vtok, amt=amt: t.matmul(pO[:, :], vtok[:, c * 128:(c + 1) * 128], amt[:, :], start=True, stop=False))
                        mO = k.op('tensor', lambda t, cs=cs, pO=pO, S_b=S_b, qt=qt, CH=CH: t.matmul(pO[:, :], S_b[:, :], qt[:, cs:cs + CH], start=False, stop=True))
                        k.wait('tensor', 'v', prevS)
                        mS = k.op('tensor', lambda t, c=c, pS=pS, ktok=ktok, vtok=vtok: t.matmul(pS[:, :], ktok[:, c * 128:(c + 1) * 128], vtok[:, c * 128:(c + 1) * 128], start=True, stop=True))
                        k.wait('vector', 't', mO)
                        prevO = k.op('vector', lambda v, cs=cs, og_h=og_h, CH=CH, pO=pO, gate_h=gate_h: v.tensor_tensor(out=og_h[:, cs:cs + CH], in0=pO[:, :], in1=gate_h[:, cs:cs + CH], op=OP.mult))
                        k.wait('vector', 't', mS)
                        k.op('vector', lambda v, cs=cs, S_f=S_f, lam=lam, CH=CH: v.tensor_scalar(out=S_f[:], in0=S_f[:], scalar1=lam[:, cs + CH - 1:cs + CH], scalar2=None, op0=OP.mult))
                        k.op('vector', lambda v, S_f=S_f, pS=pS: v.tensor_tensor(out=S_f[:], in0=S_f[:], in1=pS[:, :], op=OP.add))
                        prevS = k.op('vector', lambda v, S_b=S_b, S_f=S_f: v.tensor_copy(out=S_b[:], in_=S_f[:]))
                    k.wait('scalar', 'v', prevO)
                    ogst = k.dma('scalar', lambda sc, h=h, b=b, scr=scr, NH=NH, og_h=og_h: sc.dma_start(out=hview(scr['og_hbm'], h * NT + b * T, [[NH * NT, 128], [1, T]]), in_=og_h[:, :]), slot='ogs')
                    k.wait('vector', *ogst)
                esA.close()
                esB = ExitStack()
                psB = lambda n, s, d=F32, esB=esB, nc=nc: esB.enter_context(nc.psum_tensor(uname(n), s, d))
                pst = [psB(f'gl_pst{i}', [1, 512]) for i in range(2)]
                pbc = psB('gl_pbc', [128, 512])
                pbo = psB('gl_pbo', [128, 512])
                k.wait('gpsimd', 't', k.mark('t'))
                wo = k.dma('gpsimd', lambda g, win=win, W=W: g.dma_start(out=win[:, 0:PB * D], in_=hview(W['woutT'], 0, [[D, 128], [128 * D, PB], [1, D]])), slot='wg')
                k.wait('vector', *cd1)
                ydone = {0: 0, 1: 0}
                st = {0: None, 1: None}
                k.wait('sync', *ogst)
                ogl = None
                for tt in range(T // 512):
                    allv = k.mark('v')
                    if ogl is not None:
                        k.wait('sync', 't', ogl_t)
                        k.wait('sync', 'v', allv)
                        k.wait('sync', 's', k.mark('s'))
                    ogl_t = k.mark('t')
                    ogl = k.dma('sync', lambda y, b=b, tt=tt, ogt=ogt, scr=scr, NH=NH: y.dma_start(out=ogt[:], in_=hview(scr['og_hbm'], b * T + tt * 512, [[NH * NT, 128], [NT, NH], [1, 512]])), slot='ogl')
                    k.wait('scalar', *ogl)
                    k.wait('scalar', 't', k.mark('t'))
                    sq = k.op('scalar', lambda s, tt=tt, ogsq=ogsq, ogt=ogt: s.activation(out=ogsq[:], in_=ogt[:], func=AF.Square))
                    k.wait('tensor', 's', sq)
                    k.wait('tensor', *ogl)
                    k.wait('tensor', 'p', ms1)
                    for hb in range(NH):
                        k.raw('tensor', lambda t, hb=hb, tt=tt, pst=pst, onesb=onesb, ogt=ogt, NH=NH: t.matmul(pst[0][:, :], onesb[:, :], ogt[:, hb * 512:(hb + 1) * 512], start=hb == 0, stop=hb == NH - 1))
                    for hb in range(NH - 1):
                        k.raw('tensor', lambda t, hb=hb, pst=pst, onesb=onesb, ogsq=ogsq: t.matmul(pst[1][:, :], onesb[:, :], ogsq[:, hb * 512:(hb + 1) * 512], start=hb == 0, stop=False))
                    mst = k.op('tensor', lambda t, pst=pst, onesb=onesb, ogsq=ogsq, NH=NH: t.matmul(pst[1][:, :], onesb[:, :], ogsq[:, (NH - 1) * 512:NH * 512], start=False, stop=True))
                    mu = rows[0:1, 0:512]
                    m2_ = rows[0:1, 512:1024]
                    var = rows[0:1, 1024:1536]
                    rst = rows[0:1, 1536:2048]
                    tmp = rows[0:1, 2048:2560]
                    k.wait('vector', 't', mst)
                    k.op('vector', lambda v, mu=mu, pst=pst: v.tensor_scalar(out=mu, in0=pst[0][:, :], scalar1=1.0 / D, scalar2=None, op0=OP.mult))
                    k.op('vector', lambda v, m2_=m2_, mu=mu: v.tensor_tensor(out=m2_, in0=mu, in1=mu, op=OP.mult))
                    vv = k.op('vector', lambda v, var=var, pst=pst, m2_=m2_: v.scalar_tensor_tensor(out=var, in0=pst[1][:, :], scalar=1.0 / D, in1=m2_, op0=OP.mult, op1=OP.subtract))
                    k.wait('scalar', 'v', vv)
                    sv = k.op('scalar', lambda s, tmp=tmp, var=var: s.activation(out=tmp, in_=var, func=AF.Sqrt, bias=EPS))
                    k.wait('vector', 's', sv)
                    k.op('vector', lambda v, rst=rst, tmp=tmp: v.reciprocal(out=rst, in_=tmp))
                    k.op('vector', lambda v, crow=crow, rst=rst: v.tensor_copy(out=crow[:], in_=rst))
                    rowv = k.op('vector', lambda v, orow=orow, mu=mu, rst=rst: v.scalar_tensor_tensor(out=orow[:], in0=mu, scalar=-1.0, in1=rst, op0=OP.mult, op1=OP.mult))
                    k.wait('tensor', 'v', rowv)
                    k.op('tensor', lambda t, pbc=pbc, consts=consts, crow=crow: t.matmul(pbc[:, :], consts['ones1_bf'][0:1, :], crow[0:1, :], start=True, stop=True))
                    mbc = k.op('tensor', lambda t, pbo=pbo, consts=consts, orow=orow: t.matmul(pbo[:, :], consts['ones1_bf'][0:1, :], orow[0:1, :], start=True, stop=True))
                    k.wait('vector', 't', mbc)
                    for hb in range(NH):
                        k.op('vector', lambda v, hb=hb, tt=tt, onorm=onorm, ogt=ogt, pbc=pbc: v.tensor_tensor(out=onorm[:, hb * 512:(hb + 1) * 512], in0=ogt[:, hb * 512:(hb + 1) * 512], in1=pbc[:, :], op=OP.mult))
                        k.op('vector', lambda v, hb=hb, onorm=onorm, pbo=pbo: v.tensor_tensor(out=onorm[:, hb * 512:(hb + 1) * 512], in0=onorm[:, hb * 512:(hb + 1) * 512], in1=pbo[:, :], op=OP.add))
                        k.op('vector', lambda v, hb=hb, onorm=onorm, lnw=lnw: v.tensor_scalar(out=onorm[:, hb * 512:(hb + 1) * 512], in0=onorm[:, hb * 512:(hb + 1) * 512], scalar1=lnw[:, hb:hb + 1], scalar2=None, op0=OP.mult))
                    nv = k.mark('v')
                    k.wait('tensor', 'v', nv)
                    k.wait('tensor', 's', k.mark('s'))
                    k.wait('tensor', *wo)
                    for db in range(PB):
                        par = db % 2
                        k.wait('sync', 'v', ydone[par])
                        xd = k.dma('sync', lambda y, b=b, tt=tt, db=db, par=par, xcol=xcol, x_hbm=x_hbm: y.dma_start(out=xcol[par][:, :], in_=xchunk(x_hbm, db, b * T + tt * 512)), slot=f'yx{par}')
                        k.wait('tensor', 'v', ydone[par])
                        for eb in range(NH):
                            fn = lambda t, eb=eb, db=db, par=par, pp=pp, win=win, onorm=onorm, NH=NH: t.matmul(pp[par][:, :], win[:, eb * D + db * 128:eb * D + db * 128 + 128], onorm[:, eb * 512:(eb + 1) * 512], start=eb == 0, stop=eb == NH - 1)
                            if eb < NH - 1:
                                k.raw('tensor', fn)
                            else:
                                mm = k.op('tensor', fn)
                        k.wait('vector', 't', mm)
                        k.wait('vector', *xd)
                        if st[par]:
                            k.wait('vector', *st[par])
                        ydone[par] = k.op('vector', lambda v, db=db, par=par, ytl=ytl, pp=pp, xcol=xcol: v.tensor_tensor(out=ytl[par][:], in0=pp[par][:, :], in1=xcol[par][:, :], op=OP.add))
                        k.wait('scalar', 'v', ydone[par])
                        st[par] = k.dma('scalar', lambda s, b=b, db=db, tt=tt, par=par, out_hbm=out_hbm, ytl=ytl: s.dma_start(out=xchunk(out_hbm, db, b * T + tt * 512), in_=ytl[par][:]), slot=f'ys{par}')
                    k.wait('sync', 'v', ydone[0])
                    k.wait('sync', 'v', ydone[1])
                esB.close()
            k.barrier()
        k.barrier()

def emit_moe_mlp(k, x_hbm, out_hbm, W, consts, scr):
    nc = k.nc
    I = 4096
    JB = I // 128
    emit_ln(k, x_hbm, consts, dst_hbm=scr['xn_hbm'])
    es = ExitStack()
    with es:
        sb = lambda n, s, d, es=es, nc=nc: es.enter_context(nc.sbuf_tensor(uname(n), s, d))
        psum = lambda n, s, d=F32, es=es, nc=nc: es.enter_context(nc.psum_tensor(uname(n), s, d))
        rw1 = sb('mr_rw1', [128, PB * I], BF16)
        rw2 = sb('mr_rw2', [128, JB * 8], F32)
        rcols = sb('mr_rcols', [128, JB + 8], F32)
        hr = sb('mr_hr', [128, JB * 512], F32)
        xnc = [sb(f'mr_xnc{i}', [128, PB * 512], BF16) for i in range(2)]
        logits_sb = sb('mr_logits', [128, NT // 128 * 8], F32)
        ph = [psum(f'mr_ph{i}', [128, 512]) for i in range(2)]
        plg = [psum(f'mr_plg{i}', [128, 8]) for i in range(2)]
        w1d = k.dma('gpsimd', lambda g, rw1=rw1, W=W: g.dma_start(out=rw1[:], in_=W['rw1T']), slot='wg')
        w2d = k.dma('sync', lambda y, rw2=rw2, W=W, JB=JB: y.dma_start(out=rw2[:], in_=hview(W['rw2'], 0, [[8, 128], [128 * 8, JB], [1, 8]])), slot='wb')
        cd = k.dma('sync', lambda y, rcols=rcols, JB=JB, W=W: y.dma_start(out=rcols[:, 0:JB], in_=W['rb1_cols']), slot='wa')
        cd2 = k.dma('sync', lambda y, rcols=rcols, JB=JB, W=W: y.dma_start(out=rcols[:, JB:JB + 8], in_=W['rb2_rep']), slot='wa')
        k.wait('tensor', *w1d)
        k.wait('tensor', *w2d)
        k.wait('scalar', *cd2)
        k.wait('vector', *cd2)
        sdone = {0: 0, 1: 0}
        lgv = {0: 0, 1: 0}
        tile_t = [0] * NTT
        for tt in range(NTT):
            par = tt % 2
            if tt >= 2:
                k.wait('sync', 't', tile_t[tt - 2])
            xl = k.dma('sync', lambda y, tt=tt, par=par, xnc=xnc, scr=scr: y.dma_start(out=xnc[par][:], in_=xtile(scr['xn_hbm'], tt)), slot=f'xn{par}')
            k.wait('tensor', *xl)
            for jb in range(JB):
                jpar = jb % 2
                k.wait('tensor', 's', sdone[jpar])
                for mb in range(PB - 1):
                    k.raw('tensor', lambda t, mb=mb, jb=jb, par=par, jpar=jpar, ph=ph, rw1=rw1, I=I, xnc=xnc: t.matmul(ph[jpar][:, :], rw1[:, mb * I + jb * 128:mb * I + jb * 128 + 128], xnc[par][:, mb * 512:(mb + 1) * 512], start=mb == 0, stop=False))
                mb = PB - 1
                mh = k.op('tensor', lambda t, mb=mb, jb=jb, par=par, jpar=jpar, ph=ph, rw1=rw1, I=I, xnc=xnc: t.matmul(ph[jpar][:, :], rw1[:, mb * I + jb * 128:mb * I + jb * 128 + 128], xnc[par][:, mb * 512:(mb + 1) * 512], start=False, stop=True))
                k.wait('scalar', 't', mh)
                sdone[jpar] = k.op('scalar', lambda s, jb=jb, jpar=jpar, hr=hr, ph=ph, rcols=rcols: s.activation(out=hr[:, jb * 512:(jb + 1) * 512], in_=ph[jpar][:, :], func=AF.Silu, bias=rcols[:, jb:jb + 1]))
            k.wait('tensor', 's', sdone[0])
            k.wait('tensor', 's', sdone[1])
            for ts4 in range(4):
                qpar = ts4 % 2
                k.wait('tensor', 'v', lgv[qpar])
                for jb in range(JB - 1):
                    k.raw('tensor', lambda t, jb=jb, ts4=ts4, qpar=qpar, plg=plg, hr=hr, rw2=rw2: t.matmul(plg[qpar][:, :], hr[:, jb * 512 + ts4 * 128:jb * 512 + ts4 * 128 + 128], rw2[:, jb * 8:(jb + 1) * 8], start=jb == 0, stop=False))
                jb = JB - 1
                mlg = k.op('tensor', lambda t, jb=jb, ts4=ts4, qpar=qpar, plg=plg, hr=hr, rw2=rw2: t.matmul(plg[qpar][:, :], hr[:, jb * 512 + ts4 * 128:jb * 512 + ts4 * 128 + 128], rw2[:, jb * 8:(jb + 1) * 8], start=False, stop=True))
                k.wait('vector', 't', mlg)
                lgv[qpar] = k.op('vector', lambda v, tt=tt, ts4=ts4, qpar=qpar, logits_sb=logits_sb, plg=plg, rcols=rcols, JB=JB: v.tensor_tensor(out=logits_sb[:, (tt * 4 + ts4) * 8:(tt * 4 + ts4) * 8 + 8], in0=plg[qpar][:, :], in1=rcols[:, JB:JB + 8], op=OP.add))
            tile_t[tt] = k.mark('t')
        k.barrier()
        emit_gate_rows(k, logits_sb, scr['wrows_hbm'], consts, mode='softmax2')
        k.barrier()
    es = ExitStack()
    with es:
        es_e = ExitStack()
        sb = lambda n, s, d, es_e=es_e, nc=nc: es_e.enter_context(nc.sbuf_tensor(uname(n), s, d))
        psum = lambda n, s, d=F32, es_e=es_e, nc=nc: es_e.enter_context(nc.psum_tensor(uname(n), s, d))
        w1_sb = sb('mr_w1', [128, PB * I], BF16)
        w2_sb = sb('mr_w2', [128, JB * D], BF16)
        h_sb = sb('mr_h', [128, JB * 512], BF16)
        bcols = sb('mr_bcols', [128, 8 * JB + 8 * PB], F32)
        xnc = [sb(f'mr_exc{i}', [128, PB * 512], BF16) for i in range(2)]
        wrow = sb('mr_wrow', [1, NT], BF16)
        wbc = sb('mr_wbc', [128, NT], F32)
        ytl = [sb(f'mr_ytl{i}', [128, 512], F32) for i in range(2)]
        pg = [psum(f'mr_pg{i}', [128, 512]) for i in range(2)]
        pyy = [psum(f'mr_pyy{i}', [128, 512]) for i in range(2)]
        pwb = psum('mr_pwb', [128, 512])
        bd1 = k.dma('sync', lambda y, bcols=bcols, JB=JB, W=W: y.dma_start(out=bcols[:, 0:8 * JB], in_=W['b1_cols']), slot='wa')
        bd2 = k.dma('sync', lambda y, bcols=bcols, JB=JB, W=W: y.dma_start(out=bcols[:, 8 * JB:8 * JB + 8 * PB], in_=W['b2_cols']), slot='wa')
        k.wait('scalar', *bd2)
        k.wait('vector', *bd2)
        for ei, ex in enumerate(W['experts']):
            b1c = bcols[:, ei * JB:(ei + 1) * JB]
            b2c = bcols[:, 8 * JB + ei * PB:8 * JB + (ei + 1) * PB]
            wl1 = k.dma('gpsimd', lambda g, ex=ex, w1_sb=w1_sb: g.dma_start(out=w1_sb[:, :], in_=ex['w1']), slot='wg')
            wl2 = k.dma('gpsimd', lambda g, ex=ex, w2_sb=w2_sb: g.dma_start(out=w2_sb[:, :], in_=ex['w2']), slot='wd')
            wr = k.dma('sync', lambda y, ex=ex, wrow=wrow: y.dma_start(out=wrow[:, :], in_=ex['wrow']), slot='wr')
            prev = 0
            for q in range(NT // 512):
                k.wait('tensor', *wr)
                if prev:
                    k.wait('tensor', 'v', prev)
                mb = k.op('tensor', lambda t, q=q, pwb=pwb, consts=consts, wrow=wrow: t.matmul(pwb[:, :], consts['ones1_bf'][0:1, :], wrow[0:1, q * 512:(q + 1) * 512], start=True, stop=True))
                k.wait('vector', 't', mb)
                prev = k.op('vector', lambda v, q=q, wbc=wbc, pwb=pwb: v.tensor_copy(out=wbc[:, q * 512:(q + 1) * 512], in_=pwb[:, :]))
            k.wait('tensor', *wl1)
            k.wait('tensor', *wl2)
            hdone = 0
            sdone = {0: 0, 1: 0}
            ydone = {0: 0, 1: 0}
            st = {0: None, 1: None}
            mblk_t = [0] * NTT
            for mblk in range(NTT):
                par = mblk % 2
                if mblk >= 2:
                    k.wait('sync', 't', mblk_t[mblk - 2])
                xl = k.dma('sync', lambda y, mblk=mblk, par=par, xnc=xnc, scr=scr: y.dma_start(out=xnc[par][:, :], in_=xtile(scr['xn_hbm'], mblk)), slot=f'xn{par}')
                for jb in range(JB):
                    jpar = jb % 2
                    k.wait('tensor', *xl)
                    k.wait('tensor', 's', sdone[jpar])
                    for mb in range(PB - 1):
                        k.raw('tensor', lambda t, mb=mb, jb=jb, par=par, jpar=jpar, pg=pg, w1_sb=w1_sb, I=I, xnc=xnc: t.matmul(pg[jpar][:, :], w1_sb[:, mb * I + jb * 128:mb * I + jb * 128 + 128], xnc[par][:, mb * 512:(mb + 1) * 512], start=mb == 0, stop=False))
                    mb = PB - 1
                    mg = k.op('tensor', lambda t, mb=mb, jb=jb, par=par, jpar=jpar, pg=pg, w1_sb=w1_sb, I=I, xnc=xnc: t.matmul(pg[jpar][:, :], w1_sb[:, mb * I + jb * 128:mb * I + jb * 128 + 128], xnc[par][:, mb * 512:(mb + 1) * 512], start=False, stop=True))
                    k.wait('scalar', 't', mg)
                    sdone[jpar] = k.op('scalar', lambda s, jb=jb, jpar=jpar, b1c=b1c, h_sb=h_sb, pg=pg: s.activation(out=h_sb[:, jb * 512:(jb + 1) * 512], in_=pg[jpar][:, :], func=AF.Silu, bias=b1c[:, jb:jb + 1]))
                hdone = sdone[(JB - 1) % 2]
                for db in range(PB):
                    ypar = db % 2
                    k.wait('tensor', 's', sdone[0])
                    k.wait('tensor', 's', sdone[1])
                    k.wait('tensor', 'v', ydone[ypar])
                    for jb in range(JB - 1):
                        k.raw('tensor', lambda t, jb=jb, db=db, ypar=ypar, pyy=pyy, w2_sb=w2_sb, h_sb=h_sb: t.matmul(pyy[ypar][:, :], w2_sb[:, jb * D + db * 128:jb * D + db * 128 + 128], h_sb[:, jb * 512:(jb + 1) * 512], start=jb == 0, stop=False))
                    jb = JB - 1
                    my = k.op('tensor', lambda t, jb=jb, db=db, ypar=ypar, pyy=pyy, w2_sb=w2_sb, h_sb=h_sb: t.matmul(pyy[ypar][:, :], w2_sb[:, jb * D + db * 128:jb * D + db * 128 + 128], h_sb[:, jb * 512:(jb + 1) * 512], start=False, stop=True))
                    k.wait('vector', 't', my)
                    if st[ypar]:
                        k.wait('vector', *st[ypar])
                    ydone[ypar] = k.op('vector', lambda v, ypar=ypar, db=db, mblk=mblk, b2c=b2c, ytl=ytl, pyy=pyy, wbc=wbc: v.scalar_tensor_tensor(out=ytl[ypar][:], in0=pyy[ypar][:, :], scalar=b2c[:, db:db + 1], in1=wbc[:, mblk * 512:(mblk + 1) * 512], op0=OP.add, op1=OP.mult))
                    k.wait('gpsimd', 'v', ydone[ypar])
                    dst = xchunk(scr['acc_hbm'], db, mblk * 512)
                    if ei == 0:
                        st[ypar] = k.dma('gpsimd', lambda g, dst=dst, ypar=ypar, ytl=ytl: g.dma_start(out=dst, in_=ytl[ypar][:]), slot=f'ac{ypar}')
                    else:
                        st[ypar] = k.dma('gpsimd', lambda g, dst=dst, ypar=ypar, ytl=ytl: g.dma_start(out=dst, in_=ytl[ypar][:], accum_op=OP.add), slot=f'ac{ypar}')
                mblk_t[mblk] = k.mark('t')
                if os.environ.get('MOE_PARANOID', '0') == '1':
                    k.barrier()
            k.barrier()
        es_e.close()
        emit_add_hbm(k, x_hbm, scr['acc_hbm'], out_hbm, es)
        k.barrier()

def pmaj(a2d):
    d, ntok = a2d.shape
    a = np.ascontiguousarray(a2d).reshape(PB, 128, ntok // 512, 512)
    return np.ascontiguousarray(a.transpose(1, 2, 0, 3).reshape(128, PB * ntok))

def unpmaj(a, d, ntok):
    a4 = np.asarray(a).reshape(128, ntok // 512, PB, 512)
    return np.ascontiguousarray(a4.transpose(2, 0, 1, 3).reshape(d, ntok))

def col_layout(vec):
    d = vec.shape[-1]
    nb = d // 128
    return np.ascontiguousarray(vec.reshape(-1, nb, 128).transpose(2, 0, 1).reshape(128, -1))


def _host_consts():
    dim, theta = 64, 10000.0
    inv = 1.0 / theta ** (np.arange(0, dim, 2) / dim)
    pos = np.arange(T, dtype=np.float64)
    fr = pos[None, :] * np.concatenate([inv, inv])[:, None]
    cos = np.cos(fr)
    sin = np.sin(fr)
    sgn = np.where(np.arange(64) < 32, -1.0, 1.0)[:, None]
    sins = sgn * sin
    tab = np.concatenate([cos / 8.0, sins / 8.0, cos, sins], axis=1)
    rope = np.concatenate([tab, tab], axis=0).astype(np.float32)
    p = np.zeros((64, 64), np.float32)
    p[(np.arange(64) + 32) % 64, np.arange(64)] = 1.0
    z = np.zeros((64, 64), np.float32)
    p64 = np.block([[p, z], [z, p]]).astype(np.float32)
    m2 = np.zeros((128, 4 * 512), np.float32)
    for r in range(4):
        for pp_ in range(128):
            m2[pp_, r * 512:(r + 1) * 512] = (r * 128 + pp_) <= np.arange(512)
    glam = np.ones((128, T), np.float32)
    glam[:, ::64] = 0.0
    ltm = np.zeros((64, 64), np.float32)
    for r in range(64):
        ltm[r, r:] = 1.0
    return dict(rope=rope, p64=p64, m2=m2, glam=glam, ltm=ltm,
                ident=np.eye(128, dtype=np.float32))


def _stackT(w):
    return np.ascontiguousarray(np.concatenate([w[l].T for l in range(w.shape[0])], axis=0))


def _host_prep(inputs):
    tr = lambda a: np.ascontiguousarray(np.asarray(a).T)
    g = dict(_host_consts())
    g["a_wqT"] = tr(inputs["L0_attn_a_wq"])
    g["a_wkvT"] = tr(inputs["L0_attn_a_w_kv_down"])
    g["a_wkupT"] = tr(inputs["L0_attn_a_w_k_up"])
    g["a_wvupT"] = tr(inputs["L0_attn_a_w_v_up"])
    g["a_woT"] = tr(inputs["L0_attn_a_wo"])
    g["f0_gwT"] = tr(inputs["L0_ffn_gate_w"])
    g["f0_lncols"] = np.concatenate(
        [col_layout(np.asarray(inputs["L0_ffn_ln_scale"])),
         col_layout(np.asarray(inputs["L0_ffn_ln_bias"]))], axis=1)
    g["f0_sWg"] = np.ascontiguousarray(inputs["L0_ffn_sWg"])
    g["f0_sWu"] = np.ascontiguousarray(inputs["L0_ffn_sWu"])
    g["f0_sWd"] = np.ascontiguousarray(inputs["L0_ffn_sWd"])
    g["f0_Wg"] = np.ascontiguousarray(np.asarray(inputs["L0_ffn_Wg"]).reshape(8 * D, 2048))
    g["f0_Wu"] = np.ascontiguousarray(np.asarray(inputs["L0_ffn_Wu"]).reshape(8 * D, 2048))
    g["f0_Wd"] = np.ascontiguousarray(np.asarray(inputs["L0_ffn_Wd"]).reshape(8 * 2048, D))
    g["s_BT"] = tr(inputs["L0_attn_b_B"])
    g["s_CT"] = tr(inputs["L0_attn_b_C"])
    g["s_DT"] = tr(inputs["L0_attn_b_D"])
    g["s_diagA"] = col_layout(np.ascontiguousarray(np.diagonal(np.asarray(inputs["L0_attn_b_A"]))))
    g["r_wkT"] = _stackT(np.asarray(inputs["L1_attn_a_Wk"]))
    g["r_wvT"] = _stackT(np.asarray(inputs["L1_attn_a_Wv"]))
    g["r_wrT"] = _stackT(np.asarray(inputs["L1_attn_a_Wr"]))
    g["r_woT"] = _stackT(np.asarray(inputs["L1_attn_a_Wo"]))
    g["r_wckT"] = _stackT(np.asarray(inputs["L1_attn_a_Wck"]))
    g["r_wcvT"] = _stackT(np.asarray(inputs["L1_attn_a_Wcv"]))
    g["r_wcrT"] = _stackT(np.asarray(inputs["L1_attn_a_Wcr"]))
    rc = np.zeros((128, 368), np.float32)
    rc[:, 0:8] = col_layout(np.asarray(inputs["L1_attn_a_ln0_w"]))
    rc[:, 8:16] = col_layout(np.asarray(inputs["L1_attn_a_ln0_b"]))
    for base, nm in ((16, "ln1_w"), (48, "ln1_b"), (80, "ln2_w"), (112, "ln2_b"),
                     (144, "tmk"), (176, "tmv"), (208, "tmr"), (240, "cmk"),
                     (272, "cmr"), (304, "time_decay"), (336, "time_first")):
        rc[:, base:base + 32] = col_layout(np.asarray(inputs["L1_attn_a_" + nm]))
    g["r_cols"] = rc
    g["f1_rw1"] = np.ascontiguousarray(inputs["L1_ffn_rw1"])
    g["f1_rw2"] = np.ascontiguousarray(inputs["L1_ffn_rw2"])
    g["f1_rb1c"] = col_layout(np.asarray(inputs["L1_ffn_rb1"]))
    g["f1_rb2r"] = np.ascontiguousarray(np.broadcast_to(np.asarray(inputs["L1_ffn_rb2"]), (128, 8)))
    g["f1_b1c"] = col_layout(np.asarray(inputs["L1_ffn_b1"]))
    g["f1_b2c"] = col_layout(np.asarray(inputs["L1_ffn_b2"]))
    g["f1_w1"] = np.ascontiguousarray(np.asarray(inputs["L1_ffn_w1"]).reshape(8 * D, 4096))
    g["f1_w2"] = np.ascontiguousarray(np.asarray(inputs["L1_ffn_w2"]).reshape(8 * 4096, D))
    g["g_winT"] = tr(inputs["L1_attn_b_W_in"])
    g["g_wg1T"] = tr(inputs["L1_attn_b_W_g1"])
    g["g_wg2T"] = tr(inputs["L1_attn_b_W_g2"])
    g["g_woutT"] = tr(inputs["L1_attn_b_W_out"])
    g["g_lnw"] = col_layout(np.asarray(inputs["L1_attn_b_ln_w"]))
    return g


def build_full():
    nc = bass.Bass("TRN2", target_bir_lowering=False)
    ap = lambda name, shape: nc.dram_tensor(name, shape, F32, kind="ExternalInput").ap()
    x_in = ap("x_in", [128, PB * NT])
    names = {
        "a_wqT": [D, 1024], "a_wkvT": [D, 512], "a_wkupT": [512, 1024],
        "a_wvupT": [512, 1024], "a_woT": [1024, D],
        "rope": [128, 4 * T], "p64": [128, 128], "m2": [128, 4 * 512],
        "glam": [128, T], "ltm": [64, 64], "ident": [128, 128],
        "f0_gwT": [D, 8], "f0_lncols": [128, 2 * PB],
        "f0_sWg": [D, 2048], "f0_sWu": [D, 2048], "f0_sWd": [2048, D],
        "f0_Wg": [8 * D, 2048], "f0_Wu": [8 * D, 2048], "f0_Wd": [8 * 2048, D],
        "s_BT": [D, D], "s_CT": [D, D], "s_DT": [D, D], "s_diagA": [128, PB],
        "r_wkT": [4 * D, D], "r_wvT": [4 * D, D], "r_wrT": [4 * D, D],
        "r_woT": [4 * D, D], "r_wckT": [4 * D, 4096], "r_wcvT": [4 * 4096, D],
        "r_wcrT": [4 * D, D], "r_cols": [128, 368],
        "f1_rw1": [D, 4096], "f1_rw2": [4096, 8], "f1_rb1c": [128, 32],
        "f1_rb2r": [128, 8], "f1_b1c": [128, 8 * 32], "f1_b2c": [128, 8 * PB],
        "f1_w1": [8 * D, 4096], "f1_w2": [8 * 4096, D],
        "g_winT": [D, 3072], "g_wg1T": [D, 128], "g_wg2T": [128, D],
        "g_woutT": [D, D], "g_lnw": [128, PB],
    }
    A = {n: ap(n, sh) for n, sh in names.items()}
    y = nc.dram_tensor("y", [128, PB * NT], F32, kind="ExternalOutput").ap()
    xs = nc.dram_tensor("xs_scr", [128, PB * NT], F32).ap()
    xn_hbm = nc.dram_tensor("xn_scr", [128, PB * NT], BF16).ap()
    acc_hbm = nc.dram_tensor("acc_scr", [128, PB * NT], F32).ap()
    wrows_hbm = nc.dram_tensor("wrows_scr", [8, NT], BF16).ap()
    og_hbm = nc.dram_tensor("og_scr", [128, 8 * NT], BF16).ap()
    xr_hbm = nc.dram_tensor("xr_scr", [128, PB * NT], F32).ap()
    h_hbm = nc.dram_tensor("h_scr", [128, PB * (NT + 2)], BF16).ap()
    aux_hbm = nc.dram_tensor("aux_scr", [128, 16 * NT], BF16).ap()

    def wap(base_ap, off, rows, cols_):
        return hview(base_ap, off, [[cols_, 128], [128 * cols_, rows // 128], [1, cols_]])

    es = ExitStack()
    with es:
        k = K(nc, es)
        consts = emit_consts(k, es, dict(ident=A["ident"]))
        scr = dict(xn_hbm=xn_hbm, acc_hbm=acc_hbm, wrows_hbm=wrows_hbm,
                   og_hbm=og_hbm, xr_hbm=xr_hbm, h_hbm=h_hbm, aux_hbm=aux_hbm)
        emit_attn(k, x_in, xs, dict(wqT=A["a_wqT"], wkvT=A["a_wkvT"],
                  wkupT=A["a_wkupT"], wvupT=A["a_wvupT"], woT=A["a_woT"],
                  rope=A["rope"], p64=A["p64"], m2=A["m2"]), consts, scr)
        I0 = 2048
        f0_experts = [dict(shared=True, wg=wap(A["f0_sWg"], 0, D, I0),
                           wu=wap(A["f0_sWu"], 0, D, I0), wd=wap(A["f0_sWd"], 0, I0, D))]
        for e in range(8):
            f0_experts.append(dict(
                wg=wap(A["f0_Wg"], e * D * I0, D, I0),
                wu=wap(A["f0_Wu"], e * D * I0, D, I0),
                wd=wap(A["f0_Wd"], e * I0 * D, I0, D),
                wrow=hview(wrows_hbm, e * NT, [[NT, 1], [1, NT]])))
        W_f0 = dict(gwT=A["f0_gwT"], ln_cols=A["f0_lncols"], experts=f0_experts)
        emit_moe_swiglu(k, xs, xs, W_f0, consts, scr)
        emit_ssm(k, xs, xs, dict(BT=A["s_BT"], CT=A["s_CT"], DT=A["s_DT"],
                                 diagA=A["s_diagA"]), consts)
        emit_moe_swiglu(k, xs, xs, W_f0, consts, scr)
        emit_rwkv(k, xs, xs, dict(wkT=A["r_wkT"], wvT=A["r_wvT"], wrT=A["r_wrT"],
                                  woT=A["r_woT"], wckT=A["r_wckT"], wcvT=A["r_wcvT"],
                                  wcrT=A["r_wcrT"], cols=A["r_cols"]), consts, scr)
        I1 = 4096
        f1_experts = []
        for e in range(8):
            f1_experts.append(dict(
                w1=wap(A["f1_w1"], e * D * I1, D, I1),
                w2=wap(A["f1_w2"], e * I1 * D, I1, D),
                wrow=hview(wrows_hbm, e * NT, [[NT, 1], [1, NT]])))
        W_f1 = dict(rw1T=wap(A["f1_rw1"], 0, D, I1), rw2=A["f1_rw2"],
                    rb1_cols=A["f1_rb1c"], rb2_rep=A["f1_rb2r"],
                    b1_cols=A["f1_b1c"], b2_cols=A["f1_b2c"], experts=f1_experts)
        emit_moe_mlp(k, xs, xs, W_f1, consts, scr)
        emit_gla(k, xs, xs, dict(winT=A["g_winT"], wg1T=A["g_wg1T"],
                                 wg2T=A["g_wg2T"], woutT=A["g_woutT"],
                                 lnw_col=A["g_lnw"], glam=A["glam"], ltm=A["ltm"]),
                 consts, scr)
        emit_moe_mlp(k, xs, y, W_f1, consts, scr)
        with nc.Block() as block:
            k.finalize(block)
    return nc


def kernel(**inputs):
    g = _host_prep(inputs)
    x = np.asarray(inputs["L0_attn_a_x"], np.float32)
    in_maps = []
    for c in range(N_CORES):
        xc = np.ascontiguousarray(x[2 * c: 2 * c + 2].transpose(2, 0, 1).reshape(D, NT))
        m = dict(g)
        m["x_in"] = pmaj(xc)
        in_maps.append(m)
    nc = build_full()
    res = run_bass_kernel_spmd(nc, in_maps, core_ids=list(range(N_CORES)))
    outs = []
    for c in range(N_CORES):
        yc = unpmaj(np.asarray(res.results[c]["y"], np.float32), D, NT)
        outs.append(yc.reshape(D, BC, T).transpose(1, 2, 0))
    return np.ascontiguousarray(np.concatenate(outs, axis=0))
